# version 07
# baseline (speedup 1.0000x reference)
import numpy as np
import ml_dtypes
import concourse.bass as bass
import concourse.mybir as mybir
from concourse.bass_utils import run_bass_kernel_spmd

F32 = mybir.dt.float32
BF16 = mybir.dt.bfloat16
AF = mybir.ActivationFunctionType
ALU = mybir.AluOpType
AX = mybir.AxisListType.X

HID = 2048
S = 2048
LC = 256
NSLAB = 4
SLAB = 512
INTER = 4096
PROJ_P = 8576
NMC = 67
EPS_A = 1e-5
EPS_F = 1e-6
NHEAD = 16
SCALE = float(192.0 ** -0.5)
NE = 16
DI = 1408
SHI = 2816
BIG_NEG = -1.0e9

_BUILD_OPTS = {"dbg": None, "nslab": NSLAB, "ssd": True, "xg": True, "op": True, "inproj": True, "cache": True, "nmg": NMC, "heads": 64, "tpose": 3, "xl": True, "kv": True, "att": True, "moe": True}


class Sched:
    ENGINES = ("sync", "tensor", "vector", "scalar", "gpsimd")

    def __init__(self):
        self.q = {e: [] for e in self.ENGINES}
        self.sems = {}

    def add_sem(self, name, handle):
        self.sems[name] = [handle, 0]

    def n(self, name):
        return self.sems[name][1]

    AUTO_CHAIN = {"vector": "v", "scalar": "a"}

    LABELS = {}

    def op(self, eng, build, inc=(), wait=(), chain=True, self_inc=True, label=None):
        wait = list(wait)
        sem_self = self.AUTO_CHAIN.get(eng)
        if sem_self is not None:
            if chain and self.sems[sem_self][1] > 0:
                wait.append((sem_self, self.sems[sem_self][1]))
            if self_inc and not any(nm == sem_self for nm, _ in inc):
                inc = list(inc) + [(sem_self, 1)]
        toks = []
        for nm, amt in inc:
            self.sems[nm][1] += amt
            toks.append((nm, self.sems[nm][1]))
        wl = {}
        for nm, val in wait:
            if val > 0:
                wl[nm] = max(wl.get(nm, 0), val)
        wl = [(self.sems[nm][0], val) for nm, val in wl.items()]
        il = [(self.sems[nm][0], amt) for nm, amt in inc]

        def fn(e, build=build, wl=wl, il=il, label=label):
            for h, v in wl:
                e.wait_ge(h, v)
            inst = build(e)
            if label is not None and inst is not None:
                try:
                    Sched.LABELS[inst.ins.name] = label
                except AttributeError:
                    Sched.LABELS[str(inst.concise())[:40]] = label
            for h, amt in il:
                if inst is None:
                    e.nop().then_inc(h, amt)
                else:
                    inst = inst.then_inc(h, amt)
        self.q[eng].append(fn)
        return toks

    def wait(self, eng, *tokens):
        wl = [(self.sems[nm][0], val) for nm, val in tokens if val > 0]
        if not wl:
            return

        def fn(e, wl=wl):
            for h, v in wl:
                e.wait_ge(h, v)
        self.q[eng].append(fn)

    def aft(self, *names):
        return [(nm, self.n(nm)) for nm in names]

    ENG_DRAIN = {"sync": ("sx0", "sx1", "sw0", "sw1", "sv0", "sv1"),
                 "tensor": ("t",),
                 "vector": ("v",), "scalar": ("a",),
                 "gpsimd": ("gsp", "gbc", "gb0", "gb1", "gc0", "gc1", "pc0",
                             "pc1", "wb0", "wb1", "st", "p")}

    def barrier(self, engines=None):
        engines = engines or list(self.ENGINES)
        toks = []
        for e in engines:
            toks += self.op(e, lambda en: en.nop(), inc=[("bar", 1)],
                            wait=self.aft(*self.ENG_DRAIN[e]), chain=True,
                            self_inc=False)
        for e in engines:
            self.wait(e, *toks[-1:])


def _host_prep(inp):
    x = np.asarray(inp["L0_attn_hidden_states"], np.float32)
    ipw = np.asarray(inp["L0_attn_in_proj_w"], np.float32)
    ipb = np.asarray(inp["L0_attn_in_proj_b"], np.float32)
    convw = np.asarray(inp["L0_attn_conv_w"], np.float32)
    dtb = np.asarray(inp["L0_attn_dt_bias"], np.float32)
    alog = np.asarray(inp["L0_attn_A_log"], np.float32)
    Dv = np.asarray(inp["L0_attn_D"], np.float32)
    nw = np.asarray(inp["L0_attn_norm_weight"], np.float32)
    opw = np.asarray(inp["L0_attn_out_proj_w"], np.float32)
    opb = np.asarray(inp["L0_attn_out_proj_b"], np.float32)
    ln1 = np.asarray(inp["L0_ffn_ln1_w"], np.float32)
    ln2 = np.asarray(inp["L0_ffn_ln2_w"], np.float32)
    qw = np.asarray(inp["L0_ffn_q_w"], np.float32)
    kvw = np.asarray(inp["L0_ffn_kv_w"], np.float32)
    ow = np.asarray(inp["L0_ffn_o_w"], np.float32)
    gw = np.asarray(inp["L0_ffn_gate_w"], np.float32)
    w1 = np.asarray(inp["L0_ffn_w1"], np.float32)
    w2 = np.asarray(inp["L0_ffn_w2"], np.float32)
    shg = np.asarray(inp["L0_ffn_sh_gu_w"], np.float32)
    shd = np.asarray(inp["L0_ffn_sh_down_w"], np.float32)

    ipw_p = np.zeros((HID, PROJ_P), np.float32)
    ipw_p[:, :8512] = ipw
    ipb_p = np.zeros(PROJ_P, np.float32)
    ipb_p[:8512] = ipb

    q_live = qw.reshape(NHEAD, 192, HID)[:, :128, :].reshape(NHEAD * 128, HID)
    qwT = np.ascontiguousarray(q_live.T)
    kwT = np.ascontiguousarray(kvw[:2048].T)
    vwT = np.ascontiguousarray(kvw[2048:4096].T)
    owT = np.ascontiguousarray(ow.T)
    gwT = np.ascontiguousarray(gw.T)
    w1T = np.ascontiguousarray(np.transpose(w1, (0, 2, 1)))
    w2T = np.ascontiguousarray(np.transpose(w2, (0, 2, 1)))
    shgT = np.ascontiguousarray(shg.T)
    shdT = np.ascontiguousarray(shd.T)

    def cols_of(vec, n):
        v = np.zeros(n * 128, np.float32)
        v[:len(vec)] = vec
        return [v[i * 128:(i + 1) * 128] for i in range(n)]

    cols = []
    idx = {}

    def add(name, lst):
        idx[name] = len(cols)
        cols.extend(lst)

    add("inb", cols_of(ipb_p, NMC))
    conv_cols = []
    for mc in range(34):
        for k in range(4):
            conv_cols.append(np.ascontiguousarray(convw[mc * 128:(mc + 1) * 128, k]))
    add("convw", conv_cols)
    add("dtb", cols_of(dtb, 1))
    add("alog", cols_of(alog, 1))
    add("D", cols_of(np.repeat(Dv, 64), 32))
    add("nw", cols_of(nw, 32))
    add("ob", cols_of(opb, 16))
    add("l1", cols_of(ln1, 16))
    add("l2", cols_of(ln2, 16))
    add("ones", [np.ones(128, np.float32)])
    add("epsA", [np.full(128, EPS_A, np.float32)])
    add("epsF", [np.full(128, EPS_F, np.float32)])
    idx["m0"] = len(cols)
    idx["m1"] = len(cols) + 1
    idx["_nv"] = len(cols) + 2
    vecs_common = np.stack(cols, axis=1)

    ssdm = np.zeros((2, 128, 256), np.float32)
    for si in range(2):
        for p in range(128):
            ssdm[si, p, :si * 128 + p] = BIG_NEG
    ssdm_flat = np.ascontiguousarray(ssdm.reshape(256, 256))

    id_f32 = np.eye(128, dtype=np.float32)
    id_bf = np.eye(128, dtype=np.float32).astype(ml_dtypes.bfloat16)

    in_maps = []
    for c in range(8):
        b, r = c // 2, c % 2
        m0c = np.full((128, 1), 1.0 - r, np.float32)
        m1c = np.full((128, 1), float(r), np.float32)
        vecs = np.concatenate([vecs_common, m0c, m1c], axis=1)
        kt = np.arange(S)[:, None]
        qt = np.arange(1024)[None, :] + r * 1024
        m01 = (kt <= qt).astype(np.float32).astype(ml_dtypes.bfloat16)
        in_maps.append({
            "x0T": np.ascontiguousarray(x[b].T), "vecs": np.ascontiguousarray(vecs),
            "ssdm": ssdm_flat, "id_f32": id_f32, "id_bf": id_bf,
            "m01": np.ascontiguousarray(m01),
            "ipw": ipw_p, "opw": opw,
            "qwT": qwT, "kwT": kwT, "vwT": vwT, "owT": owT, "gwT": gwT,
            "w1T": w1T, "w2T": w2T, "shgT": shgT, "shdT": shdT,
        })
    return in_maps, idx


def _build(idx):
    dbg = _BUILD_OPTS["dbg"]
    nc = bass.Bass("TRN2", target_bir_lowering=False, debug=False)
    S_ = Sched()

    x0T = nc.dram_tensor("x0T", [HID, S], F32, kind="ExternalInput").ap()
    vecs_d = nc.dram_tensor("vecs", [128, idx["_nv"]], F32, kind="ExternalInput").ap()
    ssdm_d = nc.dram_tensor("ssdm", [256, 256], F32, kind="ExternalInput").ap()
    idf_d = nc.dram_tensor("id_f32", [128, 128], F32, kind="ExternalInput").ap()
    idb_d = nc.dram_tensor("id_bf", [128, 128], BF16, kind="ExternalInput").ap()
    m01_d = nc.dram_tensor("m01", [S, 1024], BF16, kind="ExternalInput").ap()
    ipw_d = nc.dram_tensor("ipw", [HID, PROJ_P], F32, kind="ExternalInput").ap()
    opw_d = nc.dram_tensor("opw", [INTER, HID], F32, kind="ExternalInput").ap()
    qwT_d = nc.dram_tensor("qwT", [HID, 2048], F32, kind="ExternalInput").ap()
    kwT_d = nc.dram_tensor("kwT", [HID, 2048], F32, kind="ExternalInput").ap()
    vwT_d = nc.dram_tensor("vwT", [HID, 2048], F32, kind="ExternalInput").ap()
    owT_d = nc.dram_tensor("owT", [2048, 2048], F32, kind="ExternalInput").ap()
    gwT_d = nc.dram_tensor("gwT", [HID, 16], F32, kind="ExternalInput").ap()
    w1T_d = nc.dram_tensor("w1T", [NE, HID, 2816], F32, kind="ExternalInput").ap()
    w2T_d = nc.dram_tensor("w2T", [NE, DI, 2048], F32, kind="ExternalInput").ap()
    shgT_d = nc.dram_tensor("shgT", [HID, 2 * SHI], F32, kind="ExternalInput").ap()
    shdT_d = nc.dram_tensor("shdT", [SHI, HID], F32, kind="ExternalInput").ap()
    out_d = nc.dram_tensor("out", [HID, 1024], F32, kind="ExternalOutput").ap()

    gsil_unused = None
    ipwbf_d = nc.dram_tensor("ipwbf", [HID, PROJ_P], BF16).ap()
    cumsp_d = nc.dram_tensor("cumsp", [8 * 64 * 256], F32).ap()
    cssp_d = nc.dram_tensor("cssp", [8 * 64], F32).ap()
    prex1_d = nc.dram_tensor("prex1", [HID, S], F32).ap()
    x1sp_d = nc.dram_tensor("x1sp", [HID, S], F32).ap()
    rows_d = nc.dram_tensor("rows", [64 * 1024], F32).ap()
    kspill_d = nc.dram_tensor("kspill", [2048, S], BF16).ap()
    vspill_d = nc.dram_tensor("vspill", [S, 2048], BF16).ap()
    x1own_d = nc.dram_tensor("x1own", [HID, 1024], F32).ap()
    hidsp_d = nc.dram_tensor("hidsp", [HID, 1024], F32).ap()
    wfsp_d = nc.dram_tensor("wfsp", [NE * 1024], F32).ap()
    w1c_d = nc.dram_tensor("w1c", [NE, HID, 2816], BF16).ap()
    w2c_d = nc.dram_tensor("w2c", [NE, DI, 2048], BF16).ap()
    shgc_d = nc.dram_tensor("shgc", [HID, 2 * SHI], BF16).ap()
    shdc_d = nc.dram_tensor("shdc", [SHI, HID], BF16).ap()
    dbg_d = None
    if dbg is not None:
        dbg_d = nc.dram_tensor("dbg", list(dbg[1]), F32, kind="ExternalOutput").ap()

    from contextlib import ExitStack
    with ExitStack() as stack:
        ent = stack.enter_context
        for snm in ("sx0", "sx1", "sw0", "sw1", "sv0", "sv1", "gsp", "gbc",
                    "gb0", "gb1", "gc0", "gc1", "pc0", "pc1",
                    "wb0", "wb1", "t", "v", "a", "p", "st", "bar"):
            S_.add_sem(snm, ent(nc.semaphore(snm)))

        vecs = ent(nc.sbuf_tensor("vecs_s", [128, idx["_nv"]], F32)).ap()
        idf = ent(nc.sbuf_tensor("idf", [128, 128], F32)).ap()
        idb = ent(nc.sbuf_tensor("idb", [128, 128], BF16)).ap()

        onesb = ent(nc.sbuf_tensor("onesb", [128, 1], BF16)).ap()
        expA = ent(nc.sbuf_tensor("expA", [128, 1], F32)).ap()
        pb = [ent(nc.psum_tensor(f"pb{i}", [128, 512], F32)).ap() for i in range(8)]
        rowb = ent(nc.sbuf_tensor("rowb", [1, 512], F32)).ap()
        rowb2 = ent(nc.sbuf_tensor("rowb2", [1, 512], F32)).ap()
        bcb = [ent(nc.sbuf_tensor(f"bcb{i}", [128, 512], F32)).ap() for i in range(2)]
        mxk = ent(nc.sbuf_tensor("mxk", [1, 64], F32)).ap()
        pbf = [p.bitcast(BF16) for p in pb]

        def vcol(name, j=0):
            return vecs[:, idx[name] + j: idx[name] + j + 1]

        S_.op("sync", lambda e: e.dma_start(out=vecs, in_=vecs_d), inc=[("sx0", 16)])
        S_.op("sync", lambda e: e.dma_start(out=idf, in_=idf_d), inc=[("sx0", 16)])
        S_.op("sync", lambda e: e.dma_start(out=idb, in_=idb_d), inc=[("sx0", 16)])
        S_.op("sync", lambda e: e.dma_start(
            out=ssdm.rearrange("p (a c) -> p a c", a=2),
            in_=ssdm_d.rearrange("(a b) c -> b a c", a=2)),
            inc=[("sx0", 16)])
        pre_tok = S_.aft("sx0")
        S_.op("vector", lambda e: e.tensor_copy(onesb, vcol("ones")),
              inc=[("v", 1)], wait=pre_tok)
        S_.op("scalar", lambda e: e.activation(expA, vcol("alog"), AF.Exp),
              inc=[("a", 1)], wait=pre_tok)

        def dbg_tap(name, srcs):
            if dbg is None or dbg[0] != name:
                return
            S_.barrier()
            for (r0, r1), src in srcs:
                S_.op("gpsimd", lambda e, r0=r0, r1=r1, src=src: e.dma_start(
                    out=dbg_d[r0:r1, :src.shape[1]], in_=src), inc=[("st", 16)])
            S_.barrier()

        a_sc = ExitStack()
        aent = a_sc.enter_context
        ssdm = aent(nc.sbuf_tensor("ssdm_s", [128, 512], F32)).ap()
        x0bf = [aent(nc.sbuf_tensor(f"x0bf{i}", [128, SLAB], BF16)).ap() for i in range(16)]
        x0st = [aent(nc.sbuf_tensor(f"x0st{i}", [128, SLAB], F32)).ap() for i in range(2)]
        wbf = [aent(nc.sbuf_tensor(f"wbf{i}", [128, 2048], BF16)).ap() for i in range(2)]
        hbcb = [aent(nc.sbuf_tensor(f"hbcb{i}", [128, 515], F32)).ap() for i in range(2)]
        cvb = [aent(nc.sbuf_tensor(f"cvb{i}", [128, 512], F32)).ap() for i in range(2)]
        stash = aent(nc.sbuf_tensor("stash", [128, 102], F32)).ap()
        hs = [aent(nc.sbuf_tensor(f"hs{i}", [128, SLAB], BF16)).ap() for i in range(32)]
        Bsl = aent(nc.sbuf_tensor("Bsl", [128, SLAB], BF16)).ap()
        Csl = aent(nc.sbuf_tensor("Csl", [128, SLAB], BF16)).ap()
        dtsl = aent(nc.sbuf_tensor("dtsl", [128, SLAB], F32)).ap()
        dstt = aent(nc.sbuf_tensor("dstt", [128, SLAB], F32)).ap()
        abuf = aent(nc.sbuf_tensor("abuf", [64, 256], F32)).ap()
        cumc = aent(nc.sbuf_tensor("cumc", [128, 512], F32)).ap()
        dcay = aent(nc.sbuf_tensor("dcay", [128, 256], F32)).ap()
        ncT = aent(nc.sbuf_tensor("ncT", [128, 128], F32)).ap()
        ecT = aent(nc.sbuf_tensor("ecT", [128, 128], F32)).ap()
        dtT = aent(nc.sbuf_tensor("dtT", [128, 128], F32)).ap()
        dyT = aent(nc.sbuf_tensor("dyT", [128, 128], F32)).ap()
        Btm = aent(nc.sbuf_tensor("Btm", [128, 256], BF16)).ap()
        bdec = aent(nc.sbuf_tensor("bdec", [128, 512], BF16)).ap()
        Ee = aent(nc.sbuf_tensor("Ee", [128, 64], F32)).ap()
        Sst = aent(nc.sbuf_tensor("Sst", [128, 4096], F32)).ap()
        Sbf = aent(nc.sbuf_tensor("Sbf", [128, 4096], BF16)).ap()
        RB = aent(nc.sbuf_tensor("RB", [128, 2048], F32)).ap()
        tb1 = aent(nc.sbuf_tensor("tb1", [128, 256], F32)).ap()
        tb2 = aent(nc.sbuf_tensor("tb2", [128, 256], F32)).ap()
        mtb = aent(nc.sbuf_tensor("mtb", [128, 1024], BF16)).ap()
        yev = aent(nc.sbuf_tensor("yev", [128, 64], F32)).ap()
        xg = [aent(nc.sbuf_tensor(f"xg{i}", [128, SLAB], BF16)).ap() for i in range(32)]
        gsl = xg
        sqb = [aent(nc.sbuf_tensor(f"sqb{i}", [128, SLAB], BF16)).ap() for i in range(2)]
        prb = [aent(nc.sbuf_tensor(f"prb{i}", [128, SLAB], F32)).ap() for i in range(2)]
        sqf = [aent(nc.sbuf_tensor(f"sqf{i}", [128, SLAB], F32)).ap() for i in range(2)]
        A_un = aent(nc.sbuf_tensor("A_un", [128, 8192], F32)).ap()
        wstg2 = [A_un[:, 0:2048], A_un[:, 2048:4096]]
        hsdt = [A_un[:, 0:2048].bitcast(BF16), A_un[:, 2048:4096].bitcast(BF16)]
        ytm = [A_un[:, 4096:6144].bitcast(BF16), A_un[:, 6144:8192].bitcast(BF16)]
        opst = A_un[:, 0:4096]
        opbf = [A_un[:, 4096:6144].bitcast(BF16), A_un[:, 6144:8192].bitcast(BF16)]

        S_.op("vector", lambda e: e.memset(stash, 0.0), inc=[("v", 1)])
        S_.op("vector", lambda e: e.memset(dtsl, 0.0), inc=[("v", 1)])
        S_.op("vector", lambda e: e.memset(cumc, 0.0), inc=[("v", 1)])
        S_.op("vector", lambda e: e.memset(dcay, 0.0), inc=[("v", 1)])
        S_.op("vector", lambda e: e.memset(Sst, 0.0), inc=[("v", 1)])
        S_.op("vector", lambda e: e.memset(Sbf, 0.0), inc=[("v", 1)])

        for slab in range(_BUILD_OPTS["nslab"]):
            t0 = slab * SLAB
            cast_tok = {}
            for kc in range(16):
                w = (S_.aft("a", "v") if kc < 2 else []) + cast_tok.get(kc % 2, [])
                tok = S_.op("sync", lambda e, kc=kc, t0=t0: e.dma_start(
                    out=x0st[kc % 2], in_=x0T[kc * 128:(kc + 1) * 128, t0:t0 + SLAB]),
                    inc=[(f"sx{kc % 2}", 16)], wait=w)
                cast_tok[kc % 2] = S_.op("scalar", lambda e, kc=kc: e.activation(
                    x0bf[kc], x0st[kc % 2], AF.Copy), inc=[("a", 1)],
                    wait=list(tok) + (S_.aft("t") if kc == 0 else []))
            x0_done = S_.aft("a")

            last_cast = {}
            cache_tok = {}
            mm_war = {}
            bank_ev = {}
            for mg in range((min(_BUILD_OPTS["nmg"], NMC)) if _BUILD_OPTS["inproj"] else 0):
                m0 = mg
                nmc_g = 1
                wcols = 128
                buf = wbf[mg % 2]
                war = mm_war.get(mg % 2, S_.aft("t") if slab > 0 else [])
                if slab == 0:
                    wsb = wstg2[mg % 2]
                    ldw = []
                    for kh in range(2):
                        ldw += S_.op("sync", lambda e, kh=kh, m0=m0, wcols=wcols,
                                     wsb=wsb: e.dma_start(
                            out=wsb[:, kh * 8 * wcols:(kh + 1) * 8 * wcols]
                                .rearrange("p (k c) -> p k c", k=8),
                            in_=ipw_d[kh * 1024:(kh + 1) * 1024, m0 * 128:m0 * 128 + wcols]
                                .rearrange("(k p) c -> p k c", p=128)),
                            inc=[(f"sw{mg % 2}", 16)],
                            wait=(list(war) + last_cast.get(mg % 2, []))
                            if kh == 0 else ())
                    wtok = S_.op("vector", lambda e, buf=buf, wcols=wcols, wsb=wsb:
                                 e.tensor_copy(buf, wsb),
                                 inc=[("v", 1)],
                                 wait=ldw[-1:] + cache_tok.get(mg % 2, [])
                                 + list(war))
                    last_cast[mg % 2] = wtok
                    if _BUILD_OPTS["cache"]:
                        cache_tok[mg % 2] = S_.op(
                            "gpsimd", lambda e, m0=m0, buf=buf, wcols=wcols: e.dma_start(
                                out=ipwbf_d[:, m0 * 128:m0 * 128 + wcols]
                                    .rearrange("(k p) c -> p k c", p=128),
                                in_=buf[:, :16 * wcols].rearrange("p (k c) -> p k c", k=16)),
                            inc=[(f"gc{mg % 2}", 16)], wait=wtok)
                else:
                    wtok = S_.op("sync", lambda e, m0=m0, buf=buf, wcols=wcols: e.dma_start(
                        out=buf[:, :16 * wcols].rearrange("p (k c) -> p k c", k=16),
                        in_=ipwbf_d[:, m0 * 128:m0 * 128 + wcols]
                            .rearrange("(k p) c -> p k c", p=128)),
                        inc=[(f"sw{mg % 2}", 16)],
                        wait=list(war) + S_.aft("gc0", "gc1"))
                for mi in range(nmc_g):
                    m = m0 + mi
                    bank = pb[4 + m % 4]
                    mm_tok = []
                    for k in range(16):
                        first = (k == 0)
                        mm_tok = S_.op("tensor", lambda e, k=k, mi=mi, bank=bank, buf=buf,
                                       wcols=wcols: e.matmul(
                            bank[:, :SLAB],
                            buf[:, k * wcols + mi * 128: k * wcols + (mi + 1) * 128],
                            x0bf[k], start=(k == 0), stop=(k == 15)),
                            inc=[("t", 1)] if k == 15 else (),
                            wait=(list(wtok) + x0_done
                                  + bank_ev.get(m % 4, [])) if first else ())
                    mm_war[mg % 2] = mm_tok
                    if m < 32:
                        bank_ev[m % 4] = S_.op(
                            "scalar", lambda e, bank=bank, m=m: e.activation(
                                gsl[m], bank[:, :SLAB], AF.Silu, bias=vcol("inb", m)),
                            inc=[("a", 1)], wait=mm_tok + S_.aft("v", "t"))
                    elif m < 66:
                        j = m - 32
                        hb = hbcb[j % 2]
                        cb = cvb[j % 2]
                        stok = S_.op("vector", lambda e, hb=hb, j=j: e.tensor_copy(
                            hb[:, 0:3], stash[:, j * 3:(j + 1) * 3]),
                            inc=[("v", 1)], wait=S_.aft("a"))
                        ev = S_.op("scalar", lambda e, bank=bank, hb=hb, m=m: e.activation(
                            hb[:, 3:515], bank[:, :SLAB], AF.Identity, bias=vcol("inb", m)),
                            inc=[("a", 1)], wait=mm_tok + stok)
                        bank_ev[m % 4] = ev
                        S_.op("vector", lambda e, hb=hb, j=j: e.tensor_copy(
                            stash[:, j * 3:(j + 1) * 3], hb[:, 512:515]),
                            inc=[("v", 1)], wait=ev)
                        cv = S_.op("vector", lambda e, hb=hb, cb=cb, j=j: e.tensor_scalar(
                            cb, hb[:, 0:512], vcol("convw", j * 4), None, ALU.mult),
                            inc=[("v", 1)], wait=())
                        for kk in range(1, 4):
                            cv = S_.op("vector", lambda e, hb=hb, cb=cb, j=j, kk=kk:
                                       e.scalar_tensor_tensor(
                                cb, hb[:, kk:512 + kk], vcol("convw", j * 4 + kk), cb,
                                ALU.mult, ALU.add), inc=[("v", 1)], wait=())
                        dst = hs[j] if j < 32 else (Bsl if j == 32 else Csl)
                        S_.op("scalar", lambda e, cb=cb, dst=dst: e.activation(
                            dst, cb, AF.Silu), inc=[("a", 1)], wait=cv)
                    else:
                        ev = S_.op("scalar", lambda e, bank=bank: e.activation(
                            dstt, bank[:, :SLAB], AF.Identity, bias=vcol("inb", 66)),
                            inc=[("a", 1)], wait=mm_tok + S_.aft("v"))
                        bank_ev[m % 4] = ev
                        z = dstt[0:64, :]
                        S_.op("vector", lambda e, z=z: e.tensor_scalar(
                            z, z, vcol("dtb")[0:64, :], None, ALU.add),
                            inc=[("v", 1)], wait=ev)
                        S_.op("scalar", lambda e, z=z: e.activation(
                            dtsl[0:64, :], z, AF.Abs), inc=[("a", 1)], wait=S_.aft("v"))
                        S_.op("scalar", lambda e: e.activation(
                            dtsl[0:64, :], dtsl[0:64, :], AF.Exp, scale=-1.0),
                            inc=[("a", 1)], wait=())
                        S_.op("scalar", lambda e: e.activation(
                            dtsl[0:64, :], dtsl[0:64, :], AF.Ln,
                            bias=vcol("ones")[0:64, :]), inc=[("a", 1)], wait=())
                        S_.op("vector", lambda e, z=z: e.tensor_scalar(
                            z, z, 0.0, None, ALU.max), inc=[("v", 1)],
                            wait=S_.aft("a"))
                        S_.op("vector", lambda e, z=z: e.tensor_add(
                            dtsl[0:64, :], dtsl[0:64, :], z),
                            inc=[("v", 1)], wait=S_.aft("a"))

            dbg_tap("convout", [((j * 128, (j + 1) * 128),
                                 hs[j] if j < 32 else (Bsl if j == 32 else Csl))
                                for j in range(34)] if slab == 0 else [])
            dbg_tap("dt", [((0, 64), dtsl[0:64, :])] if slab == 0 else [])
            dbg_tap("gate", [((j * 128, (j + 1) * 128), gsl[j])
                             for j in range(32)] if slab == 0 else [])

            if _BUILD_OPTS["ssd"]:
                for ci in range(2):
                    c = slab * 2 + ci
                    off = ci * 256
                    pre = S_.aft("a", "v")
                    S_.op("vector", lambda e, off=off: e.tensor_scalar(
                        abuf, dtsl[0:64, off:off + 256], expA[0:64, :], -1.0,
                        ALU.mult, ALU.mult), inc=[("v", 1)], wait=pre)
                    S_.op("vector", lambda e, off=off: e.tensor_tensor_scan(
                        cumc[0:64, off:off + 256], abuf, abuf, 0.0,
                        ALU.add, ALU.bypass), inc=[("v", 1)], wait=())
                    cum_tok = S_.aft("v")
                    S_.op("gpsimd", lambda e, c=c, off=off: e.dma_start(
                        out=cumsp_d[c * 16384:(c + 1) * 16384]
                            .rearrange("(h t) -> h t", h=64),
                        in_=cumc[0:64, off:off + 256]),
                        inc=[("gsp", 16)], wait=cum_tok)
                    S_.op("gpsimd", lambda e, c=c, off=off: e.dma_start(
                        out=cssp_d[c * 64:(c + 1) * 64].unsqueeze(1),
                        in_=cumc[0:64, off + 255:off + 256]),
                        inc=[("gsp", 16)], wait=())
            cumsp_tok = S_.aft("gsp")
            for ci in range(2 if _BUILD_OPTS["ssd"] else 0):
                c = slab * 2 + ci
                off = ci * 256
                pre = S_.aft("a", "v")
                eb = S_.op("gpsimd", lambda e, c=c: e.dma_start(
                    out=Ee[:, 0:64],
                    in_=cssp_d[c * 64:(c + 1) * 64][None, :].to_broadcast([128, 64])),
                    inc=[("gsp", 16)], wait=cumsp_tok + S_.aft("a", "v")),
                S_.op("scalar", lambda e: e.activation(Ee, Ee, AF.Exp),
                      inc=[("a", 1)], wait=eb[0])
                S_.op("scalar", lambda e, off=off: e.activation(
                    dcay[0:64, :], cumc[0:64, off:off + 256], AF.Exp,
                    bias=cumc[0:64, off + 255:off + 256], scale=-1.0),
                    inc=[("a", 1)], wait=())
                dcay_tok = S_.aft("a")
                ev_tok = {}
                for nm, srcf, dst, extra in (
                        ("cum", lambda i, off=off: cumc[:, off + i * 128:
                                                        off + (i + 1) * 128],
                         None, True),
                        ("dt", lambda i, off=off: dtsl[:, off + i * 128:
                                                       off + (i + 1) * 128],
                         dtT, False),
                        ("dy", lambda i: dcay[:, i * 128:(i + 1) * 128], dyT, False)):
                    for i in range(2):
                        bank = pb[2 + i]
                        tt = S_.op("tensor", lambda e, srcf=srcf, i=i, bank=bank:
                                   e.transpose(bank[:, 0:128], srcf(i), idf),
                                   inc=[("t", 1)],
                                   wait=(dcay_tok if nm == "dy" else cum_tok)
                                   + ev_tok.get(("e", i), S_.aft("v", "a", "p")))
                        if nm == "cum":
                            e1 = S_.op("vector", lambda e, i=i, bank=bank:
                                       e.tensor_scalar(
                                ncT[:, i * 64:(i + 1) * 64], bank[:, 0:64],
                                -1.0, None, ALU.mult), inc=[("v", 1)], wait=tt,
                                label=("ncTevac", slab, ci, i))
                            e2 = S_.op("scalar", lambda e, i=i, bank=bank:
                                       e.activation(
                                ecT[:, i * 64:(i + 1) * 64], bank[:, 0:64],
                                AF.Exp), inc=[("a", 1)], wait=tt + e1)
                            ev_tok[("e", i)] = e1 + e2
                        else:
                            e1 = S_.op("vector", lambda e, i=i, bank=bank, dst=dst:
                                       e.tensor_copy(
                                dst[:, i * 64:(i + 1) * 64], bank[:, 0:64]),
                                inc=[("v", 1)], wait=tt)
                            ev_tok[("e", i)] = e1
                for i in range(2):
                    bank = pbf[2 + i]
                    tt = S_.op("tensor", lambda e, i=i, off=off, bank=bank:
                               e.transpose(
                        bank[:, 0:128], Bsl[:, off + i * 128: off + (i + 1) * 128],
                        idb), inc=[("t", 1)], wait=ev_tok.get(("e", i), []) + pre)
                    ev_tok[("e", i)] = S_.op("vector", lambda e, i=i, bank=bank:
                                             e.tensor_copy(
                        Btm[:, i * 128:(i + 1) * 128], bank[:, 0:128]),
                        inc=[("v", 1)], wait=tt)
                hs_ev = {}
                for j in range(32):
                    for li in range(2):
                        par = (j * 2 + li) % 2
                        bank = pbf[4 + par]
                        tt = S_.op("tensor", lambda e, j=j, li=li, off=off,
                                   bank=bank: e.transpose(
                            bank[:, 0:128],
                            hs[j][:, off + li * 128: off + (li + 1) * 128], idb),
                            inc=[("t", 1)], wait=hs_ev.get(par, ev_tok.get(("e", li), [])))
                        ee = []
                        for hh in range(2):
                            ee += S_.op("vector", lambda e, j=j, li=li, hh=hh,
                                        bank=bank: e.tensor_scalar(
                                hsdt[li][:, j * 128 + hh * 64: j * 128 + (hh + 1) * 64],
                                bank[:, hh * 64:(hh + 1) * 64],
                                dtT[:, li * 64 + 2 * j + hh: li * 64 + 2 * j + hh + 1],
                                None, ALU.mult), inc=[("v", 1)],
                                wait=tt if hh == 0 else ())
                        hs_ev[par] = ee
                gm_tok = []
                for si in range(2):
                    gm_tok += S_.op("tensor", lambda e, si=si, off=off: e.matmul(
                        pb[si][:, 0:256], Bsl[:, off + si * 128: off + (si + 1) * 128],
                        Csl[:, off:off + 256], start=True, stop=True),
                        inc=[("t", 1)], wait=S_.aft("v", "a") + pre)
                hb_tok = {}
                last_mult = []
                mt_done = {}
                rb_tok = []

                def emit_mt(h):
                    nonlocal rb_tok, last_mult
                    if h % 8 == 0:
                        rb_tok = S_.op("gpsimd", lambda e, c=c, h=h: e.dma_start(
                            out=RB,
                            in_=cumsp_d[c * 16384 + h * 256: c * 16384 + (h + 8) * 256]
                                [None, :].to_broadcast([128, 2048])),
                            inc=[(f"gb{(c * 8 + h // 8) % 2}", 16)],
                            label=("rb", slab, ci, h),
                            wait=(cumsp_tok if h == 0 else []) + S_.aft("v", "p"))
                    ro = (h % 8) * 256
                    mt_tok = []
                    for si in range(2):
                        s1 = S_.op("vector", lambda e, si=si, h=h, ro=ro:
                                   e.scalar_tensor_tensor(
                            tb1, RB[:, ro:ro + 256],
                            ncT[:, si * 64 + h: si * 64 + h + 1],
                            ssdm[:, si * 256:(si + 1) * 256], ALU.add, ALU.add),
                            label=("s1", slab, ci, h, si),
                            inc=[("v", 1)],
                            wait=list(rb_tok)
                            + (gm_tok if si == 0 and h == 0 else [])
                            + S_.aft("a"))
                        s2 = S_.op("scalar", lambda e: e.activation(
                            tb2, tb1, AF.Exp), inc=[("a", 1)],
                            wait=s1 + last_mult,
                            label=("s2", slab, ci, h, si))
                        last_mult = S_.op("vector", lambda e, si=si, h=h: e.tensor_mul(
                            mtb[:, (h % 2) * 512 + si * 256:
                                (h % 2) * 512 + (si + 1) * 256],
                            tb2, pb[si][:, 0:256]),
                            inc=[("v", 1)],
                            wait=s2 + (hb_tok.get(("ymm", h % 2), [])
                                       if si == 0 else []))
                        mt_tok += last_mult
                    bd_tok = []
                    for si in range(2):
                        bd_tok += S_.op("vector", lambda e, si=si, h=h: e.tensor_scalar(
                            bdec[:, (h % 2) * 256 + si * 128:
                                 (h % 2) * 256 + (si + 1) * 128],
                            Btm[:, si * 128:(si + 1) * 128],
                            dyT[:, si * 64 + h: si * 64 + h + 1], None, ALU.mult),
                            inc=[("v", 1)],
                            wait=hb_tok.get(("lmm", h % 2), []) if si == 0 else ())
                    mt_done[h] = (mt_tok, bd_tok)

                emit_mt(0)
                for h in range(64):
                    mt_tok, bd_tok = mt_done.pop(h)
                    yb = pb[6 + h % 2]
                    lb = pb[4 + h % 2]
                    ymm = []
                    for li in range(2):
                        for si in range(2):
                            ymm = S_.op("tensor", lambda e, li=li, si=si, h=h, yb=yb:
                                        e.matmul(
                                yb[:, li * 128: li * 128 + 64],
                                mtb[:, (h % 2) * 512 + si * 256 + li * 128:
                                    (h % 2) * 512 + si * 256 + (li + 1) * 128],
                                hsdt[si][:, h * 64:(h + 1) * 64],
                                start=(si == 0), stop=(si == 1)),
                                inc=[("t", 1)] if si == 1 else (),
                                wait=(mt_tok + hb_tok.get(("yev", h % 2), []))
                                if (li == 0 and si == 0) else ())
                        if c > 0:
                            ymm = S_.op("tensor", lambda e, li=li, h=h, off=off,
                                        yb=yb: e.matmul(
                                yb[:, li * 128 + 64: li * 128 + 128],
                                Csl[:, off + li * 128: off + (li + 1) * 128],
                                Sbf[:, h * 64:(h + 1) * 64],
                                start=True, stop=True), inc=[("t", 1)], wait=())
                    hb_tok[("ymm", h % 2)] = ymm
                    lmm = []
                    for li in range(2):
                        lmm = S_.op("tensor", lambda e, li=li, h=h, lb=lb: e.matmul(
                            lb[:, 0:64],
                            bdec[:, (h % 2) * 256 + li * 128:
                                 (h % 2) * 256 + (li + 1) * 128],
                            hsdt[li][:, h * 64:(h + 1) * 64],
                            start=(li == 0), stop=(li == 1)),
                            inc=[("t", 1)] if li == 1 else (),
                            wait=(bd_tok + hb_tok.get(("sup", h % 2), []))
                            if li == 0 else ())
                    hb_tok[("lmm", h % 2)] = lmm
                    if h + 1 < 64:
                        emit_mt(h + 1)
                    yev_t = []
                    for li in range(2):
                        if c > 0:
                            ye = S_.op("vector", lambda e, li=li, h=h, yb=yb:
                                       e.tensor_scalar(
                                yev, yb[:, li * 128 + 64: li * 128 + 128],
                                ecT[:, li * 64 + h: li * 64 + h + 1], None, ALU.mult),
                                inc=[("v", 1)], wait=ymm)
                            yev_t += S_.op("vector", lambda e, li=li, h=h, yb=yb:
                                           e.tensor_add(
                                ytm[li][:, h * 64:(h + 1) * 64], yev,
                                yb[:, li * 128: li * 128 + 64]),
                                inc=[("v", 1)], wait=())
                        else:
                            yev_t += S_.op("vector", lambda e, li=li, h=h, yb=yb:
                                           e.tensor_copy(
                                ytm[li][:, h * 64:(h + 1) * 64],
                                yb[:, li * 128: li * 128 + 64]),
                                inc=[("v", 1)], wait=ymm)
                    hb_tok[("yev", h % 2)] = yev_t
                    su = S_.op("vector", lambda e, h=h: e.tensor_scalar(
                        Sst[:, h * 64:(h + 1) * 64], Sst[:, h * 64:(h + 1) * 64],
                        Ee[:, h:h + 1], None, ALU.mult),
                        inc=[("v", 1)], wait=lmm)
                    su = S_.op("vector", lambda e, h=h, lb=lb: e.tensor_add(
                        Sst[:, h * 64:(h + 1) * 64], Sst[:, h * 64:(h + 1) * 64],
                        lb[:, 0:64]), inc=[("v", 1)], wait=())
                    su = S_.op("vector", lambda e, h=h: e.tensor_copy(
                        Sbf[:, h * 64:(h + 1) * 64], Sst[:, h * 64:(h + 1) * 64]),
                        inc=[("v", 1)], wait=())
                    hb_tok[("sup", h % 2)] = su

                dbg_tap("cum", [((0, 64), cumc[0:64, 0:256])] if c == 0 else [])
                dbg_tap("hsdt", [((0, 128), hsdt[0]), ((128, 256), hsdt[1])]
                        if c == 0 else [])
                dbg_tap("S1", [((0, 128), Sst)] if c == 0 else [])

                xg_ev = {}
                for j in range(32 if _BUILD_OPTS["xg"] else 0):
                    par = j % 2
                    bank = pbf[6 + par]
                    tt = []
                    for li in range(2):
                        tt = S_.op("tensor", lambda e, j=j, li=li, bank=bank:
                                   e.transpose(
                            bank[:, li * 128:(li + 1) * 128],
                            ytm[li][:, j * 128:(j + 1) * 128], idb),
                            inc=[("t", 1)] if li == 1 else (),
                            wait=xg_ev.get(par, S_.aft("v")) if li == 0 else ())
                    s1 = S_.op("vector", lambda e, j=j, off=off,
                               bank=bank: e.scalar_tensor_tensor(
                        sqf[0][:, 0:256], hs[j][:, off: off + 256],
                        vcol("D", j), bank[:, 0:256], ALU.mult, ALU.add),
                        inc=[("v", 1)], wait=tt)
                    xg_ev[par] = S_.op("vector", lambda e, j=j, off=off:
                                       e.tensor_mul(
                        xg[j][:, off: off + 256], sqf[0][:, 0:256],
                        gsl[j][:, off: off + 256]),
                        inc=[("v", 1)], wait=())

            if not _BUILD_OPTS["op"]:
                continue
            sq_tok = []
            prev_mm = {}
            for j in range(32):
                sj = S_.op("scalar", lambda e, j=j: e.activation(
                    sqb[j % 2], xg[j], AF.Square), inc=[("a", 1)],
                    wait=(S_.aft("v") if j == 0 else []) + prev_mm.get(j % 2, []))
                sq_tok = S_.op("tensor", lambda e, j=j: e.matmul(
                    pb[5][0:1, 0:SLAB], onesb, sqb[j % 2],
                    start=(j == 0), stop=(j == 31)),
                    inc=[("t", 1)], wait=sj)
                prev_mm[j % 2] = sq_tok
            rv = S_.op("vector", lambda e: e.tensor_copy(rowb, pb[5][0:1, 0:SLAB]),
                       inc=[("v", 1)], wait=sq_tok)
            rv = S_.op("vector", lambda e: e.tensor_scalar(
                rowb, rowb, 1.0 / 4096.0, None, ALU.mult), inc=[("v", 1)], wait=())
            rv = S_.op("scalar", lambda e: e.activation(
                rowb, rowb, AF.Sqrt, bias=vcol("epsA")[0:1, :]), inc=[("a", 1)], wait=rv)
            rv = S_.op("vector", lambda e: e.reciprocal(rowb, rowb),
                       inc=[("v", 1)], wait=rv)
            rd = S_.op("gpsimd", lambda e, slab=slab: e.dma_start(
                out=rows_d[slab * 512:(slab + 1) * 512].unsqueeze(0),
                in_=rowb), inc=[("gsp", 16)], wait=rv)
            rb2 = S_.op("gpsimd", lambda e, slab=slab: e.dma_start(
                out=bcb[0], in_=rows_d[slab * 512:(slab + 1) * 512]
                    [None, :].to_broadcast([128, 512])),
                inc=[("gsp", 16)], wait=rd)
            for j in range(32):
                S_.op("vector", lambda e, j=j: e.scalar_tensor_tensor(
                    xg[j], xg[j], vcol("nw", j), bcb[0], ALU.mult, ALU.mult),
                    inc=[("v", 1)], wait=rb2)
            xgn_done = S_.aft("v")

            for mg in range(4):
                ld_all = []
                mm_tok = []
                kq_tok = {}
                for kq in range(4):
                    lw = S_.op("sync", lambda e, mg=mg, kq=kq: e.dma_start(
                        out=opst.rearrange("p (k c) -> p k c", k=8),
                        in_=opw_d[kq * 1024:(kq + 1) * 1024, mg * 512:(mg + 1) * 512]
                            .rearrange("(k p) c -> p k c", p=128)),
                        inc=[(f"sw{kq % 2}", 16)], wait=S_.aft("v") if (mg + kq) > 0 else S_.aft("t"))
                    cw = S_.op("vector", lambda e, kq=kq: e.tensor_copy(
                        opbf[kq % 2], opst), inc=[("v", 1)],
                        wait=list(lw) + kq_tok.get(kq % 2, S_.aft("t")))
                    for k8 in range(8):
                        for mi in range(4):
                            kk = kq * 8 + k8
                            tok = S_.op("tensor", lambda e, kq=kq, k8=k8, mi=mi, kk=kk:
                                        e.matmul(
                                pb[mi][:, 0:SLAB],
                                opbf[kq % 2][:, k8 * 512 + mi * 128: k8 * 512 + (mi + 1) * 128],
                                xg[kk], start=(kk == 0), stop=(kk == 31)),
                                inc=[("t", 1)] if (kk == 31 or (k8 == 7 and mi == 3))
                                else (),
                                wait=(list(cw) + xgn_done) if (k8 == 0 and mi == 0) else ())
                            if k8 == 7 and mi == 3:
                                kq_tok[kq % 2] = tok
                            if kk == 31:
                                mm_tok += tok
                for mi in range(4):
                    m = mg * 4 + mi
                    xl = S_.op("sync", lambda e, mi=mi, m=m, t0=t0: e.dma_start(
                        out=x0st[mi % 2], in_=x0T[m * 128:(m + 1) * 128, t0:t0 + SLAB]),
                        inc=[(f"sx{mi % 2}", 16)], wait=S_.aft("v"))
                    evt = S_.op("vector", lambda e, mi=mi, m=m: e.scalar_tensor_tensor(
                        prb[mi % 2], pb[mi][:, 0:SLAB], vcol("ob", m), x0st[mi % 2],
                        ALU.add, ALU.add), inc=[("v", 1)],
                        wait=list(xl) + mm_tok + prev_mm.get(("pr", m % 2), [])
                        + prev_mm.get(("sqa", m % 2), []) + S_.aft("gsp"))
                    S_.op("gpsimd", lambda e, mi=mi, m=m, t0=t0: e.dma_start(
                        out=prex1_d[m * 128:(m + 1) * 128, t0:t0 + SLAB],
                        in_=prb[mi % 2]), inc=[("gsp", 16)], wait=evt)
                    sq2 = S_.op("scalar", lambda e, mi=mi: e.activation(
                        sqf[mi % 2], prb[mi % 2], AF.Square), inc=[("a", 1)],
                        wait=evt + prev_mm.get(("sq", m % 2), []))
                    prev_mm[("sqa", m % 2)] = sq2
                    smm = S_.op("tensor", lambda e, m=m, mi=mi: e.matmul(
                        pb[6][0:1, 0:SLAB], vcol("ones"), prb[mi % 2],
                        start=(m == 0), stop=(m == 15)),
                        inc=[("t", 1)], wait=evt)
                    prev_mm[("pr", m % 2)] = smm
                    smm2 = S_.op("tensor", lambda e, m=m, mi=mi: e.matmul(
                        pb[7][0:1, 0:SLAB], vcol("ones"), sqf[mi % 2],
                        start=(m == 0), stop=(m == 15)),
                        inc=[("t", 1)], wait=sq2)
                    prev_mm[("sq", m % 2)] = smm2
            st_tok = S_.aft("t")
            rv = S_.op("vector", lambda e: e.tensor_scalar(
                rowb, pb[6][0:1, 0:SLAB], 1.0 / 2048.0, None, ALU.mult),
                inc=[("v", 1)], wait=st_tok)
            rv2 = S_.op("vector", lambda e: e.tensor_scalar(
                rowb2, pb[7][0:1, 0:SLAB], 1.0 / 2048.0, None, ALU.mult),
                inc=[("v", 1)], wait=())
            sq3 = S_.op("scalar", lambda e: e.activation(
                bcb[1][0:1, :], rowb, AF.Square), inc=[("a", 1)], wait=rv)
            vv = S_.op("vector", lambda e: e.tensor_sub(
                rowb2, rowb2, bcb[1][0:1, :]), inc=[("v", 1)], wait=sq3)
            s1 = S_.op("scalar", lambda e: e.activation(
                bcb[1][0:1, :], rowb2, AF.Sqrt, bias=vcol("epsA")[0:1, :]), inc=[("a", 1)], wait=vv)
            r1 = S_.op("vector", lambda e: e.reciprocal(
                bcb[1][0:1, :], bcb[1][0:1, :]), inc=[("v", 1)], wait=s1)
            S_.op("gpsimd", lambda e, slab=slab: e.dma_start(
                out=rows_d[2048 + slab * 512: 2048 + (slab + 1) * 512].unsqueeze(0),
                in_=rowb), inc=[("gsp", 16)], wait=rv)
            S_.op("gpsimd", lambda e, slab=slab: e.dma_start(
                out=rows_d[4096 + slab * 512: 4096 + (slab + 1) * 512].unsqueeze(0),
                in_=bcb[1][0:1, :]), inc=[("gsp", 16)], wait=r1)
            t1 = S_.op("vector", lambda e: e.tensor_mul(
                rowb2, rowb2, bcb[1][0:1, :]), inc=[("v", 1)], wait=r1)
            t1 = S_.op("vector", lambda e: e.tensor_mul(
                rowb2, rowb2, bcb[1][0:1, :]), inc=[("v", 1)], wait=())
            t2 = S_.op("scalar", lambda e: e.activation(
                rowb2, rowb2, AF.Sqrt, bias=vcol("epsF")[0:1, :]), inc=[("a", 1)], wait=t1)
            t3 = S_.op("vector", lambda e: e.reciprocal(rowb2, rowb2),
                       inc=[("v", 1)], wait=t2)
            S_.op("gpsimd", lambda e, slab=slab: e.dma_start(
                out=rows_d[6144 + slab * 512: 6144 + (slab + 1) * 512].unsqueeze(0),
                in_=rowb2), inc=[("gsp", 16)], wait=t3)

        dbg_tap("prex1", [((0, 2048), prex1_d[:, 0:512])])
        dbg_tap("rows", [((0, 1), rows_d[0:1024].unsqueeze(0)),
                         ((1, 2), rows_d[2048:3072].unsqueeze(0)),
                         ((2, 3), rows_d[4096:5120].unsqueeze(0)),
                         ((3, 4), rows_d[6144:7168].unsqueeze(0))])
        a_sc.close()

        S_.barrier()
        b_sc = ExitStack()
        bent = b_sc.enter_context
        XL = [bent(nc.sbuf_tensor(f"XL{i}", [128, S], BF16)).ap() for i in range(16)]
        prl = [bent(nc.sbuf_tensor(f"prl{i}", [128, SLAB], F32)).ap() for i in range(2)]
        x1b = [bent(nc.sbuf_tensor(f"x1b{i}", [128, SLAB], F32)).ap() for i in range(2)]
        bcB = [bent(nc.sbuf_tensor(f"bcB{i}", [128, SLAB], F32)).ap() for i in range(3)]
        if _BUILD_OPTS["xl"]:
            prev = {}
            for slab in range(NSLAB):
                t0 = slab * SLAB
                bc_tok = []
                for j, roff in enumerate((2048, 4096, 6144)):
                    bc_tok += S_.op("gpsimd", lambda e, j=j, roff=roff, slab=slab:
                                    e.dma_start(
                        out=bcB[j],
                        in_=rows_d[roff + slab * 512: roff + (slab + 1) * 512]
                            [None, :].to_broadcast([128, 512])),
                        inc=[("gbc", 16)], wait=S_.aft("v") if slab else ())
                for m in range(16):
                    pl = S_.op("sync", lambda e, m=m, t0=t0: e.dma_start(
                        out=prl[m % 2], in_=prex1_d[m * 128:(m + 1) * 128, t0:t0 + SLAB]),
                        inc=[(f"sx{m % 2}", 16)],
                        wait=prev.get(("sub", m % 2), S_.aft("gsp")))
                    s1 = S_.op("vector", lambda e, m=m: e.tensor_sub(
                        x1b[m % 2], prl[m % 2], bcB[0]),
                        inc=[("v", 1)], wait=list(pl) + (bc_tok if m == 0 else [])
                        + prev.get(("sp", m % 2), []))
                    prev[("sub", m % 2)] = s1
                    s1 = S_.op("vector", lambda e, m=m: e.tensor_mul(
                        x1b[m % 2], x1b[m % 2], bcB[1]), inc=[("v", 1)], wait=())
                    prev[("sp", m % 2)] = S_.op("gpsimd", lambda e, m=m, t0=t0:
                                                e.dma_start(
                        out=x1sp_d[m * 128:(m + 1) * 128, t0:t0 + SLAB],
                        in_=x1b[m % 2]), inc=[(f"gc{m % 2}", 16)], wait=s1)
                    prev[("xl", m % 2)] = S_.op("vector", lambda e, m=m, t0=t0:
                                                e.scalar_tensor_tensor(
                        XL[m][:, t0:t0 + SLAB], x1b[m % 2], vcol("l1", m), bcB[2],
                        ALU.mult, ALU.mult), inc=[("v", 1)], wait=())
            bl0 = bent(nc.sbuf_tensor("bl0", [128, 1024], F32)).ap()
            bl1 = bent(nc.sbuf_tensor("bl1", [128, 1024], F32)).ap()
            blo = bent(nc.sbuf_tensor("blo", [128, 1024], F32)).ap()
            for m in range(16):
                h0 = S_.op("sync", lambda e, m=m: e.dma_start(
                    out=bl0, in_=x1sp_d[m * 128:(m + 1) * 128, 0:1024]),
                    inc=[("sx0", 16)], wait=S_.aft("gc0", "gc1", "v") if m == 0
                    else S_.aft("gbc"))
                h1 = S_.op("sync", lambda e, m=m: e.dma_start(
                    out=bl1, in_=x1sp_d[m * 128:(m + 1) * 128, 1024:2048]),
                    inc=[("sx1", 16)], wait=())
                bl = S_.op("vector", lambda e, m=m: e.tensor_scalar(
                    blo, bl0, vcol("m0"), None, ALU.mult),
                    inc=[("v", 1)], wait=list(h0) + list(h1) + S_.aft("gbc"))
                bl = S_.op("vector", lambda e, m=m: e.scalar_tensor_tensor(
                    blo, bl1, vcol("m1"), blo, ALU.mult, ALU.add),
                    inc=[("v", 1)], wait=())
                S_.op("gpsimd", lambda e, m=m: e.dma_start(
                    out=x1own_d[m * 128:(m + 1) * 128, :],
                    in_=blo), inc=[("gbc", 16)], wait=bl)

        wst2p = [bent(nc.sbuf_tensor(f"wst2{i}", [128, 2048], F32)).ap()
                 for i in range(2)]
        wkbf = [bent(nc.sbuf_tensor(f"wkbf{i}", [128, 2048], BF16)).ap()
                for i in range(2)]
        kvb = [bent(nc.sbuf_tensor(f"kvb{i}", [128, SLAB], BF16)).ap()
               for i in range(2)]
        sqk = bent(nc.sbuf_tensor("sqk", [128, SLAB], BF16)).ap()
        if _BUILD_OPTS["kv"]:
            S_.op("vector", lambda e: e.memset(mxk, 0.0), inc=[("v", 1)])
            prev = {}
            for m in range(16):
                lw = []
                for kh in range(2):
                    lw += S_.op("sync", lambda e, kh=kh, m=m: e.dma_start(
                        out=wst2p[m % 2][:, kh * 1024:(kh + 1) * 1024]
                            .rearrange("p (k c) -> p k c", k=8),
                        in_=kwT_d[kh * 1024:(kh + 1) * 1024, m * 128:(m + 1) * 128]
                            .rearrange("(k p) c -> p k c", p=128)),
                        inc=[(f"sw{m % 2}", 16)],
                        wait=prev.get(("cst", m % 2), S_.aft("t", "v"))
                        if kh == 0 else ())
                cst = S_.op("vector", lambda e, m=m: e.tensor_copy(
                    wkbf[m % 2], wst2p[m % 2]), inc=[("v", 1)],
                    wait=lw[-1:] + prev.get(("mmk", m % 2), []))
                prev[("cst", m % 2)] = cst
                for tc in range(4):
                    mmt = []
                    for k in range(16):
                        mmt = S_.op("tensor", lambda e, m=m, tc=tc, k=k: e.matmul(
                            pb[tc % 2][:, 0:SLAB],
                            wkbf[m % 2][:, k * 128:(k + 1) * 128],
                            XL[k][:, tc * 512:(tc + 1) * 512],
                            start=(k == 0), stop=(k == 15)),
                            inc=[("t", 1)] if k == 15 else (),
                            wait=(list(cst) + prev.get(("kev", tc % 2), []))
                            if k == 0 else ())
                    if tc == 3:
                        prev[("mmk", m % 2)] = mmt
                    ev = S_.op("scalar", lambda e, m=m, tc=tc: e.activation(
                        kvb[tc % 2], pb[tc % 2][:, 0:SLAB], AF.Copy),
                        inc=[("a", 1)], wait=mmt + prev.get(("ksp", tc % 2), []))
                    prev[("kev", tc % 2)] = ev
                    prev[("ksp", tc % 2)] = S_.op(
                        "gpsimd", lambda e, m=m, tc=tc: e.dma_start(
                            out=kspill_d[m * 128:(m + 1) * 128,
                                         tc * 512:(tc + 1) * 512],
                            in_=kvb[tc % 2]), inc=[(f"gc{tc % 2}", 16)], wait=ev)
                    sq = S_.op("scalar", lambda e, tc=tc: e.activation(
                        sqk, kvb[tc % 2], AF.Square), inc=[("a", 1)],
                        wait=ev + prev.get(("sqk",), []))
                    smm = S_.op("tensor", lambda e: e.matmul(
                        pb[6][0:1, 0:SLAB], onesb, sqk, start=True, stop=True),
                        inc=[("t", 1)], wait=sq + prev.get(("nrm",), []))
                    prev[("sqk",)] = smm
                    rm = S_.op("vector", lambda e: e.reduce_max(
                        rowb[0:1, 0:1], pb[6][0:1, 0:SLAB], axis=AX),
                        inc=[("v", 1)], wait=smm)
                    prev[("nrm",)] = S_.op("vector", lambda e, m=m: e.tensor_max(
                        mxk[0:1, m:m + 1], mxk[0:1, m:m + 1], rowb[0:1, 0:1]),
                        inc=[("v", 1)], wait=())
        vvbf = [bent(nc.sbuf_tensor(f"vvbf{i}", [128, SLAB], BF16)).ap()
                for i in range(16)]
        vst = [bent(nc.sbuf_tensor(f"vst{i}", [128, SLAB], F32)).ap()
               for i in range(2)]
        if _BUILD_OPTS["kv"]:
            lastmm = []
            for vq in range(4):
                casts = []
                for k in range(16):
                    lw = S_.op("sync", lambda e, vq=vq, k=k: e.dma_start(
                        out=vst[k % 2],
                        in_=vwT_d[k * 128:(k + 1) * 128, vq * 512:(vq + 1) * 512]),
                        inc=[(f"sx{k % 2}", 16)],
                        wait=prev.get(("vst", k % 2), S_.aft("v", "a")))
                    casts = S_.op("vector", lambda e, k=k: e.tensor_copy(
                        vvbf[k], vst[k % 2]), inc=[("v", 1)],
                        wait=list(lw) + lastmm)
                    prev[("vst", k % 2)] = casts
                for tb in range(16):
                    mmt = []
                    for k in range(16):
                        mmt = S_.op("tensor", lambda e, vq=vq, tb=tb, k=k: e.matmul(
                            pb[tb % 2][:, 0:SLAB],
                            XL[k][:, tb * 128:(tb + 1) * 128], vvbf[k],
                            start=(k == 0), stop=(k == 15)),
                            inc=[("t", 1)] if k == 15 else (),
                            wait=(casts + prev.get(("vev", tb % 2), []))
                            if k == 0 else ())
                    if tb >= 14:
                        lastmm = lastmm + mmt if tb == 15 else list(mmt)
                    ev = S_.op("scalar", lambda e, tb=tb: e.activation(
                        kvb[tb % 2], pb[tb % 2][:, 0:SLAB], AF.Copy),
                        inc=[("a", 1)],
                        wait=mmt + prev.get(("vsp", tb % 2),
                                            prev.get(("ksp", tb % 2), [])))
                    prev[("vev", tb % 2)] = ev
                    prev[("vsp", tb % 2)] = S_.op(
                        "gpsimd", lambda e, vq=vq, tb=tb: e.dma_start(
                            out=vspill_d[tb * 128:(tb + 1) * 128,
                                         vq * 512:(vq + 1) * 512],
                            in_=kvb[tb % 2]), inc=[(f"gc{tb % 2}", 16)], wait=ev)
                lastmm = lastmm[-1:]
        dbg_tap("kfm", [((0, 2048), kspill_d[:, 0:512])])
        dbg_tap("vtm", [((0, 2048), vspill_d[:, 0:512])])
        S_.barrier()
        b_sc.close()

        x2_sc = ExitStack()
        x2t = [x2_sc.enter_context(
            nc.sbuf_tensor(f"x2t{i}", [128, 1024], BF16, side="right")).ap()
            for i in range(16)]
        de_sc = ExitStack()
        afm = [de_sc.enter_context(
            nc.sbuf_tensor(f"afm{i}", [128, 1024], BF16, side="right")).ap()
            for i in range(16)]
        d_sc = ExitStack()
        dent = d_sc.enter_context
        XLo = [dent(nc.sbuf_tensor(f"XLo{i}", [128, 1024], BF16)).ap()
               for i in range(16)]
        m01t = [dent(nc.sbuf_tensor(f"m01t{i}", [128, 1024], BF16)).ap()
                for i in range(16)]
        q_all = [dent(nc.sbuf_tensor(f"qall{i}", [128, 1024], BF16)).ap()
                 for i in range(16)]
        kh_b2 = [dent(nc.sbuf_tensor(f"kh_b{i}", [128, 2048], BF16)).ap()
                 for i in range(2)]
        vh_b0 = dent(nc.sbuf_tensor("vh_b0", [128, 16 * 129], BF16)).ap()
        vh_b2 = [vh_b0, vh_b0]
        qst0 = dent(nc.sbuf_tensor("qst0", [128, 2048], F32)).ap()
        qst = [qst0, qst0]
        prl2 = [qst0[:, 0:512], qst0[:, 512:1024]]
        qbf = [dent(nc.sbuf_tensor(f"qbf{i}", [128, 2048], BF16)).ap()
               for i in range(2)]
        praw = [bcb[0].bitcast(BF16)[:, 0:512], bcb[1].bitcast(BF16)[:, 0:512]]
        sqq = praw[0]
        pT = [dent(nc.sbuf_tensor(f"pT{i}", [128, SLAB], BF16)).ap()
              for i in range(2)]
        att = [dent(nc.sbuf_tensor(f"att{i}", [128, 128], BF16)).ap()
               for i in range(2)]
        rcp = dent(nc.sbuf_tensor("rcp", [128, 1], F32)).ap()
        ncol = dent(nc.sbuf_tensor("ncol", [128, 16], F32)).ap()

        if _BUILD_OPTS["att"]:
            for tc in range(2):
                b0 = S_.op("gpsimd", lambda e, tc=tc: e.dma_start(
                    out=bcb[0], in_=rows_d[6144 + tc * 512: 6144 + (tc + 1) * 512]
                        [None, :].to_broadcast([128, 512])),
                    inc=[("gbc", 16)], wait=S_.aft("v"))
                b1 = S_.op("gpsimd", lambda e, tc=tc: e.dma_start(
                    out=bcb[1], in_=rows_d[7168 + tc * 512: 7168 + (tc + 1) * 512]
                        [None, :].to_broadcast([128, 512])),
                    inc=[("gbc", 16)], wait=())
                bl = S_.op("vector", lambda e: e.tensor_scalar(
                    bcb[0], bcb[0], vcol("m0"), None, ALU.mult),
                    inc=[("v", 1)], wait=list(b0) + list(b1))
                bl = S_.op("vector", lambda e: e.scalar_tensor_tensor(
                    bcb[0], bcb[1], vcol("m1"), bcb[0], ALU.mult, ALU.add),
                    inc=[("v", 1)], wait=())
                prev2 = {}
                for m in range(16):
                    xo = S_.op("sync", lambda e, m=m, tc=tc: e.dma_start(
                        out=prl2[m % 2],
                        in_=x1own_d[m * 128:(m + 1) * 128, tc * 512:(tc + 1) * 512]),
                        inc=[(f"sx{m % 2}", 16)],
                        wait=prev2.get(m % 2, S_.aft("v", "gbc")))
                    prev2[m % 2] = S_.op("vector", lambda e, m=m, tc=tc:
                                         e.scalar_tensor_tensor(
                        XLo[m][:, tc * 512:(tc + 1) * 512], prl2[m % 2],
                        vcol("l1", m), bcb[0], ALU.mult, ALU.mult),
                        inc=[("v", 1)], wait=list(xo) + (bl if m == 0 else []))
            for m in range(16):
                S_.op("sync", lambda e, m=m: e.dma_start(
                    out=m01t[m], in_=m01_d[m * 128:(m + 1) * 128, :]),
                    inc=[("st", 16)], wait=())
            m01_tok = S_.aft("st")
            S_.op("vector", lambda e: e.memset(vh_b0, 1.0), inc=[("v", 1)])
            prevq = {}
            for h in range(16):
                lw = []
                for kh in range(2):
                    lw += S_.op("sync", lambda e, kh=kh, h=h: e.dma_start(
                        out=qst[h % 2][:, kh * 1024:(kh + 1) * 1024]
                            .rearrange("p (k c) -> p k c", k=8),
                        in_=qwT_d[kh * 1024:(kh + 1) * 1024, h * 128:(h + 1) * 128]
                            .rearrange("(k p) c -> p k c", p=128)),
                        inc=[(f"sw{h % 2}", 16)],
                        wait=prevq.get(("cst", (h - 1) % 2), S_.aft("v", "t"))
                        if kh == 0 else ())
                cst = S_.op("vector", lambda e, h=h: e.tensor_copy(
                    qbf[h % 2], qst[h % 2]), inc=[("v", 1)],
                    wait=lw[-1:] + prevq.get(("mm", h % 2), []))
                prevq[("cst", h % 2)] = cst
                for tc in range(2):
                    mmt = []
                    for k in range(16):
                        mmt = S_.op("tensor", lambda e, h=h, tc=tc, k=k: e.matmul(
                            pb[tc % 2][:, 0:SLAB],
                            qbf[h % 2][:, k * 128:(k + 1) * 128],
                            XLo[k][:, tc * 512:(tc + 1) * 512],
                            start=(k == 0), stop=(k == 15)),
                            inc=[("t", 1)] if k == 15 else (),
                            wait=(list(cst) + prevq.get(("ev", tc % 2), []))
                            if k == 0 else ())
                    if tc == 1:
                        prevq[("mm", h % 2)] = mmt
                    ev = S_.op("scalar", lambda e, h=h, tc=tc: e.activation(
                        q_all[h][:, tc * 512:(tc + 1) * 512],
                        pb[tc % 2][:, 0:SLAB], AF.Copy),
                        inc=[("a", 1)], wait=mmt)
                    prevq[("ev", tc % 2)] = ev
                    sq = S_.op("scalar", lambda e, h=h, tc=tc: e.activation(
                        sqq, q_all[h][:, tc * 512:(tc + 1) * 512], AF.Square),
                        inc=[("a", 1)], wait=())
                    smm = S_.op("tensor", lambda e: e.matmul(
                        pb[6][0:1, 0:SLAB], onesb, sqq, start=True, stop=True),
                        inc=[("t", 1)], wait=sq + prevq.get("nrm", []))
                    rm = S_.op("vector", lambda e: e.reduce_max(
                        rowb[0:1, 0:1], pb[6][0:1, 0:SLAB], axis=AX),
                        inc=[("v", 1)], wait=smm)
                    prevq["nrm"] = S_.op("vector", lambda e, h=h: e.tensor_max(
                        mxk[0:1, 16 + h:17 + h], mxk[0:1, 16 + h:17 + h],
                        rowb[0:1, 0:1]), inc=[("v", 1)], wait=())
            nr = S_.op("vector", lambda e: e.tensor_mul(
                mxk[0:1, 32:48], mxk[0:1, 0:16], mxk[0:1, 16:32]),
                inc=[("v", 1)], wait=prevq["nrm"])
            nr = S_.op("scalar", lambda e: e.activation(
                mxk[0:1, 32:48], mxk[0:1, 32:48], AF.Sqrt), inc=[("a", 1)], wait=nr)
            nr = S_.op("vector", lambda e: e.tensor_scalar(
                mxk[0:1, 32:48], mxk[0:1, 32:48], -SCALE, None, ALU.mult),
                inc=[("v", 1)], wait=nr)
            nr = S_.op("gpsimd", lambda e: e.dma_start(
                out=rows_d[8192:8208].unsqueeze(0), in_=mxk[0:1, 32:48]),
                inc=[("gsp", 16)], wait=nr)
            nr = S_.op("gpsimd", lambda e: e.dma_start(
                out=ncol, in_=rows_d[8192:8208][None, :].to_broadcast([128, 16])),
                inc=[("gsp", 16)], wait=nr)
            ncol_tok = S_.aft("gsp")
            prevh = {}
            for h in range(16):
                kh_b = kh_b2[h % 2]
                vh_b = vh_b2[h % 2]
                klw = S_.op("sync", lambda e, h=h, kh_b=kh_b: e.dma_start(
                    out=kh_b, in_=kspill_d[h * 128:(h + 1) * 128, :]),
                    inc=[(f"sx{h % 2}", 16)],
                    wait=prevh.get(("ks", h % 2),
                                   S_.aft("gc0", "gc1", "a", "v")))
                vlw = []
                for half in range(2):
                    vlw += S_.op("sync", lambda e, h=h, half=half, vh_b=vh_b:
                                 e.dma_start(
                        out=vh_b.rearrange("p (tb c) -> p tb c", c=129)
                            [:, half * 8:(half + 1) * 8, 0:128],
                        in_=vspill_d[:, h * 128:(h + 1) * 128]
                            .rearrange("(tb p) c -> p tb c", p=128)
                            [:, half * 8:(half + 1) * 8, :]),
                        inc=[(f"sx{h % 2}", 16)],
                        wait=prevh.get(("ks", (h - 1) % 2), []) if half == 0 else ())
                for tc in range(2):
                    pv_first = []
                    for kt in range(16):
                        smm = S_.op("tensor", lambda e, h=h, tc=tc, kt=kt,
                                    kh_b=kh_b: e.matmul(
                            pb[kt % 2][:, 0:SLAB],
                            kh_b[:, kt * 128:(kt + 1) * 128],
                            q_all[h][:, tc * 512:(tc + 1) * 512],
                            start=True, stop=True), inc=[("t", 1)],
                            wait=(list(klw) + vlw + ncol_tok + m01_tok
                                  + S_.aft("v", "a"))
                            if kt == 0 and tc == 0 else
                            prevh.get(("ex", kt % 2), []))
                        ex = S_.op("scalar", lambda e, h=h, tc=tc, kt=kt:
                                   e.activation(
                            praw[kt % 2], pb[kt % 2][:, 0:SLAB], AF.Exp,
                            bias=ncol[:, h:h + 1], scale=SCALE),
                            inc=[("a", 1)],
                            wait=smm + prevh.get(("mul", kt % 2), []))
                        prevh[("ex", kt % 2)] = ex
                        mul = S_.op("vector", lambda e, h=h, tc=tc, kt=kt:
                                    e.tensor_mul(
                            pT[kt % 2], praw[kt % 2],
                            m01t[kt][:, tc * 512:(tc + 1) * 512]),
                            inc=[("v", 1)],
                            wait=ex + prevh.get(("pv", kt % 2), []))
                        prevh[("mul", kt % 2)] = mul
                        pvt = []
                        for qs in range(4):
                            pvt = S_.op("tensor", lambda e, kt=kt, qs=qs,
                                        vh_b=vh_b: e.matmul(
                                pb[2 + qs][:, 0:129],
                                pT[kt % 2][:, qs * 128:(qs + 1) * 128],
                                vh_b.rearrange("p (tb c) -> p tb c", c=129)[:, kt, :],
                                start=(kt == 0), stop=(kt == 15)),
                                inc=[("t", 1)] if kt == 15 else (),
                                wait=mul if qs == 0 else ())
                        prevh[("pv", kt % 2)] = pvt if kt == 15 else mul
                    for qs in range(4):
                        rc = S_.op("vector", lambda e, qs=qs: e.reciprocal(
                            rcp, pb[2 + qs][:, 128:129]), inc=[("v", 1)],
                            wait=pvt)
                        at = S_.op("vector", lambda e, qs=qs: e.tensor_scalar(
                            att[qs % 2], pb[2 + qs][:, 0:128], rcp, None,
                            ALU.mult), inc=[("v", 1)],
                            wait=prevh.get(("atv", qs % 2), []))
                        tt = S_.op("tensor", lambda e, qs=qs: e.transpose(
                            pbf[6][:, 0:128], att[qs % 2], idb),
                            inc=[("t", 1)], wait=at + prevh.get(("afm",), []))
                        prevh[("afm",)] = S_.op(
                            "vector", lambda e, h=h, tc=tc, qs=qs: e.tensor_copy(
                                afm[h][:, tc * 512 + qs * 128:
                                       tc * 512 + (qs + 1) * 128],
                                pbf[6][:, 0:128]), inc=[("v", 1)], wait=tt)
                        prevh[("atv", qs % 2)] = prevh[("afm",)]
                prevh[("ks", h % 2)] = smm + pvt
        dbg_tap("attn", [((h * 128, (h + 1) * 128), afm[h]) for h in range(16)])
        d_sc.close()

        e_sc = ExitStack()
        eent = e_sc.enter_context
        hid = [eent(nc.sbuf_tensor(f"hid{i}", [128, 1024], F32)).ap()
               for i in range(16)]
        ost2 = [eent(nc.sbuf_tensor(f"ost{i}", [128, 2048], F32)).ap()
                for i in range(2)]
        obf = [eent(nc.sbuf_tensor(f"obf{i}", [128, 2048], BF16)).ap()
               for i in range(2)]
        x1r = [eent(nc.sbuf_tensor(f"x1r{i}", [128, SLAB], F32)).ap()
               for i in range(2)]
        sqe = [eent(nc.sbuf_tensor(f"sqe{i}", [128, SLAB], F32)).ap()
               for i in range(2)]
        if _BUILD_OPTS["moe"]:
            prevo = {}
            for m in range(16):
                lw = []
                for kh in range(2):
                    lw += S_.op("sync", lambda e, kh=kh, m=m: e.dma_start(
                        out=ost2[m % 2][:, kh * 1024:(kh + 1) * 1024]
                            .rearrange("p (k c) -> p k c", k=8),
                        in_=owT_d[kh * 1024:(kh + 1) * 1024, m * 128:(m + 1) * 128]
                            .rearrange("(k p) c -> p k c", p=128)),
                        inc=[(f"sw{m % 2}", 16)],
                        wait=prevo.get(("cst", m % 2), S_.aft("v", "t"))
                        if kh == 0 else ())
                cst = S_.op("vector", lambda e, m=m: e.tensor_copy(
                    obf[m % 2], ost2[m % 2]), inc=[("v", 1)],
                    wait=lw[-1:] + prevo.get(("mm", m % 2), []))
                prevo[("cst", m % 2)] = cst
                for tc in range(2):
                    mmt = []
                    for k in range(16):
                        mmt = S_.op("tensor", lambda e, m=m, tc=tc, k=k: e.matmul(
                            pb[tc % 2][:, 0:SLAB],
                            obf[m % 2][:, k * 128:(k + 1) * 128],
                            afm[k][:, tc * 512:(tc + 1) * 512],
                            start=(k == 0), stop=(k == 15)),
                            inc=[("t", 1)] if k == 15 else (),
                            wait=(list(cst) + prevo.get(("ev", tc % 2), []))
                            if k == 0 else ())
                    if tc == 1:
                        prevo[("mm", m % 2)] = mmt
                    xo = S_.op("sync", lambda e, m=m, tc=tc: e.dma_start(
                        out=x1r[tc % 2],
                        in_=x1own_d[m * 128:(m + 1) * 128, tc * 512:(tc + 1) * 512]),
                        inc=[(f"sx{tc % 2}", 16)],
                        wait=prevo.get(("x1", tc % 2), S_.aft("v")))
                    ev = S_.op("vector", lambda e, m=m, tc=tc: e.tensor_add(
                        hid[m][:, tc * 512:(tc + 1) * 512], pb[tc % 2][:, 0:SLAB],
                        x1r[tc % 2]), inc=[("v", 1)], wait=mmt + list(xo))
                    prevo[("ev", tc % 2)] = ev
                    prevo[("x1", tc % 2)] = ev
            for tc in range(2):
                smt = []
                prevs = {}
                for m in range(16):
                    sq = S_.op("scalar", lambda e, m=m, tc=tc: e.activation(
                        sqe[m % 2], hid[m][:, tc * 512:(tc + 1) * 512], AF.Square),
                        inc=[("a", 1)],
                        wait=(S_.aft("v") if m == 0 else [])
                        + prevs.get(m % 2, []))
                    smt = S_.op("tensor", lambda e, m=m: e.matmul(
                        pb[6][0:1, 0:SLAB], vcol("ones"), sqe[m % 2],
                        start=(m == 0), stop=(m == 15)),
                        inc=[("t", 1)], wait=sq)
                    prevs[m % 2] = smt
                rv = S_.op("vector", lambda e: e.tensor_scalar(
                    rowb, pb[6][0:1, 0:SLAB], 1.0 / 2048.0, None, ALU.mult),
                    inc=[("v", 1)], wait=smt)
                rv = S_.op("scalar", lambda e: e.activation(
                    rowb, rowb, AF.Sqrt, bias=vcol("epsF")[0:1, :]),
                    inc=[("a", 1)], wait=rv)
                rv = S_.op("vector", lambda e: e.reciprocal(rowb, rowb),
                           inc=[("v", 1)], wait=rv)
                rd = S_.op("gpsimd", lambda e, tc=tc: e.dma_start(
                    out=rows_d[8320 + tc * 512: 8320 + (tc + 1) * 512].unsqueeze(0),
                    in_=rowb), inc=[("gsp", 16)], wait=rv)
                rb = S_.op("gpsimd", lambda e, tc=tc: e.dma_start(
                    out=bcb[0], in_=rows_d[8320 + tc * 512: 8320 + (tc + 1) * 512]
                        [None, :].to_broadcast([128, 512])),
                    inc=[("gsp", 16)], wait=rd)
                for m in range(16):
                    x2w = S_.op("vector", lambda e, m=m, tc=tc:
                                e.scalar_tensor_tensor(
                        x2t[m][:, tc * 512:(tc + 1) * 512],
                        hid[m][:, tc * 512:(tc + 1) * 512], vcol("l2", m),
                        bcb[0], ALU.mult, ALU.mult), inc=[("v", 1)],
                        wait=rb if m == 0 else ())
            for m in range(16):
                S_.op("gpsimd", lambda e, m=m: e.dma_start(
                    out=hidsp_d[m * 128:(m + 1) * 128, :], in_=hid[m]),
                    inc=[(f"gc{m % 2}", 16)], wait=S_.aft("v"))
        dbg_tap("x2", [((m * 128, (m + 1) * 128), x2t[m]) for m in range(16)])
        de_sc.close()
        e_sc.close()
        S_.barrier()

        g_sc = ExitStack()
        gent = g_sc.enter_context
        acc = [gent(nc.sbuf_tensor(f"acc{i}", [128, SLAB], F32)).ap()
               for i in range(16)]
        w1bf = [gent(nc.sbuf_tensor(f"w1bf{i}", [128, 1408], BF16)).ap()
                for i in range(16)]
        w1st = [gent(nc.sbuf_tensor(f"w1st{i}", [128, 1408], F32)).ap()
                for i in range(2)]
        gsb2 = [gent(nc.sbuf_tensor(f"gsb2{i}", [128, SLAB], BF16)).ap()
                for i in range(22)]
        ab2 = gsb2
        w2bf = [gent(nc.sbuf_tensor(f"w2bf{i}", [128, SLAB], BF16)).ap()
                for i in range(22)]
        w2st = [gent(nc.sbuf_tensor(f"w2st{i}", [128, SLAB], F32)).ap()
                for i in range(2)]
        Wb = [gent(nc.sbuf_tensor(f"Wb{i}", [128, SLAB], F32)).ap()
              for i in range(2)]
        gwb = gent(nc.sbuf_tensor("gwb", [128, 256], BF16)).ap()
        gwst = gent(nc.sbuf_tensor("gwst", [128, 256], F32)).ap()
        lbuf = gent(nc.sbuf_tensor("lbuf", [128, 16], F32)).ap()
        ebuf = gent(nc.sbuf_tensor("ebuf", [128, 16], F32)).ap()
        curb = gent(nc.sbuf_tensor("curb", [128, 16], F32)).ap()
        selb = gent(nc.sbuf_tensor("selb", [128, 16], F32)).ap()
        eqb = gent(nc.sbuf_tensor("eqb", [128, 16], F32)).ap()
        mxc = gent(nc.sbuf_tensor("mxc", [128, 1], F32)).ap()
        wfp = gent(nc.sbuf_tensor("wfp", [128, 128], BF16)).ap()
        wfT = gent(nc.sbuf_tensor("wfT", [16, 1024], F32)).ap()
        hrl = [gent(nc.sbuf_tensor(f"hrl{i}", [128, SLAB], F32)).ap()
               for i in range(2)]
        orow = gent(nc.sbuf_tensor("orow", [1, 512], F32)).ap()
        orow2 = gent(nc.sbuf_tensor("orow2", [1, 512], F32)).ap()
        obuf = [w1st[0][:, 512:1024], w1st[1][:, 512:1024]]
        hr2 = Wb
        sqg = [w1st[0][:, 0:512], w1st[1][:, 0:512]]
        if _BUILD_OPTS["moe"]:
            gw16 = gwst
            glw = []
            for k in range(16):
                glw += S_.op("sync", lambda e, k=k: e.dma_start(
                    out=gw16[:, k * 16:(k + 1) * 16],
                    in_=gwT_d[k * 128:(k + 1) * 128, :]),
                    inc=[("sw0", 16)], wait=S_.aft("v", "a") if k == 0 else ())
            gcst = S_.op("vector", lambda e: e.tensor_copy(gwb, gw16),
                         inc=[("v", 1)], wait=glw[-1:])
            S_.op("vector", lambda e: e.memset(wfp, 0.0), inc=[("v", 1)])

            prevr = {}
            for ti in range(8):
                rmm = []
                for k in range(16):
                    rmm = S_.op("tensor", lambda e, ti=ti, k=k: e.matmul(
                        pb[4][:, 0:16], x2t[k][:, ti * 128:(ti + 1) * 128],
                        gwb[:, k * 16:(k + 1) * 16],
                        start=(k == 0), stop=(k == 15)),
                        inc=[("t", 1)] if k == 15 else (),
                        wait=(gcst + S_.aft("v", "a")) if (ti == 0 and k == 0)
                        else (prevr.get("lb", []) if k == 0 else ()))
                lb = S_.op("vector", lambda e: e.tensor_copy(lbuf, pb[4][:, 0:16]),
                           inc=[("v", 1)], wait=rmm)
                prevr["lb"] = lb
                ng = S_.op("vector", lambda e: e.reduce_max(
                    mxc, lbuf, axis=AX, negate=True), inc=[("v", 1)], wait=())
                S_.op("scalar", lambda e: e.activation(
                    ebuf, lbuf, AF.Exp, bias=mxc), inc=[("a", 1)], wait=ng)
                S_.op("vector", lambda e: e.tensor_copy(curb, lbuf),
                      inc=[("v", 1)], wait=())
                S_.op("vector", lambda e: e.memset(selb, 0.0), inc=[("v", 1)])
                for it in range(4):
                    S_.op("vector", lambda e: e.reduce_max(mxc, curb, axis=AX),
                          inc=[("v", 1)], wait=S_.aft("a") if it == 0 else ())
                    S_.op("vector", lambda e: e.tensor_scalar(
                        eqb, curb, mxc, None, ALU.is_equal), inc=[("v", 1)], wait=())
                    S_.op("vector", lambda e: e.tensor_add(selb, selb, eqb),
                          inc=[("v", 1)], wait=())
                    if it < 3:
                        S_.op("vector", lambda e: e.scalar_tensor_tensor(
                            curb, eqb, -1.0e30, curb, ALU.mult, ALU.add),
                            inc=[("v", 1)], wait=())
                S_.op("vector", lambda e: e.tensor_scalar(
                    selb, selb, 1.0, None, ALU.min), inc=[("v", 1)], wait=())
                S_.op("vector", lambda e: e.tensor_mul(eqb, ebuf, selb),
                      inc=[("v", 1)], wait=S_.aft("a"))
                S_.op("vector", lambda e: e.tensor_reduce(
                    mxc, eqb, AX, ALU.add), inc=[("v", 1)], wait=())
                S_.op("vector", lambda e: e.reciprocal(mxc, mxc),
                      inc=[("v", 1)], wait=())
                wf = S_.op("vector", lambda e: e.tensor_scalar(
                    wfp[:, 0:16], eqb, mxc, None, ALU.mult),
                    inc=[("v", 1)], wait=prevr.get("tt", []))
                ttk = S_.op("tensor", lambda e: e.transpose(
                    pbf[5][:, 0:128], wfp, idb), inc=[("t", 1)],
                    wait=wf + prevr.get("wfev", []))
                prevr["tt"] = ttk
                prevr["wfev"] = S_.op("vector", lambda e, ti=ti: e.tensor_copy(
                    wfT[:, ti * 128:(ti + 1) * 128], pbf[5][0:16, 0:128]),
                    inc=[("v", 1)], wait=ttk)
            wsp = S_.op("gpsimd", lambda e: e.dma_start(
                out=wfsp_d.rearrange("(a b) -> a b", a=16), in_=wfT),
                inc=[("gsp", 16)], wait=prevr["wfev"])
            dbg_tap("wf", [((0, 16), wfsp_d.rearrange("(a b) -> a b", a=16))])

            ccache = {}
            for tc in range(2):
                if tc == 1:
                    S_.barrier()
                prevg = {}
                first_acc = [True] * 16
                for ep in range(9):
                    shared = (ep == 8)
                    for ei in range(2):
                        eg = ep * 2 + ei
                        if not shared:
                            prevg[("wb", ei)] = S_.op(
                                "gpsimd", lambda e, eg=eg, ei=ei, tc=tc: e.dma_start(
                                    out=Wb[ei],
                                    in_=wfsp_d[eg * 1024 + tc * 512:
                                               eg * 1024 + (tc + 1) * 512]
                                    [None, :].to_broadcast([128, 512])),
                                inc=[(f"wb{ei}", 16)],
                                wait=wsp + prevg.get(("wbuse", ei), []))
                        for ih in range(2):
                            q4 = ei * 2 + ih
                            is_g = (q4 < 2) if shared else (ih == 0)
                            for kb in range(16):
                                if shared:
                                    wsrc = shgT_d[kb * 128:(kb + 1) * 128,
                                                  q4 * 1408:(q4 + 1) * 1408]
                                    wdst = shgc_d[kb * 128:(kb + 1) * 128,
                                                  q4 * 1408:(q4 + 1) * 1408]
                                else:
                                    wsrc = w1T_d[eg, kb * 128:(kb + 1) * 128,
                                                 ih * 1408:(ih + 1) * 1408]
                                    wdst = w1c_d[eg, kb * 128:(kb + 1) * 128,
                                                 ih * 1408:(ih + 1) * 1408]
                                if tc == 0:
                                    lw = S_.op("sync", lambda e, wsrc=wsrc, kb=kb:
                                               e.dma_start(
                                        out=w1st[kb % 2], in_=wsrc),
                                        inc=[(f"sw{kb % 2}", 16)],
                                        wait=prevg.get(("w1c", kb % 2),
                                                       S_.aft("v", "a", "t",
                                                              "gc0", "gc1")))
                                    ceng = "vector" if kb % 2 == 0 else "scalar"
                                    cop = ((lambda e, kb=kb: e.tensor_copy(
                                        w1bf[kb], w1st[kb % 2]))
                                        if ceng == "vector" else
                                        (lambda e, kb=kb: e.activation(
                                            w1bf[kb], w1st[kb % 2], AF.Copy)))
                                    ck = S_.op(ceng, cop,
                                               wait=list(lw)
                                               + prevg.get(("w1u", kb), [])
                                               + (S_.aft("pc0", "pc1")
                                                  if kb < 2 else []))
                                    prevg[("w1c", kb % 2)] = ck
                                    S_.op("gpsimd", lambda e, kb=kb, wdst=wdst:
                                          e.dma_start(out=wdst, in_=w1bf[kb]),
                                          inc=[(f"pc{kb % 2}", 16)], wait=ck)
                                else:
                                    ck = S_.op("sync", lambda e, kb=kb, wdst=wdst:
                                               e.dma_start(
                                        out=w1bf[kb], in_=wdst),
                                        inc=[(f"sw{kb % 2}", 16)],
                                        wait=prevg.get(("w1u", kb),
                                                       S_.aft("v", "a", "t")))
                                    prevg[("w1c", kb % 2)] = ck
                            cst_last = (prevg.get(("w1c", 0), [])
                                        + prevg.get(("w1c", 1), []))
                            for i in range(11):
                                gm = []
                                for kb in range(16):
                                    gm = S_.op("tensor", lambda e, i=i, kb=kb,
                                               tc=tc: e.matmul(
                                        pb[i % 2][:, 0:SLAB],
                                        w1bf[kb][:, i * 128:(i + 1) * 128],
                                        x2t[kb][:, tc * 512:(tc + 1) * 512],
                                        start=(kb == 0), stop=(kb == 15)),
                                        inc=[("t", 1)] if (kb == 15 or i == 10)
                                        else (),
                                        wait=(list(cst_last)
                                              + prevg.get(("guev", i % 2), []))
                                        if kb == 0 else ())
                                    if i == 10:
                                        prevg[("w1u", kb)] = gm
                                idx2 = ((q4 % 2) if shared else ei) * 11 + i
                                if is_g:
                                    ev = S_.op("scalar", lambda e, i=i, idx2=idx2:
                                               e.activation(
                                        gsb2[idx2], pb[i % 2][:, 0:SLAB], AF.Silu),
                                        inc=[("a", 1)],
                                        wait=gm + prevg.get(("gsb", idx2), []))
                                    prevg[("guev", i % 2)] = ev
                                else:
                                    if not shared:
                                        mv = S_.op("vector", lambda e, idx2=idx2,
                                                   ei=ei: e.tensor_mul(
                                            gsb2[idx2], gsb2[idx2], Wb[ei]),
                                            inc=[("v", 1)],
                                            wait=prevg.get(("wb", ei), [])
                                            + S_.aft("a"))
                                        prevg[("wbuse", ei)] = mv
                                    av = S_.op("vector", lambda e, i=i, idx2=idx2:
                                               e.tensor_mul(
                                        ab2[idx2], gsb2[idx2], pb[i % 2][:, 0:SLAB]),
                                        inc=[("v", 1)], wait=gm + S_.aft("a"))
                                    prevg[("guev", i % 2)] = av
                                    prevg[("gsb", idx2)] = av
                    dsrc_rows = 22
                    for hq in range(4):
                        for j in range(22):
                            if shared:
                                wsrc = shdT_d[j * 128:(j + 1) * 128,
                                              hq * 512:(hq + 1) * 512]
                                wdst = shdc_d[j * 128:(j + 1) * 128,
                                              hq * 512:(hq + 1) * 512]
                            else:
                                e2 = ep * 2 + j // 11
                                kb2 = j % 11
                                wsrc = w2T_d[e2, kb2 * 128:(kb2 + 1) * 128,
                                             hq * 512:(hq + 1) * 512]
                                wdst = w2c_d[e2, kb2 * 128:(kb2 + 1) * 128,
                                             hq * 512:(hq + 1) * 512]
                            if tc == 0:
                                lw = S_.op("sync", lambda e, wsrc=wsrc, j=j:
                                           e.dma_start(
                                    out=w2st[j % 2], in_=wsrc),
                                    inc=[(f"sv{j % 2}", 16)],
                                    wait=prevg.get(("w2l", j % 2),
                                                   S_.aft("v", "a", "t",
                                                          "gc0", "gc1")))
                                ceng = "vector" if j % 2 == 0 else "scalar"
                                cop = ((lambda e, j=j: e.tensor_copy(
                                    w2bf[j], w2st[j % 2]))
                                    if ceng == "vector" else
                                    (lambda e, j=j: e.activation(
                                        w2bf[j], w2st[j % 2], AF.Copy)))
                                ck = S_.op(ceng, cop,
                                           wait=list(lw)
                                           + prevg.get(("w2u", j), [])
                                           + (S_.aft("pc0", "pc1")
                                              if j < 2 else []))
                                prevg[("w2l", j % 2)] = ck
                                S_.op("gpsimd", lambda e, j=j, wdst=wdst:
                                      e.dma_start(out=wdst, in_=w2bf[j]),
                                      inc=[(f"pc{j % 2}", 16)], wait=ck)
                            else:
                                ck = S_.op("sync", lambda e, j=j, wdst=wdst:
                                           e.dma_start(
                                    out=w2bf[j], in_=wdst),
                                    inc=[(f"sv{j % 2}", 16)],
                                    wait=prevg.get(("w2u", j),
                                                   S_.aft("v", "a", "t")))
                                prevg[("w2l", j % 2)] = ck
                        cst2 = (prevg.get(("w2l", 0), [])
                                + prevg.get(("w2l", 1), []))
                        for hc in range(4):
                            h = hq * 4 + hc
                            dm = []
                            for j in range(22):
                                dm = S_.op("tensor", lambda e, j=j, hc=hc: e.matmul(
                                    pb[2 + hc % 2][:, 0:SLAB],
                                    w2bf[j][:, hc * 128:(hc + 1) * 128],
                                    ab2[j], start=(j == 0), stop=(j == 21)),
                                    inc=[("t", 1)] if (j == 21 or hc == 3)
                                    else (),
                                    wait=(list(cst2)
                                          + prevg.get(("dev", hc % 2), [])
                                          + (S_.aft("v", "a")
                                             if (hq == 0 and hc == 0) else []))
                                    if j == 0 else ())
                                if hc == 3:
                                    prevg[("w2u", j)] = dm
                            if first_acc[h]:
                                ev = S_.op("vector", lambda e, h=h, hc=hc:
                                           e.tensor_copy(
                                    acc[h], pb[2 + hc % 2][:, 0:SLAB]),
                                    inc=[("v", 1)], wait=dm)
                                first_acc[h] = False
                            else:
                                ev = S_.op("vector", lambda e, h=h, hc=hc:
                                           e.tensor_add(
                                    acc[h], acc[h], pb[2 + hc % 2][:, 0:SLAB]),
                                    inc=[("v", 1)], wait=dm)
                            prevg[("dev", hc % 2)] = ev
                prevf = {}
                smt = smt2 = []
                for m in range(16):
                    hl = S_.op("sync", lambda e, m=m, tc=tc: e.dma_start(
                        out=hrl[m % 2],
                        in_=hidsp_d[m * 128:(m + 1) * 128, tc * 512:(tc + 1) * 512]),
                        inc=[(f"sx{m % 2}", 16)],
                        wait=prevf.get(("h", m % 2), S_.aft("v")))
                    xl2 = S_.op("sync", lambda e, m=m, tc=tc: e.dma_start(
                        out=hr2[m % 2],
                        in_=x1own_d[m * 128:(m + 1) * 128, tc * 512:(tc + 1) * 512]),
                        inc=[(f"sv{m % 2}", 16)],
                        wait=prevf.get(("x", m % 2),
                                       prevg.get(("w2l", m % 2), ())))
                    a1 = S_.op("vector", lambda e, m=m: e.tensor_add(
                        acc[m], acc[m], hrl[m % 2]), inc=[("v", 1)],
                        wait=list(hl) + list(xl2))
                    a1 = S_.op("vector", lambda e, m=m: e.tensor_add(
                        acc[m], acc[m], hr2[m % 2]), inc=[("v", 1)], wait=())
                    prevf[("h", m % 2)] = a1
                    prevf[("x", m % 2)] = a1
                    sq = S_.op("scalar", lambda e, m=m: e.activation(
                        sqg[m % 2], acc[m], AF.Square), inc=[("a", 1)],
                        wait=a1 + prevf.get(("sq", m % 2), []))
                    smt = S_.op("tensor", lambda e, m=m: e.matmul(
                        pb[6][0:1, 0:SLAB], vcol("ones"), acc[m],
                        start=(m == 0), stop=(m == 15)),
                        inc=[("t", 1)], wait=a1)
                    smt2 = S_.op("tensor", lambda e, m=m: e.matmul(
                        pb[7][0:1, 0:SLAB], vcol("ones"), sqg[m % 2],
                        start=(m == 0), stop=(m == 15)),
                        inc=[("t", 1)], wait=sq)
                    prevf[("sq", m % 2)] = smt2
                rv = S_.op("vector", lambda e: e.tensor_scalar(
                    rowb, pb[6][0:1, 0:SLAB], 1.0 / 2048.0, None, ALU.mult),
                    inc=[("v", 1)], wait=smt + smt2)
                rv2 = S_.op("vector", lambda e: e.tensor_scalar(
                    rowb2, pb[7][0:1, 0:SLAB], 1.0 / 2048.0, None, ALU.mult),
                    inc=[("v", 1)], wait=())
                sq3 = S_.op("scalar", lambda e: e.activation(
                    orow, rowb, AF.Square), inc=[("a", 1)], wait=rv)
                vv = S_.op("vector", lambda e: e.tensor_sub(
                    rowb2, rowb2, orow), inc=[("v", 1)], wait=sq3)
                s1 = S_.op("scalar", lambda e: e.activation(
                    orow, rowb2, AF.Sqrt, bias=vcol("epsA")[0:1, :]),
                    inc=[("a", 1)], wait=vv)
                r1 = S_.op("vector", lambda e: e.reciprocal(orow, orow),
                           inc=[("v", 1)], wait=s1)
                rd = S_.op("gpsimd", lambda e, tc=tc: e.dma_start(
                    out=rows_d[9344 + tc * 512: 9344 + (tc + 1) * 512].unsqueeze(0),
                    in_=rowb), inc=[("gsp", 16)], wait=rv)
                rd2 = S_.op("gpsimd", lambda e, tc=tc: e.dma_start(
                    out=rows_d[10368 + tc * 512: 10368 + (tc + 1) * 512].unsqueeze(0),
                    in_=orow), inc=[("gsp", 16)], wait=r1)
                b1 = S_.op("gpsimd", lambda e, tc=tc: e.dma_start(
                    out=bcb[0], in_=rows_d[9344 + tc * 512: 9344 + (tc + 1) * 512]
                        [None, :].to_broadcast([128, 512])),
                    inc=[("gsp", 16)], wait=S_.aft("gsp") + S_.aft("v"))
                b2 = S_.op("gpsimd", lambda e, tc=tc: e.dma_start(
                    out=bcb[1], in_=rows_d[10368 + tc * 512: 10368 + (tc + 1) * 512]
                        [None, :].to_broadcast([128, 512])),
                    inc=[("gsp", 16)], wait=())
                for m in range(16):
                    o1 = S_.op("vector", lambda e, m=m: e.tensor_sub(
                        obuf[m % 2], acc[m], bcb[0]), inc=[("v", 1)],
                        wait=(list(b1) + list(b2) if m == 0 else [])
                        + prevf.get(("ob", m % 2), []))
                    o1 = S_.op("vector", lambda e, m=m: e.tensor_mul(
                        obuf[m % 2], obuf[m % 2], bcb[1]), inc=[("v", 1)], wait=())
                    prevf[("ob", m % 2)] = S_.op(
                        "gpsimd", lambda e, m=m, tc=tc: e.dma_start(
                            out=out_d[m * 128:(m + 1) * 128,
                                      tc * 512:(tc + 1) * 512],
                            in_=obuf[m % 2]), inc=[(f"gc{m % 2}", 16)], wait=o1)
        g_sc.close()
        x2_sc.close()
        S_.wait("gpsimd", *S_.aft("gc0", "gc1", "st"))
        S_.barrier()

        with nc.Block() as block:
            @block.sync
            def _(e):
                for fn in S_.q["sync"]:
                    fn(e)

            @block.tensor
            def _(e):
                for fn in S_.q["tensor"]:
                    fn(e)

            @block.vector
            def _(e):
                for fn in S_.q["vector"]:
                    fn(e)

            @block.scalar
            def _(e):
                for fn in S_.q["scalar"]:
                    fn(e)

            @block.gpsimd
            def _(e):
                for fn in S_.q["gpsimd"]:
                    fn(e)
    return nc


_CACHE = {}


def kernel(**inputs):
    in_maps, idx = _host_prep(inputs)
    key = ("main", _BUILD_OPTS["dbg"], _BUILD_OPTS["nslab"], _BUILD_OPTS["ssd"], _BUILD_OPTS["xg"], _BUILD_OPTS["op"], _BUILD_OPTS["inproj"], _BUILD_OPTS["cache"], _BUILD_OPTS["nmg"], _BUILD_OPTS["heads"], _BUILD_OPTS["tpose"])
    if key not in _CACHE:
        _CACHE[key] = _build(idx)
    nc = _CACHE[key]
    res = run_bass_kernel_spmd(nc, in_maps, core_ids=list(range(8)))
    full = np.zeros((4, S, HID), np.float32)
    for c in range(8):
        b, r = c // 2, c % 2
        full[b, r * 1024:(r + 1) * 1024, :] = np.asarray(res.results[c]["out"]).T
    if _BUILD_OPTS["dbg"] is not None:
        kernel.dbg = [np.asarray(res.results[c].get("dbg")) for c in range(8)]
    return full


# version 08
# speedup vs baseline: 1.0733x; 1.0733x over previous
import numpy as np
import ml_dtypes
import concourse.bass as bass
import concourse.mybir as mybir
from concourse.bass_utils import run_bass_kernel_spmd

F32 = mybir.dt.float32
BF16 = mybir.dt.bfloat16
AF = mybir.ActivationFunctionType
ALU = mybir.AluOpType
AX = mybir.AxisListType.X

HID = 2048
S = 2048
LC = 256
NSLAB = 4
SLAB = 512
INTER = 4096
PROJ_P = 8576
NMC = 67
EPS_A = 1e-5
EPS_F = 1e-6
NHEAD = 16
SCALE = float(192.0 ** -0.5)
NE = 16
DI = 1408
SHI = 2816
BIG_NEG = -1.0e9

_BUILD_OPTS = {"dbg": None, "nslab": NSLAB, "ssd": True, "xg": True, "op": True, "inproj": True, "cache": True, "nmg": NMC, "heads": 64, "tpose": 3, "xl": True, "kv": True, "att": True, "moe": True}


class Sched:
    ENGINES = ("sync", "tensor", "vector", "scalar", "gpsimd")

    def __init__(self):
        self.q = {e: [] for e in self.ENGINES}
        self.sems = {}

    def add_sem(self, name, handle):
        self.sems[name] = [handle, 0]

    def n(self, name):
        return self.sems[name][1]

    AUTO_CHAIN = {"vector": "v", "scalar": "a"}

    LABELS = {}

    def op(self, eng, build, inc=(), wait=(), chain=True, self_inc=True, label=None):
        wait = list(wait)
        sem_self = self.AUTO_CHAIN.get(eng)
        if sem_self is not None:
            if chain and self.sems[sem_self][1] > 0:
                wait.append((sem_self, self.sems[sem_self][1]))
            if self_inc and not any(nm == sem_self for nm, _ in inc):
                inc = list(inc) + [(sem_self, 1)]
        toks = []
        for nm, amt in inc:
            self.sems[nm][1] += amt
            toks.append((nm, self.sems[nm][1]))
        wl = {}
        for nm, val in wait:
            if val > 0:
                wl[nm] = max(wl.get(nm, 0), val)
        wl = [(self.sems[nm][0], val) for nm, val in wl.items()]
        il = [(self.sems[nm][0], amt) for nm, amt in inc]

        qpos = (eng, len(self.q[eng]))

        def fn(e, build=build, wl=wl, il=il, label=label, qpos=qpos):
            for h, v in wl:
                e.wait_ge(h, v)
            inst = build(e)
            if inst is not None:
                try:
                    Sched.LABELS[inst.ins.name] = label or qpos
                except AttributeError:
                    if label is not None:
                        Sched.LABELS[str(inst.concise())[:40]] = label
            for h, amt in il:
                if inst is None:
                    e.nop().then_inc(h, amt)
                else:
                    inst = inst.then_inc(h, amt)
        self.q[eng].append(fn)
        return toks

    def wait(self, eng, *tokens):
        wl = [(self.sems[nm][0], val) for nm, val in tokens if val > 0]
        if not wl:
            return

        def fn(e, wl=wl):
            for h, v in wl:
                e.wait_ge(h, v)
        self.q[eng].append(fn)

    def aft(self, *names):
        return [(nm, self.n(nm)) for nm in names]

    ENG_DRAIN = {"sync": ("sx0", "sx1", "sw0", "sw1", "sv0", "sv1"),
                 "tensor": ("t",),
                 "vector": ("v",), "scalar": ("a",),
                 "gpsimd": ("gsp", "gbc", "gb0", "gb1", "gc0", "gc1", "pc0",
                             "pc1", "wb0", "wb1", "st", "p")}

    def barrier(self, engines=None):
        engines = engines or list(self.ENGINES)
        toks = []
        for e in engines:
            toks += self.op(e, lambda en: en.nop(), inc=[("bar", 1)],
                            wait=self.aft(*self.ENG_DRAIN[e]), chain=True,
                            self_inc=False)
        for e in engines:
            self.wait(e, *toks[-1:])


def _host_prep(inp):
    x = np.asarray(inp["L0_attn_hidden_states"], np.float32)
    ipw = np.asarray(inp["L0_attn_in_proj_w"], np.float32)
    ipb = np.asarray(inp["L0_attn_in_proj_b"], np.float32)
    convw = np.asarray(inp["L0_attn_conv_w"], np.float32)
    dtb = np.asarray(inp["L0_attn_dt_bias"], np.float32)
    alog = np.asarray(inp["L0_attn_A_log"], np.float32)
    Dv = np.asarray(inp["L0_attn_D"], np.float32)
    nw = np.asarray(inp["L0_attn_norm_weight"], np.float32)
    opw = np.asarray(inp["L0_attn_out_proj_w"], np.float32)
    opb = np.asarray(inp["L0_attn_out_proj_b"], np.float32)
    ln1 = np.asarray(inp["L0_ffn_ln1_w"], np.float32)
    ln2 = np.asarray(inp["L0_ffn_ln2_w"], np.float32)
    qw = np.asarray(inp["L0_ffn_q_w"], np.float32)
    kvw = np.asarray(inp["L0_ffn_kv_w"], np.float32)
    ow = np.asarray(inp["L0_ffn_o_w"], np.float32)
    gw = np.asarray(inp["L0_ffn_gate_w"], np.float32)
    w1 = np.asarray(inp["L0_ffn_w1"], np.float32)
    w2 = np.asarray(inp["L0_ffn_w2"], np.float32)
    shg = np.asarray(inp["L0_ffn_sh_gu_w"], np.float32)
    shd = np.asarray(inp["L0_ffn_sh_down_w"], np.float32)

    ipw_p = np.zeros((HID, PROJ_P), np.float32)
    ipw_p[:, :8512] = ipw
    ipb_p = np.zeros(PROJ_P, np.float32)
    ipb_p[:8512] = ipb

    q_live = qw.reshape(NHEAD, 192, HID)[:, :128, :].reshape(NHEAD * 128, HID)
    qwT = np.ascontiguousarray(q_live.T)
    kwT = np.ascontiguousarray(kvw[:2048].T)
    vwT = np.ascontiguousarray(kvw[2048:4096].T)
    owT = np.ascontiguousarray(ow.T)
    gwT = np.ascontiguousarray(gw.T)
    w1T = np.ascontiguousarray(np.transpose(w1, (0, 2, 1)))
    w2T = np.ascontiguousarray(np.transpose(w2, (0, 2, 1)))
    shgT = np.ascontiguousarray(shg.T)
    shdT = np.ascontiguousarray(shd.T)

    def cols_of(vec, n):
        v = np.zeros(n * 128, np.float32)
        v[:len(vec)] = vec
        return [v[i * 128:(i + 1) * 128] for i in range(n)]

    cols = []
    idx = {}

    def add(name, lst):
        idx[name] = len(cols)
        cols.extend(lst)

    add("inb", cols_of(ipb_p, NMC))
    conv_cols = []
    for mc in range(34):
        for k in range(4):
            conv_cols.append(np.ascontiguousarray(convw[mc * 128:(mc + 1) * 128, k]))
    add("convw", conv_cols)
    add("dtb", cols_of(dtb, 1))
    add("alog", cols_of(alog, 1))
    add("D", cols_of(np.repeat(Dv, 64), 32))
    add("nw", cols_of(nw, 32))
    add("ob", cols_of(opb, 16))
    add("l1", cols_of(ln1, 16))
    add("l2", cols_of(ln2, 16))
    add("ones", [np.ones(128, np.float32)])
    add("epsA", [np.full(128, EPS_A, np.float32)])
    add("epsF", [np.full(128, EPS_F, np.float32)])
    idx["m0"] = len(cols)
    idx["m1"] = len(cols) + 1
    idx["_nv"] = len(cols) + 2
    vecs_common = np.stack(cols, axis=1)

    ssdm = np.zeros((2, 128, 256), np.float32)
    for si in range(2):
        for p in range(128):
            ssdm[si, p, :si * 128 + p] = BIG_NEG
    ssdm_flat = np.ascontiguousarray(ssdm.reshape(256, 256))

    id_f32 = np.eye(128, dtype=np.float32)
    id_bf = np.eye(128, dtype=np.float32).astype(ml_dtypes.bfloat16)

    in_maps = []
    for c in range(8):
        b, r = c // 2, c % 2
        m0c = np.full((128, 1), 1.0 - r, np.float32)
        m1c = np.full((128, 1), float(r), np.float32)
        vecs = np.concatenate([vecs_common, m0c, m1c], axis=1)
        kt = np.arange(S)[:, None]
        qt = np.arange(1024)[None, :] + r * 1024
        m01 = (kt <= qt).astype(np.float32).astype(ml_dtypes.bfloat16)
        in_maps.append({
            "x0T": np.ascontiguousarray(x[b].T), "vecs": np.ascontiguousarray(vecs),
            "ssdm": ssdm_flat, "id_f32": id_f32, "id_bf": id_bf,
            "m01": np.ascontiguousarray(m01),
            "ipw": ipw_p, "opw": opw,
            "qwT": qwT, "kwT": kwT, "vwT": vwT, "owT": owT, "gwT": gwT,
            "w1T": w1T, "w2T": w2T, "shgT": shgT, "shdT": shdT,
        })
    return in_maps, idx


def _build(idx):
    dbg = _BUILD_OPTS["dbg"]
    nc = bass.Bass("TRN2", target_bir_lowering=False, debug=False)
    S_ = Sched()

    x0T = nc.dram_tensor("x0T", [HID, S], F32, kind="ExternalInput").ap()
    vecs_d = nc.dram_tensor("vecs", [128, idx["_nv"]], F32, kind="ExternalInput").ap()
    ssdm_d = nc.dram_tensor("ssdm", [256, 256], F32, kind="ExternalInput").ap()
    idf_d = nc.dram_tensor("id_f32", [128, 128], F32, kind="ExternalInput").ap()
    idb_d = nc.dram_tensor("id_bf", [128, 128], BF16, kind="ExternalInput").ap()
    m01_d = nc.dram_tensor("m01", [S, 1024], BF16, kind="ExternalInput").ap()
    ipw_d = nc.dram_tensor("ipw", [HID, PROJ_P], F32, kind="ExternalInput").ap()
    opw_d = nc.dram_tensor("opw", [INTER, HID], F32, kind="ExternalInput").ap()
    qwT_d = nc.dram_tensor("qwT", [HID, 2048], F32, kind="ExternalInput").ap()
    kwT_d = nc.dram_tensor("kwT", [HID, 2048], F32, kind="ExternalInput").ap()
    vwT_d = nc.dram_tensor("vwT", [HID, 2048], F32, kind="ExternalInput").ap()
    owT_d = nc.dram_tensor("owT", [2048, 2048], F32, kind="ExternalInput").ap()
    gwT_d = nc.dram_tensor("gwT", [HID, 16], F32, kind="ExternalInput").ap()
    w1T_d = nc.dram_tensor("w1T", [NE, HID, 2816], F32, kind="ExternalInput").ap()
    w2T_d = nc.dram_tensor("w2T", [NE, DI, 2048], F32, kind="ExternalInput").ap()
    shgT_d = nc.dram_tensor("shgT", [HID, 2 * SHI], F32, kind="ExternalInput").ap()
    shdT_d = nc.dram_tensor("shdT", [SHI, HID], F32, kind="ExternalInput").ap()
    out_d = nc.dram_tensor("out", [HID, 1024], F32, kind="ExternalOutput").ap()

    gsil_unused = None
    ipwbf_d = nc.dram_tensor("ipwbf", [HID, PROJ_P], BF16).ap()
    cumsp_d = nc.dram_tensor("cumsp", [8 * 64 * 256], F32).ap()
    cssp_d = nc.dram_tensor("cssp", [8 * 64], F32).ap()
    prex1_d = nc.dram_tensor("prex1", [HID, S], F32).ap()
    x1sp_d = nc.dram_tensor("x1sp", [HID, S], F32).ap()
    rows_d = nc.dram_tensor("rows", [64 * 1024], F32).ap()
    kspill_d = nc.dram_tensor("kspill", [2048, S], BF16).ap()
    vspill_d = nc.dram_tensor("vspill", [S, 2048], BF16).ap()
    x1own_d = nc.dram_tensor("x1own", [HID, 1024], F32).ap()
    hidsp_d = nc.dram_tensor("hidsp", [HID, 1024], F32).ap()
    wfsp_d = nc.dram_tensor("wfsp", [NE * 1024], F32).ap()
    w1c_d = nc.dram_tensor("w1c", [NE, HID, 2816], BF16).ap()
    w2c_d = nc.dram_tensor("w2c", [NE, DI, 2048], BF16).ap()
    shgc_d = nc.dram_tensor("shgc", [HID, 2 * SHI], BF16).ap()
    shdc_d = nc.dram_tensor("shdc", [SHI, HID], BF16).ap()
    dbg_d = None
    if dbg is not None:
        dbg_d = nc.dram_tensor("dbg", list(dbg[1]), F32, kind="ExternalOutput").ap()

    from contextlib import ExitStack
    with ExitStack() as stack:
        ent = stack.enter_context
        for snm in ("sx0", "sx1", "sw0", "sw1", "sv0", "sv1", "gsp", "gbc",
                    "gb0", "gb1", "gc0", "gc1", "pc0", "pc1",
                    "wb0", "wb1", "t", "v", "a", "p", "st", "bar"):
            S_.add_sem(snm, ent(nc.semaphore(snm)))

        vecs = ent(nc.sbuf_tensor("vecs_s", [128, idx["_nv"]], F32)).ap()
        idf = ent(nc.sbuf_tensor("idf", [128, 128], F32)).ap()
        idb = ent(nc.sbuf_tensor("idb", [128, 128], BF16)).ap()

        onesb = ent(nc.sbuf_tensor("onesb", [128, 1], BF16)).ap()
        expA = ent(nc.sbuf_tensor("expA", [128, 1], F32)).ap()
        pb = [ent(nc.psum_tensor(f"pb{i}", [128, 512], F32)).ap() for i in range(8)]
        rowb = ent(nc.sbuf_tensor("rowb", [1, 512], F32)).ap()
        rowb2 = ent(nc.sbuf_tensor("rowb2", [1, 512], F32)).ap()
        bcb = [ent(nc.sbuf_tensor(f"bcb{i}", [128, 512], F32)).ap() for i in range(2)]
        mxk = ent(nc.sbuf_tensor("mxk", [1, 64], F32)).ap()
        pbf = [p.bitcast(BF16) for p in pb]

        def vcol(name, j=0):
            return vecs[:, idx[name] + j: idx[name] + j + 1]

        S_.op("sync", lambda e: e.dma_start(out=vecs, in_=vecs_d), inc=[("sx0", 16)])
        S_.op("sync", lambda e: e.dma_start(out=idf, in_=idf_d), inc=[("sx0", 16)])
        S_.op("sync", lambda e: e.dma_start(out=idb, in_=idb_d), inc=[("sx0", 16)])
        S_.op("sync", lambda e: e.dma_start(
            out=ssdm.rearrange("p (a c) -> p a c", a=2),
            in_=ssdm_d.rearrange("(a b) c -> b a c", a=2)),
            inc=[("sx0", 16)])
        pre_tok = S_.aft("sx0")
        S_.op("vector", lambda e: e.tensor_copy(onesb, vcol("ones")),
              inc=[("v", 1)], wait=pre_tok)
        S_.op("scalar", lambda e: e.activation(expA, vcol("alog"), AF.Exp),
              inc=[("a", 1)], wait=pre_tok)

        def dbg_tap(name, srcs):
            if dbg is None or dbg[0] != name:
                return
            S_.barrier()
            for (r0, r1), src in srcs:
                S_.op("gpsimd", lambda e, r0=r0, r1=r1, src=src: e.dma_start(
                    out=dbg_d[r0:r1, :src.shape[1]], in_=src), inc=[("st", 16)])
            S_.barrier()

        a_sc = ExitStack()
        aent = a_sc.enter_context
        ssdm = aent(nc.sbuf_tensor("ssdm_s", [128, 512], F32)).ap()
        x0bf = [aent(nc.sbuf_tensor(f"x0bf{i}", [128, SLAB], BF16)).ap() for i in range(16)]
        x0st = [aent(nc.sbuf_tensor(f"x0st{i}", [128, SLAB], F32)).ap() for i in range(2)]
        wbf = [aent(nc.sbuf_tensor(f"wbf{i}", [128, 2048], BF16)).ap() for i in range(2)]
        hbcb = [aent(nc.sbuf_tensor(f"hbcb{i}", [128, 515], F32)).ap() for i in range(2)]
        cvb = [aent(nc.sbuf_tensor(f"cvb{i}", [128, 512], F32)).ap() for i in range(2)]
        stash = aent(nc.sbuf_tensor("stash", [128, 102], F32)).ap()
        hs = [aent(nc.sbuf_tensor(f"hs{i}", [128, SLAB], BF16)).ap() for i in range(32)]
        Bsl = aent(nc.sbuf_tensor("Bsl", [128, SLAB], BF16)).ap()
        Csl = aent(nc.sbuf_tensor("Csl", [128, SLAB], BF16)).ap()
        dtsl = aent(nc.sbuf_tensor("dtsl", [128, SLAB], F32)).ap()
        dstt = aent(nc.sbuf_tensor("dstt", [128, SLAB], F32)).ap()
        abuf = aent(nc.sbuf_tensor("abuf", [64, 256], F32)).ap()
        cumc = aent(nc.sbuf_tensor("cumc", [128, 512], F32)).ap()
        dcay = aent(nc.sbuf_tensor("dcay", [128, 256], F32)).ap()
        ncT = aent(nc.sbuf_tensor("ncT", [128, 128], F32)).ap()
        ecT = aent(nc.sbuf_tensor("ecT", [128, 128], F32)).ap()
        dtT = aent(nc.sbuf_tensor("dtT", [128, 128], F32)).ap()
        dyT = aent(nc.sbuf_tensor("dyT", [128, 128], F32)).ap()
        Btm = aent(nc.sbuf_tensor("Btm", [128, 256], BF16)).ap()
        bdec = aent(nc.sbuf_tensor("bdec", [128, 512], BF16)).ap()
        Ee = aent(nc.sbuf_tensor("Ee", [128, 64], F32)).ap()
        Sst = aent(nc.sbuf_tensor("Sst", [128, 4096], F32)).ap()
        Sbf = aent(nc.sbuf_tensor("Sbf", [128, 4096], BF16)).ap()
        RB = aent(nc.sbuf_tensor("RB", [128, 2048], F32)).ap()
        tb1 = aent(nc.sbuf_tensor("tb1", [128, 256], F32)).ap()
        tb2 = aent(nc.sbuf_tensor("tb2", [128, 256], F32)).ap()
        mtb = aent(nc.sbuf_tensor("mtb", [128, 1024], BF16)).ap()
        yev = aent(nc.sbuf_tensor("yev", [128, 64], F32)).ap()
        xg = [aent(nc.sbuf_tensor(f"xg{i}", [128, SLAB], BF16)).ap() for i in range(32)]
        gsl = xg
        sqb = [aent(nc.sbuf_tensor(f"sqb{i}", [128, SLAB], BF16)).ap() for i in range(2)]
        prb = [aent(nc.sbuf_tensor(f"prb{i}", [128, SLAB], F32)).ap() for i in range(2)]
        sqf = [aent(nc.sbuf_tensor(f"sqf{i}", [128, SLAB], F32)).ap() for i in range(2)]
        A_un = aent(nc.sbuf_tensor("A_un", [128, 8192], F32)).ap()
        wstg2 = [A_un[:, 0:2048], A_un[:, 2048:4096]]
        hsdt = [A_un[:, 0:2048].bitcast(BF16), A_un[:, 2048:4096].bitcast(BF16)]
        ytm = [A_un[:, 4096:6144].bitcast(BF16), A_un[:, 6144:8192].bitcast(BF16)]
        opst = A_un[:, 0:4096]
        opbf = [A_un[:, 4096:6144].bitcast(BF16), A_un[:, 6144:8192].bitcast(BF16)]

        S_.op("vector", lambda e: e.memset(stash, 0.0), inc=[("v", 1)])
        S_.op("vector", lambda e: e.memset(dtsl, 0.0), inc=[("v", 1)])
        S_.op("vector", lambda e: e.memset(cumc, 0.0), inc=[("v", 1)])
        S_.op("vector", lambda e: e.memset(dcay, 0.0), inc=[("v", 1)])
        S_.op("vector", lambda e: e.memset(Sst, 0.0), inc=[("v", 1)])
        S_.op("vector", lambda e: e.memset(Sbf, 0.0), inc=[("v", 1)])

        for slab in range(_BUILD_OPTS["nslab"]):
            t0 = slab * SLAB
            cast_tok = {}
            for kc in range(16):
                w = (S_.aft("a", "v") if kc < 2 else []) + cast_tok.get(kc % 2, [])
                tok = S_.op("sync", lambda e, kc=kc, t0=t0: e.dma_start(
                    out=x0st[kc % 2], in_=x0T[kc * 128:(kc + 1) * 128, t0:t0 + SLAB]),
                    inc=[(f"sx{kc % 2}", 16)], wait=w)
                cast_tok[kc % 2] = S_.op("scalar", lambda e, kc=kc: e.activation(
                    x0bf[kc], x0st[kc % 2], AF.Copy), inc=[("a", 1)],
                    wait=list(tok) + (S_.aft("t") if kc == 0 else []))
            x0_done = S_.aft("a")

            last_cast = {}
            cache_tok = {}
            mm_war = {}
            bank_ev = {}
            for mg in range((min(_BUILD_OPTS["nmg"], NMC)) if _BUILD_OPTS["inproj"] else 0):
                m0 = mg
                nmc_g = 1
                wcols = 128
                buf = wbf[mg % 2]
                war = mm_war.get(mg % 2, S_.aft("t") if slab > 0 else [])
                if slab == 0:
                    wsb = wstg2[mg % 2]
                    ldw = []
                    for kh in range(2):
                        ldw += S_.op("sync", lambda e, kh=kh, m0=m0, wcols=wcols,
                                     wsb=wsb: e.dma_start(
                            out=wsb[:, kh * 8 * wcols:(kh + 1) * 8 * wcols]
                                .rearrange("p (k c) -> p k c", k=8),
                            in_=ipw_d[kh * 1024:(kh + 1) * 1024, m0 * 128:m0 * 128 + wcols]
                                .rearrange("(k p) c -> p k c", p=128)),
                            inc=[(f"sw{mg % 2}", 16)],
                            wait=(list(war) + last_cast.get(mg % 2, []))
                            if kh == 0 else ())
                    wtok = S_.op("vector", lambda e, buf=buf, wcols=wcols, wsb=wsb:
                                 e.tensor_copy(buf, wsb),
                                 inc=[("v", 1)],
                                 wait=ldw[-1:] + cache_tok.get(mg % 2, [])
                                 + list(war))
                    last_cast[mg % 2] = wtok
                    if _BUILD_OPTS["cache"]:
                        cache_tok[mg % 2] = S_.op(
                            "gpsimd", lambda e, m0=m0, buf=buf, wcols=wcols: e.dma_start(
                                out=ipwbf_d[:, m0 * 128:m0 * 128 + wcols]
                                    .rearrange("(k p) c -> p k c", p=128),
                                in_=buf[:, :16 * wcols].rearrange("p (k c) -> p k c", k=16)),
                            inc=[(f"gc{mg % 2}", 16)], wait=wtok)
                else:
                    wtok = S_.op("sync", lambda e, m0=m0, buf=buf, wcols=wcols: e.dma_start(
                        out=buf[:, :16 * wcols].rearrange("p (k c) -> p k c", k=16),
                        in_=ipwbf_d[:, m0 * 128:m0 * 128 + wcols]
                            .rearrange("(k p) c -> p k c", p=128)),
                        inc=[(f"sw{mg % 2}", 16)],
                        wait=list(war) + S_.aft("gc0", "gc1"))
                for mi in range(nmc_g):
                    m = m0 + mi
                    bank = pb[4 + m % 4]
                    mm_tok = []
                    for k in range(16):
                        first = (k == 0)
                        mm_tok = S_.op("tensor", lambda e, k=k, mi=mi, bank=bank, buf=buf,
                                       wcols=wcols: e.matmul(
                            bank[:, :SLAB],
                            buf[:, k * wcols + mi * 128: k * wcols + (mi + 1) * 128],
                            x0bf[k], start=(k == 0), stop=(k == 15)),
                            inc=[("t", 1)] if k == 15 else (),
                            wait=(list(wtok) + x0_done
                                  + bank_ev.get(m % 4, [])) if first else ())
                    mm_war[mg % 2] = mm_tok
                    if m < 32:
                        bank_ev[m % 4] = S_.op(
                            "scalar", lambda e, bank=bank, m=m: e.activation(
                                gsl[m], bank[:, :SLAB], AF.Silu, bias=vcol("inb", m)),
                            inc=[("a", 1)], wait=mm_tok + S_.aft("v", "t"))
                    elif m < 66:
                        j = m - 32
                        hb = hbcb[j % 2]
                        cb = cvb[j % 2]
                        stok = S_.op("vector", lambda e, hb=hb, j=j: e.tensor_copy(
                            hb[:, 0:3], stash[:, j * 3:(j + 1) * 3]),
                            inc=[("v", 1)], wait=S_.aft("a"))
                        ev = S_.op("scalar", lambda e, bank=bank, hb=hb, m=m: e.activation(
                            hb[:, 3:515], bank[:, :SLAB], AF.Identity, bias=vcol("inb", m)),
                            inc=[("a", 1)], wait=mm_tok + stok)
                        bank_ev[m % 4] = ev
                        S_.op("vector", lambda e, hb=hb, j=j: e.tensor_copy(
                            stash[:, j * 3:(j + 1) * 3], hb[:, 512:515]),
                            inc=[("v", 1)], wait=ev)
                        cv = S_.op("vector", lambda e, hb=hb, cb=cb, j=j: e.tensor_scalar(
                            cb, hb[:, 0:512], vcol("convw", j * 4), None, ALU.mult),
                            inc=[("v", 1)], wait=())
                        for kk in range(1, 4):
                            cv = S_.op("vector", lambda e, hb=hb, cb=cb, j=j, kk=kk:
                                       e.scalar_tensor_tensor(
                                cb, hb[:, kk:512 + kk], vcol("convw", j * 4 + kk), cb,
                                ALU.mult, ALU.add), inc=[("v", 1)], wait=())
                        dst = hs[j] if j < 32 else (Bsl if j == 32 else Csl)
                        S_.op("scalar", lambda e, cb=cb, dst=dst: e.activation(
                            dst, cb, AF.Silu), inc=[("a", 1)], wait=cv)
                    else:
                        ev = S_.op("scalar", lambda e, bank=bank: e.activation(
                            dstt, bank[:, :SLAB], AF.Identity, bias=vcol("inb", 66)),
                            inc=[("a", 1)], wait=mm_tok + S_.aft("v"))
                        bank_ev[m % 4] = ev
                        z = dstt[0:64, :]
                        S_.op("vector", lambda e, z=z: e.tensor_scalar(
                            z, z, vcol("dtb")[0:64, :], None, ALU.add),
                            inc=[("v", 1)], wait=ev)
                        S_.op("scalar", lambda e, z=z: e.activation(
                            dtsl[0:64, :], z, AF.Abs), inc=[("a", 1)], wait=S_.aft("v"))
                        S_.op("scalar", lambda e: e.activation(
                            dtsl[0:64, :], dtsl[0:64, :], AF.Exp, scale=-1.0),
                            inc=[("a", 1)], wait=())
                        S_.op("scalar", lambda e: e.activation(
                            dtsl[0:64, :], dtsl[0:64, :], AF.Ln,
                            bias=vcol("ones")[0:64, :]), inc=[("a", 1)], wait=())
                        S_.op("vector", lambda e, z=z: e.tensor_scalar(
                            z, z, 0.0, None, ALU.max), inc=[("v", 1)],
                            wait=S_.aft("a"))
                        S_.op("vector", lambda e, z=z: e.tensor_add(
                            dtsl[0:64, :], dtsl[0:64, :], z),
                            inc=[("v", 1)], wait=S_.aft("a"))

            dbg_tap("convout", [((j * 128, (j + 1) * 128),
                                 hs[j] if j < 32 else (Bsl if j == 32 else Csl))
                                for j in range(34)] if slab == 0 else [])
            dbg_tap("dt", [((0, 64), dtsl[0:64, :])] if slab == 0 else [])
            dbg_tap("gate", [((j * 128, (j + 1) * 128), gsl[j])
                             for j in range(32)] if slab == 0 else [])

            if _BUILD_OPTS["ssd"]:
                for ci in range(2):
                    c = slab * 2 + ci
                    off = ci * 256
                    pre = S_.aft("a", "v")
                    S_.op("vector", lambda e, off=off: e.tensor_scalar(
                        abuf, dtsl[0:64, off:off + 256], expA[0:64, :], -1.0,
                        ALU.mult, ALU.mult), inc=[("v", 1)], wait=pre)
                    S_.op("vector", lambda e, off=off: e.tensor_tensor_scan(
                        cumc[0:64, off:off + 256], abuf, abuf, 0.0,
                        ALU.add, ALU.bypass), inc=[("v", 1)], wait=())
                    cum_tok = S_.aft("v")
                    S_.op("gpsimd", lambda e, c=c, off=off: e.dma_start(
                        out=cumsp_d[c * 16384:(c + 1) * 16384]
                            .rearrange("(h t) -> h t", h=64),
                        in_=cumc[0:64, off:off + 256]),
                        inc=[("gsp", 16)], wait=cum_tok)
                    S_.op("gpsimd", lambda e, c=c, off=off: e.dma_start(
                        out=cssp_d[c * 64:(c + 1) * 64].unsqueeze(1),
                        in_=cumc[0:64, off + 255:off + 256]),
                        inc=[("gsp", 16)], wait=())
            cumsp_tok = S_.aft("gsp")
            for ci in range(2 if _BUILD_OPTS["ssd"] else 0):
                c = slab * 2 + ci
                off = ci * 256
                pre = S_.aft("a", "v")
                eb = S_.op("gpsimd", lambda e, c=c: e.dma_start(
                    out=Ee[:, 0:64],
                    in_=cssp_d[c * 64:(c + 1) * 64][None, :].to_broadcast([128, 64])),
                    inc=[("gsp", 16)], wait=cumsp_tok + S_.aft("a", "v")),
                S_.op("scalar", lambda e: e.activation(Ee, Ee, AF.Exp),
                      inc=[("a", 1)], wait=eb[0])
                S_.op("scalar", lambda e, off=off: e.activation(
                    dcay[0:64, :], cumc[0:64, off:off + 256], AF.Exp,
                    bias=cumc[0:64, off + 255:off + 256], scale=-1.0),
                    inc=[("a", 1)], wait=())
                dcay_tok = S_.aft("a")
                ev_tok = {}
                for nm, srcf, dst, extra in (
                        ("cum", lambda i, off=off: cumc[:, off + i * 128:
                                                        off + (i + 1) * 128],
                         None, True),
                        ("dt", lambda i, off=off: dtsl[:, off + i * 128:
                                                       off + (i + 1) * 128],
                         dtT, False),
                        ("dy", lambda i: dcay[:, i * 128:(i + 1) * 128], dyT, False)):
                    for i in range(2):
                        bank = pb[2 + i]
                        tt = S_.op("tensor", lambda e, srcf=srcf, i=i, bank=bank:
                                   e.transpose(bank[:, 0:128], srcf(i), idf),
                                   inc=[("t", 1)],
                                   wait=(dcay_tok if nm == "dy" else cum_tok)
                                   + ev_tok.get(("e", i), S_.aft("v", "a", "p")))
                        if nm == "cum":
                            e1 = S_.op("vector", lambda e, i=i, bank=bank:
                                       e.tensor_scalar(
                                ncT[:, i * 64:(i + 1) * 64], bank[:, 0:64],
                                -1.0, None, ALU.mult), inc=[("v", 1)], wait=tt,
                                label=("ncTevac", slab, ci, i))
                            e2 = S_.op("scalar", lambda e, i=i, bank=bank:
                                       e.activation(
                                ecT[:, i * 64:(i + 1) * 64], bank[:, 0:64],
                                AF.Exp), inc=[("a", 1)], wait=tt + e1)
                            ev_tok[("e", i)] = e1 + e2
                        else:
                            e1 = S_.op("vector", lambda e, i=i, bank=bank, dst=dst:
                                       e.tensor_copy(
                                dst[:, i * 64:(i + 1) * 64], bank[:, 0:64]),
                                inc=[("v", 1)], wait=tt)
                            ev_tok[("e", i)] = e1
                for i in range(2):
                    bank = pbf[2 + i]
                    tt = S_.op("tensor", lambda e, i=i, off=off, bank=bank:
                               e.transpose(
                        bank[:, 0:128], Bsl[:, off + i * 128: off + (i + 1) * 128],
                        idb), inc=[("t", 1)], wait=ev_tok.get(("e", i), []) + pre)
                    ev_tok[("e", i)] = S_.op("vector", lambda e, i=i, bank=bank:
                                             e.tensor_copy(
                        Btm[:, i * 128:(i + 1) * 128], bank[:, 0:128]),
                        inc=[("v", 1)], wait=tt)
                hs_ev = {}
                for j in range(32):
                    for li in range(2):
                        par = (j * 2 + li) % 2
                        bank = pbf[4 + par]
                        tt = S_.op("tensor", lambda e, j=j, li=li, off=off,
                                   bank=bank: e.transpose(
                            bank[:, 0:128],
                            hs[j][:, off + li * 128: off + (li + 1) * 128], idb),
                            inc=[("t", 1)], wait=hs_ev.get(par, ev_tok.get(("e", li), [])))
                        ee = []
                        for hh in range(2):
                            ee += S_.op("vector", lambda e, j=j, li=li, hh=hh,
                                        bank=bank: e.tensor_scalar(
                                hsdt[li][:, j * 128 + hh * 64: j * 128 + (hh + 1) * 64],
                                bank[:, hh * 64:(hh + 1) * 64],
                                dtT[:, li * 64 + 2 * j + hh: li * 64 + 2 * j + hh + 1],
                                None, ALU.mult), inc=[("v", 1)],
                                wait=tt if hh == 0 else ())
                        hs_ev[par] = ee
                gm_tok = []
                for si in range(2):
                    gm_tok += S_.op("tensor", lambda e, si=si, off=off: e.matmul(
                        pb[si][:, 0:256], Bsl[:, off + si * 128: off + (si + 1) * 128],
                        Csl[:, off:off + 256], start=True, stop=True),
                        inc=[("t", 1)], wait=S_.aft("v", "a") + pre)
                hb_tok = {}
                last_mult = []
                mt_done = {}
                rb_tok = []

                def emit_mt(h):
                    nonlocal rb_tok, last_mult
                    if h % 8 == 0:
                        rb_tok = S_.op("gpsimd", lambda e, c=c, h=h: e.dma_start(
                            out=RB,
                            in_=cumsp_d[c * 16384 + h * 256: c * 16384 + (h + 8) * 256]
                                [None, :].to_broadcast([128, 2048])),
                            inc=[(f"gb{(c * 8 + h // 8) % 2}", 16)],
                            label=("rb", slab, ci, h),
                            wait=(cumsp_tok if h == 0 else []) + S_.aft("v", "p"))
                    ro = (h % 8) * 256
                    mt_tok = []
                    for si in range(2):
                        s1 = S_.op("vector", lambda e, si=si, h=h, ro=ro:
                                   e.scalar_tensor_tensor(
                            tb1, RB[:, ro:ro + 256],
                            ncT[:, si * 64 + h: si * 64 + h + 1],
                            ssdm[:, si * 256:(si + 1) * 256], ALU.add, ALU.add),
                            label=("s1", slab, ci, h, si),
                            inc=[("v", 1)],
                            wait=list(rb_tok)
                            + (gm_tok if si == 0 and h == 0 else [])
                            + S_.aft("a"))
                        s2 = S_.op("scalar", lambda e: e.activation(
                            tb2, tb1, AF.Exp), inc=[("a", 1)],
                            wait=s1 + last_mult,
                            label=("s2", slab, ci, h, si))
                        last_mult = S_.op("vector", lambda e, si=si, h=h: e.tensor_mul(
                            mtb[:, (h % 2) * 512 + si * 256:
                                (h % 2) * 512 + (si + 1) * 256],
                            tb2, pb[si][:, 0:256]),
                            inc=[("v", 1)],
                            wait=s2 + (hb_tok.get(("ymm", h % 2), [])
                                       if si == 0 else []))
                        mt_tok += last_mult
                    bd_tok = []
                    for si in range(2):
                        bd_tok += S_.op("vector", lambda e, si=si, h=h: e.tensor_scalar(
                            bdec[:, (h % 2) * 256 + si * 128:
                                 (h % 2) * 256 + (si + 1) * 128],
                            Btm[:, si * 128:(si + 1) * 128],
                            dyT[:, si * 64 + h: si * 64 + h + 1], None, ALU.mult),
                            inc=[("v", 1)],
                            wait=hb_tok.get(("lmm", h % 2), []) if si == 0 else ())
                    mt_done[h] = (mt_tok, bd_tok)

                emit_mt(0)
                for h in range(64):
                    mt_tok, bd_tok = mt_done.pop(h)
                    yb = pb[6 + h % 2]
                    lb = pb[4 + h % 2]
                    ymm = []
                    for li in range(2):
                        for si in range(2):
                            ymm = S_.op("tensor", lambda e, li=li, si=si, h=h, yb=yb:
                                        e.matmul(
                                yb[:, li * 128: li * 128 + 64],
                                mtb[:, (h % 2) * 512 + si * 256 + li * 128:
                                    (h % 2) * 512 + si * 256 + (li + 1) * 128],
                                hsdt[si][:, h * 64:(h + 1) * 64],
                                start=(si == 0), stop=(si == 1)),
                                inc=[("t", 1)] if si == 1 else (),
                                wait=(mt_tok + hb_tok.get(("yev", h % 2), []))
                                if (li == 0 and si == 0) else ())
                        if c > 0:
                            ymm = S_.op("tensor", lambda e, li=li, h=h, off=off,
                                        yb=yb: e.matmul(
                                yb[:, li * 128 + 64: li * 128 + 128],
                                Csl[:, off + li * 128: off + (li + 1) * 128],
                                Sbf[:, h * 64:(h + 1) * 64],
                                start=True, stop=True), inc=[("t", 1)], wait=())
                    hb_tok[("ymm", h % 2)] = ymm
                    lmm = []
                    for li in range(2):
                        lmm = S_.op("tensor", lambda e, li=li, h=h, lb=lb: e.matmul(
                            lb[:, 0:64],
                            bdec[:, (h % 2) * 256 + li * 128:
                                 (h % 2) * 256 + (li + 1) * 128],
                            hsdt[li][:, h * 64:(h + 1) * 64],
                            start=(li == 0), stop=(li == 1)),
                            inc=[("t", 1)] if li == 1 else (),
                            wait=(bd_tok + hb_tok.get(("sup", h % 2), []))
                            if li == 0 else ())
                    hb_tok[("lmm", h % 2)] = lmm
                    if h + 1 < 64:
                        emit_mt(h + 1)
                    yev_t = []
                    for li in range(2):
                        if c > 0:
                            ye = S_.op("vector", lambda e, li=li, h=h, yb=yb:
                                       e.tensor_scalar(
                                yev, yb[:, li * 128 + 64: li * 128 + 128],
                                ecT[:, li * 64 + h: li * 64 + h + 1], None, ALU.mult),
                                inc=[("v", 1)], wait=ymm)
                            yev_t += S_.op("vector", lambda e, li=li, h=h, yb=yb:
                                           e.tensor_add(
                                ytm[li][:, h * 64:(h + 1) * 64], yev,
                                yb[:, li * 128: li * 128 + 64]),
                                inc=[("v", 1)], wait=())
                        else:
                            yev_t += S_.op("vector", lambda e, li=li, h=h, yb=yb:
                                           e.tensor_copy(
                                ytm[li][:, h * 64:(h + 1) * 64],
                                yb[:, li * 128: li * 128 + 64]),
                                inc=[("v", 1)], wait=ymm)
                    hb_tok[("yev", h % 2)] = yev_t
                    su = S_.op("vector", lambda e, h=h: e.tensor_scalar(
                        Sst[:, h * 64:(h + 1) * 64], Sst[:, h * 64:(h + 1) * 64],
                        Ee[:, h:h + 1], None, ALU.mult),
                        inc=[("v", 1)], wait=lmm)
                    su = S_.op("vector", lambda e, h=h, lb=lb: e.tensor_add(
                        Sst[:, h * 64:(h + 1) * 64], Sst[:, h * 64:(h + 1) * 64],
                        lb[:, 0:64]), inc=[("v", 1)], wait=())
                    su = S_.op("vector", lambda e, h=h: e.tensor_copy(
                        Sbf[:, h * 64:(h + 1) * 64], Sst[:, h * 64:(h + 1) * 64]),
                        inc=[("v", 1)], wait=())
                    hb_tok[("sup", h % 2)] = su

                dbg_tap("cum", [((0, 64), cumc[0:64, 0:256])] if c == 0 else [])
                dbg_tap("hsdt", [((0, 128), hsdt[0]), ((128, 256), hsdt[1])]
                        if c == 0 else [])
                dbg_tap("S1", [((0, 128), Sst)] if c == 0 else [])

                xg_ev = {}
                for j in range(32 if _BUILD_OPTS["xg"] else 0):
                    par = j % 2
                    bank = pbf[6 + par]
                    tt = []
                    for li in range(2):
                        tt = S_.op("tensor", lambda e, j=j, li=li, bank=bank:
                                   e.transpose(
                            bank[:, li * 128:(li + 1) * 128],
                            ytm[li][:, j * 128:(j + 1) * 128], idb),
                            inc=[("t", 1)] if li == 1 else (),
                            wait=xg_ev.get(par, S_.aft("v")) if li == 0 else ())
                    s1 = S_.op("vector", lambda e, j=j, off=off,
                               bank=bank: e.scalar_tensor_tensor(
                        sqf[0][:, 0:256], hs[j][:, off: off + 256],
                        vcol("D", j), bank[:, 0:256], ALU.mult, ALU.add),
                        inc=[("v", 1)], wait=tt)
                    xg_ev[par] = S_.op("vector", lambda e, j=j, off=off:
                                       e.tensor_mul(
                        xg[j][:, off: off + 256], sqf[0][:, 0:256],
                        gsl[j][:, off: off + 256]),
                        inc=[("v", 1)], wait=())

            if not _BUILD_OPTS["op"]:
                continue
            sq_tok = []
            prev_mm = {}
            for j in range(32):
                sj = S_.op("scalar", lambda e, j=j: e.activation(
                    sqb[j % 2], xg[j], AF.Square), inc=[("a", 1)],
                    wait=(S_.aft("v") if j == 0 else []) + prev_mm.get(j % 2, []))
                sq_tok = S_.op("tensor", lambda e, j=j: e.matmul(
                    pb[5][0:1, 0:SLAB], onesb, sqb[j % 2],
                    start=(j == 0), stop=(j == 31)),
                    inc=[("t", 1)], wait=sj)
                prev_mm[j % 2] = sq_tok
            rv = S_.op("vector", lambda e: e.tensor_copy(rowb, pb[5][0:1, 0:SLAB]),
                       inc=[("v", 1)], wait=sq_tok)
            rv = S_.op("vector", lambda e: e.tensor_scalar(
                rowb, rowb, 1.0 / 4096.0, None, ALU.mult), inc=[("v", 1)], wait=())
            rv = S_.op("scalar", lambda e: e.activation(
                rowb, rowb, AF.Sqrt, bias=vcol("epsA")[0:1, :]), inc=[("a", 1)], wait=rv)
            rv = S_.op("vector", lambda e: e.reciprocal(rowb, rowb),
                       inc=[("v", 1)], wait=rv)
            rd = S_.op("gpsimd", lambda e, slab=slab: e.dma_start(
                out=rows_d[slab * 512:(slab + 1) * 512].unsqueeze(0),
                in_=rowb), inc=[("gsp", 16)], wait=rv)
            rb2 = S_.op("gpsimd", lambda e, slab=slab: e.dma_start(
                out=bcb[0], in_=rows_d[slab * 512:(slab + 1) * 512]
                    [None, :].to_broadcast([128, 512])),
                inc=[("gsp", 16)], wait=rd)
            for j in range(32):
                S_.op("vector", lambda e, j=j: e.scalar_tensor_tensor(
                    xg[j], xg[j], vcol("nw", j), bcb[0], ALU.mult, ALU.mult),
                    inc=[("v", 1)], wait=rb2)
            xgn_done = S_.aft("v")

            for mg in range(4):
                ld_all = []
                mm_tok = []
                kq_tok = {}
                for kq in range(4):
                    lw = S_.op("sync", lambda e, mg=mg, kq=kq: e.dma_start(
                        out=opst.rearrange("p (k c) -> p k c", k=8),
                        in_=opw_d[kq * 1024:(kq + 1) * 1024, mg * 512:(mg + 1) * 512]
                            .rearrange("(k p) c -> p k c", p=128)),
                        inc=[(f"sw{kq % 2}", 16)], wait=S_.aft("v") if (mg + kq) > 0 else S_.aft("t"))
                    cw = S_.op("vector", lambda e, kq=kq: e.tensor_copy(
                        opbf[kq % 2], opst), inc=[("v", 1)],
                        wait=list(lw) + kq_tok.get(kq % 2, S_.aft("t")))
                    for k8 in range(8):
                        for mi in range(4):
                            kk = kq * 8 + k8
                            tok = S_.op("tensor", lambda e, kq=kq, k8=k8, mi=mi, kk=kk:
                                        e.matmul(
                                pb[mi][:, 0:SLAB],
                                opbf[kq % 2][:, k8 * 512 + mi * 128: k8 * 512 + (mi + 1) * 128],
                                xg[kk], start=(kk == 0), stop=(kk == 31)),
                                inc=[("t", 1)] if (kk == 31 or (k8 == 7 and mi == 3))
                                else (),
                                wait=(list(cw) + xgn_done) if (k8 == 0 and mi == 0) else ())
                            if k8 == 7 and mi == 3:
                                kq_tok[kq % 2] = tok
                            if kk == 31:
                                mm_tok += tok
                for mi in range(4):
                    m = mg * 4 + mi
                    xl = S_.op("sync", lambda e, mi=mi, m=m, t0=t0: e.dma_start(
                        out=x0st[mi % 2], in_=x0T[m * 128:(m + 1) * 128, t0:t0 + SLAB]),
                        inc=[(f"sx{mi % 2}", 16)], wait=S_.aft("v"))
                    evt = S_.op("vector", lambda e, mi=mi, m=m: e.scalar_tensor_tensor(
                        prb[mi % 2], pb[mi][:, 0:SLAB], vcol("ob", m), x0st[mi % 2],
                        ALU.add, ALU.add), inc=[("v", 1)],
                        wait=list(xl) + mm_tok + prev_mm.get(("pr", m % 2), [])
                        + prev_mm.get(("sqa", m % 2), []) + S_.aft("gsp"))
                    S_.op("gpsimd", lambda e, mi=mi, m=m, t0=t0: e.dma_start(
                        out=prex1_d[m * 128:(m + 1) * 128, t0:t0 + SLAB],
                        in_=prb[mi % 2]), inc=[("gsp", 16)], wait=evt)
                    sq2 = S_.op("scalar", lambda e, mi=mi: e.activation(
                        sqf[mi % 2], prb[mi % 2], AF.Square), inc=[("a", 1)],
                        wait=evt + prev_mm.get(("sq", m % 2), []))
                    prev_mm[("sqa", m % 2)] = sq2
                    smm = S_.op("tensor", lambda e, m=m, mi=mi: e.matmul(
                        pb[6][0:1, 0:SLAB], vcol("ones"), prb[mi % 2],
                        start=(m == 0), stop=(m == 15)),
                        inc=[("t", 1)], wait=evt)
                    prev_mm[("pr", m % 2)] = smm
                    smm2 = S_.op("tensor", lambda e, m=m, mi=mi: e.matmul(
                        pb[7][0:1, 0:SLAB], vcol("ones"), sqf[mi % 2],
                        start=(m == 0), stop=(m == 15)),
                        inc=[("t", 1)], wait=sq2)
                    prev_mm[("sq", m % 2)] = smm2
            st_tok = S_.aft("t")
            rv = S_.op("vector", lambda e: e.tensor_scalar(
                rowb, pb[6][0:1, 0:SLAB], 1.0 / 2048.0, None, ALU.mult),
                inc=[("v", 1)], wait=st_tok)
            rv2 = S_.op("vector", lambda e: e.tensor_scalar(
                rowb2, pb[7][0:1, 0:SLAB], 1.0 / 2048.0, None, ALU.mult),
                inc=[("v", 1)], wait=())
            sq3 = S_.op("scalar", lambda e: e.activation(
                bcb[1][0:1, :], rowb, AF.Square), inc=[("a", 1)], wait=rv)
            vv = S_.op("vector", lambda e: e.tensor_sub(
                rowb2, rowb2, bcb[1][0:1, :]), inc=[("v", 1)], wait=sq3)
            s1 = S_.op("scalar", lambda e: e.activation(
                bcb[1][0:1, :], rowb2, AF.Sqrt, bias=vcol("epsA")[0:1, :]), inc=[("a", 1)], wait=vv)
            r1 = S_.op("vector", lambda e: e.reciprocal(
                bcb[1][0:1, :], bcb[1][0:1, :]), inc=[("v", 1)], wait=s1)
            S_.op("gpsimd", lambda e, slab=slab: e.dma_start(
                out=rows_d[2048 + slab * 512: 2048 + (slab + 1) * 512].unsqueeze(0),
                in_=rowb), inc=[("gsp", 16)], wait=rv)
            S_.op("gpsimd", lambda e, slab=slab: e.dma_start(
                out=rows_d[4096 + slab * 512: 4096 + (slab + 1) * 512].unsqueeze(0),
                in_=bcb[1][0:1, :]), inc=[("gsp", 16)], wait=r1)
            t1 = S_.op("vector", lambda e: e.tensor_mul(
                rowb2, rowb2, bcb[1][0:1, :]), inc=[("v", 1)], wait=r1)
            t1 = S_.op("vector", lambda e: e.tensor_mul(
                rowb2, rowb2, bcb[1][0:1, :]), inc=[("v", 1)], wait=())
            t2 = S_.op("scalar", lambda e: e.activation(
                rowb2, rowb2, AF.Sqrt, bias=vcol("epsF")[0:1, :]), inc=[("a", 1)], wait=t1)
            t3 = S_.op("vector", lambda e: e.reciprocal(rowb2, rowb2),
                       inc=[("v", 1)], wait=t2)
            S_.op("gpsimd", lambda e, slab=slab: e.dma_start(
                out=rows_d[6144 + slab * 512: 6144 + (slab + 1) * 512].unsqueeze(0),
                in_=rowb2), inc=[("gsp", 16)], wait=t3)

        dbg_tap("prex1", [((0, 2048), prex1_d[:, 0:512])])
        dbg_tap("rows", [((0, 1), rows_d[0:1024].unsqueeze(0)),
                         ((1, 2), rows_d[2048:3072].unsqueeze(0)),
                         ((2, 3), rows_d[4096:5120].unsqueeze(0)),
                         ((3, 4), rows_d[6144:7168].unsqueeze(0))])
        a_sc.close()

        S_.barrier()
        b_sc = ExitStack()
        bent = b_sc.enter_context
        XL = [bent(nc.sbuf_tensor(f"XL{i}", [128, S], BF16)).ap() for i in range(16)]
        prl = [bent(nc.sbuf_tensor(f"prl{i}", [128, SLAB], F32)).ap() for i in range(2)]
        x1b = [bent(nc.sbuf_tensor(f"x1b{i}", [128, SLAB], F32)).ap() for i in range(2)]
        bcB = [bent(nc.sbuf_tensor(f"bcB{i}", [128, SLAB], F32)).ap() for i in range(3)]
        if _BUILD_OPTS["xl"]:
            prev = {}
            for slab in range(NSLAB):
                t0 = slab * SLAB
                bc_tok = []
                for j, roff in enumerate((2048, 4096, 6144)):
                    bc_tok += S_.op("gpsimd", lambda e, j=j, roff=roff, slab=slab:
                                    e.dma_start(
                        out=bcB[j],
                        in_=rows_d[roff + slab * 512: roff + (slab + 1) * 512]
                            [None, :].to_broadcast([128, 512])),
                        inc=[("gbc", 16)], wait=S_.aft("v") if slab else ())
                for m in range(16):
                    pl = S_.op("sync", lambda e, m=m, t0=t0: e.dma_start(
                        out=prl[m % 2], in_=prex1_d[m * 128:(m + 1) * 128, t0:t0 + SLAB]),
                        inc=[(f"sx{m % 2}", 16)],
                        wait=prev.get(("sub", m % 2), S_.aft("gsp")))
                    s1 = S_.op("vector", lambda e, m=m: e.tensor_sub(
                        x1b[m % 2], prl[m % 2], bcB[0]),
                        inc=[("v", 1)], wait=list(pl) + (bc_tok if m == 0 else [])
                        + prev.get(("sp", m % 2), []))
                    prev[("sub", m % 2)] = s1
                    s1 = S_.op("vector", lambda e, m=m: e.tensor_mul(
                        x1b[m % 2], x1b[m % 2], bcB[1]), inc=[("v", 1)], wait=())
                    prev[("sp", m % 2)] = S_.op("gpsimd", lambda e, m=m, t0=t0:
                                                e.dma_start(
                        out=x1sp_d[m * 128:(m + 1) * 128, t0:t0 + SLAB],
                        in_=x1b[m % 2]), inc=[(f"gc{m % 2}", 16)], wait=s1)
                    prev[("xl", m % 2)] = S_.op("vector", lambda e, m=m, t0=t0:
                                                e.scalar_tensor_tensor(
                        XL[m][:, t0:t0 + SLAB], x1b[m % 2], vcol("l1", m), bcB[2],
                        ALU.mult, ALU.mult), inc=[("v", 1)], wait=())
            bl0 = bent(nc.sbuf_tensor("bl0", [128, 1024], F32)).ap()
            bl1 = bent(nc.sbuf_tensor("bl1", [128, 1024], F32)).ap()
            blo = bent(nc.sbuf_tensor("blo", [128, 1024], F32)).ap()
            for m in range(16):
                h0 = S_.op("sync", lambda e, m=m: e.dma_start(
                    out=bl0, in_=x1sp_d[m * 128:(m + 1) * 128, 0:1024]),
                    inc=[("sx0", 16)], wait=S_.aft("gc0", "gc1", "v") if m == 0
                    else S_.aft("gbc"))
                h1 = S_.op("sync", lambda e, m=m: e.dma_start(
                    out=bl1, in_=x1sp_d[m * 128:(m + 1) * 128, 1024:2048]),
                    inc=[("sx1", 16)], wait=())
                bl = S_.op("vector", lambda e, m=m: e.tensor_scalar(
                    blo, bl0, vcol("m0"), None, ALU.mult),
                    inc=[("v", 1)], wait=list(h0) + list(h1) + S_.aft("gbc"))
                bl = S_.op("vector", lambda e, m=m: e.scalar_tensor_tensor(
                    blo, bl1, vcol("m1"), blo, ALU.mult, ALU.add),
                    inc=[("v", 1)], wait=())
                S_.op("gpsimd", lambda e, m=m: e.dma_start(
                    out=x1own_d[m * 128:(m + 1) * 128, :],
                    in_=blo), inc=[("gbc", 16)], wait=bl)

        wst2p = [bent(nc.sbuf_tensor(f"wst2{i}", [128, 2048], F32)).ap()
                 for i in range(2)]
        wkbf = [bent(nc.sbuf_tensor(f"wkbf{i}", [128, 2048], BF16)).ap()
                for i in range(2)]
        kvb = [bent(nc.sbuf_tensor(f"kvb{i}", [128, SLAB], BF16)).ap()
               for i in range(2)]
        sqk = bent(nc.sbuf_tensor("sqk", [128, SLAB], BF16)).ap()
        if _BUILD_OPTS["kv"]:
            S_.op("vector", lambda e: e.memset(mxk, 0.0), inc=[("v", 1)])
            prev = {}
            for m in range(16):
                lw = []
                for kh in range(2):
                    lw += S_.op("sync", lambda e, kh=kh, m=m: e.dma_start(
                        out=wst2p[m % 2][:, kh * 1024:(kh + 1) * 1024]
                            .rearrange("p (k c) -> p k c", k=8),
                        in_=kwT_d[kh * 1024:(kh + 1) * 1024, m * 128:(m + 1) * 128]
                            .rearrange("(k p) c -> p k c", p=128)),
                        inc=[(f"sw{m % 2}", 16)],
                        wait=prev.get(("cst", m % 2), S_.aft("t", "v"))
                        if kh == 0 else ())
                cst = S_.op("vector", lambda e, m=m: e.tensor_copy(
                    wkbf[m % 2], wst2p[m % 2]), inc=[("v", 1)],
                    wait=lw[-1:] + prev.get(("mmk", m % 2), []))
                prev[("cst", m % 2)] = cst
                for tc in range(4):
                    mmt = []
                    for k in range(16):
                        mmt = S_.op("tensor", lambda e, m=m, tc=tc, k=k: e.matmul(
                            pb[tc % 2][:, 0:SLAB],
                            wkbf[m % 2][:, k * 128:(k + 1) * 128],
                            XL[k][:, tc * 512:(tc + 1) * 512],
                            start=(k == 0), stop=(k == 15)),
                            inc=[("t", 1)] if k == 15 else (),
                            wait=(list(cst) + prev.get(("kev", tc % 2), []))
                            if k == 0 else ())
                    if tc == 3:
                        prev[("mmk", m % 2)] = mmt
                    ev = S_.op("scalar", lambda e, m=m, tc=tc: e.activation(
                        kvb[tc % 2], pb[tc % 2][:, 0:SLAB], AF.Copy),
                        inc=[("a", 1)], wait=mmt + prev.get(("ksp", tc % 2), []))
                    prev[("kev", tc % 2)] = ev
                    prev[("ksp", tc % 2)] = S_.op(
                        "gpsimd", lambda e, m=m, tc=tc: e.dma_start(
                            out=kspill_d[m * 128:(m + 1) * 128,
                                         tc * 512:(tc + 1) * 512],
                            in_=kvb[tc % 2]), inc=[(f"gc{tc % 2}", 16)], wait=ev)
                    sq = S_.op("scalar", lambda e, tc=tc: e.activation(
                        sqk, kvb[tc % 2], AF.Square), inc=[("a", 1)],
                        wait=ev + prev.get(("sqk",), []))
                    smm = S_.op("tensor", lambda e: e.matmul(
                        pb[6][0:1, 0:SLAB], onesb, sqk, start=True, stop=True),
                        inc=[("t", 1)], wait=sq + prev.get(("nrm",), []))
                    prev[("sqk",)] = smm
                    rm = S_.op("vector", lambda e: e.reduce_max(
                        rowb[0:1, 0:1], pb[6][0:1, 0:SLAB], axis=AX),
                        inc=[("v", 1)], wait=smm)
                    prev[("nrm",)] = S_.op("vector", lambda e, m=m: e.tensor_max(
                        mxk[0:1, m:m + 1], mxk[0:1, m:m + 1], rowb[0:1, 0:1]),
                        inc=[("v", 1)], wait=())
        vvbf = [bent(nc.sbuf_tensor(f"vvbf{i}", [128, SLAB], BF16)).ap()
                for i in range(16)]
        vst = [bent(nc.sbuf_tensor(f"vst{i}", [128, SLAB], F32)).ap()
               for i in range(2)]
        if _BUILD_OPTS["kv"]:
            lastmm = []
            for vq in range(4):
                casts = []
                for k in range(16):
                    lw = S_.op("sync", lambda e, vq=vq, k=k: e.dma_start(
                        out=vst[k % 2],
                        in_=vwT_d[k * 128:(k + 1) * 128, vq * 512:(vq + 1) * 512]),
                        inc=[(f"sx{k % 2}", 16)],
                        wait=prev.get(("vst", k % 2), S_.aft("v", "a")))
                    casts = S_.op("vector", lambda e, k=k: e.tensor_copy(
                        vvbf[k], vst[k % 2]), inc=[("v", 1)],
                        wait=list(lw) + lastmm)
                    prev[("vst", k % 2)] = casts
                for tb in range(16):
                    mmt = []
                    for k in range(16):
                        mmt = S_.op("tensor", lambda e, vq=vq, tb=tb, k=k: e.matmul(
                            pb[tb % 2][:, 0:SLAB],
                            XL[k][:, tb * 128:(tb + 1) * 128], vvbf[k],
                            start=(k == 0), stop=(k == 15)),
                            inc=[("t", 1)] if k == 15 else (),
                            wait=(casts + prev.get(("vev", tb % 2), []))
                            if k == 0 else ())
                    if tb >= 14:
                        lastmm = lastmm + mmt if tb == 15 else list(mmt)
                    ev = S_.op("scalar", lambda e, tb=tb: e.activation(
                        kvb[tb % 2], pb[tb % 2][:, 0:SLAB], AF.Copy),
                        inc=[("a", 1)],
                        wait=mmt + prev.get(("vsp", tb % 2),
                                            prev.get(("ksp", tb % 2), [])))
                    prev[("vev", tb % 2)] = ev
                    prev[("vsp", tb % 2)] = S_.op(
                        "gpsimd", lambda e, vq=vq, tb=tb: e.dma_start(
                            out=vspill_d[tb * 128:(tb + 1) * 128,
                                         vq * 512:(vq + 1) * 512],
                            in_=kvb[tb % 2]), inc=[(f"gc{tb % 2}", 16)], wait=ev)
                lastmm = lastmm[-1:]
        dbg_tap("kfm", [((0, 2048), kspill_d[:, 0:512])])
        dbg_tap("vtm", [((0, 2048), vspill_d[:, 0:512])])
        S_.barrier()
        b_sc.close()

        x2_sc = ExitStack()
        x2t = [x2_sc.enter_context(
            nc.sbuf_tensor(f"x2t{i}", [128, 1024], BF16, side="right")).ap()
            for i in range(16)]
        de_sc = ExitStack()
        afm = [de_sc.enter_context(
            nc.sbuf_tensor(f"afm{i}", [128, 1024], BF16, side="right")).ap()
            for i in range(16)]
        d_sc = ExitStack()
        dent = d_sc.enter_context
        XLo = [dent(nc.sbuf_tensor(f"XLo{i}", [128, 1024], BF16)).ap()
               for i in range(16)]
        m01t = [dent(nc.sbuf_tensor(f"m01t{i}", [128, 1024], BF16)).ap()
                for i in range(16)]
        q_all = [dent(nc.sbuf_tensor(f"qall{i}", [128, 1024], BF16)).ap()
                 for i in range(16)]
        kh_b2 = [dent(nc.sbuf_tensor(f"kh_b{i}", [128, 2048], BF16)).ap()
                 for i in range(2)]
        vh_b0 = dent(nc.sbuf_tensor("vh_b0", [128, 16 * 129], BF16)).ap()
        vh_b2 = [vh_b0, vh_b0]
        qst0 = dent(nc.sbuf_tensor("qst0", [128, 2048], F32)).ap()
        qst = [qst0, qst0]
        prl2 = [qst0[:, 0:512], qst0[:, 512:1024]]
        qbf = [dent(nc.sbuf_tensor(f"qbf{i}", [128, 2048], BF16)).ap()
               for i in range(2)]
        praw = [bcb[0].bitcast(BF16)[:, 0:512], bcb[1].bitcast(BF16)[:, 0:512]]
        sqq = praw[0]
        pT = [dent(nc.sbuf_tensor(f"pT{i}", [128, SLAB], BF16)).ap()
              for i in range(2)]
        att = [dent(nc.sbuf_tensor(f"att{i}", [128, 128], BF16)).ap()
               for i in range(2)]
        rcp = dent(nc.sbuf_tensor("rcp", [128, 1], F32)).ap()
        ncol = dent(nc.sbuf_tensor("ncol", [128, 16], F32)).ap()

        if _BUILD_OPTS["att"]:
            for tc in range(2):
                b0 = S_.op("gpsimd", lambda e, tc=tc: e.dma_start(
                    out=bcb[0], in_=rows_d[6144 + tc * 512: 6144 + (tc + 1) * 512]
                        [None, :].to_broadcast([128, 512])),
                    inc=[("gbc", 16)], wait=S_.aft("v"))
                b1 = S_.op("gpsimd", lambda e, tc=tc: e.dma_start(
                    out=bcb[1], in_=rows_d[7168 + tc * 512: 7168 + (tc + 1) * 512]
                        [None, :].to_broadcast([128, 512])),
                    inc=[("gbc", 16)], wait=())
                bl = S_.op("vector", lambda e: e.tensor_scalar(
                    bcb[0], bcb[0], vcol("m0"), None, ALU.mult),
                    inc=[("v", 1)], wait=list(b0) + list(b1))
                bl = S_.op("vector", lambda e: e.scalar_tensor_tensor(
                    bcb[0], bcb[1], vcol("m1"), bcb[0], ALU.mult, ALU.add),
                    inc=[("v", 1)], wait=())
                prev2 = {}
                for m in range(16):
                    xo = S_.op("sync", lambda e, m=m, tc=tc: e.dma_start(
                        out=prl2[m % 2],
                        in_=x1own_d[m * 128:(m + 1) * 128, tc * 512:(tc + 1) * 512]),
                        inc=[(f"sx{m % 2}", 16)],
                        wait=prev2.get(m % 2, S_.aft("v", "gbc")))
                    prev2[m % 2] = S_.op("vector", lambda e, m=m, tc=tc:
                                         e.scalar_tensor_tensor(
                        XLo[m][:, tc * 512:(tc + 1) * 512], prl2[m % 2],
                        vcol("l1", m), bcb[0], ALU.mult, ALU.mult),
                        inc=[("v", 1)], wait=list(xo) + (bl if m == 0 else []))
            for m in range(16):
                S_.op("sync", lambda e, m=m: e.dma_start(
                    out=m01t[m], in_=m01_d[m * 128:(m + 1) * 128, :]),
                    inc=[("st", 16)], wait=())
            m01_tok = S_.aft("st")
            S_.op("vector", lambda e: e.memset(vh_b0, 1.0), inc=[("v", 1)])
            prevq = {}
            for h in range(16):
                lw = []
                for kh in range(2):
                    lw += S_.op("sync", lambda e, kh=kh, h=h: e.dma_start(
                        out=qst[h % 2][:, kh * 1024:(kh + 1) * 1024]
                            .rearrange("p (k c) -> p k c", k=8),
                        in_=qwT_d[kh * 1024:(kh + 1) * 1024, h * 128:(h + 1) * 128]
                            .rearrange("(k p) c -> p k c", p=128)),
                        inc=[(f"sw{h % 2}", 16)],
                        wait=prevq.get(("cst", (h - 1) % 2), S_.aft("v", "t"))
                        if kh == 0 else ())
                cst = S_.op("vector", lambda e, h=h: e.tensor_copy(
                    qbf[h % 2], qst[h % 2]), inc=[("v", 1)],
                    wait=lw[-1:] + prevq.get(("mm", h % 2), []))
                prevq[("cst", h % 2)] = cst
                for tc in range(2):
                    mmt = []
                    for k in range(16):
                        mmt = S_.op("tensor", lambda e, h=h, tc=tc, k=k: e.matmul(
                            pb[tc % 2][:, 0:SLAB],
                            qbf[h % 2][:, k * 128:(k + 1) * 128],
                            XLo[k][:, tc * 512:(tc + 1) * 512],
                            start=(k == 0), stop=(k == 15)),
                            inc=[("t", 1)] if k == 15 else (),
                            wait=(list(cst) + prevq.get(("ev", tc % 2), []))
                            if k == 0 else ())
                    if tc == 1:
                        prevq[("mm", h % 2)] = mmt
                    ev = S_.op("scalar", lambda e, h=h, tc=tc: e.activation(
                        q_all[h][:, tc * 512:(tc + 1) * 512],
                        pb[tc % 2][:, 0:SLAB], AF.Copy),
                        inc=[("a", 1)], wait=mmt)
                    prevq[("ev", tc % 2)] = ev
                    sq = S_.op("scalar", lambda e, h=h, tc=tc: e.activation(
                        sqq, q_all[h][:, tc * 512:(tc + 1) * 512], AF.Square),
                        inc=[("a", 1)], wait=())
                    smm = S_.op("tensor", lambda e: e.matmul(
                        pb[6][0:1, 0:SLAB], onesb, sqq, start=True, stop=True),
                        inc=[("t", 1)], wait=sq + prevq.get("nrm", []))
                    rm = S_.op("vector", lambda e: e.reduce_max(
                        rowb[0:1, 0:1], pb[6][0:1, 0:SLAB], axis=AX),
                        inc=[("v", 1)], wait=smm)
                    prevq["nrm"] = S_.op("vector", lambda e, h=h: e.tensor_max(
                        mxk[0:1, 16 + h:17 + h], mxk[0:1, 16 + h:17 + h],
                        rowb[0:1, 0:1]), inc=[("v", 1)], wait=())
            nr = S_.op("vector", lambda e: e.tensor_mul(
                mxk[0:1, 32:48], mxk[0:1, 0:16], mxk[0:1, 16:32]),
                inc=[("v", 1)], wait=prevq["nrm"])
            nr = S_.op("scalar", lambda e: e.activation(
                mxk[0:1, 32:48], mxk[0:1, 32:48], AF.Sqrt), inc=[("a", 1)], wait=nr)
            nr = S_.op("vector", lambda e: e.tensor_scalar(
                mxk[0:1, 32:48], mxk[0:1, 32:48], -SCALE, None, ALU.mult),
                inc=[("v", 1)], wait=nr)
            nr = S_.op("gpsimd", lambda e: e.dma_start(
                out=rows_d[8192:8208].unsqueeze(0), in_=mxk[0:1, 32:48]),
                inc=[("gsp", 16)], wait=nr)
            nr = S_.op("gpsimd", lambda e: e.dma_start(
                out=ncol, in_=rows_d[8192:8208][None, :].to_broadcast([128, 16])),
                inc=[("gsp", 16)], wait=nr)
            ncol_tok = S_.aft("gsp")
            prevh = {}
            for h in range(16):
                kh_b = kh_b2[h % 2]
                vh_b = vh_b2[h % 2]
                klw = S_.op("sync", lambda e, h=h, kh_b=kh_b: e.dma_start(
                    out=kh_b, in_=kspill_d[h * 128:(h + 1) * 128, :]),
                    inc=[(f"sx{h % 2}", 16)],
                    wait=prevh.get(("ks", h % 2),
                                   S_.aft("gc0", "gc1", "a", "v")))
                vlw = []
                for half in range(2):
                    vlw += S_.op("sync", lambda e, h=h, half=half, vh_b=vh_b:
                                 e.dma_start(
                        out=vh_b.rearrange("p (tb c) -> p tb c", c=129)
                            [:, half * 8:(half + 1) * 8, 0:128],
                        in_=vspill_d[:, h * 128:(h + 1) * 128]
                            .rearrange("(tb p) c -> p tb c", p=128)
                            [:, half * 8:(half + 1) * 8, :]),
                        inc=[(f"sx{h % 2}", 16)],
                        wait=prevh.get(("ks", (h - 1) % 2), []) if half == 0 else ())
                for tc in range(2):
                    pv_first = []
                    for kt in range(16):
                        smm = S_.op("tensor", lambda e, h=h, tc=tc, kt=kt,
                                    kh_b=kh_b: e.matmul(
                            pb[kt % 2][:, 0:SLAB],
                            kh_b[:, kt * 128:(kt + 1) * 128],
                            q_all[h][:, tc * 512:(tc + 1) * 512],
                            start=True, stop=True), inc=[("t", 1)],
                            wait=(list(klw) + vlw + ncol_tok + m01_tok
                                  + S_.aft("v", "a"))
                            if kt == 0 and tc == 0 else
                            prevh.get(("ex", kt % 2), []))
                        ex = S_.op("scalar", lambda e, h=h, tc=tc, kt=kt:
                                   e.activation(
                            praw[kt % 2], pb[kt % 2][:, 0:SLAB], AF.Exp,
                            bias=ncol[:, h:h + 1], scale=SCALE),
                            inc=[("a", 1)],
                            wait=smm + prevh.get(("mul", kt % 2), []))
                        prevh[("ex", kt % 2)] = ex
                        mul = S_.op("vector", lambda e, h=h, tc=tc, kt=kt:
                                    e.tensor_mul(
                            pT[kt % 2], praw[kt % 2],
                            m01t[kt][:, tc * 512:(tc + 1) * 512]),
                            inc=[("v", 1)],
                            wait=ex + prevh.get(("pv", kt % 2), []))
                        prevh[("mul", kt % 2)] = mul
                        pvt = []
                        for qs in range(4):
                            pvt = S_.op("tensor", lambda e, kt=kt, qs=qs,
                                        vh_b=vh_b: e.matmul(
                                pb[2 + qs][:, 0:129],
                                pT[kt % 2][:, qs * 128:(qs + 1) * 128],
                                vh_b.rearrange("p (tb c) -> p tb c", c=129)[:, kt, :],
                                start=(kt == 0), stop=(kt == 15)),
                                inc=[("t", 1)] if kt == 15 else (),
                                wait=mul if qs == 0 else ())
                        prevh[("pv", kt % 2)] = pvt if kt == 15 else mul
                    for qs in range(4):
                        rc = S_.op("vector", lambda e, qs=qs: e.reciprocal(
                            rcp, pb[2 + qs][:, 128:129]), inc=[("v", 1)],
                            wait=pvt)
                        at = S_.op("vector", lambda e, qs=qs: e.tensor_scalar(
                            att[qs % 2], pb[2 + qs][:, 0:128], rcp, None,
                            ALU.mult), inc=[("v", 1)],
                            wait=prevh.get(("atv", qs % 2), []))
                        tt = S_.op("tensor", lambda e, qs=qs: e.transpose(
                            pbf[6][:, 0:128], att[qs % 2], idb),
                            inc=[("t", 1)], wait=at + prevh.get(("afm",), []))
                        prevh[("afm",)] = S_.op(
                            "vector", lambda e, h=h, tc=tc, qs=qs: e.tensor_copy(
                                afm[h][:, tc * 512 + qs * 128:
                                       tc * 512 + (qs + 1) * 128],
                                pbf[6][:, 0:128]), inc=[("v", 1)], wait=tt)
                        prevh[("atv", qs % 2)] = prevh[("afm",)]
                prevh[("ks", h % 2)] = smm + pvt
        dbg_tap("attn", [((h * 128, (h + 1) * 128), afm[h]) for h in range(16)])
        d_sc.close()

        e_sc = ExitStack()
        eent = e_sc.enter_context
        hid = [eent(nc.sbuf_tensor(f"hid{i}", [128, 1024], F32)).ap()
               for i in range(16)]
        ost2 = [eent(nc.sbuf_tensor(f"ost{i}", [128, 2048], F32)).ap()
                for i in range(2)]
        obf = [eent(nc.sbuf_tensor(f"obf{i}", [128, 2048], BF16)).ap()
               for i in range(2)]
        x1r = [eent(nc.sbuf_tensor(f"x1r{i}", [128, SLAB], F32)).ap()
               for i in range(2)]
        sqe = [eent(nc.sbuf_tensor(f"sqe{i}", [128, SLAB], F32)).ap()
               for i in range(2)]
        if _BUILD_OPTS["moe"]:
            prevo = {}
            for m in range(16):
                lw = []
                for kh in range(2):
                    lw += S_.op("sync", lambda e, kh=kh, m=m: e.dma_start(
                        out=ost2[m % 2][:, kh * 1024:(kh + 1) * 1024]
                            .rearrange("p (k c) -> p k c", k=8),
                        in_=owT_d[kh * 1024:(kh + 1) * 1024, m * 128:(m + 1) * 128]
                            .rearrange("(k p) c -> p k c", p=128)),
                        inc=[(f"sw{m % 2}", 16)],
                        wait=prevo.get(("cst", m % 2), S_.aft("v", "t"))
                        if kh == 0 else ())
                cst = S_.op("vector", lambda e, m=m: e.tensor_copy(
                    obf[m % 2], ost2[m % 2]), inc=[("v", 1)],
                    wait=lw[-1:] + prevo.get(("mm", m % 2), []))
                prevo[("cst", m % 2)] = cst
                for tc in range(2):
                    mmt = []
                    for k in range(16):
                        mmt = S_.op("tensor", lambda e, m=m, tc=tc, k=k: e.matmul(
                            pb[tc % 2][:, 0:SLAB],
                            obf[m % 2][:, k * 128:(k + 1) * 128],
                            afm[k][:, tc * 512:(tc + 1) * 512],
                            start=(k == 0), stop=(k == 15)),
                            inc=[("t", 1)] if k == 15 else (),
                            wait=(list(cst) + prevo.get(("ev", tc % 2), []))
                            if k == 0 else ())
                    if tc == 1:
                        prevo[("mm", m % 2)] = mmt
                    xo = S_.op("sync", lambda e, m=m, tc=tc: e.dma_start(
                        out=x1r[tc % 2],
                        in_=x1own_d[m * 128:(m + 1) * 128, tc * 512:(tc + 1) * 512]),
                        inc=[(f"sx{tc % 2}", 16)],
                        wait=prevo.get(("x1", tc % 2), S_.aft("v")))
                    ev = S_.op("vector", lambda e, m=m, tc=tc: e.tensor_add(
                        hid[m][:, tc * 512:(tc + 1) * 512], pb[tc % 2][:, 0:SLAB],
                        x1r[tc % 2]), inc=[("v", 1)], wait=mmt + list(xo))
                    prevo[("ev", tc % 2)] = ev
                    prevo[("x1", tc % 2)] = ev
            for tc in range(2):
                smt = []
                prevs = {}
                for m in range(16):
                    sq = S_.op("scalar", lambda e, m=m, tc=tc: e.activation(
                        sqe[m % 2], hid[m][:, tc * 512:(tc + 1) * 512], AF.Square),
                        inc=[("a", 1)],
                        wait=(S_.aft("v") if m == 0 else [])
                        + prevs.get(m % 2, []))
                    smt = S_.op("tensor", lambda e, m=m: e.matmul(
                        pb[6][0:1, 0:SLAB], vcol("ones"), sqe[m % 2],
                        start=(m == 0), stop=(m == 15)),
                        inc=[("t", 1)], wait=sq)
                    prevs[m % 2] = smt
                rv = S_.op("vector", lambda e: e.tensor_scalar(
                    rowb, pb[6][0:1, 0:SLAB], 1.0 / 2048.0, None, ALU.mult),
                    inc=[("v", 1)], wait=smt)
                rv = S_.op("scalar", lambda e: e.activation(
                    rowb, rowb, AF.Sqrt, bias=vcol("epsF")[0:1, :]),
                    inc=[("a", 1)], wait=rv)
                rv = S_.op("vector", lambda e: e.reciprocal(rowb, rowb),
                           inc=[("v", 1)], wait=rv)
                rd = S_.op("gpsimd", lambda e, tc=tc: e.dma_start(
                    out=rows_d[8320 + tc * 512: 8320 + (tc + 1) * 512].unsqueeze(0),
                    in_=rowb), inc=[("gsp", 16)], wait=rv)
                rb = S_.op("gpsimd", lambda e, tc=tc: e.dma_start(
                    out=bcb[0], in_=rows_d[8320 + tc * 512: 8320 + (tc + 1) * 512]
                        [None, :].to_broadcast([128, 512])),
                    inc=[("gsp", 16)], wait=rd)
                for m in range(16):
                    x2w = S_.op("vector", lambda e, m=m, tc=tc:
                                e.scalar_tensor_tensor(
                        x2t[m][:, tc * 512:(tc + 1) * 512],
                        hid[m][:, tc * 512:(tc + 1) * 512], vcol("l2", m),
                        bcb[0], ALU.mult, ALU.mult), inc=[("v", 1)],
                        wait=rb if m == 0 else ())
            for m in range(16):
                S_.op("gpsimd", lambda e, m=m: e.dma_start(
                    out=hidsp_d[m * 128:(m + 1) * 128, :], in_=hid[m]),
                    inc=[(f"gc{m % 2}", 16)], wait=S_.aft("v"))
        dbg_tap("x2", [((m * 128, (m + 1) * 128), x2t[m]) for m in range(16)])
        de_sc.close()
        e_sc.close()
        S_.barrier()

        g_sc = ExitStack()
        gent = g_sc.enter_context
        acc = [gent(nc.sbuf_tensor(f"acc{i}", [128, SLAB], F32)).ap()
               for i in range(16)]
        w1bf = [gent(nc.sbuf_tensor(f"w1bf{i}", [128, 1408], BF16)).ap()
                for i in range(16)]
        w1st = [gent(nc.sbuf_tensor(f"w1st{i}", [128, 1408], F32)).ap()
                for i in range(2)]
        gsb2 = [gent(nc.sbuf_tensor(f"gsb2{i}", [128, SLAB], BF16)).ap()
                for i in range(22)]
        ab2 = gsb2
        w2bf = [gent(nc.sbuf_tensor(f"w2bf{i}", [128, SLAB], BF16)).ap()
                for i in range(22)]
        w2st = [gent(nc.sbuf_tensor(f"w2st{i}", [128, SLAB], F32)).ap()
                for i in range(2)]
        Wb = [gent(nc.sbuf_tensor(f"Wb{i}", [128, SLAB], F32)).ap()
              for i in range(2)]
        gwb = gent(nc.sbuf_tensor("gwb", [128, 256], BF16)).ap()
        gwst = gent(nc.sbuf_tensor("gwst", [128, 256], F32)).ap()
        lbuf = gent(nc.sbuf_tensor("lbuf", [128, 16], F32)).ap()
        ebuf = gent(nc.sbuf_tensor("ebuf", [128, 16], F32)).ap()
        curb = gent(nc.sbuf_tensor("curb", [128, 16], F32)).ap()
        selb = gent(nc.sbuf_tensor("selb", [128, 16], F32)).ap()
        eqb = gent(nc.sbuf_tensor("eqb", [128, 16], F32)).ap()
        mxc = gent(nc.sbuf_tensor("mxc", [128, 1], F32)).ap()
        wfp = gent(nc.sbuf_tensor("wfp", [128, 128], BF16)).ap()
        wfT = gent(nc.sbuf_tensor("wfT", [16, 1024], F32)).ap()
        hrl = [gent(nc.sbuf_tensor(f"hrl{i}", [128, SLAB], F32)).ap()
               for i in range(2)]
        orow = gent(nc.sbuf_tensor("orow", [1, 512], F32)).ap()
        orow2 = gent(nc.sbuf_tensor("orow2", [1, 512], F32)).ap()
        obuf = [w1st[0][:, 512:1024], w1st[1][:, 512:1024]]
        hr2 = Wb
        sqg = [w1st[0][:, 0:512], w1st[1][:, 0:512]]
        if _BUILD_OPTS["moe"]:
            gw16 = gwst
            glw = []
            for k in range(16):
                glw += S_.op("sync", lambda e, k=k: e.dma_start(
                    out=gw16[:, k * 16:(k + 1) * 16],
                    in_=gwT_d[k * 128:(k + 1) * 128, :]),
                    inc=[("sw0", 16)], wait=S_.aft("v", "a") if k == 0 else ())
            gcst = S_.op("vector", lambda e: e.tensor_copy(gwb, gw16),
                         inc=[("v", 1)], wait=glw[-1:])
            S_.op("vector", lambda e: e.memset(wfp, 0.0), inc=[("v", 1)])

            prevr = {}
            for ti in range(8):
                rmm = []
                for k in range(16):
                    rmm = S_.op("tensor", lambda e, ti=ti, k=k: e.matmul(
                        pb[4][:, 0:16], x2t[k][:, ti * 128:(ti + 1) * 128],
                        gwb[:, k * 16:(k + 1) * 16],
                        start=(k == 0), stop=(k == 15)),
                        inc=[("t", 1)] if k == 15 else (),
                        wait=(gcst + S_.aft("v", "a")) if (ti == 0 and k == 0)
                        else (prevr.get("lb", []) if k == 0 else ()))
                lb = S_.op("vector", lambda e: e.tensor_copy(lbuf, pb[4][:, 0:16]),
                           inc=[("v", 1)], wait=rmm)
                prevr["lb"] = lb
                ng = S_.op("vector", lambda e: e.reduce_max(
                    mxc, lbuf, axis=AX, negate=True), inc=[("v", 1)], wait=())
                S_.op("scalar", lambda e: e.activation(
                    ebuf, lbuf, AF.Exp, bias=mxc), inc=[("a", 1)], wait=ng)
                S_.op("vector", lambda e: e.tensor_copy(curb, lbuf),
                      inc=[("v", 1)], wait=())
                S_.op("vector", lambda e: e.memset(selb, 0.0), inc=[("v", 1)])
                for it in range(4):
                    S_.op("vector", lambda e: e.reduce_max(mxc, curb, axis=AX),
                          inc=[("v", 1)], wait=S_.aft("a") if it == 0 else ())
                    S_.op("vector", lambda e: e.tensor_scalar(
                        eqb, curb, mxc, None, ALU.is_equal), inc=[("v", 1)], wait=())
                    S_.op("vector", lambda e: e.tensor_add(selb, selb, eqb),
                          inc=[("v", 1)], wait=())
                    if it < 3:
                        S_.op("vector", lambda e: e.scalar_tensor_tensor(
                            curb, eqb, -1.0e30, curb, ALU.mult, ALU.add),
                            inc=[("v", 1)], wait=())
                S_.op("vector", lambda e: e.tensor_scalar(
                    selb, selb, 1.0, None, ALU.min), inc=[("v", 1)], wait=())
                S_.op("vector", lambda e: e.tensor_mul(eqb, ebuf, selb),
                      inc=[("v", 1)], wait=S_.aft("a"))
                S_.op("vector", lambda e: e.tensor_reduce(
                    mxc, eqb, AX, ALU.add), inc=[("v", 1)], wait=())
                S_.op("vector", lambda e: e.reciprocal(mxc, mxc),
                      inc=[("v", 1)], wait=())
                wf = S_.op("vector", lambda e: e.tensor_scalar(
                    wfp[:, 0:16], eqb, mxc, None, ALU.mult),
                    inc=[("v", 1)], wait=prevr.get("tt", []))
                ttk = S_.op("tensor", lambda e: e.transpose(
                    pbf[5][:, 0:128], wfp, idb), inc=[("t", 1)],
                    wait=wf + prevr.get("wfev", []))
                prevr["tt"] = ttk
                prevr["wfev"] = S_.op("vector", lambda e, ti=ti: e.tensor_copy(
                    wfT[:, ti * 128:(ti + 1) * 128], pbf[5][0:16, 0:128]),
                    inc=[("v", 1)], wait=ttk)
            wsp = S_.op("gpsimd", lambda e: e.dma_start(
                out=wfsp_d.rearrange("(a b) -> a b", a=16), in_=wfT),
                inc=[("gsp", 16)], wait=prevr["wfev"])
            dbg_tap("wf", [((0, 16), wfsp_d.rearrange("(a b) -> a b", a=16))])

            ccache = {}
            for tc in range(2):
                if tc == 1:
                    S_.barrier()
                prevg = {}
                first_acc = [True] * 16
                for ep in range(9):
                    shared = (ep == 8)
                    for ei in range(2):
                        eg = ep * 2 + ei
                        if not shared:
                            prevg[("wb", ei)] = S_.op(
                                "gpsimd", lambda e, eg=eg, ei=ei, tc=tc: e.dma_start(
                                    out=Wb[ei],
                                    in_=wfsp_d[eg * 1024 + tc * 512:
                                               eg * 1024 + (tc + 1) * 512]
                                    [None, :].to_broadcast([128, 512])),
                                inc=[(f"wb{ei}", 16)],
                                wait=wsp + prevg.get(("wbuse", ei), []))
                        for ih in range(2):
                            q4 = ei * 2 + ih
                            is_g = (q4 < 2) if shared else (ih == 0)
                            castk = {}
                            for kb in range(16):
                                if shared:
                                    wsrc = shgT_d[kb * 128:(kb + 1) * 128,
                                                  q4 * 1408:(q4 + 1) * 1408]
                                    wdst = shgc_d[kb * 128:(kb + 1) * 128,
                                                  q4 * 1408:(q4 + 1) * 1408]
                                else:
                                    wsrc = w1T_d[eg, kb * 128:(kb + 1) * 128,
                                                 ih * 1408:(ih + 1) * 1408]
                                    wdst = w1c_d[eg, kb * 128:(kb + 1) * 128,
                                                 ih * 1408:(ih + 1) * 1408]
                                if tc == 0:
                                    lw = S_.op("sync", lambda e, wsrc=wsrc, kb=kb:
                                               e.dma_start(
                                        out=w1st[kb % 2], in_=wsrc),
                                        inc=[(f"sw{kb % 2}", 16)],
                                        wait=prevg.get(("w1c", kb % 2),
                                                       S_.aft("v", "a", "t",
                                                              "gc0", "gc1")))
                                    ceng = "vector" if kb % 2 == 0 else "scalar"
                                    cop = ((lambda e, kb=kb: e.tensor_copy(
                                        w1bf[kb], w1st[kb % 2]))
                                        if ceng == "vector" else
                                        (lambda e, kb=kb: e.activation(
                                            w1bf[kb], w1st[kb % 2], AF.Copy)))
                                    ck = S_.op(ceng, cop,
                                               wait=list(lw)
                                               + prevg.get(("w1u", kb), [])
                                               + (S_.aft("pc0", "pc1")
                                                  if kb < 2 else []))
                                    prevg[("w1c", kb % 2)] = ck
                                    S_.op("gpsimd", lambda e, kb=kb, wdst=wdst:
                                          e.dma_start(out=wdst, in_=w1bf[kb]),
                                          inc=[(f"pc{kb % 2}", 16)], wait=ck)
                                else:
                                    ck = S_.op("sync", lambda e, kb=kb, wdst=wdst:
                                               e.dma_start(
                                        out=w1bf[kb], in_=wdst),
                                        inc=[(f"sw{kb % 2}", 16)],
                                        wait=prevg.get(("w1u", kb),
                                                       S_.aft("v", "a", "t")))
                                    prevg[("w1c", kb % 2)] = ck
                                castk[kb] = ck
                            for wv, irng in ((0, range(0, 5)), (1, range(5, 11))):
                                mm_tok = {}
                                for kb in range(16):
                                    for i in irng:
                                        bnk = i if wv == 0 else i - 5
                                        w = []
                                        if i == irng.start:
                                            if wv == 0:
                                                if tc == 0:
                                                    w += castk[kb]
                                                elif kb == 0:
                                                    w += (castk[14]
                                                          + castk[15])
                                            if kb == 0:
                                                for ii in irng:
                                                    b2 = ii if wv == 0 else ii - 5
                                                    w += prevg.get(
                                                        ("bev", b2),
                                                        S_.aft("v", "a"))
                                        gm = S_.op("tensor", lambda e, i=i, kb=kb,
                                                   tc=tc, bnk=bnk: e.matmul(
                                            pb[bnk][:, 0:SLAB],
                                            w1bf[kb][:, i * 128:(i + 1) * 128],
                                            x2t[kb][:, tc * 512:(tc + 1) * 512],
                                            start=(kb == 0), stop=(kb == 15)),
                                            inc=[("t", 1)]
                                            if (kb == 15 or (wv == 1 and i == 10))
                                            else (), wait=w)
                                        if kb == 15:
                                            mm_tok[i] = gm
                                        if wv == 1 and i == 10:
                                            prevg[("w1u", kb)] = gm
                                for i in irng:
                                    bnk = i if wv == 0 else i - 5
                                    idx2 = ((q4 % 2) if shared else ei) * 11 + i
                                    if is_g:
                                        ev = S_.op("scalar", lambda e, idx2=idx2,
                                                   bnk=bnk: e.activation(
                                            gsb2[idx2], pb[bnk][:, 0:SLAB],
                                            AF.Silu), inc=[("a", 1)],
                                            wait=mm_tok[i]
                                            + prevg.get(("gsb", idx2), []))
                                        prevg[("gtok", idx2)] = ev
                                    else:
                                        if not shared:
                                            ev = S_.op("vector", lambda e,
                                                       idx2=idx2, ei=ei:
                                                       e.tensor_mul(
                                                gsb2[idx2], gsb2[idx2], Wb[ei]),
                                                inc=[("v", 1)],
                                                wait=prevg.get(("wb", ei), [])
                                                + prevg.get(("gtok", idx2), []))
                                            prevg[("wbuse", ei)] = ev
                                            gw_ = ()
                                        else:
                                            gw_ = prevg.get(("gtok", idx2), [])
                                        ev = S_.op("vector", lambda e, idx2=idx2,
                                                   bnk=bnk: e.tensor_mul(
                                            ab2[idx2], gsb2[idx2],
                                            pb[bnk][:, 0:SLAB]),
                                            inc=[("v", 1)],
                                            wait=mm_tok[i] + list(gw_))
                                        prevg[("gsb", idx2)] = ev
                                    prevg[("bev", bnk)] = ev
                    for hq in range(4):
                        castj = {}
                        for j in range(22):
                            if shared:
                                wsrc = shdT_d[j * 128:(j + 1) * 128,
                                              hq * 512:(hq + 1) * 512]
                                wdst = shdc_d[j * 128:(j + 1) * 128,
                                              hq * 512:(hq + 1) * 512]
                            else:
                                e2 = ep * 2 + j // 11
                                kb2 = j % 11
                                wsrc = w2T_d[e2, kb2 * 128:(kb2 + 1) * 128,
                                             hq * 512:(hq + 1) * 512]
                                wdst = w2c_d[e2, kb2 * 128:(kb2 + 1) * 128,
                                             hq * 512:(hq + 1) * 512]
                            if tc == 0:
                                lw = S_.op("sync", lambda e, wsrc=wsrc, j=j:
                                           e.dma_start(
                                    out=w2st[j % 2], in_=wsrc),
                                    inc=[(f"sv{j % 2}", 16)],
                                    wait=prevg.get(("w2l", j % 2),
                                                   S_.aft("v", "a", "t",
                                                          "gc0", "gc1")))
                                ceng = "vector" if j % 2 == 0 else "scalar"
                                cop = ((lambda e, j=j: e.tensor_copy(
                                    w2bf[j], w2st[j % 2]))
                                    if ceng == "vector" else
                                    (lambda e, j=j: e.activation(
                                        w2bf[j], w2st[j % 2], AF.Copy)))
                                ck = S_.op(ceng, cop,
                                           wait=list(lw)
                                           + prevg.get(("w2u", j), [])
                                           + (S_.aft("pc0", "pc1")
                                              if j < 2 else []))
                                prevg[("w2l", j % 2)] = ck
                                S_.op("gpsimd", lambda e, j=j, wdst=wdst:
                                      e.dma_start(out=wdst, in_=w2bf[j]),
                                      inc=[(f"pc{j % 2}", 16)], wait=ck)
                            else:
                                ck = S_.op("sync", lambda e, j=j, wdst=wdst:
                                           e.dma_start(
                                    out=w2bf[j], in_=wdst),
                                    inc=[(f"sv{j % 2}", 16)],
                                    wait=prevg.get(("w2u", j),
                                                   S_.aft("v", "a", "t")))
                                prevg[("w2l", j % 2)] = ck
                            castj[j] = ck
                        dm_tok = {}
                        for j in range(22):
                            for hc in range(4):
                                w = []
                                if hc == 0:
                                    if tc == 0:
                                        w += castj[j]
                                    elif j == 0:
                                        w += castj[20] + castj[21]
                                    if hq == 0:
                                        w += prevg.get(("gsb", j), [])
                                    if j == 0:
                                        for b2 in range(4, 8):
                                            w += prevg.get(("bev", b2),
                                                           S_.aft("v", "a"))
                                dm = S_.op("tensor", lambda e, j=j, hc=hc:
                                           e.matmul(
                                    pb[4 + hc][:, 0:SLAB],
                                    w2bf[j][:, hc * 128:(hc + 1) * 128],
                                    ab2[j], start=(j == 0), stop=(j == 21)),
                                    inc=[("t", 1)] if (hc == 3 or j == 21)
                                    else (), wait=w)
                                if hc == 3:
                                    prevg[("w2u", j)] = dm
                                if j == 21:
                                    dm_tok[hc] = dm
                        for hc in range(4):
                            h = hq * 4 + hc
                            if first_acc[h]:
                                ev = S_.op("vector", lambda e, h=h, hc=hc:
                                           e.tensor_copy(
                                    acc[h], pb[4 + hc][:, 0:SLAB]),
                                    inc=[("v", 1)], wait=dm_tok[hc])
                                first_acc[h] = False
                            else:
                                ev = S_.op("vector", lambda e, h=h, hc=hc:
                                           e.tensor_add(
                                    acc[h], acc[h], pb[4 + hc][:, 0:SLAB]),
                                    inc=[("v", 1)], wait=dm_tok[hc])
                            prevg[("bev", 4 + hc)] = ev
                prevf = {}
                smt = smt2 = []
                for m in range(16):
                    hl = S_.op("sync", lambda e, m=m, tc=tc: e.dma_start(
                        out=hrl[m % 2],
                        in_=hidsp_d[m * 128:(m + 1) * 128, tc * 512:(tc + 1) * 512]),
                        inc=[(f"sx{m % 2}", 16)],
                        wait=prevf.get(("h", m % 2), S_.aft("v")))
                    xl2 = S_.op("sync", lambda e, m=m, tc=tc: e.dma_start(
                        out=hr2[m % 2],
                        in_=x1own_d[m * 128:(m + 1) * 128, tc * 512:(tc + 1) * 512]),
                        inc=[(f"sv{m % 2}", 16)],
                        wait=prevf.get(("x", m % 2),
                                       prevg.get(("w2l", m % 2), ())))
                    a1 = S_.op("vector", lambda e, m=m: e.tensor_add(
                        acc[m], acc[m], hrl[m % 2]), inc=[("v", 1)],
                        wait=list(hl) + list(xl2))
                    a1 = S_.op("vector", lambda e, m=m: e.tensor_add(
                        acc[m], acc[m], hr2[m % 2]), inc=[("v", 1)], wait=())
                    prevf[("h", m % 2)] = a1
                    prevf[("x", m % 2)] = a1
                    sq = S_.op("scalar", lambda e, m=m: e.activation(
                        sqg[m % 2], acc[m], AF.Square), inc=[("a", 1)],
                        wait=a1 + prevf.get(("sq", m % 2), []))
                    smt = S_.op("tensor", lambda e, m=m: e.matmul(
                        pb[6][0:1, 0:SLAB], vcol("ones"), acc[m],
                        start=(m == 0), stop=(m == 15)),
                        inc=[("t", 1)], wait=a1)
                    smt2 = S_.op("tensor", lambda e, m=m: e.matmul(
                        pb[7][0:1, 0:SLAB], vcol("ones"), sqg[m % 2],
                        start=(m == 0), stop=(m == 15)),
                        inc=[("t", 1)], wait=sq)
                    prevf[("sq", m % 2)] = smt2
                rv = S_.op("vector", lambda e: e.tensor_scalar(
                    rowb, pb[6][0:1, 0:SLAB], 1.0 / 2048.0, None, ALU.mult),
                    inc=[("v", 1)], wait=smt + smt2)
                rv2 = S_.op("vector", lambda e: e.tensor_scalar(
                    rowb2, pb[7][0:1, 0:SLAB], 1.0 / 2048.0, None, ALU.mult),
                    inc=[("v", 1)], wait=())
                sq3 = S_.op("scalar", lambda e: e.activation(
                    orow, rowb, AF.Square), inc=[("a", 1)], wait=rv)
                vv = S_.op("vector", lambda e: e.tensor_sub(
                    rowb2, rowb2, orow), inc=[("v", 1)], wait=sq3)
                s1 = S_.op("scalar", lambda e: e.activation(
                    orow, rowb2, AF.Sqrt, bias=vcol("epsA")[0:1, :]),
                    inc=[("a", 1)], wait=vv)
                r1 = S_.op("vector", lambda e: e.reciprocal(orow, orow),
                           inc=[("v", 1)], wait=s1)
                rd = S_.op("gpsimd", lambda e, tc=tc: e.dma_start(
                    out=rows_d[9344 + tc * 512: 9344 + (tc + 1) * 512].unsqueeze(0),
                    in_=rowb), inc=[("gsp", 16)], wait=rv)
                rd2 = S_.op("gpsimd", lambda e, tc=tc: e.dma_start(
                    out=rows_d[10368 + tc * 512: 10368 + (tc + 1) * 512].unsqueeze(0),
                    in_=orow), inc=[("gsp", 16)], wait=r1)
                b1 = S_.op("gpsimd", lambda e, tc=tc: e.dma_start(
                    out=bcb[0], in_=rows_d[9344 + tc * 512: 9344 + (tc + 1) * 512]
                        [None, :].to_broadcast([128, 512])),
                    inc=[("gsp", 16)], wait=S_.aft("gsp") + S_.aft("v"))
                b2 = S_.op("gpsimd", lambda e, tc=tc: e.dma_start(
                    out=bcb[1], in_=rows_d[10368 + tc * 512: 10368 + (tc + 1) * 512]
                        [None, :].to_broadcast([128, 512])),
                    inc=[("gsp", 16)], wait=())
                for m in range(16):
                    o1 = S_.op("vector", lambda e, m=m: e.tensor_sub(
                        obuf[m % 2], acc[m], bcb[0]), inc=[("v", 1)],
                        wait=(list(b1) + list(b2) if m == 0 else [])
                        + prevf.get(("ob", m % 2), []))
                    o1 = S_.op("vector", lambda e, m=m: e.tensor_mul(
                        obuf[m % 2], obuf[m % 2], bcb[1]), inc=[("v", 1)], wait=())
                    prevf[("ob", m % 2)] = S_.op(
                        "gpsimd", lambda e, m=m, tc=tc: e.dma_start(
                            out=out_d[m * 128:(m + 1) * 128,
                                      tc * 512:(tc + 1) * 512],
                            in_=obuf[m % 2]), inc=[(f"gc{m % 2}", 16)], wait=o1)
        g_sc.close()
        x2_sc.close()
        S_.wait("gpsimd", *S_.aft("gc0", "gc1", "st"))
        S_.barrier()

        with nc.Block() as block:
            @block.sync
            def _(e):
                for fn in S_.q["sync"]:
                    fn(e)

            @block.tensor
            def _(e):
                for fn in S_.q["tensor"]:
                    fn(e)

            @block.vector
            def _(e):
                for fn in S_.q["vector"]:
                    fn(e)

            @block.scalar
            def _(e):
                for fn in S_.q["scalar"]:
                    fn(e)

            @block.gpsimd
            def _(e):
                for fn in S_.q["gpsimd"]:
                    fn(e)
    return nc


_CACHE = {}


def kernel(**inputs):
    in_maps, idx = _host_prep(inputs)
    key = ("main", _BUILD_OPTS["dbg"], _BUILD_OPTS["nslab"], _BUILD_OPTS["ssd"], _BUILD_OPTS["xg"], _BUILD_OPTS["op"], _BUILD_OPTS["inproj"], _BUILD_OPTS["cache"], _BUILD_OPTS["nmg"], _BUILD_OPTS["heads"], _BUILD_OPTS["tpose"])
    if key not in _CACHE:
        _CACHE[key] = _build(idx)
    nc = _CACHE[key]
    res = run_bass_kernel_spmd(nc, in_maps, core_ids=list(range(8)))
    full = np.zeros((4, S, HID), np.float32)
    for c in range(8):
        b, r = c // 2, c % 2
        full[b, r * 1024:(r + 1) * 1024, :] = np.asarray(res.results[c]["out"]).T
    if _BUILD_OPTS["dbg"] is not None:
        kernel.dbg = [np.asarray(res.results[c].get("dbg")) for c in range(8)]
    return full


# version 09
# speedup vs baseline: 1.1018x; 1.0265x over previous
import numpy as np
import ml_dtypes
import concourse.bass as bass
import concourse.mybir as mybir
from concourse.bass_utils import run_bass_kernel_spmd

F32 = mybir.dt.float32
BF16 = mybir.dt.bfloat16
AF = mybir.ActivationFunctionType
ALU = mybir.AluOpType
AX = mybir.AxisListType.X

HID = 2048
S = 2048
LC = 256
NSLAB = 4
SLAB = 512
INTER = 4096
PROJ_P = 8576
NMC = 67
EPS_A = 1e-5
EPS_F = 1e-6
NHEAD = 16
SCALE = float(192.0 ** -0.5)
NE = 16
DI = 1408
SHI = 2816
BIG_NEG = -1.0e9

_BUILD_OPTS = {"dbg": None, "nslab": NSLAB, "ssd": True, "xg": True, "op": True, "inproj": True, "cache": True, "nmg": NMC, "heads": 64, "tpose": 3, "xl": True, "kv": True, "att": True, "moe": True}


class Sched:
    ENGINES = ("sync", "tensor", "vector", "scalar", "gpsimd")

    def __init__(self):
        self.q = {e: [] for e in self.ENGINES}
        self.sems = {}

    def add_sem(self, name, handle):
        self.sems[name] = [handle, 0]

    def n(self, name):
        return self.sems[name][1]

    AUTO_CHAIN = {"vector": "v", "scalar": "a"}

    LABELS = {}

    def op(self, eng, build, inc=(), wait=(), chain=True, self_inc=True, label=None):
        wait = list(wait)
        sem_self = self.AUTO_CHAIN.get(eng)
        if sem_self is not None:
            if chain and self.sems[sem_self][1] > 0:
                wait.append((sem_self, self.sems[sem_self][1]))
            if self_inc and not any(nm == sem_self for nm, _ in inc):
                inc = list(inc) + [(sem_self, 1)]
        toks = []
        for nm, amt in inc:
            self.sems[nm][1] += amt
            toks.append((nm, self.sems[nm][1]))
        wl = {}
        for nm, val in wait:
            if val > 0:
                wl[nm] = max(wl.get(nm, 0), val)
        wl = [(self.sems[nm][0], val) for nm, val in wl.items()]
        il = [(self.sems[nm][0], amt) for nm, amt in inc]

        qpos = (eng, len(self.q[eng]))

        def fn(e, build=build, wl=wl, il=il, label=label, qpos=qpos):
            for h, v in wl:
                e.wait_ge(h, v)
            inst = build(e)
            if inst is not None:
                try:
                    Sched.LABELS[inst.ins.name] = label or qpos
                except AttributeError:
                    if label is not None:
                        Sched.LABELS[str(inst.concise())[:40]] = label
            for h, amt in il:
                if inst is None:
                    e.nop().then_inc(h, amt)
                else:
                    inst = inst.then_inc(h, amt)
        self.q[eng].append(fn)
        return toks

    def wait(self, eng, *tokens):
        wl = [(self.sems[nm][0], val) for nm, val in tokens if val > 0]
        if not wl:
            return

        def fn(e, wl=wl):
            for h, v in wl:
                e.wait_ge(h, v)
        self.q[eng].append(fn)

    def aft(self, *names):
        return [(nm, self.n(nm)) for nm in names]

    ENG_DRAIN = {"sync": ("sx0", "sx1", "sw0", "sw1", "sv0", "sv1"),
                 "tensor": ("t",),
                 "vector": ("v",), "scalar": ("a",),
                 "gpsimd": ("gsp", "gbc", "gb0", "gb1", "gc0", "gc1", "pc0",
                             "pc1", "wb0", "wb1", "st", "p")}

    def barrier(self, engines=None):
        engines = engines or list(self.ENGINES)
        toks = []
        for e in engines:
            toks += self.op(e, lambda en: en.nop(), inc=[("bar", 1)],
                            wait=self.aft(*self.ENG_DRAIN[e]), chain=True,
                            self_inc=False)
        for e in engines:
            self.wait(e, *toks[-1:])


def _host_prep(inp):
    x = np.asarray(inp["L0_attn_hidden_states"], np.float32)
    ipw = np.asarray(inp["L0_attn_in_proj_w"], np.float32)
    ipb = np.asarray(inp["L0_attn_in_proj_b"], np.float32)
    convw = np.asarray(inp["L0_attn_conv_w"], np.float32)
    dtb = np.asarray(inp["L0_attn_dt_bias"], np.float32)
    alog = np.asarray(inp["L0_attn_A_log"], np.float32)
    Dv = np.asarray(inp["L0_attn_D"], np.float32)
    nw = np.asarray(inp["L0_attn_norm_weight"], np.float32)
    opw = np.asarray(inp["L0_attn_out_proj_w"], np.float32)
    opb = np.asarray(inp["L0_attn_out_proj_b"], np.float32)
    ln1 = np.asarray(inp["L0_ffn_ln1_w"], np.float32)
    ln2 = np.asarray(inp["L0_ffn_ln2_w"], np.float32)
    qw = np.asarray(inp["L0_ffn_q_w"], np.float32)
    kvw = np.asarray(inp["L0_ffn_kv_w"], np.float32)
    ow = np.asarray(inp["L0_ffn_o_w"], np.float32)
    gw = np.asarray(inp["L0_ffn_gate_w"], np.float32)
    w1 = np.asarray(inp["L0_ffn_w1"], np.float32)
    w2 = np.asarray(inp["L0_ffn_w2"], np.float32)
    shg = np.asarray(inp["L0_ffn_sh_gu_w"], np.float32)
    shd = np.asarray(inp["L0_ffn_sh_down_w"], np.float32)

    ipw_p = np.zeros((HID, PROJ_P), np.float32)
    ipw_p[:, :8512] = ipw
    ipb_p = np.zeros(PROJ_P, np.float32)
    ipb_p[:8512] = ipb

    q_live = qw.reshape(NHEAD, 192, HID)[:, :128, :].reshape(NHEAD * 128, HID)
    qwT = np.ascontiguousarray(q_live.T)
    kwT = np.ascontiguousarray(kvw[:2048].T)
    vwT = np.ascontiguousarray(kvw[2048:4096].T)
    owT = np.ascontiguousarray(ow.T)
    gwT = np.ascontiguousarray(gw.T)
    w1T = np.ascontiguousarray(np.transpose(w1, (0, 2, 1)))
    w2T = np.ascontiguousarray(np.transpose(w2, (0, 2, 1)))
    shgT = np.ascontiguousarray(shg.T)
    shdT = np.ascontiguousarray(shd.T)

    def cols_of(vec, n):
        v = np.zeros(n * 128, np.float32)
        v[:len(vec)] = vec
        return [v[i * 128:(i + 1) * 128] for i in range(n)]

    cols = []
    idx = {}

    def add(name, lst):
        idx[name] = len(cols)
        cols.extend(lst)

    add("inb", cols_of(ipb_p, NMC))
    conv_cols = []
    for mc in range(34):
        for k in range(4):
            conv_cols.append(np.ascontiguousarray(convw[mc * 128:(mc + 1) * 128, k]))
    add("convw", conv_cols)
    add("dtb", cols_of(dtb, 1))
    add("alog", cols_of(alog, 1))
    add("D", cols_of(np.repeat(Dv, 64), 32))
    add("nw", cols_of(nw, 32))
    add("ob", cols_of(opb, 16))
    add("l1", cols_of(ln1, 16))
    add("l2", cols_of(ln2, 16))
    add("ones", [np.ones(128, np.float32)])
    add("epsA", [np.full(128, EPS_A, np.float32)])
    add("epsF", [np.full(128, EPS_F, np.float32)])
    idx["m0"] = len(cols)
    idx["m1"] = len(cols) + 1
    idx["_nv"] = len(cols) + 2
    vecs_common = np.stack(cols, axis=1)

    ssdm = np.zeros((2, 128, 256), np.float32)
    for si in range(2):
        for p in range(128):
            ssdm[si, p, :si * 128 + p] = BIG_NEG
    ssdm_flat = np.ascontiguousarray(ssdm.reshape(256, 256))

    id_f32 = np.eye(128, dtype=np.float32)
    id_bf = np.eye(128, dtype=np.float32).astype(ml_dtypes.bfloat16)

    in_maps = []
    for c in range(8):
        b, r = c // 2, c % 2
        m0c = np.full((128, 1), 1.0 - r, np.float32)
        m1c = np.full((128, 1), float(r), np.float32)
        vecs = np.concatenate([vecs_common, m0c, m1c], axis=1)
        kt = np.arange(S)[:, None]
        qt = np.arange(1024)[None, :] + r * 1024
        m01 = (kt <= qt).astype(np.float32).astype(ml_dtypes.bfloat16)
        in_maps.append({
            "x0T": np.ascontiguousarray(x[b].T), "vecs": np.ascontiguousarray(vecs),
            "ssdm": ssdm_flat, "id_f32": id_f32, "id_bf": id_bf,
            "m01": np.ascontiguousarray(m01),
            "ipw": ipw_p, "opw": opw,
            "qwT": qwT, "kwT": kwT, "vwT": vwT, "owT": owT, "gwT": gwT,
            "w1T": w1T, "w2T": w2T, "shgT": shgT, "shdT": shdT,
        })
    return in_maps, idx


def _build(idx):
    dbg = _BUILD_OPTS["dbg"]
    nc = bass.Bass("TRN2", target_bir_lowering=False, debug=False)
    S_ = Sched()

    x0T = nc.dram_tensor("x0T", [HID, S], F32, kind="ExternalInput").ap()
    vecs_d = nc.dram_tensor("vecs", [128, idx["_nv"]], F32, kind="ExternalInput").ap()
    ssdm_d = nc.dram_tensor("ssdm", [256, 256], F32, kind="ExternalInput").ap()
    idf_d = nc.dram_tensor("id_f32", [128, 128], F32, kind="ExternalInput").ap()
    idb_d = nc.dram_tensor("id_bf", [128, 128], BF16, kind="ExternalInput").ap()
    m01_d = nc.dram_tensor("m01", [S, 1024], BF16, kind="ExternalInput").ap()
    ipw_d = nc.dram_tensor("ipw", [HID, PROJ_P], F32, kind="ExternalInput").ap()
    opw_d = nc.dram_tensor("opw", [INTER, HID], F32, kind="ExternalInput").ap()
    qwT_d = nc.dram_tensor("qwT", [HID, 2048], F32, kind="ExternalInput").ap()
    kwT_d = nc.dram_tensor("kwT", [HID, 2048], F32, kind="ExternalInput").ap()
    vwT_d = nc.dram_tensor("vwT", [HID, 2048], F32, kind="ExternalInput").ap()
    owT_d = nc.dram_tensor("owT", [2048, 2048], F32, kind="ExternalInput").ap()
    gwT_d = nc.dram_tensor("gwT", [HID, 16], F32, kind="ExternalInput").ap()
    w1T_d = nc.dram_tensor("w1T", [NE, HID, 2816], F32, kind="ExternalInput").ap()
    w2T_d = nc.dram_tensor("w2T", [NE, DI, 2048], F32, kind="ExternalInput").ap()
    shgT_d = nc.dram_tensor("shgT", [HID, 2 * SHI], F32, kind="ExternalInput").ap()
    shdT_d = nc.dram_tensor("shdT", [SHI, HID], F32, kind="ExternalInput").ap()
    out_d = nc.dram_tensor("out", [HID, 1024], F32, kind="ExternalOutput").ap()

    gsil_unused = None
    ipwbf_d = nc.dram_tensor("ipwbf", [HID, PROJ_P], BF16).ap()
    cumsp_d = nc.dram_tensor("cumsp", [8 * 64 * 256], F32).ap()
    cssp_d = nc.dram_tensor("cssp", [8 * 64], F32).ap()
    prex1_d = nc.dram_tensor("prex1", [HID, S], F32).ap()
    x1sp_d = nc.dram_tensor("x1sp", [HID, S], F32).ap()
    rows_d = nc.dram_tensor("rows", [64 * 1024], F32).ap()
    kspill_d = nc.dram_tensor("kspill", [2048, S], BF16).ap()
    vspill_d = nc.dram_tensor("vspill", [S, 2048], BF16).ap()
    x1own_d = nc.dram_tensor("x1own", [HID, 1024], F32).ap()
    hidsp_d = nc.dram_tensor("hidsp", [HID, 1024], F32).ap()
    wfsp_d = nc.dram_tensor("wfsp", [NE * 1024], F32).ap()
    w1c_d = nc.dram_tensor("w1c", [NE, HID, 2816], BF16).ap()
    w2c_d = nc.dram_tensor("w2c", [NE, DI, 2048], BF16).ap()
    shgc_d = nc.dram_tensor("shgc", [HID, 2 * SHI], BF16).ap()
    shdc_d = nc.dram_tensor("shdc", [SHI, HID], BF16).ap()
    dbg_d = None
    if dbg is not None:
        dbg_d = nc.dram_tensor("dbg", list(dbg[1]), F32, kind="ExternalOutput").ap()

    from contextlib import ExitStack
    with ExitStack() as stack:
        ent = stack.enter_context
        for snm in ("sx0", "sx1", "sw0", "sw1", "sv0", "sv1", "gsp", "gbc",
                    "gb0", "gb1", "gc0", "gc1", "pc0", "pc1",
                    "wb0", "wb1", "t", "v", "a", "p", "st", "bar"):
            S_.add_sem(snm, ent(nc.semaphore(snm)))

        vecs = ent(nc.sbuf_tensor("vecs_s", [128, idx["_nv"]], F32)).ap()
        idf = ent(nc.sbuf_tensor("idf", [128, 128], F32)).ap()
        idb = ent(nc.sbuf_tensor("idb", [128, 128], BF16)).ap()

        onesb = ent(nc.sbuf_tensor("onesb", [128, 1], BF16)).ap()
        expA = ent(nc.sbuf_tensor("expA", [128, 1], F32)).ap()
        pb = [ent(nc.psum_tensor(f"pb{i}", [128, 512], F32)).ap() for i in range(8)]
        rowb = ent(nc.sbuf_tensor("rowb", [1, 512], F32)).ap()
        rowb2 = ent(nc.sbuf_tensor("rowb2", [1, 512], F32)).ap()
        bcb = [ent(nc.sbuf_tensor(f"bcb{i}", [128, 512], F32)).ap() for i in range(2)]
        mxk = ent(nc.sbuf_tensor("mxk", [1, 64], F32)).ap()
        pbf = [p.bitcast(BF16) for p in pb]

        def vcol(name, j=0):
            return vecs[:, idx[name] + j: idx[name] + j + 1]

        S_.op("sync", lambda e: e.dma_start(out=vecs, in_=vecs_d), inc=[("sx0", 16)])
        S_.op("sync", lambda e: e.dma_start(out=idf, in_=idf_d), inc=[("sx0", 16)])
        S_.op("sync", lambda e: e.dma_start(out=idb, in_=idb_d), inc=[("sx0", 16)])
        S_.op("sync", lambda e: e.dma_start(
            out=ssdm.rearrange("p (a c) -> p a c", a=2),
            in_=ssdm_d.rearrange("(a b) c -> b a c", a=2)),
            inc=[("sx0", 16)])
        pre_tok = S_.aft("sx0")
        S_.op("vector", lambda e: e.tensor_copy(onesb, vcol("ones")),
              inc=[("v", 1)], wait=pre_tok)
        S_.op("scalar", lambda e: e.activation(expA, vcol("alog"), AF.Exp),
              inc=[("a", 1)], wait=pre_tok)

        def dbg_tap(name, srcs):
            if dbg is None or dbg[0] != name:
                return
            S_.barrier()
            for (r0, r1), src in srcs:
                S_.op("gpsimd", lambda e, r0=r0, r1=r1, src=src: e.dma_start(
                    out=dbg_d[r0:r1, :src.shape[1]], in_=src), inc=[("st", 16)])
            S_.barrier()

        a_sc = ExitStack()
        aent = a_sc.enter_context
        ssdm = aent(nc.sbuf_tensor("ssdm_s", [128, 512], F32)).ap()
        x0bf = [aent(nc.sbuf_tensor(f"x0bf{i}", [128, SLAB], BF16)).ap() for i in range(16)]
        x0st = [aent(nc.sbuf_tensor(f"x0st{i}", [128, SLAB], F32)).ap() for i in range(2)]
        wbf = [aent(nc.sbuf_tensor(f"wbf{i}", [128, 2048], BF16)).ap() for i in range(2)]
        hbcb = [aent(nc.sbuf_tensor(f"hbcb{i}", [128, 515], F32)).ap() for i in range(2)]
        cvb = [aent(nc.sbuf_tensor(f"cvb{i}", [128, 512], F32)).ap() for i in range(2)]
        stash = aent(nc.sbuf_tensor("stash", [128, 102], F32)).ap()
        hs = [aent(nc.sbuf_tensor(f"hs{i}", [128, SLAB], BF16)).ap() for i in range(32)]
        Bsl = aent(nc.sbuf_tensor("Bsl", [128, SLAB], BF16)).ap()
        Csl = aent(nc.sbuf_tensor("Csl", [128, SLAB], BF16)).ap()
        dtsl = aent(nc.sbuf_tensor("dtsl", [128, SLAB], F32)).ap()
        dstt = aent(nc.sbuf_tensor("dstt", [128, SLAB], F32)).ap()
        abuf = aent(nc.sbuf_tensor("abuf", [64, 256], F32)).ap()
        cumc = aent(nc.sbuf_tensor("cumc", [128, 512], F32)).ap()
        dcay = aent(nc.sbuf_tensor("dcay", [128, 256], F32)).ap()
        ncT = aent(nc.sbuf_tensor("ncT", [128, 128], F32)).ap()
        ecT = aent(nc.sbuf_tensor("ecT", [128, 128], F32)).ap()
        dtT = aent(nc.sbuf_tensor("dtT", [128, 128], F32)).ap()
        dyT = aent(nc.sbuf_tensor("dyT", [128, 128], F32)).ap()
        Btm = aent(nc.sbuf_tensor("Btm", [128, 256], BF16)).ap()
        bdec = aent(nc.sbuf_tensor("bdec", [128, 512], BF16)).ap()
        Ee = aent(nc.sbuf_tensor("Ee", [128, 64], F32)).ap()
        Sst = aent(nc.sbuf_tensor("Sst", [128, 4096], F32)).ap()
        Sbf = aent(nc.sbuf_tensor("Sbf", [128, 4096], BF16)).ap()
        RB = aent(nc.sbuf_tensor("RB", [128, 1024], F32)).ap()
        tb1 = aent(nc.sbuf_tensor("tb1", [128, 1024], F32)).ap()
        tb2 = aent(nc.sbuf_tensor("tb2", [128, 1024], BF16)).ap()
        mtb = aent(nc.sbuf_tensor("mtb", [128, 1024], BF16)).ap()
        yev = aent(nc.sbuf_tensor("yev", [128, 64], F32)).ap()
        xg = [aent(nc.sbuf_tensor(f"xg{i}", [128, SLAB], BF16)).ap() for i in range(32)]
        gsl = xg
        sqb = [aent(nc.sbuf_tensor(f"sqb{i}", [128, SLAB], BF16)).ap() for i in range(2)]
        prb = [aent(nc.sbuf_tensor(f"prb{i}", [128, SLAB], F32)).ap() for i in range(2)]
        sqf = [aent(nc.sbuf_tensor(f"sqf{i}", [128, SLAB], F32)).ap() for i in range(2)]
        A_un = aent(nc.sbuf_tensor("A_un", [128, 8192], F32)).ap()
        wstg2 = [A_un[:, 0:2048], A_un[:, 2048:4096]]
        hsdt = [A_un[:, 0:2048].bitcast(BF16), A_un[:, 2048:4096].bitcast(BF16)]
        ytm = [A_un[:, 4096:6144].bitcast(BF16), A_un[:, 6144:8192].bitcast(BF16)]
        opst = A_un[:, 0:4096]
        opbf = [A_un[:, 4096:6144].bitcast(BF16), A_un[:, 6144:8192].bitcast(BF16)]

        S_.op("vector", lambda e: e.memset(stash, 0.0), inc=[("v", 1)])
        S_.op("vector", lambda e: e.memset(dtsl, 0.0), inc=[("v", 1)])
        S_.op("vector", lambda e: e.memset(cumc, 0.0), inc=[("v", 1)])
        S_.op("vector", lambda e: e.memset(dcay, 0.0), inc=[("v", 1)])
        S_.op("vector", lambda e: e.memset(Sst, 0.0), inc=[("v", 1)])
        S_.op("vector", lambda e: e.memset(Sbf, 0.0), inc=[("v", 1)])
        mt_s2 = {}
        mt_mul = {}
        mt_su3 = {}

        for slab in range(_BUILD_OPTS["nslab"]):
            t0 = slab * SLAB
            cast_tok = {}
            for kc in range(16):
                w = (S_.aft("a", "v") if kc < 2 else []) + cast_tok.get(kc % 2, [])
                tok = S_.op("sync", lambda e, kc=kc, t0=t0: e.dma_start(
                    out=x0st[kc % 2], in_=x0T[kc * 128:(kc + 1) * 128, t0:t0 + SLAB]),
                    inc=[(f"sx{kc % 2}", 16)], wait=w)
                cast_tok[kc % 2] = S_.op("scalar", lambda e, kc=kc: e.activation(
                    x0bf[kc], x0st[kc % 2], AF.Copy), inc=[("a", 1)],
                    wait=list(tok) + (S_.aft("t") if kc == 0 else []))
            x0_done = S_.aft("a")

            last_cast = {}
            cache_tok = {}
            mm_war = {}
            bank_ev = {}
            for mg in range((min(_BUILD_OPTS["nmg"], NMC)) if _BUILD_OPTS["inproj"] else 0):
                m0 = mg
                nmc_g = 1
                wcols = 128
                buf = wbf[mg % 2]
                war = mm_war.get(mg % 2, S_.aft("t") if slab > 0 else [])
                if slab == 0:
                    wsb = wstg2[mg % 2]
                    ldw = []
                    for kh in range(2):
                        ldw += S_.op("sync", lambda e, kh=kh, m0=m0, wcols=wcols,
                                     wsb=wsb: e.dma_start(
                            out=wsb[:, kh * 8 * wcols:(kh + 1) * 8 * wcols]
                                .rearrange("p (k c) -> p k c", k=8),
                            in_=ipw_d[kh * 1024:(kh + 1) * 1024, m0 * 128:m0 * 128 + wcols]
                                .rearrange("(k p) c -> p k c", p=128)),
                            inc=[(f"sw{mg % 2}", 16)],
                            wait=(list(war) + last_cast.get(mg % 2, []))
                            if kh == 0 else ())
                    wtok = S_.op("vector", lambda e, buf=buf, wcols=wcols, wsb=wsb:
                                 e.tensor_copy(buf, wsb),
                                 inc=[("v", 1)],
                                 wait=ldw[-1:] + cache_tok.get(mg % 2, [])
                                 + list(war))
                    last_cast[mg % 2] = wtok
                    if _BUILD_OPTS["cache"]:
                        cache_tok[mg % 2] = S_.op(
                            "gpsimd", lambda e, m0=m0, buf=buf, wcols=wcols: e.dma_start(
                                out=ipwbf_d[:, m0 * 128:m0 * 128 + wcols]
                                    .rearrange("(k p) c -> p k c", p=128),
                                in_=buf[:, :16 * wcols].rearrange("p (k c) -> p k c", k=16)),
                            inc=[(f"gc{mg % 2}", 16)], wait=wtok)
                else:
                    wtok = S_.op("sync", lambda e, m0=m0, buf=buf, wcols=wcols: e.dma_start(
                        out=buf[:, :16 * wcols].rearrange("p (k c) -> p k c", k=16),
                        in_=ipwbf_d[:, m0 * 128:m0 * 128 + wcols]
                            .rearrange("(k p) c -> p k c", p=128)),
                        inc=[(f"sw{mg % 2}", 16)],
                        wait=list(war) + S_.aft("gc0", "gc1"))
                for mi in range(nmc_g):
                    m = m0 + mi
                    bank = pb[4 + m % 4]
                    mm_tok = []
                    for k in range(16):
                        first = (k == 0)
                        mm_tok = S_.op("tensor", lambda e, k=k, mi=mi, bank=bank, buf=buf,
                                       wcols=wcols: e.matmul(
                            bank[:, :SLAB],
                            buf[:, k * wcols + mi * 128: k * wcols + (mi + 1) * 128],
                            x0bf[k], start=(k == 0), stop=(k == 15)),
                            inc=[("t", 1)] if k == 15 else (),
                            wait=(list(wtok) + x0_done
                                  + bank_ev.get(m % 4, [])) if first else ())
                    mm_war[mg % 2] = mm_tok
                    if m < 32:
                        bank_ev[m % 4] = S_.op(
                            "scalar", lambda e, bank=bank, m=m: e.activation(
                                gsl[m], bank[:, :SLAB], AF.Silu, bias=vcol("inb", m)),
                            inc=[("a", 1)], wait=mm_tok + S_.aft("v", "t"))
                    elif m < 66:
                        j = m - 32
                        hb = hbcb[j % 2]
                        cb = cvb[j % 2]
                        stok = S_.op("vector", lambda e, hb=hb, j=j: e.tensor_copy(
                            hb[:, 0:3], stash[:, j * 3:(j + 1) * 3]),
                            inc=[("v", 1)], wait=S_.aft("a"))
                        ev = S_.op("scalar", lambda e, bank=bank, hb=hb, m=m: e.activation(
                            hb[:, 3:515], bank[:, :SLAB], AF.Identity, bias=vcol("inb", m)),
                            inc=[("a", 1)], wait=mm_tok + stok)
                        bank_ev[m % 4] = ev
                        S_.op("vector", lambda e, hb=hb, j=j: e.tensor_copy(
                            stash[:, j * 3:(j + 1) * 3], hb[:, 512:515]),
                            inc=[("v", 1)], wait=ev)
                        cv = S_.op("vector", lambda e, hb=hb, cb=cb, j=j: e.tensor_scalar(
                            cb, hb[:, 0:512], vcol("convw", j * 4), None, ALU.mult),
                            inc=[("v", 1)], wait=())
                        for kk in range(1, 4):
                            cv = S_.op("vector", lambda e, hb=hb, cb=cb, j=j, kk=kk:
                                       e.scalar_tensor_tensor(
                                cb, hb[:, kk:512 + kk], vcol("convw", j * 4 + kk), cb,
                                ALU.mult, ALU.add), inc=[("v", 1)], wait=())
                        dst = hs[j] if j < 32 else (Bsl if j == 32 else Csl)
                        S_.op("scalar", lambda e, cb=cb, dst=dst: e.activation(
                            dst, cb, AF.Silu), inc=[("a", 1)], wait=cv)
                    else:
                        ev = S_.op("scalar", lambda e, bank=bank: e.activation(
                            dstt, bank[:, :SLAB], AF.Identity, bias=vcol("inb", 66)),
                            inc=[("a", 1)], wait=mm_tok + S_.aft("v"))
                        bank_ev[m % 4] = ev
                        z = dstt[0:64, :]
                        S_.op("vector", lambda e, z=z: e.tensor_scalar(
                            z, z, vcol("dtb")[0:64, :], None, ALU.add),
                            inc=[("v", 1)], wait=ev)
                        S_.op("scalar", lambda e, z=z: e.activation(
                            dtsl[0:64, :], z, AF.Abs), inc=[("a", 1)], wait=S_.aft("v"))
                        S_.op("scalar", lambda e: e.activation(
                            dtsl[0:64, :], dtsl[0:64, :], AF.Exp, scale=-1.0),
                            inc=[("a", 1)], wait=())
                        S_.op("scalar", lambda e: e.activation(
                            dtsl[0:64, :], dtsl[0:64, :], AF.Ln,
                            bias=vcol("ones")[0:64, :]), inc=[("a", 1)], wait=())
                        S_.op("vector", lambda e, z=z: e.tensor_scalar(
                            z, z, 0.0, None, ALU.max), inc=[("v", 1)],
                            wait=S_.aft("a"))
                        S_.op("vector", lambda e, z=z: e.tensor_add(
                            dtsl[0:64, :], dtsl[0:64, :], z),
                            inc=[("v", 1)], wait=S_.aft("a"))

            dbg_tap("convout", [((j * 128, (j + 1) * 128),
                                 hs[j] if j < 32 else (Bsl if j == 32 else Csl))
                                for j in range(34)] if slab == 0 else [])
            dbg_tap("dt", [((0, 64), dtsl[0:64, :])] if slab == 0 else [])
            dbg_tap("gate", [((j * 128, (j + 1) * 128), gsl[j])
                             for j in range(32)] if slab == 0 else [])

            if _BUILD_OPTS["ssd"]:
                for ci in range(2):
                    c = slab * 2 + ci
                    off = ci * 256
                    pre = S_.aft("a", "v")
                    S_.op("vector", lambda e, off=off: e.tensor_scalar(
                        abuf, dtsl[0:64, off:off + 256], expA[0:64, :], -1.0,
                        ALU.mult, ALU.mult), inc=[("v", 1)], wait=pre)
                    S_.op("vector", lambda e, off=off: e.tensor_tensor_scan(
                        cumc[0:64, off:off + 256], abuf, abuf, 0.0,
                        ALU.add, ALU.bypass), inc=[("v", 1)], wait=())
                    cum_tok = S_.aft("v")
                    S_.op("gpsimd", lambda e, c=c, off=off: e.dma_start(
                        out=cumsp_d[c * 16384:(c + 1) * 16384]
                            .rearrange("(h t) -> h t", h=64),
                        in_=cumc[0:64, off:off + 256]),
                        inc=[("gsp", 16)], wait=cum_tok)
                    S_.op("gpsimd", lambda e, c=c, off=off: e.dma_start(
                        out=cssp_d[c * 64:(c + 1) * 64].unsqueeze(1),
                        in_=cumc[0:64, off + 255:off + 256]),
                        inc=[("gsp", 16)], wait=())
            cumsp_tok = S_.aft("gsp")
            for ci in range(2 if _BUILD_OPTS["ssd"] else 0):
                c = slab * 2 + ci
                off = ci * 256
                pre = S_.aft("a", "v")
                eb = S_.op("gpsimd", lambda e, c=c: e.dma_start(
                    out=Ee[:, 0:64],
                    in_=cssp_d[c * 64:(c + 1) * 64][None, :].to_broadcast([128, 64])),
                    inc=[("gsp", 16)], wait=cumsp_tok + S_.aft("a", "v")),
                S_.op("scalar", lambda e: e.activation(Ee, Ee, AF.Exp),
                      inc=[("a", 1)], wait=eb[0])
                S_.op("scalar", lambda e, off=off: e.activation(
                    dcay[0:64, :], cumc[0:64, off:off + 256], AF.Exp,
                    bias=cumc[0:64, off + 255:off + 256], scale=-1.0),
                    inc=[("a", 1)], wait=())
                dcay_tok = S_.aft("a")
                ev_tok = {}
                for nm, srcf, dst, extra in (
                        ("cum", lambda i, off=off: cumc[:, off + i * 128:
                                                        off + (i + 1) * 128],
                         None, True),
                        ("dt", lambda i, off=off: dtsl[:, off + i * 128:
                                                       off + (i + 1) * 128],
                         dtT, False),
                        ("dy", lambda i: dcay[:, i * 128:(i + 1) * 128], dyT, False)):
                    for i in range(2):
                        bank = pb[2 + i]
                        tt = S_.op("tensor", lambda e, srcf=srcf, i=i, bank=bank:
                                   e.transpose(bank[:, 0:128], srcf(i), idf),
                                   inc=[("t", 1)],
                                   wait=(dcay_tok if nm == "dy" else cum_tok)
                                   + ev_tok.get(("e", i), S_.aft("v", "a", "p")))
                        if nm == "cum":
                            e1 = S_.op("vector", lambda e, i=i, bank=bank:
                                       e.tensor_scalar(
                                ncT[:, i * 64:(i + 1) * 64], bank[:, 0:64],
                                -1.0, None, ALU.mult), inc=[("v", 1)], wait=tt,
                                label=("ncTevac", slab, ci, i))
                            e2 = S_.op("scalar", lambda e, i=i, bank=bank:
                                       e.activation(
                                ecT[:, i * 64:(i + 1) * 64], bank[:, 0:64],
                                AF.Exp), inc=[("a", 1)], wait=tt + e1)
                            ev_tok[("e", i)] = e1 + e2
                        else:
                            e1 = S_.op("vector", lambda e, i=i, bank=bank, dst=dst:
                                       e.tensor_copy(
                                dst[:, i * 64:(i + 1) * 64], bank[:, 0:64]),
                                inc=[("v", 1)], wait=tt)
                            ev_tok[("e", i)] = e1
                for i in range(2):
                    bank = pbf[2 + i]
                    tt = S_.op("tensor", lambda e, i=i, off=off, bank=bank:
                               e.transpose(
                        bank[:, 0:128], Bsl[:, off + i * 128: off + (i + 1) * 128],
                        idb), inc=[("t", 1)], wait=ev_tok.get(("e", i), []) + pre)
                    ev_tok[("e", i)] = S_.op("vector", lambda e, i=i, bank=bank:
                                             e.tensor_copy(
                        Btm[:, i * 128:(i + 1) * 128], bank[:, 0:128]),
                        inc=[("v", 1)], wait=tt)
                hs_ev = {}
                for j in range(32):
                    for li in range(2):
                        par = (j * 2 + li) % 2
                        bank = pbf[4 + par]
                        tt = S_.op("tensor", lambda e, j=j, li=li, off=off,
                                   bank=bank: e.transpose(
                            bank[:, 0:128],
                            hs[j][:, off + li * 128: off + (li + 1) * 128], idb),
                            inc=[("t", 1)], wait=hs_ev.get(par, ev_tok.get(("e", li), [])))
                        ee = []
                        for hh in range(2):
                            ee += S_.op("vector", lambda e, j=j, li=li, hh=hh,
                                        bank=bank: e.tensor_scalar(
                                hsdt[li][:, j * 128 + hh * 64: j * 128 + (hh + 1) * 64],
                                bank[:, hh * 64:(hh + 1) * 64],
                                dtT[:, li * 64 + 2 * j + hh: li * 64 + 2 * j + hh + 1],
                                None, ALU.mult), inc=[("v", 1)],
                                wait=tt if hh == 0 else ())
                        hs_ev[par] = ee
                gm_tok = []
                for si in range(2):
                    gm_tok += S_.op("tensor", lambda e, si=si, off=off: e.matmul(
                        pb[si][:, 0:256], Bsl[:, off + si * 128: off + (si + 1) * 128],
                        Csl[:, off:off + 256], start=True, stop=True),
                        inc=[("t", 1)], wait=S_.aft("v", "a") + pre)
                hb_tok = {}
                mt_done = {}
                rb_tok = []

                def emit_mt(h):
                    nonlocal rb_tok
                    if h % 4 == 0:
                        rb_tok = S_.op("gpsimd", lambda e, c=c, h=h: e.dma_start(
                            out=RB,
                            in_=cumsp_d[c * 16384 + h * 256: c * 16384 + (h + 4) * 256]
                                [None, :].to_broadcast([128, 1024])),
                            inc=[(f"gb{(c * 16 + h // 4) % 2}", 16)],
                            label=("rb", slab, ci, h),
                            wait=(cumsp_tok if h == 0 else []) + S_.aft("v", "p"))
                    ro = (h % 4) * 256
                    hp = (h % 2) * 512
                    s1s = {}
                    for si in range(2):
                        s1s[si] = S_.op("vector", lambda e, si=si, h=h, ro=ro,
                                        hp=hp: e.scalar_tensor_tensor(
                            tb1[:, hp + si * 256: hp + (si + 1) * 256],
                            RB[:, ro:ro + 256],
                            ncT[:, si * 64 + h: si * 64 + h + 1],
                            ssdm[:, si * 256:(si + 1) * 256], ALU.add, ALU.add),
                            label=("s1", slab, ci, h, si),
                            inc=[("v", 1)],
                            wait=list(rb_tok)
                            + (gm_tok if si == 0 and h == 0 else [])
                            + mt_s2.pop((h % 2, si), []))
                    for si in range(2):
                        mt_s2[(h % 2, si)] = S_.op(
                            "scalar", lambda e, si=si, hp=hp: e.activation(
                                tb2[:, hp + si * 256: hp + (si + 1) * 256],
                                tb1[:, hp + si * 256: hp + (si + 1) * 256],
                                AF.Exp), inc=[("a", 1)],
                            wait=s1s[si] + mt_mul.pop((h % 2, si), []),
                            label=("s2", slab, ci, h, si))
                    mt_tok = []
                    for si in range(2):
                        mt_mul[(h % 2, si)] = S_.op(
                            "vector", lambda e, si=si, h=h, hp=hp: e.tensor_mul(
                                mtb[:, hp + si * 256: hp + (si + 1) * 256],
                                tb2[:, hp + si * 256: hp + (si + 1) * 256],
                                pb[si][:, 0:256]),
                            inc=[("v", 1)],
                            wait=mt_s2[(h % 2, si)]
                            + (hb_tok.get(("ymm", h % 2), [])
                               if si == 0 else []))
                        mt_tok += mt_mul[(h % 2, si)]
                    bd_tok = []
                    for si in range(2):
                        bd_tok += S_.op("vector", lambda e, si=si, h=h: e.tensor_scalar(
                            bdec[:, (h % 2) * 256 + si * 128:
                                 (h % 2) * 256 + (si + 1) * 128],
                            Btm[:, si * 128:(si + 1) * 128],
                            dyT[:, si * 64 + h: si * 64 + h + 1], None, ALU.mult),
                            inc=[("v", 1)],
                            wait=hb_tok.get(("lmm", h % 2), []) if si == 0 else ())
                    mt_done[h] = (mt_tok, bd_tok)

                emit_mt(0)
                for h in range(64):
                    mt_tok, bd_tok = mt_done.pop(h)
                    yb = pb[6 + h % 2]
                    lb = pb[4 + h % 2]
                    ymm = []
                    for li in range(2):
                        for si in range(2):
                            ymm = S_.op("tensor", lambda e, li=li, si=si, h=h, yb=yb:
                                        e.matmul(
                                yb[:, li * 128: li * 128 + 64],
                                mtb[:, (h % 2) * 512 + si * 256 + li * 128:
                                    (h % 2) * 512 + si * 256 + (li + 1) * 128],
                                hsdt[si][:, h * 64:(h + 1) * 64],
                                start=(si == 0), stop=(si == 1)),
                                inc=[("t", 1)] if si == 1 else (),
                                wait=(mt_tok + hb_tok.get(("yev", h % 2), []))
                                if (li == 0 and si == 0) else ())
                        if c > 0:
                            ymm = S_.op("tensor", lambda e, li=li, h=h, off=off,
                                        yb=yb: e.matmul(
                                yb[:, li * 128 + 64: li * 128 + 128],
                                Csl[:, off + li * 128: off + (li + 1) * 128],
                                Sbf[:, h * 64:(h + 1) * 64],
                                start=True, stop=True), inc=[("t", 1)], wait=())
                    hb_tok[("ymm", h % 2)] = ymm
                    lmm = []
                    for li in range(2):
                        lmm = S_.op("tensor", lambda e, li=li, h=h, lb=lb: e.matmul(
                            lb[:, 0:64],
                            bdec[:, (h % 2) * 256 + li * 128:
                                 (h % 2) * 256 + (li + 1) * 128],
                            hsdt[li][:, h * 64:(h + 1) * 64],
                            start=(li == 0), stop=(li == 1)),
                            inc=[("t", 1)] if li == 1 else (),
                            wait=(bd_tok + hb_tok.get(("sup", h % 2), []))
                            if li == 0 else ())
                    hb_tok[("lmm", h % 2)] = lmm
                    if h + 1 < 64:
                        emit_mt(h + 1)
                    yev_t = []
                    for li in range(2):
                        if c > 0:
                            ye = S_.op("vector", lambda e, li=li, h=h, yb=yb:
                                       e.tensor_scalar(
                                yev, yb[:, li * 128 + 64: li * 128 + 128],
                                ecT[:, li * 64 + h: li * 64 + h + 1], None, ALU.mult),
                                inc=[("v", 1)], wait=ymm)
                            yev_t += S_.op("vector", lambda e, li=li, h=h, yb=yb:
                                           e.tensor_add(
                                ytm[li][:, h * 64:(h + 1) * 64], yev,
                                yb[:, li * 128: li * 128 + 64]),
                                inc=[("v", 1)], wait=())
                        else:
                            yev_t += S_.op("vector", lambda e, li=li, h=h, yb=yb:
                                           e.tensor_copy(
                                ytm[li][:, h * 64:(h + 1) * 64],
                                yb[:, li * 128: li * 128 + 64]),
                                inc=[("v", 1)], wait=ymm)
                    hb_tok[("yev", h % 2)] = yev_t
                    su = S_.op("vector", lambda e, h=h: e.tensor_scalar(
                        Sst[:, h * 64:(h + 1) * 64], Sst[:, h * 64:(h + 1) * 64],
                        Ee[:, h:h + 1], None, ALU.mult),
                        inc=[("v", 1)], wait=lmm + mt_su3.pop(h, []))
                    su = S_.op("vector", lambda e, h=h, lb=lb: e.tensor_add(
                        Sst[:, h * 64:(h + 1) * 64], Sst[:, h * 64:(h + 1) * 64],
                        lb[:, 0:64]), inc=[("v", 1)], wait=())
                    hb_tok[("sup", h % 2)] = su
                    mt_su3[h] = S_.op("scalar", lambda e, h=h: e.activation(
                        Sbf[:, h * 64:(h + 1) * 64], Sst[:, h * 64:(h + 1) * 64],
                        AF.Copy), inc=[("a", 1)], wait=su)

                dbg_tap("cum", [((0, 64), cumc[0:64, 0:256])] if c == 0 else [])
                dbg_tap("hsdt", [((0, 128), hsdt[0]), ((128, 256), hsdt[1])]
                        if c == 0 else [])
                dbg_tap("S1", [((0, 128), Sst)] if c == 0 else [])

                xg_ev = {}
                for j in range(32 if _BUILD_OPTS["xg"] else 0):
                    par = j % 2
                    bank = pbf[6 + par]
                    tt = []
                    for li in range(2):
                        tt = S_.op("tensor", lambda e, j=j, li=li, bank=bank:
                                   e.transpose(
                            bank[:, li * 128:(li + 1) * 128],
                            ytm[li][:, j * 128:(j + 1) * 128], idb),
                            inc=[("t", 1)] if li == 1 else (),
                            wait=xg_ev.get(par, S_.aft("v")) if li == 0 else ())
                    s1 = S_.op("vector", lambda e, j=j, off=off,
                               bank=bank: e.scalar_tensor_tensor(
                        sqf[0][:, 0:256], hs[j][:, off: off + 256],
                        vcol("D", j), bank[:, 0:256], ALU.mult, ALU.add),
                        inc=[("v", 1)], wait=tt)
                    xg_ev[par] = S_.op("vector", lambda e, j=j, off=off:
                                       e.tensor_mul(
                        xg[j][:, off: off + 256], sqf[0][:, 0:256],
                        gsl[j][:, off: off + 256]),
                        inc=[("v", 1)], wait=())

            if not _BUILD_OPTS["op"]:
                continue
            sq_tok = []
            prev_mm = {}
            for j in range(32):
                sj = S_.op("scalar", lambda e, j=j: e.activation(
                    sqb[j % 2], xg[j], AF.Square), inc=[("a", 1)],
                    wait=(S_.aft("v") if j == 0 else []) + prev_mm.get(j % 2, []))
                sq_tok = S_.op("tensor", lambda e, j=j: e.matmul(
                    pb[5][0:1, 0:SLAB], onesb, sqb[j % 2],
                    start=(j == 0), stop=(j == 31)),
                    inc=[("t", 1)], wait=sj)
                prev_mm[j % 2] = sq_tok
            rv = S_.op("vector", lambda e: e.tensor_copy(rowb, pb[5][0:1, 0:SLAB]),
                       inc=[("v", 1)], wait=sq_tok)
            rv = S_.op("vector", lambda e: e.tensor_scalar(
                rowb, rowb, 1.0 / 4096.0, None, ALU.mult), inc=[("v", 1)], wait=())
            rv = S_.op("scalar", lambda e: e.activation(
                rowb, rowb, AF.Sqrt, bias=vcol("epsA")[0:1, :]), inc=[("a", 1)], wait=rv)
            rv = S_.op("vector", lambda e: e.reciprocal(rowb, rowb),
                       inc=[("v", 1)], wait=rv)
            rd = S_.op("gpsimd", lambda e, slab=slab: e.dma_start(
                out=rows_d[slab * 512:(slab + 1) * 512].unsqueeze(0),
                in_=rowb), inc=[("gsp", 16)], wait=rv)
            rb2 = S_.op("gpsimd", lambda e, slab=slab: e.dma_start(
                out=bcb[0], in_=rows_d[slab * 512:(slab + 1) * 512]
                    [None, :].to_broadcast([128, 512])),
                inc=[("gsp", 16)], wait=rd)
            for j in range(32):
                S_.op("vector", lambda e, j=j: e.scalar_tensor_tensor(
                    xg[j], xg[j], vcol("nw", j), bcb[0], ALU.mult, ALU.mult),
                    inc=[("v", 1)], wait=rb2)
            xgn_done = S_.aft("v")

            for mg in range(4):
                ld_all = []
                mm_tok = []
                kq_tok = {}
                for kq in range(4):
                    lw = S_.op("sync", lambda e, mg=mg, kq=kq: e.dma_start(
                        out=opst.rearrange("p (k c) -> p k c", k=8),
                        in_=opw_d[kq * 1024:(kq + 1) * 1024, mg * 512:(mg + 1) * 512]
                            .rearrange("(k p) c -> p k c", p=128)),
                        inc=[(f"sw{kq % 2}", 16)], wait=S_.aft("v") if (mg + kq) > 0 else S_.aft("t"))
                    cw = S_.op("vector", lambda e, kq=kq: e.tensor_copy(
                        opbf[kq % 2], opst), inc=[("v", 1)],
                        wait=list(lw) + kq_tok.get(kq % 2, S_.aft("t")))
                    for k8 in range(8):
                        for mi in range(4):
                            kk = kq * 8 + k8
                            tok = S_.op("tensor", lambda e, kq=kq, k8=k8, mi=mi, kk=kk:
                                        e.matmul(
                                pb[mi][:, 0:SLAB],
                                opbf[kq % 2][:, k8 * 512 + mi * 128: k8 * 512 + (mi + 1) * 128],
                                xg[kk], start=(kk == 0), stop=(kk == 31)),
                                inc=[("t", 1)] if (kk == 31 or (k8 == 7 and mi == 3))
                                else (),
                                wait=(list(cw) + xgn_done) if (k8 == 0 and mi == 0) else ())
                            if k8 == 7 and mi == 3:
                                kq_tok[kq % 2] = tok
                            if kk == 31:
                                mm_tok += tok
                for mi in range(4):
                    m = mg * 4 + mi
                    xl = S_.op("sync", lambda e, mi=mi, m=m, t0=t0: e.dma_start(
                        out=x0st[mi % 2], in_=x0T[m * 128:(m + 1) * 128, t0:t0 + SLAB]),
                        inc=[(f"sx{mi % 2}", 16)], wait=S_.aft("v"))
                    evt = S_.op("vector", lambda e, mi=mi, m=m: e.scalar_tensor_tensor(
                        prb[mi % 2], pb[mi][:, 0:SLAB], vcol("ob", m), x0st[mi % 2],
                        ALU.add, ALU.add), inc=[("v", 1)],
                        wait=list(xl) + mm_tok + prev_mm.get(("pr", m % 2), [])
                        + prev_mm.get(("sqa", m % 2), []) + S_.aft("gsp"))
                    S_.op("gpsimd", lambda e, mi=mi, m=m, t0=t0: e.dma_start(
                        out=prex1_d[m * 128:(m + 1) * 128, t0:t0 + SLAB],
                        in_=prb[mi % 2]), inc=[("gsp", 16)], wait=evt)
                    sq2 = S_.op("scalar", lambda e, mi=mi: e.activation(
                        sqf[mi % 2], prb[mi % 2], AF.Square), inc=[("a", 1)],
                        wait=evt + prev_mm.get(("sq", m % 2), []))
                    prev_mm[("sqa", m % 2)] = sq2
                    smm = S_.op("tensor", lambda e, m=m, mi=mi: e.matmul(
                        pb[6][0:1, 0:SLAB], vcol("ones"), prb[mi % 2],
                        start=(m == 0), stop=(m == 15)),
                        inc=[("t", 1)], wait=evt)
                    prev_mm[("pr", m % 2)] = smm
                    smm2 = S_.op("tensor", lambda e, m=m, mi=mi: e.matmul(
                        pb[7][0:1, 0:SLAB], vcol("ones"), sqf[mi % 2],
                        start=(m == 0), stop=(m == 15)),
                        inc=[("t", 1)], wait=sq2)
                    prev_mm[("sq", m % 2)] = smm2
            st_tok = S_.aft("t")
            rv = S_.op("vector", lambda e: e.tensor_scalar(
                rowb, pb[6][0:1, 0:SLAB], 1.0 / 2048.0, None, ALU.mult),
                inc=[("v", 1)], wait=st_tok)
            rv2 = S_.op("vector", lambda e: e.tensor_scalar(
                rowb2, pb[7][0:1, 0:SLAB], 1.0 / 2048.0, None, ALU.mult),
                inc=[("v", 1)], wait=())
            sq3 = S_.op("scalar", lambda e: e.activation(
                bcb[1][0:1, :], rowb, AF.Square), inc=[("a", 1)], wait=rv)
            vv = S_.op("vector", lambda e: e.tensor_sub(
                rowb2, rowb2, bcb[1][0:1, :]), inc=[("v", 1)], wait=sq3)
            s1 = S_.op("scalar", lambda e: e.activation(
                bcb[1][0:1, :], rowb2, AF.Sqrt, bias=vcol("epsA")[0:1, :]), inc=[("a", 1)], wait=vv)
            r1 = S_.op("vector", lambda e: e.reciprocal(
                bcb[1][0:1, :], bcb[1][0:1, :]), inc=[("v", 1)], wait=s1)
            S_.op("gpsimd", lambda e, slab=slab: e.dma_start(
                out=rows_d[2048 + slab * 512: 2048 + (slab + 1) * 512].unsqueeze(0),
                in_=rowb), inc=[("gsp", 16)], wait=rv)
            S_.op("gpsimd", lambda e, slab=slab: e.dma_start(
                out=rows_d[4096 + slab * 512: 4096 + (slab + 1) * 512].unsqueeze(0),
                in_=bcb[1][0:1, :]), inc=[("gsp", 16)], wait=r1)
            t1 = S_.op("vector", lambda e: e.tensor_mul(
                rowb2, rowb2, bcb[1][0:1, :]), inc=[("v", 1)], wait=r1)
            t1 = S_.op("vector", lambda e: e.tensor_mul(
                rowb2, rowb2, bcb[1][0:1, :]), inc=[("v", 1)], wait=())
            t2 = S_.op("scalar", lambda e: e.activation(
                rowb2, rowb2, AF.Sqrt, bias=vcol("epsF")[0:1, :]), inc=[("a", 1)], wait=t1)
            t3 = S_.op("vector", lambda e: e.reciprocal(rowb2, rowb2),
                       inc=[("v", 1)], wait=t2)
            S_.op("gpsimd", lambda e, slab=slab: e.dma_start(
                out=rows_d[6144 + slab * 512: 6144 + (slab + 1) * 512].unsqueeze(0),
                in_=rowb2), inc=[("gsp", 16)], wait=t3)

        dbg_tap("prex1", [((0, 2048), prex1_d[:, 0:512])])
        dbg_tap("rows", [((0, 1), rows_d[0:1024].unsqueeze(0)),
                         ((1, 2), rows_d[2048:3072].unsqueeze(0)),
                         ((2, 3), rows_d[4096:5120].unsqueeze(0)),
                         ((3, 4), rows_d[6144:7168].unsqueeze(0))])
        a_sc.close()

        S_.barrier()
        b_sc = ExitStack()
        bent = b_sc.enter_context
        XL = [bent(nc.sbuf_tensor(f"XL{i}", [128, S], BF16)).ap() for i in range(16)]
        prl = [bent(nc.sbuf_tensor(f"prl{i}", [128, SLAB], F32)).ap() for i in range(2)]
        x1b = [bent(nc.sbuf_tensor(f"x1b{i}", [128, SLAB], F32)).ap() for i in range(2)]
        bcB = [bent(nc.sbuf_tensor(f"bcB{i}", [128, SLAB], F32)).ap() for i in range(3)]
        if _BUILD_OPTS["xl"]:
            prev = {}
            for slab in range(NSLAB):
                t0 = slab * SLAB
                bc_tok = []
                for j, roff in enumerate((2048, 4096, 6144)):
                    bc_tok += S_.op("gpsimd", lambda e, j=j, roff=roff, slab=slab:
                                    e.dma_start(
                        out=bcB[j],
                        in_=rows_d[roff + slab * 512: roff + (slab + 1) * 512]
                            [None, :].to_broadcast([128, 512])),
                        inc=[("gbc", 16)], wait=S_.aft("v") if slab else ())
                for m in range(16):
                    pl = S_.op("sync", lambda e, m=m, t0=t0: e.dma_start(
                        out=prl[m % 2], in_=prex1_d[m * 128:(m + 1) * 128, t0:t0 + SLAB]),
                        inc=[(f"sx{m % 2}", 16)],
                        wait=prev.get(("sub", m % 2), S_.aft("gsp")))
                    s1 = S_.op("vector", lambda e, m=m: e.tensor_sub(
                        x1b[m % 2], prl[m % 2], bcB[0]),
                        inc=[("v", 1)], wait=list(pl) + (bc_tok if m == 0 else [])
                        + prev.get(("sp", m % 2), []))
                    prev[("sub", m % 2)] = s1
                    s1 = S_.op("vector", lambda e, m=m: e.tensor_mul(
                        x1b[m % 2], x1b[m % 2], bcB[1]), inc=[("v", 1)], wait=())
                    prev[("sp", m % 2)] = S_.op("gpsimd", lambda e, m=m, t0=t0:
                                                e.dma_start(
                        out=x1sp_d[m * 128:(m + 1) * 128, t0:t0 + SLAB],
                        in_=x1b[m % 2]), inc=[(f"gc{m % 2}", 16)], wait=s1)
                    prev[("xl", m % 2)] = S_.op("vector", lambda e, m=m, t0=t0:
                                                e.scalar_tensor_tensor(
                        XL[m][:, t0:t0 + SLAB], x1b[m % 2], vcol("l1", m), bcB[2],
                        ALU.mult, ALU.mult), inc=[("v", 1)], wait=())
            bl0 = bent(nc.sbuf_tensor("bl0", [128, 1024], F32)).ap()
            bl1 = bent(nc.sbuf_tensor("bl1", [128, 1024], F32)).ap()
            blo = bent(nc.sbuf_tensor("blo", [128, 1024], F32)).ap()
            for m in range(16):
                h0 = S_.op("sync", lambda e, m=m: e.dma_start(
                    out=bl0, in_=x1sp_d[m * 128:(m + 1) * 128, 0:1024]),
                    inc=[("sx0", 16)], wait=S_.aft("gc0", "gc1", "v") if m == 0
                    else S_.aft("gbc"))
                h1 = S_.op("sync", lambda e, m=m: e.dma_start(
                    out=bl1, in_=x1sp_d[m * 128:(m + 1) * 128, 1024:2048]),
                    inc=[("sx1", 16)], wait=())
                bl = S_.op("vector", lambda e, m=m: e.tensor_scalar(
                    blo, bl0, vcol("m0"), None, ALU.mult),
                    inc=[("v", 1)], wait=list(h0) + list(h1) + S_.aft("gbc"))
                bl = S_.op("vector", lambda e, m=m: e.scalar_tensor_tensor(
                    blo, bl1, vcol("m1"), blo, ALU.mult, ALU.add),
                    inc=[("v", 1)], wait=())
                S_.op("gpsimd", lambda e, m=m: e.dma_start(
                    out=x1own_d[m * 128:(m + 1) * 128, :],
                    in_=blo), inc=[("gbc", 16)], wait=bl)

        wst2p = [bent(nc.sbuf_tensor(f"wst2{i}", [128, 2048], F32)).ap()
                 for i in range(2)]
        wkbf = [bent(nc.sbuf_tensor(f"wkbf{i}", [128, 2048], BF16)).ap()
                for i in range(2)]
        kvb = [bent(nc.sbuf_tensor(f"kvb{i}", [128, SLAB], BF16)).ap()
               for i in range(2)]
        sqk = bent(nc.sbuf_tensor("sqk", [128, SLAB], BF16)).ap()
        if _BUILD_OPTS["kv"]:
            S_.op("vector", lambda e: e.memset(mxk, 0.0), inc=[("v", 1)])
            prev = {}
            for m in range(16):
                lw = []
                for kh in range(2):
                    lw += S_.op("sync", lambda e, kh=kh, m=m: e.dma_start(
                        out=wst2p[m % 2][:, kh * 1024:(kh + 1) * 1024]
                            .rearrange("p (k c) -> p k c", k=8),
                        in_=kwT_d[kh * 1024:(kh + 1) * 1024, m * 128:(m + 1) * 128]
                            .rearrange("(k p) c -> p k c", p=128)),
                        inc=[(f"sw{m % 2}", 16)],
                        wait=prev.get(("cst", m % 2), S_.aft("t", "v"))
                        if kh == 0 else ())
                cst = S_.op("vector", lambda e, m=m: e.tensor_copy(
                    wkbf[m % 2], wst2p[m % 2]), inc=[("v", 1)],
                    wait=lw[-1:] + prev.get(("mmk", m % 2), []))
                prev[("cst", m % 2)] = cst
                for tc in range(4):
                    mmt = []
                    for k in range(16):
                        mmt = S_.op("tensor", lambda e, m=m, tc=tc, k=k: e.matmul(
                            pb[tc % 2][:, 0:SLAB],
                            wkbf[m % 2][:, k * 128:(k + 1) * 128],
                            XL[k][:, tc * 512:(tc + 1) * 512],
                            start=(k == 0), stop=(k == 15)),
                            inc=[("t", 1)] if k == 15 else (),
                            wait=(list(cst) + prev.get(("kev", tc % 2), []))
                            if k == 0 else ())
                    if tc == 3:
                        prev[("mmk", m % 2)] = mmt
                    ev = S_.op("scalar", lambda e, m=m, tc=tc: e.activation(
                        kvb[tc % 2], pb[tc % 2][:, 0:SLAB], AF.Copy),
                        inc=[("a", 1)], wait=mmt + prev.get(("ksp", tc % 2), []))
                    prev[("kev", tc % 2)] = ev
                    prev[("ksp", tc % 2)] = S_.op(
                        "gpsimd", lambda e, m=m, tc=tc: e.dma_start(
                            out=kspill_d[m * 128:(m + 1) * 128,
                                         tc * 512:(tc + 1) * 512],
                            in_=kvb[tc % 2]), inc=[(f"gc{tc % 2}", 16)], wait=ev)
                    sq = S_.op("scalar", lambda e, tc=tc: e.activation(
                        sqk, kvb[tc % 2], AF.Square), inc=[("a", 1)],
                        wait=ev + prev.get(("sqk",), []))
                    smm = S_.op("tensor", lambda e: e.matmul(
                        pb[6][0:1, 0:SLAB], onesb, sqk, start=True, stop=True),
                        inc=[("t", 1)], wait=sq + prev.get(("nrm",), []))
                    prev[("sqk",)] = smm
                    rm = S_.op("vector", lambda e: e.reduce_max(
                        rowb[0:1, 0:1], pb[6][0:1, 0:SLAB], axis=AX),
                        inc=[("v", 1)], wait=smm)
                    prev[("nrm",)] = S_.op("vector", lambda e, m=m: e.tensor_max(
                        mxk[0:1, m:m + 1], mxk[0:1, m:m + 1], rowb[0:1, 0:1]),
                        inc=[("v", 1)], wait=())
        vvbf = [bent(nc.sbuf_tensor(f"vvbf{i}", [128, SLAB], BF16)).ap()
                for i in range(16)]
        vst = [bent(nc.sbuf_tensor(f"vst{i}", [128, SLAB], F32)).ap()
               for i in range(2)]
        if _BUILD_OPTS["kv"]:
            lastmm = []
            for vq in range(4):
                casts = []
                for k in range(16):
                    lw = S_.op("sync", lambda e, vq=vq, k=k: e.dma_start(
                        out=vst[k % 2],
                        in_=vwT_d[k * 128:(k + 1) * 128, vq * 512:(vq + 1) * 512]),
                        inc=[(f"sx{k % 2}", 16)],
                        wait=prev.get(("vst", k % 2), S_.aft("v", "a")))
                    casts = S_.op("vector", lambda e, k=k: e.tensor_copy(
                        vvbf[k], vst[k % 2]), inc=[("v", 1)],
                        wait=list(lw) + lastmm)
                    prev[("vst", k % 2)] = casts
                for tb in range(16):
                    mmt = []
                    for k in range(16):
                        mmt = S_.op("tensor", lambda e, vq=vq, tb=tb, k=k: e.matmul(
                            pb[tb % 2][:, 0:SLAB],
                            XL[k][:, tb * 128:(tb + 1) * 128], vvbf[k],
                            start=(k == 0), stop=(k == 15)),
                            inc=[("t", 1)] if k == 15 else (),
                            wait=(casts + prev.get(("vev", tb % 2), []))
                            if k == 0 else ())
                    if tb >= 14:
                        lastmm = lastmm + mmt if tb == 15 else list(mmt)
                    ev = S_.op("scalar", lambda e, tb=tb: e.activation(
                        kvb[tb % 2], pb[tb % 2][:, 0:SLAB], AF.Copy),
                        inc=[("a", 1)],
                        wait=mmt + prev.get(("vsp", tb % 2),
                                            prev.get(("ksp", tb % 2), [])))
                    prev[("vev", tb % 2)] = ev
                    prev[("vsp", tb % 2)] = S_.op(
                        "gpsimd", lambda e, vq=vq, tb=tb: e.dma_start(
                            out=vspill_d[tb * 128:(tb + 1) * 128,
                                         vq * 512:(vq + 1) * 512],
                            in_=kvb[tb % 2]), inc=[(f"gc{tb % 2}", 16)], wait=ev)
                lastmm = lastmm[-1:]
        dbg_tap("kfm", [((0, 2048), kspill_d[:, 0:512])])
        dbg_tap("vtm", [((0, 2048), vspill_d[:, 0:512])])
        S_.barrier()
        b_sc.close()

        x2_sc = ExitStack()
        x2t = [x2_sc.enter_context(
            nc.sbuf_tensor(f"x2t{i}", [128, 1024], BF16, side="right")).ap()
            for i in range(16)]
        de_sc = ExitStack()
        afm = [de_sc.enter_context(
            nc.sbuf_tensor(f"afm{i}", [128, 1024], BF16, side="right")).ap()
            for i in range(16)]
        d_sc = ExitStack()
        dent = d_sc.enter_context
        XLo = [dent(nc.sbuf_tensor(f"XLo{i}", [128, 1024], BF16)).ap()
               for i in range(16)]
        m01t = [dent(nc.sbuf_tensor(f"m01t{i}", [128, 1024], BF16)).ap()
                for i in range(16)]
        q_all = [dent(nc.sbuf_tensor(f"qall{i}", [128, 1024], BF16)).ap()
                 for i in range(16)]
        kh_b2 = [dent(nc.sbuf_tensor(f"kh_b{i}", [128, 2048], BF16)).ap()
                 for i in range(2)]
        vh_b0 = dent(nc.sbuf_tensor("vh_b0", [128, 16 * 129], BF16)).ap()
        vh_b2 = [vh_b0, vh_b0]
        qst0 = dent(nc.sbuf_tensor("qst0", [128, 2048], F32)).ap()
        qst = [qst0, qst0]
        prl2 = [qst0[:, 0:512], qst0[:, 512:1024]]
        qbf = [dent(nc.sbuf_tensor(f"qbf{i}", [128, 2048], BF16)).ap()
               for i in range(2)]
        praw = [bcb[0].bitcast(BF16)[:, 0:512], bcb[1].bitcast(BF16)[:, 0:512]]
        sqq = praw[0]
        pT = [dent(nc.sbuf_tensor(f"pT{i}", [128, SLAB], BF16)).ap()
              for i in range(2)]
        att = [dent(nc.sbuf_tensor(f"att{i}", [128, 128], BF16)).ap()
               for i in range(2)]
        rcp = dent(nc.sbuf_tensor("rcp", [128, 1], F32)).ap()
        ncol = dent(nc.sbuf_tensor("ncol", [128, 16], F32)).ap()

        if _BUILD_OPTS["att"]:
            for tc in range(2):
                b0 = S_.op("gpsimd", lambda e, tc=tc: e.dma_start(
                    out=bcb[0], in_=rows_d[6144 + tc * 512: 6144 + (tc + 1) * 512]
                        [None, :].to_broadcast([128, 512])),
                    inc=[("gbc", 16)], wait=S_.aft("v"))
                b1 = S_.op("gpsimd", lambda e, tc=tc: e.dma_start(
                    out=bcb[1], in_=rows_d[7168 + tc * 512: 7168 + (tc + 1) * 512]
                        [None, :].to_broadcast([128, 512])),
                    inc=[("gbc", 16)], wait=())
                bl = S_.op("vector", lambda e: e.tensor_scalar(
                    bcb[0], bcb[0], vcol("m0"), None, ALU.mult),
                    inc=[("v", 1)], wait=list(b0) + list(b1))
                bl = S_.op("vector", lambda e: e.scalar_tensor_tensor(
                    bcb[0], bcb[1], vcol("m1"), bcb[0], ALU.mult, ALU.add),
                    inc=[("v", 1)], wait=())
                prev2 = {}
                for m in range(16):
                    xo = S_.op("sync", lambda e, m=m, tc=tc: e.dma_start(
                        out=prl2[m % 2],
                        in_=x1own_d[m * 128:(m + 1) * 128, tc * 512:(tc + 1) * 512]),
                        inc=[(f"sx{m % 2}", 16)],
                        wait=prev2.get(m % 2, S_.aft("v", "gbc")))
                    prev2[m % 2] = S_.op("vector", lambda e, m=m, tc=tc:
                                         e.scalar_tensor_tensor(
                        XLo[m][:, tc * 512:(tc + 1) * 512], prl2[m % 2],
                        vcol("l1", m), bcb[0], ALU.mult, ALU.mult),
                        inc=[("v", 1)], wait=list(xo) + (bl if m == 0 else []))
            for m in range(16):
                S_.op("sync", lambda e, m=m: e.dma_start(
                    out=m01t[m], in_=m01_d[m * 128:(m + 1) * 128, :]),
                    inc=[("st", 16)], wait=())
            m01_tok = S_.aft("st")
            S_.op("vector", lambda e: e.memset(vh_b0, 1.0), inc=[("v", 1)])
            prevq = {}
            for h in range(16):
                lw = []
                for kh in range(2):
                    lw += S_.op("sync", lambda e, kh=kh, h=h: e.dma_start(
                        out=qst[h % 2][:, kh * 1024:(kh + 1) * 1024]
                            .rearrange("p (k c) -> p k c", k=8),
                        in_=qwT_d[kh * 1024:(kh + 1) * 1024, h * 128:(h + 1) * 128]
                            .rearrange("(k p) c -> p k c", p=128)),
                        inc=[(f"sw{h % 2}", 16)],
                        wait=prevq.get(("cst", (h - 1) % 2), S_.aft("v", "t"))
                        if kh == 0 else ())
                cst = S_.op("vector", lambda e, h=h: e.tensor_copy(
                    qbf[h % 2], qst[h % 2]), inc=[("v", 1)],
                    wait=lw[-1:] + prevq.get(("mm", h % 2), []))
                prevq[("cst", h % 2)] = cst
                for tc in range(2):
                    mmt = []
                    for k in range(16):
                        mmt = S_.op("tensor", lambda e, h=h, tc=tc, k=k: e.matmul(
                            pb[tc % 2][:, 0:SLAB],
                            qbf[h % 2][:, k * 128:(k + 1) * 128],
                            XLo[k][:, tc * 512:(tc + 1) * 512],
                            start=(k == 0), stop=(k == 15)),
                            inc=[("t", 1)] if k == 15 else (),
                            wait=(list(cst) + prevq.get(("ev", tc % 2), []))
                            if k == 0 else ())
                    if tc == 1:
                        prevq[("mm", h % 2)] = mmt
                    ev = S_.op("scalar", lambda e, h=h, tc=tc: e.activation(
                        q_all[h][:, tc * 512:(tc + 1) * 512],
                        pb[tc % 2][:, 0:SLAB], AF.Copy),
                        inc=[("a", 1)], wait=mmt)
                    prevq[("ev", tc % 2)] = ev
                    sq = S_.op("scalar", lambda e, h=h, tc=tc: e.activation(
                        sqq, q_all[h][:, tc * 512:(tc + 1) * 512], AF.Square),
                        inc=[("a", 1)], wait=())
                    smm = S_.op("tensor", lambda e: e.matmul(
                        pb[6][0:1, 0:SLAB], onesb, sqq, start=True, stop=True),
                        inc=[("t", 1)], wait=sq + prevq.get("nrm", []))
                    rm = S_.op("vector", lambda e: e.reduce_max(
                        rowb[0:1, 0:1], pb[6][0:1, 0:SLAB], axis=AX),
                        inc=[("v", 1)], wait=smm)
                    prevq["nrm"] = S_.op("vector", lambda e, h=h: e.tensor_max(
                        mxk[0:1, 16 + h:17 + h], mxk[0:1, 16 + h:17 + h],
                        rowb[0:1, 0:1]), inc=[("v", 1)], wait=())
            nr = S_.op("vector", lambda e: e.tensor_mul(
                mxk[0:1, 32:48], mxk[0:1, 0:16], mxk[0:1, 16:32]),
                inc=[("v", 1)], wait=prevq["nrm"])
            nr = S_.op("scalar", lambda e: e.activation(
                mxk[0:1, 32:48], mxk[0:1, 32:48], AF.Sqrt), inc=[("a", 1)], wait=nr)
            nr = S_.op("vector", lambda e: e.tensor_scalar(
                mxk[0:1, 32:48], mxk[0:1, 32:48], -SCALE, None, ALU.mult),
                inc=[("v", 1)], wait=nr)
            nr = S_.op("gpsimd", lambda e: e.dma_start(
                out=rows_d[8192:8208].unsqueeze(0), in_=mxk[0:1, 32:48]),
                inc=[("gsp", 16)], wait=nr)
            nr = S_.op("gpsimd", lambda e: e.dma_start(
                out=ncol, in_=rows_d[8192:8208][None, :].to_broadcast([128, 16])),
                inc=[("gsp", 16)], wait=nr)
            ncol_tok = S_.aft("gsp")
            prevh = {}
            for h in range(16):
                kh_b = kh_b2[h % 2]
                vh_b = vh_b2[h % 2]
                klw = S_.op("sync", lambda e, h=h, kh_b=kh_b: e.dma_start(
                    out=kh_b, in_=kspill_d[h * 128:(h + 1) * 128, :]),
                    inc=[(f"sx{h % 2}", 16)],
                    wait=prevh.get(("ks", h % 2),
                                   S_.aft("gc0", "gc1", "a", "v")))
                vlw = []
                for half in range(2):
                    vlw += S_.op("sync", lambda e, h=h, half=half, vh_b=vh_b:
                                 e.dma_start(
                        out=vh_b.rearrange("p (tb c) -> p tb c", c=129)
                            [:, half * 8:(half + 1) * 8, 0:128],
                        in_=vspill_d[:, h * 128:(h + 1) * 128]
                            .rearrange("(tb p) c -> p tb c", p=128)
                            [:, half * 8:(half + 1) * 8, :]),
                        inc=[(f"sx{h % 2}", 16)],
                        wait=prevh.get(("ks", (h - 1) % 2), []) if half == 0 else ())
                for tc in range(2):
                    pv_first = []
                    for kt in range(16):
                        smm = S_.op("tensor", lambda e, h=h, tc=tc, kt=kt,
                                    kh_b=kh_b: e.matmul(
                            pb[kt % 2][:, 0:SLAB],
                            kh_b[:, kt * 128:(kt + 1) * 128],
                            q_all[h][:, tc * 512:(tc + 1) * 512],
                            start=True, stop=True), inc=[("t", 1)],
                            wait=(list(klw) + vlw + ncol_tok + m01_tok
                                  + S_.aft("v", "a"))
                            if kt == 0 and tc == 0 else
                            prevh.get(("ex", kt % 2), []))
                        ex = S_.op("scalar", lambda e, h=h, tc=tc, kt=kt:
                                   e.activation(
                            praw[kt % 2], pb[kt % 2][:, 0:SLAB], AF.Exp,
                            bias=ncol[:, h:h + 1], scale=SCALE),
                            inc=[("a", 1)],
                            wait=smm + prevh.get(("mul", kt % 2), []))
                        prevh[("ex", kt % 2)] = ex
                        mul = S_.op("vector", lambda e, h=h, tc=tc, kt=kt:
                                    e.tensor_mul(
                            pT[kt % 2], praw[kt % 2],
                            m01t[kt][:, tc * 512:(tc + 1) * 512]),
                            inc=[("v", 1)],
                            wait=ex + prevh.get(("pv", kt % 2), []))
                        prevh[("mul", kt % 2)] = mul
                        pvt = []
                        for qs in range(4):
                            pvt = S_.op("tensor", lambda e, kt=kt, qs=qs,
                                        vh_b=vh_b: e.matmul(
                                pb[2 + qs][:, 0:129],
                                pT[kt % 2][:, qs * 128:(qs + 1) * 128],
                                vh_b.rearrange("p (tb c) -> p tb c", c=129)[:, kt, :],
                                start=(kt == 0), stop=(kt == 15)),
                                inc=[("t", 1)] if kt == 15 else (),
                                wait=mul if qs == 0 else ())
                        prevh[("pv", kt % 2)] = pvt if kt == 15 else mul
                    for qs in range(4):
                        rc = S_.op("vector", lambda e, qs=qs: e.reciprocal(
                            rcp, pb[2 + qs][:, 128:129]), inc=[("v", 1)],
                            wait=pvt)
                        at = S_.op("vector", lambda e, qs=qs: e.tensor_scalar(
                            att[qs % 2], pb[2 + qs][:, 0:128], rcp, None,
                            ALU.mult), inc=[("v", 1)],
                            wait=prevh.get(("atv", qs % 2), []))
                        tt = S_.op("tensor", lambda e, qs=qs: e.transpose(
                            pbf[6][:, 0:128], att[qs % 2], idb),
                            inc=[("t", 1)], wait=at + prevh.get(("afm",), []))
                        prevh[("afm",)] = S_.op(
                            "vector", lambda e, h=h, tc=tc, qs=qs: e.tensor_copy(
                                afm[h][:, tc * 512 + qs * 128:
                                       tc * 512 + (qs + 1) * 128],
                                pbf[6][:, 0:128]), inc=[("v", 1)], wait=tt)
                        prevh[("atv", qs % 2)] = prevh[("afm",)]
                prevh[("ks", h % 2)] = smm + pvt
        dbg_tap("attn", [((h * 128, (h + 1) * 128), afm[h]) for h in range(16)])
        d_sc.close()

        e_sc = ExitStack()
        eent = e_sc.enter_context
        hid = [eent(nc.sbuf_tensor(f"hid{i}", [128, 1024], F32)).ap()
               for i in range(16)]
        ost2 = [eent(nc.sbuf_tensor(f"ost{i}", [128, 2048], F32)).ap()
                for i in range(2)]
        obf = [eent(nc.sbuf_tensor(f"obf{i}", [128, 2048], BF16)).ap()
               for i in range(2)]
        x1r = [eent(nc.sbuf_tensor(f"x1r{i}", [128, SLAB], F32)).ap()
               for i in range(2)]
        sqe = [eent(nc.sbuf_tensor(f"sqe{i}", [128, SLAB], F32)).ap()
               for i in range(2)]
        if _BUILD_OPTS["moe"]:
            prevo = {}
            for m in range(16):
                lw = []
                for kh in range(2):
                    lw += S_.op("sync", lambda e, kh=kh, m=m: e.dma_start(
                        out=ost2[m % 2][:, kh * 1024:(kh + 1) * 1024]
                            .rearrange("p (k c) -> p k c", k=8),
                        in_=owT_d[kh * 1024:(kh + 1) * 1024, m * 128:(m + 1) * 128]
                            .rearrange("(k p) c -> p k c", p=128)),
                        inc=[(f"sw{m % 2}", 16)],
                        wait=prevo.get(("cst", m % 2), S_.aft("v", "t"))
                        if kh == 0 else ())
                cst = S_.op("vector", lambda e, m=m: e.tensor_copy(
                    obf[m % 2], ost2[m % 2]), inc=[("v", 1)],
                    wait=lw[-1:] + prevo.get(("mm", m % 2), []))
                prevo[("cst", m % 2)] = cst
                for tc in range(2):
                    mmt = []
                    for k in range(16):
                        mmt = S_.op("tensor", lambda e, m=m, tc=tc, k=k: e.matmul(
                            pb[tc % 2][:, 0:SLAB],
                            obf[m % 2][:, k * 128:(k + 1) * 128],
                            afm[k][:, tc * 512:(tc + 1) * 512],
                            start=(k == 0), stop=(k == 15)),
                            inc=[("t", 1)] if k == 15 else (),
                            wait=(list(cst) + prevo.get(("ev", tc % 2), []))
                            if k == 0 else ())
                    if tc == 1:
                        prevo[("mm", m % 2)] = mmt
                    xo = S_.op("sync", lambda e, m=m, tc=tc: e.dma_start(
                        out=x1r[tc % 2],
                        in_=x1own_d[m * 128:(m + 1) * 128, tc * 512:(tc + 1) * 512]),
                        inc=[(f"sx{tc % 2}", 16)],
                        wait=prevo.get(("x1", tc % 2), S_.aft("v")))
                    ev = S_.op("vector", lambda e, m=m, tc=tc: e.tensor_add(
                        hid[m][:, tc * 512:(tc + 1) * 512], pb[tc % 2][:, 0:SLAB],
                        x1r[tc % 2]), inc=[("v", 1)], wait=mmt + list(xo))
                    prevo[("ev", tc % 2)] = ev
                    prevo[("x1", tc % 2)] = ev
            for tc in range(2):
                smt = []
                prevs = {}
                for m in range(16):
                    sq = S_.op("scalar", lambda e, m=m, tc=tc: e.activation(
                        sqe[m % 2], hid[m][:, tc * 512:(tc + 1) * 512], AF.Square),
                        inc=[("a", 1)],
                        wait=(S_.aft("v") if m == 0 else [])
                        + prevs.get(m % 2, []))
                    smt = S_.op("tensor", lambda e, m=m: e.matmul(
                        pb[6][0:1, 0:SLAB], vcol("ones"), sqe[m % 2],
                        start=(m == 0), stop=(m == 15)),
                        inc=[("t", 1)], wait=sq)
                    prevs[m % 2] = smt
                rv = S_.op("vector", lambda e: e.tensor_scalar(
                    rowb, pb[6][0:1, 0:SLAB], 1.0 / 2048.0, None, ALU.mult),
                    inc=[("v", 1)], wait=smt)
                rv = S_.op("scalar", lambda e: e.activation(
                    rowb, rowb, AF.Sqrt, bias=vcol("epsF")[0:1, :]),
                    inc=[("a", 1)], wait=rv)
                rv = S_.op("vector", lambda e: e.reciprocal(rowb, rowb),
                           inc=[("v", 1)], wait=rv)
                rd = S_.op("gpsimd", lambda e, tc=tc: e.dma_start(
                    out=rows_d[8320 + tc * 512: 8320 + (tc + 1) * 512].unsqueeze(0),
                    in_=rowb), inc=[("gsp", 16)], wait=rv)
                rb = S_.op("gpsimd", lambda e, tc=tc: e.dma_start(
                    out=bcb[0], in_=rows_d[8320 + tc * 512: 8320 + (tc + 1) * 512]
                        [None, :].to_broadcast([128, 512])),
                    inc=[("gsp", 16)], wait=rd)
                for m in range(16):
                    x2w = S_.op("vector", lambda e, m=m, tc=tc:
                                e.scalar_tensor_tensor(
                        x2t[m][:, tc * 512:(tc + 1) * 512],
                        hid[m][:, tc * 512:(tc + 1) * 512], vcol("l2", m),
                        bcb[0], ALU.mult, ALU.mult), inc=[("v", 1)],
                        wait=rb if m == 0 else ())
            for m in range(16):
                S_.op("gpsimd", lambda e, m=m: e.dma_start(
                    out=hidsp_d[m * 128:(m + 1) * 128, :], in_=hid[m]),
                    inc=[(f"gc{m % 2}", 16)], wait=S_.aft("v"))
        dbg_tap("x2", [((m * 128, (m + 1) * 128), x2t[m]) for m in range(16)])
        de_sc.close()
        e_sc.close()
        S_.barrier()

        g_sc = ExitStack()
        gent = g_sc.enter_context
        acc = [gent(nc.sbuf_tensor(f"acc{i}", [128, SLAB], F32)).ap()
               for i in range(16)]
        w1bf = [gent(nc.sbuf_tensor(f"w1bf{i}", [128, 1408], BF16)).ap()
                for i in range(16)]
        w1st = [gent(nc.sbuf_tensor(f"w1st{i}", [128, 1408], F32)).ap()
                for i in range(2)]
        gsb2 = [gent(nc.sbuf_tensor(f"gsb2{i}", [128, SLAB], BF16)).ap()
                for i in range(22)]
        ab2 = gsb2
        w2bf = [gent(nc.sbuf_tensor(f"w2bf{i}", [128, SLAB], BF16)).ap()
                for i in range(22)]
        w2st = [gent(nc.sbuf_tensor(f"w2st{i}", [128, SLAB], F32)).ap()
                for i in range(2)]
        Wb = [gent(nc.sbuf_tensor(f"Wb{i}", [128, SLAB], F32)).ap()
              for i in range(2)]
        gwb = gent(nc.sbuf_tensor("gwb", [128, 256], BF16)).ap()
        gwst = gent(nc.sbuf_tensor("gwst", [128, 256], F32)).ap()
        lbuf = gent(nc.sbuf_tensor("lbuf", [128, 16], F32)).ap()
        ebuf = gent(nc.sbuf_tensor("ebuf", [128, 16], F32)).ap()
        curb = gent(nc.sbuf_tensor("curb", [128, 16], F32)).ap()
        selb = gent(nc.sbuf_tensor("selb", [128, 16], F32)).ap()
        eqb = gent(nc.sbuf_tensor("eqb", [128, 16], F32)).ap()
        mxc = gent(nc.sbuf_tensor("mxc", [128, 1], F32)).ap()
        wfp = gent(nc.sbuf_tensor("wfp", [128, 128], BF16)).ap()
        wfT = gent(nc.sbuf_tensor("wfT", [16, 1024], F32)).ap()
        hrl = [gent(nc.sbuf_tensor(f"hrl{i}", [128, SLAB], F32)).ap()
               for i in range(2)]
        orow = gent(nc.sbuf_tensor("orow", [1, 512], F32)).ap()
        orow2 = gent(nc.sbuf_tensor("orow2", [1, 512], F32)).ap()
        obuf = [w1st[0][:, 512:1024], w1st[1][:, 512:1024]]
        hr2 = Wb
        sqg = [w1st[0][:, 0:512], w1st[1][:, 0:512]]
        if _BUILD_OPTS["moe"]:
            gw16 = gwst
            glw = []
            for k in range(16):
                glw += S_.op("sync", lambda e, k=k: e.dma_start(
                    out=gw16[:, k * 16:(k + 1) * 16],
                    in_=gwT_d[k * 128:(k + 1) * 128, :]),
                    inc=[("sw0", 16)], wait=S_.aft("v", "a") if k == 0 else ())
            gcst = S_.op("vector", lambda e: e.tensor_copy(gwb, gw16),
                         inc=[("v", 1)], wait=glw[-1:])
            S_.op("vector", lambda e: e.memset(wfp, 0.0), inc=[("v", 1)])

            prevr = {}
            for ti in range(8):
                rmm = []
                for k in range(16):
                    rmm = S_.op("tensor", lambda e, ti=ti, k=k: e.matmul(
                        pb[4][:, 0:16], x2t[k][:, ti * 128:(ti + 1) * 128],
                        gwb[:, k * 16:(k + 1) * 16],
                        start=(k == 0), stop=(k == 15)),
                        inc=[("t", 1)] if k == 15 else (),
                        wait=(gcst + S_.aft("v", "a")) if (ti == 0 and k == 0)
                        else (prevr.get("lb", []) if k == 0 else ()))
                lb = S_.op("vector", lambda e: e.tensor_copy(lbuf, pb[4][:, 0:16]),
                           inc=[("v", 1)], wait=rmm)
                prevr["lb"] = lb
                ng = S_.op("vector", lambda e: e.reduce_max(
                    mxc, lbuf, axis=AX, negate=True), inc=[("v", 1)], wait=())
                S_.op("scalar", lambda e: e.activation(
                    ebuf, lbuf, AF.Exp, bias=mxc), inc=[("a", 1)], wait=ng)
                S_.op("vector", lambda e: e.tensor_copy(curb, lbuf),
                      inc=[("v", 1)], wait=())
                S_.op("vector", lambda e: e.memset(selb, 0.0), inc=[("v", 1)])
                for it in range(4):
                    S_.op("vector", lambda e: e.reduce_max(mxc, curb, axis=AX),
                          inc=[("v", 1)], wait=S_.aft("a") if it == 0 else ())
                    S_.op("vector", lambda e: e.tensor_scalar(
                        eqb, curb, mxc, None, ALU.is_equal), inc=[("v", 1)], wait=())
                    S_.op("vector", lambda e: e.tensor_add(selb, selb, eqb),
                          inc=[("v", 1)], wait=())
                    if it < 3:
                        S_.op("vector", lambda e: e.scalar_tensor_tensor(
                            curb, eqb, -1.0e30, curb, ALU.mult, ALU.add),
                            inc=[("v", 1)], wait=())
                S_.op("vector", lambda e: e.tensor_scalar(
                    selb, selb, 1.0, None, ALU.min), inc=[("v", 1)], wait=())
                S_.op("vector", lambda e: e.tensor_mul(eqb, ebuf, selb),
                      inc=[("v", 1)], wait=S_.aft("a"))
                S_.op("vector", lambda e: e.tensor_reduce(
                    mxc, eqb, AX, ALU.add), inc=[("v", 1)], wait=())
                S_.op("vector", lambda e: e.reciprocal(mxc, mxc),
                      inc=[("v", 1)], wait=())
                wf = S_.op("vector", lambda e: e.tensor_scalar(
                    wfp[:, 0:16], eqb, mxc, None, ALU.mult),
                    inc=[("v", 1)], wait=prevr.get("tt", []))
                ttk = S_.op("tensor", lambda e: e.transpose(
                    pbf[5][:, 0:128], wfp, idb), inc=[("t", 1)],
                    wait=wf + prevr.get("wfev", []))
                prevr["tt"] = ttk
                prevr["wfev"] = S_.op("vector", lambda e, ti=ti: e.tensor_copy(
                    wfT[:, ti * 128:(ti + 1) * 128], pbf[5][0:16, 0:128]),
                    inc=[("v", 1)], wait=ttk)
            wsp = S_.op("gpsimd", lambda e: e.dma_start(
                out=wfsp_d.rearrange("(a b) -> a b", a=16), in_=wfT),
                inc=[("gsp", 16)], wait=prevr["wfev"])
            dbg_tap("wf", [((0, 16), wfsp_d.rearrange("(a b) -> a b", a=16))])

            ccache = {}
            for tc in range(2):
                if tc == 1:
                    S_.barrier()
                prevg = {}
                first_acc = [True] * 16
                for ep in range(9):
                    shared = (ep == 8)
                    for ei in range(2):
                        eg = ep * 2 + ei
                        if not shared:
                            prevg[("wb", ei)] = S_.op(
                                "gpsimd", lambda e, eg=eg, ei=ei, tc=tc: e.dma_start(
                                    out=Wb[ei],
                                    in_=wfsp_d[eg * 1024 + tc * 512:
                                               eg * 1024 + (tc + 1) * 512]
                                    [None, :].to_broadcast([128, 512])),
                                inc=[(f"wb{ei}", 16)],
                                wait=wsp + prevg.get(("wbuse", ei), []))
                        for ih in range(2):
                            q4 = ei * 2 + ih
                            is_g = (q4 < 2) if shared else (ih == 0)
                            castk = {}
                            for kb in range(16):
                                if shared:
                                    wsrc = shgT_d[kb * 128:(kb + 1) * 128,
                                                  q4 * 1408:(q4 + 1) * 1408]
                                    wdst = shgc_d[kb * 128:(kb + 1) * 128,
                                                  q4 * 1408:(q4 + 1) * 1408]
                                else:
                                    wsrc = w1T_d[eg, kb * 128:(kb + 1) * 128,
                                                 ih * 1408:(ih + 1) * 1408]
                                    wdst = w1c_d[eg, kb * 128:(kb + 1) * 128,
                                                 ih * 1408:(ih + 1) * 1408]
                                if tc == 0:
                                    lw = S_.op("sync", lambda e, wsrc=wsrc, kb=kb:
                                               e.dma_start(
                                        out=w1st[kb % 2], in_=wsrc),
                                        inc=[(f"sw{kb % 2}", 16)],
                                        wait=prevg.get(("w1c", kb % 2),
                                                       S_.aft("v", "a", "t",
                                                              "gc0", "gc1")))
                                    ceng = "vector" if kb % 2 == 0 else "scalar"
                                    cop = ((lambda e, kb=kb: e.tensor_copy(
                                        w1bf[kb], w1st[kb % 2]))
                                        if ceng == "vector" else
                                        (lambda e, kb=kb: e.activation(
                                            w1bf[kb], w1st[kb % 2], AF.Copy)))
                                    ck = S_.op(ceng, cop,
                                               wait=list(lw)
                                               + prevg.get(("w1u", kb), [])
                                               + (S_.aft("pc0", "pc1")
                                                  if kb < 2 else []))
                                    prevg[("w1c", kb % 2)] = ck
                                    S_.op("gpsimd", lambda e, kb=kb, wdst=wdst:
                                          e.dma_start(out=wdst, in_=w1bf[kb]),
                                          inc=[(f"pc{kb % 2}", 16)], wait=ck)
                                else:
                                    ck = S_.op("sync", lambda e, kb=kb, wdst=wdst:
                                               e.dma_start(
                                        out=w1bf[kb], in_=wdst),
                                        inc=[(f"sw{kb % 2}", 16)],
                                        wait=prevg.get(("w1u", kb),
                                                       S_.aft("v", "a", "t")))
                                    prevg[("w1c", kb % 2)] = ck
                                castk[kb] = ck
                            for wv, irng in ((0, range(0, 5)), (1, range(5, 11))):
                                mm_tok = {}
                                for kb in range(16):
                                    for i in irng:
                                        bnk = i if wv == 0 else i - 5
                                        w = []
                                        if i == irng.start:
                                            if wv == 0:
                                                if tc == 0:
                                                    w += castk[kb]
                                                elif kb == 0:
                                                    w += (castk[14]
                                                          + castk[15])
                                            if kb == 0:
                                                for ii in irng:
                                                    b2 = ii if wv == 0 else ii - 5
                                                    w += prevg.get(
                                                        ("bev", b2),
                                                        S_.aft("v", "a"))
                                        gm = S_.op("tensor", lambda e, i=i, kb=kb,
                                                   tc=tc, bnk=bnk: e.matmul(
                                            pb[bnk][:, 0:SLAB],
                                            w1bf[kb][:, i * 128:(i + 1) * 128],
                                            x2t[kb][:, tc * 512:(tc + 1) * 512],
                                            start=(kb == 0), stop=(kb == 15)),
                                            inc=[("t", 1)]
                                            if (kb == 15 or (wv == 1 and i == 10))
                                            else (), wait=w)
                                        if kb == 15:
                                            mm_tok[i] = gm
                                        if wv == 1 and i == 10:
                                            prevg[("w1u", kb)] = gm
                                for i in irng:
                                    bnk = i if wv == 0 else i - 5
                                    idx2 = ((q4 % 2) if shared else ei) * 11 + i
                                    if is_g:
                                        ev = S_.op("scalar", lambda e, idx2=idx2,
                                                   bnk=bnk: e.activation(
                                            gsb2[idx2], pb[bnk][:, 0:SLAB],
                                            AF.Silu), inc=[("a", 1)],
                                            wait=mm_tok[i]
                                            + prevg.get(("gsb", idx2), []))
                                        prevg[("gtok", idx2)] = ev
                                    else:
                                        if not shared:
                                            ev = S_.op("vector", lambda e,
                                                       idx2=idx2, ei=ei:
                                                       e.tensor_mul(
                                                gsb2[idx2], gsb2[idx2], Wb[ei]),
                                                inc=[("v", 1)],
                                                wait=prevg.get(("wb", ei), [])
                                                + prevg.get(("gtok", idx2), []))
                                            prevg[("wbuse", ei)] = ev
                                            gw_ = ()
                                        else:
                                            gw_ = prevg.get(("gtok", idx2), [])
                                        ev = S_.op("vector", lambda e, idx2=idx2,
                                                   bnk=bnk: e.tensor_mul(
                                            ab2[idx2], gsb2[idx2],
                                            pb[bnk][:, 0:SLAB]),
                                            inc=[("v", 1)],
                                            wait=mm_tok[i] + list(gw_))
                                        prevg[("gsb", idx2)] = ev
                                    prevg[("bev", bnk)] = ev
                    for hq in range(4):
                        castj = {}
                        for j in range(22):
                            if shared:
                                wsrc = shdT_d[j * 128:(j + 1) * 128,
                                              hq * 512:(hq + 1) * 512]
                                wdst = shdc_d[j * 128:(j + 1) * 128,
                                              hq * 512:(hq + 1) * 512]
                            else:
                                e2 = ep * 2 + j // 11
                                kb2 = j % 11
                                wsrc = w2T_d[e2, kb2 * 128:(kb2 + 1) * 128,
                                             hq * 512:(hq + 1) * 512]
                                wdst = w2c_d[e2, kb2 * 128:(kb2 + 1) * 128,
                                             hq * 512:(hq + 1) * 512]
                            if tc == 0:
                                lw = S_.op("sync", lambda e, wsrc=wsrc, j=j:
                                           e.dma_start(
                                    out=w2st[j % 2], in_=wsrc),
                                    inc=[(f"sv{j % 2}", 16)],
                                    wait=prevg.get(("w2l", j % 2),
                                                   S_.aft("v", "a", "t",
                                                          "gc0", "gc1")))
                                ceng = "vector" if j % 2 == 0 else "scalar"
                                cop = ((lambda e, j=j: e.tensor_copy(
                                    w2bf[j], w2st[j % 2]))
                                    if ceng == "vector" else
                                    (lambda e, j=j: e.activation(
                                        w2bf[j], w2st[j % 2], AF.Copy)))
                                ck = S_.op(ceng, cop,
                                           wait=list(lw)
                                           + prevg.get(("w2u", j), [])
                                           + (S_.aft("pc0", "pc1")
                                              if j < 2 else []))
                                prevg[("w2l", j % 2)] = ck
                                S_.op("gpsimd", lambda e, j=j, wdst=wdst:
                                      e.dma_start(out=wdst, in_=w2bf[j]),
                                      inc=[(f"pc{j % 2}", 16)], wait=ck)
                            else:
                                ck = S_.op("sync", lambda e, j=j, wdst=wdst:
                                           e.dma_start(
                                    out=w2bf[j], in_=wdst),
                                    inc=[(f"sv{j % 2}", 16)],
                                    wait=prevg.get(("w2u", j),
                                                   S_.aft("v", "a", "t")))
                                prevg[("w2l", j % 2)] = ck
                            castj[j] = ck
                        dm_tok = {}
                        for j in range(22):
                            for hc in range(4):
                                w = []
                                if hc == 0:
                                    if tc == 0:
                                        w += castj[j]
                                    elif j == 0:
                                        w += castj[20] + castj[21]
                                    if hq == 0:
                                        w += prevg.get(("gsb", j), [])
                                    if j == 0:
                                        for b2 in range(4, 8):
                                            w += prevg.get(("bev", b2),
                                                           S_.aft("v", "a"))
                                dm = S_.op("tensor", lambda e, j=j, hc=hc:
                                           e.matmul(
                                    pb[4 + hc][:, 0:SLAB],
                                    w2bf[j][:, hc * 128:(hc + 1) * 128],
                                    ab2[j], start=(j == 0), stop=(j == 21)),
                                    inc=[("t", 1)] if (hc == 3 or j == 21)
                                    else (), wait=w)
                                if hc == 3:
                                    prevg[("w2u", j)] = dm
                                if j == 21:
                                    dm_tok[hc] = dm
                        for hc in range(4):
                            h = hq * 4 + hc
                            if first_acc[h]:
                                ev = S_.op("vector", lambda e, h=h, hc=hc:
                                           e.tensor_copy(
                                    acc[h], pb[4 + hc][:, 0:SLAB]),
                                    inc=[("v", 1)], wait=dm_tok[hc])
                                first_acc[h] = False
                            else:
                                ev = S_.op("vector", lambda e, h=h, hc=hc:
                                           e.tensor_add(
                                    acc[h], acc[h], pb[4 + hc][:, 0:SLAB]),
                                    inc=[("v", 1)], wait=dm_tok[hc])
                            prevg[("bev", 4 + hc)] = ev
                prevf = {}
                smt = smt2 = []
                for m in range(16):
                    hl = S_.op("sync", lambda e, m=m, tc=tc: e.dma_start(
                        out=hrl[m % 2],
                        in_=hidsp_d[m * 128:(m + 1) * 128, tc * 512:(tc + 1) * 512]),
                        inc=[(f"sx{m % 2}", 16)],
                        wait=prevf.get(("h", m % 2), S_.aft("v")))
                    xl2 = S_.op("sync", lambda e, m=m, tc=tc: e.dma_start(
                        out=hr2[m % 2],
                        in_=x1own_d[m * 128:(m + 1) * 128, tc * 512:(tc + 1) * 512]),
                        inc=[(f"sv{m % 2}", 16)],
                        wait=prevf.get(("x", m % 2),
                                       prevg.get(("w2l", m % 2), ())))
                    a1 = S_.op("vector", lambda e, m=m: e.tensor_add(
                        acc[m], acc[m], hrl[m % 2]), inc=[("v", 1)],
                        wait=list(hl) + list(xl2))
                    a1 = S_.op("vector", lambda e, m=m: e.tensor_add(
                        acc[m], acc[m], hr2[m % 2]), inc=[("v", 1)], wait=())
                    prevf[("h", m % 2)] = a1
                    prevf[("x", m % 2)] = a1
                    sq = S_.op("scalar", lambda e, m=m: e.activation(
                        sqg[m % 2], acc[m], AF.Square), inc=[("a", 1)],
                        wait=a1 + prevf.get(("sq", m % 2), []))
                    smt = S_.op("tensor", lambda e, m=m: e.matmul(
                        pb[6][0:1, 0:SLAB], vcol("ones"), acc[m],
                        start=(m == 0), stop=(m == 15)),
                        inc=[("t", 1)], wait=a1)
                    smt2 = S_.op("tensor", lambda e, m=m: e.matmul(
                        pb[7][0:1, 0:SLAB], vcol("ones"), sqg[m % 2],
                        start=(m == 0), stop=(m == 15)),
                        inc=[("t", 1)], wait=sq)
                    prevf[("sq", m % 2)] = smt2
                rv = S_.op("vector", lambda e: e.tensor_scalar(
                    rowb, pb[6][0:1, 0:SLAB], 1.0 / 2048.0, None, ALU.mult),
                    inc=[("v", 1)], wait=smt + smt2)
                rv2 = S_.op("vector", lambda e: e.tensor_scalar(
                    rowb2, pb[7][0:1, 0:SLAB], 1.0 / 2048.0, None, ALU.mult),
                    inc=[("v", 1)], wait=())
                sq3 = S_.op("scalar", lambda e: e.activation(
                    orow, rowb, AF.Square), inc=[("a", 1)], wait=rv)
                vv = S_.op("vector", lambda e: e.tensor_sub(
                    rowb2, rowb2, orow), inc=[("v", 1)], wait=sq3)
                s1 = S_.op("scalar", lambda e: e.activation(
                    orow, rowb2, AF.Sqrt, bias=vcol("epsA")[0:1, :]),
                    inc=[("a", 1)], wait=vv)
                r1 = S_.op("vector", lambda e: e.reciprocal(orow, orow),
                           inc=[("v", 1)], wait=s1)
                rd = S_.op("gpsimd", lambda e, tc=tc: e.dma_start(
                    out=rows_d[9344 + tc * 512: 9344 + (tc + 1) * 512].unsqueeze(0),
                    in_=rowb), inc=[("gsp", 16)], wait=rv)
                rd2 = S_.op("gpsimd", lambda e, tc=tc: e.dma_start(
                    out=rows_d[10368 + tc * 512: 10368 + (tc + 1) * 512].unsqueeze(0),
                    in_=orow), inc=[("gsp", 16)], wait=r1)
                b1 = S_.op("gpsimd", lambda e, tc=tc: e.dma_start(
                    out=bcb[0], in_=rows_d[9344 + tc * 512: 9344 + (tc + 1) * 512]
                        [None, :].to_broadcast([128, 512])),
                    inc=[("gsp", 16)], wait=S_.aft("gsp") + S_.aft("v"))
                b2 = S_.op("gpsimd", lambda e, tc=tc: e.dma_start(
                    out=bcb[1], in_=rows_d[10368 + tc * 512: 10368 + (tc + 1) * 512]
                        [None, :].to_broadcast([128, 512])),
                    inc=[("gsp", 16)], wait=())
                for m in range(16):
                    o1 = S_.op("vector", lambda e, m=m: e.tensor_sub(
                        obuf[m % 2], acc[m], bcb[0]), inc=[("v", 1)],
                        wait=(list(b1) + list(b2) if m == 0 else [])
                        + prevf.get(("ob", m % 2), []))
                    o1 = S_.op("vector", lambda e, m=m: e.tensor_mul(
                        obuf[m % 2], obuf[m % 2], bcb[1]), inc=[("v", 1)], wait=())
                    prevf[("ob", m % 2)] = S_.op(
                        "gpsimd", lambda e, m=m, tc=tc: e.dma_start(
                            out=out_d[m * 128:(m + 1) * 128,
                                      tc * 512:(tc + 1) * 512],
                            in_=obuf[m % 2]), inc=[(f"gc{m % 2}", 16)], wait=o1)
        g_sc.close()
        x2_sc.close()
        S_.wait("gpsimd", *S_.aft("gc0", "gc1", "st"))
        S_.barrier()

        with nc.Block() as block:
            @block.sync
            def _(e):
                for fn in S_.q["sync"]:
                    fn(e)

            @block.tensor
            def _(e):
                for fn in S_.q["tensor"]:
                    fn(e)

            @block.vector
            def _(e):
                for fn in S_.q["vector"]:
                    fn(e)

            @block.scalar
            def _(e):
                for fn in S_.q["scalar"]:
                    fn(e)

            @block.gpsimd
            def _(e):
                for fn in S_.q["gpsimd"]:
                    fn(e)
    return nc


_CACHE = {}


def kernel(**inputs):
    in_maps, idx = _host_prep(inputs)
    key = ("main", _BUILD_OPTS["dbg"], _BUILD_OPTS["nslab"], _BUILD_OPTS["ssd"], _BUILD_OPTS["xg"], _BUILD_OPTS["op"], _BUILD_OPTS["inproj"], _BUILD_OPTS["cache"], _BUILD_OPTS["nmg"], _BUILD_OPTS["heads"], _BUILD_OPTS["tpose"])
    if key not in _CACHE:
        _CACHE[key] = _build(idx)
    nc = _CACHE[key]
    res = run_bass_kernel_spmd(nc, in_maps, core_ids=list(range(8)))
    full = np.zeros((4, S, HID), np.float32)
    for c in range(8):
        b, r = c // 2, c % 2
        full[b, r * 1024:(r + 1) * 1024, :] = np.asarray(res.results[c]["out"]).T
    if _BUILD_OPTS["dbg"] is not None:
        kernel.dbg = [np.asarray(res.results[c].get("dbg")) for c in range(8)]
    return full


# version 10 (final)
# speedup vs baseline: 1.1202x; 1.0167x over previous
import numpy as np
import ml_dtypes
import concourse.bass as bass
import concourse.mybir as mybir
from concourse.bass_utils import run_bass_kernel_spmd

F32 = mybir.dt.float32
BF16 = mybir.dt.bfloat16
AF = mybir.ActivationFunctionType
ALU = mybir.AluOpType
AX = mybir.AxisListType.X

HID = 2048
S = 2048
LC = 256
NSLAB = 4
SLAB = 512
INTER = 4096
PROJ_P = 8576
NMC = 67
EPS_A = 1e-5
EPS_F = 1e-6
NHEAD = 16
SCALE = float(192.0 ** -0.5)
NE = 16
DI = 1408
SHI = 2816
BIG_NEG = -1.0e9

_BUILD_OPTS = {"dbg": None, "nslab": NSLAB, "ssd": True, "xg": True, "op": True, "inproj": True, "cache": True, "nmg": NMC, "heads": 64, "tpose": 3, "xl": True, "kv": True, "att": True, "moe": True}


class Sched:
    ENGINES = ("sync", "tensor", "vector", "scalar", "gpsimd")

    def __init__(self):
        self.q = {e: [] for e in self.ENGINES}
        self.sems = {}

    def add_sem(self, name, handle):
        self.sems[name] = [handle, 0]

    def n(self, name):
        return self.sems[name][1]

    AUTO_CHAIN = {"vector": "v", "scalar": "a"}

    LABELS = {}

    def op(self, eng, build, inc=(), wait=(), chain=True, self_inc=True, label=None):
        wait = list(wait)
        sem_self = self.AUTO_CHAIN.get(eng)
        if sem_self is not None:
            if chain and self.sems[sem_self][1] > 0:
                wait.append((sem_self, self.sems[sem_self][1]))
            if self_inc and not any(nm == sem_self for nm, _ in inc):
                inc = list(inc) + [(sem_self, 1)]
        toks = []
        for nm, amt in inc:
            self.sems[nm][1] += amt
            toks.append((nm, self.sems[nm][1]))
        wl = {}
        for nm, val in wait:
            if val > 0:
                wl[nm] = max(wl.get(nm, 0), val)
        wl = [(self.sems[nm][0], val) for nm, val in wl.items()]
        il = [(self.sems[nm][0], amt) for nm, amt in inc]

        qpos = (eng, len(self.q[eng]))

        def fn(e, build=build, wl=wl, il=il, label=label, qpos=qpos):
            for h, v in wl:
                e.wait_ge(h, v)
            inst = build(e)
            if inst is not None:
                try:
                    Sched.LABELS[inst.ins.name] = label or qpos
                except AttributeError:
                    if label is not None:
                        Sched.LABELS[str(inst.concise())[:40]] = label
            for h, amt in il:
                if inst is None:
                    e.nop().then_inc(h, amt)
                else:
                    inst = inst.then_inc(h, amt)
        self.q[eng].append(fn)
        return toks

    def wait(self, eng, *tokens):
        wl = [(self.sems[nm][0], val) for nm, val in tokens if val > 0]
        if not wl:
            return

        def fn(e, wl=wl):
            for h, v in wl:
                e.wait_ge(h, v)
        self.q[eng].append(fn)

    def aft(self, *names):
        return [(nm, self.n(nm)) for nm in names]

    ENG_DRAIN = {"sync": ("sx0", "sx1", "sw0", "sw1", "sv0", "sv1"),
                 "tensor": ("t",),
                 "vector": ("v",), "scalar": ("a",),
                 "gpsimd": ("gsp", "gbc", "gb0", "gb1", "gc0", "gc1", "pc0",
                             "pc1", "wb0", "wb1", "st", "p")}

    def barrier(self, engines=None):
        engines = engines or list(self.ENGINES)
        toks = []
        for e in engines:
            toks += self.op(e, lambda en: en.nop(), inc=[("bar", 1)],
                            wait=self.aft(*self.ENG_DRAIN[e]), chain=True,
                            self_inc=False)
        for e in engines:
            self.wait(e, *toks[-1:])


def _host_prep(inp):
    x = np.asarray(inp["L0_attn_hidden_states"], np.float32)
    ipw = np.asarray(inp["L0_attn_in_proj_w"], np.float32)
    ipb = np.asarray(inp["L0_attn_in_proj_b"], np.float32)
    convw = np.asarray(inp["L0_attn_conv_w"], np.float32)
    dtb = np.asarray(inp["L0_attn_dt_bias"], np.float32)
    alog = np.asarray(inp["L0_attn_A_log"], np.float32)
    Dv = np.asarray(inp["L0_attn_D"], np.float32)
    nw = np.asarray(inp["L0_attn_norm_weight"], np.float32)
    opw = np.asarray(inp["L0_attn_out_proj_w"], np.float32)
    opb = np.asarray(inp["L0_attn_out_proj_b"], np.float32)
    ln1 = np.asarray(inp["L0_ffn_ln1_w"], np.float32)
    ln2 = np.asarray(inp["L0_ffn_ln2_w"], np.float32)
    qw = np.asarray(inp["L0_ffn_q_w"], np.float32)
    kvw = np.asarray(inp["L0_ffn_kv_w"], np.float32)
    ow = np.asarray(inp["L0_ffn_o_w"], np.float32)
    gw = np.asarray(inp["L0_ffn_gate_w"], np.float32)
    w1 = np.asarray(inp["L0_ffn_w1"], np.float32)
    w2 = np.asarray(inp["L0_ffn_w2"], np.float32)
    shg = np.asarray(inp["L0_ffn_sh_gu_w"], np.float32)
    shd = np.asarray(inp["L0_ffn_sh_down_w"], np.float32)

    ipw_p = np.zeros((HID, PROJ_P), np.float32)
    ipw_p[:, :8512] = ipw
    ipb_p = np.zeros(PROJ_P, np.float32)
    ipb_p[:8512] = ipb

    q_live = qw.reshape(NHEAD, 192, HID)[:, :128, :].reshape(NHEAD * 128, HID)
    qwT = np.ascontiguousarray(q_live.T)
    kwT = np.ascontiguousarray(kvw[:2048].T)
    vwT = np.ascontiguousarray(kvw[2048:4096].T)
    owT = np.ascontiguousarray(ow.T)
    gwT = np.ascontiguousarray(gw.T)
    w1T = np.ascontiguousarray(np.transpose(w1, (0, 2, 1)))
    w2T = np.ascontiguousarray(np.transpose(w2, (0, 2, 1)))
    shgT = np.ascontiguousarray(shg.T)
    shdT = np.ascontiguousarray(shd.T)

    def cols_of(vec, n):
        v = np.zeros(n * 128, np.float32)
        v[:len(vec)] = vec
        return [v[i * 128:(i + 1) * 128] for i in range(n)]

    cols = []
    idx = {}

    def add(name, lst):
        idx[name] = len(cols)
        cols.extend(lst)

    add("inb", cols_of(ipb_p, NMC))
    conv_cols = []
    for mc in range(34):
        for k in range(4):
            conv_cols.append(np.ascontiguousarray(convw[mc * 128:(mc + 1) * 128, k]))
    add("convw", conv_cols)
    add("dtb", cols_of(dtb, 1))
    add("alog", cols_of(alog, 1))
    add("D", cols_of(np.repeat(Dv, 64), 32))
    add("nw", cols_of(nw, 32))
    add("ob", cols_of(opb, 16))
    add("l1", cols_of(ln1, 16))
    add("l2", cols_of(ln2, 16))
    add("ones", [np.ones(128, np.float32)])
    add("epsA", [np.full(128, EPS_A, np.float32)])
    add("epsF", [np.full(128, EPS_F, np.float32)])
    idx["m0"] = len(cols)
    idx["m1"] = len(cols) + 1
    idx["_nv"] = len(cols) + 2
    vecs_common = np.stack(cols, axis=1)

    ssdm = np.zeros((2, 128, 256), np.float32)
    for si in range(2):
        for p in range(128):
            ssdm[si, p, :si * 128 + p] = BIG_NEG
    ssdm_flat = np.ascontiguousarray(ssdm.reshape(256, 256))

    id_f32 = np.eye(128, dtype=np.float32)
    id_bf = np.eye(128, dtype=np.float32).astype(ml_dtypes.bfloat16)

    in_maps = []
    for c in range(8):
        b, r = c // 2, c % 2
        m0c = np.full((128, 1), 1.0 - r, np.float32)
        m1c = np.full((128, 1), float(r), np.float32)
        vecs = np.concatenate([vecs_common, m0c, m1c], axis=1)
        kt = np.arange(S)[:, None]
        qt = np.arange(1024)[None, :] + r * 1024
        m01 = (kt <= qt).astype(np.float32).astype(ml_dtypes.bfloat16)
        in_maps.append({
            "x0T": np.ascontiguousarray(x[b].T), "vecs": np.ascontiguousarray(vecs),
            "ssdm": ssdm_flat, "id_f32": id_f32, "id_bf": id_bf,
            "m01": np.ascontiguousarray(m01),
            "ipw": ipw_p, "opw": opw,
            "qwT": qwT, "kwT": kwT, "vwT": vwT, "owT": owT, "gwT": gwT,
            "w1T": w1T, "w2T": w2T, "shgT": shgT, "shdT": shdT,
        })
    return in_maps, idx


def _build(idx):
    dbg = _BUILD_OPTS["dbg"]
    nc = bass.Bass("TRN2", target_bir_lowering=False, debug=False)
    S_ = Sched()

    x0T = nc.dram_tensor("x0T", [HID, S], F32, kind="ExternalInput").ap()
    vecs_d = nc.dram_tensor("vecs", [128, idx["_nv"]], F32, kind="ExternalInput").ap()
    ssdm_d = nc.dram_tensor("ssdm", [256, 256], F32, kind="ExternalInput").ap()
    idf_d = nc.dram_tensor("id_f32", [128, 128], F32, kind="ExternalInput").ap()
    idb_d = nc.dram_tensor("id_bf", [128, 128], BF16, kind="ExternalInput").ap()
    m01_d = nc.dram_tensor("m01", [S, 1024], BF16, kind="ExternalInput").ap()
    ipw_d = nc.dram_tensor("ipw", [HID, PROJ_P], F32, kind="ExternalInput").ap()
    opw_d = nc.dram_tensor("opw", [INTER, HID], F32, kind="ExternalInput").ap()
    qwT_d = nc.dram_tensor("qwT", [HID, 2048], F32, kind="ExternalInput").ap()
    kwT_d = nc.dram_tensor("kwT", [HID, 2048], F32, kind="ExternalInput").ap()
    vwT_d = nc.dram_tensor("vwT", [HID, 2048], F32, kind="ExternalInput").ap()
    owT_d = nc.dram_tensor("owT", [2048, 2048], F32, kind="ExternalInput").ap()
    gwT_d = nc.dram_tensor("gwT", [HID, 16], F32, kind="ExternalInput").ap()
    w1T_d = nc.dram_tensor("w1T", [NE, HID, 2816], F32, kind="ExternalInput").ap()
    w2T_d = nc.dram_tensor("w2T", [NE, DI, 2048], F32, kind="ExternalInput").ap()
    shgT_d = nc.dram_tensor("shgT", [HID, 2 * SHI], F32, kind="ExternalInput").ap()
    shdT_d = nc.dram_tensor("shdT", [SHI, HID], F32, kind="ExternalInput").ap()
    out_d = nc.dram_tensor("out", [HID, 1024], F32, kind="ExternalOutput").ap()

    gsil_unused = None
    ipwbf_d = nc.dram_tensor("ipwbf", [HID, PROJ_P], BF16).ap()
    cumsp_d = nc.dram_tensor("cumsp", [8 * 64 * 256], F32).ap()
    cssp_d = nc.dram_tensor("cssp", [8 * 64], F32).ap()
    prex1_d = nc.dram_tensor("prex1", [HID, S], F32).ap()
    x1sp_d = nc.dram_tensor("x1sp", [HID, S], F32).ap()
    rows_d = nc.dram_tensor("rows", [64 * 1024], F32).ap()
    kspill_d = nc.dram_tensor("kspill", [2048, S], BF16).ap()
    vspill_d = nc.dram_tensor("vspill", [S, 2048], BF16).ap()
    x1own_d = nc.dram_tensor("x1own", [HID, 1024], F32).ap()
    hidsp_d = nc.dram_tensor("hidsp", [HID, 1024], F32).ap()
    wfsp_d = nc.dram_tensor("wfsp", [NE * 1024], F32).ap()
    w1c_d = nc.dram_tensor("w1c", [NE, HID, 2816], BF16).ap()
    w2c_d = nc.dram_tensor("w2c", [NE, DI, 2048], BF16).ap()
    shgc_d = nc.dram_tensor("shgc", [HID, 2 * SHI], BF16).ap()
    shdc_d = nc.dram_tensor("shdc", [SHI, HID], BF16).ap()
    dbg_d = None
    if dbg is not None:
        dbg_d = nc.dram_tensor("dbg", list(dbg[1]), F32, kind="ExternalOutput").ap()

    from contextlib import ExitStack
    with ExitStack() as stack:
        ent = stack.enter_context
        for snm in ("sx0", "sx1", "sw0", "sw1", "sv0", "sv1", "gsp", "gbc",
                    "gb0", "gb1", "gc0", "gc1", "pc0", "pc1",
                    "wb0", "wb1", "t", "v", "a", "p", "st", "bar"):
            S_.add_sem(snm, ent(nc.semaphore(snm)))

        vecs = ent(nc.sbuf_tensor("vecs_s", [128, idx["_nv"]], F32)).ap()
        idf = ent(nc.sbuf_tensor("idf", [128, 128], F32)).ap()
        idb = ent(nc.sbuf_tensor("idb", [128, 128], BF16)).ap()

        onesb = ent(nc.sbuf_tensor("onesb", [128, 1], BF16)).ap()
        expA = ent(nc.sbuf_tensor("expA", [128, 1], F32)).ap()
        pb = [ent(nc.psum_tensor(f"pb{i}", [128, 512], F32)).ap() for i in range(8)]
        rowb = ent(nc.sbuf_tensor("rowb", [1, 512], F32)).ap()
        rowb2 = ent(nc.sbuf_tensor("rowb2", [1, 512], F32)).ap()
        bcb = [ent(nc.sbuf_tensor(f"bcb{i}", [128, 512], F32)).ap() for i in range(2)]
        mxk = ent(nc.sbuf_tensor("mxk", [1, 64], F32)).ap()
        pbf = [p.bitcast(BF16) for p in pb]

        def vcol(name, j=0):
            return vecs[:, idx[name] + j: idx[name] + j + 1]

        S_.op("sync", lambda e: e.dma_start(out=vecs, in_=vecs_d), inc=[("sx0", 16)])
        S_.op("sync", lambda e: e.dma_start(out=idf, in_=idf_d), inc=[("sx0", 16)])
        S_.op("sync", lambda e: e.dma_start(out=idb, in_=idb_d), inc=[("sx0", 16)])
        S_.op("sync", lambda e: e.dma_start(
            out=ssdm.rearrange("p (a c) -> p a c", a=2),
            in_=ssdm_d.rearrange("(a b) c -> b a c", a=2)),
            inc=[("sx0", 16)])
        pre_tok = S_.aft("sx0")
        S_.op("vector", lambda e: e.tensor_copy(onesb, vcol("ones")),
              inc=[("v", 1)], wait=pre_tok)
        S_.op("scalar", lambda e: e.activation(expA, vcol("alog"), AF.Exp),
              inc=[("a", 1)], wait=pre_tok)

        def dbg_tap(name, srcs):
            if dbg is None or dbg[0] != name:
                return
            S_.barrier()
            for (r0, r1), src in srcs:
                S_.op("gpsimd", lambda e, r0=r0, r1=r1, src=src: e.dma_start(
                    out=dbg_d[r0:r1, :src.shape[1]], in_=src), inc=[("st", 16)])
            S_.barrier()

        a_sc = ExitStack()
        aent = a_sc.enter_context
        ssdm = aent(nc.sbuf_tensor("ssdm_s", [128, 512], F32)).ap()
        x0bf = [aent(nc.sbuf_tensor(f"x0bf{i}", [128, SLAB], BF16)).ap() for i in range(16)]
        x0st = [aent(nc.sbuf_tensor(f"x0st{i}", [128, SLAB], F32)).ap() for i in range(2)]
        wbf = [aent(nc.sbuf_tensor(f"wbf{i}", [128, 2048], BF16)).ap() for i in range(2)]
        hbcb = [aent(nc.sbuf_tensor(f"hbcb{i}", [128, 515], F32)).ap() for i in range(2)]
        cvb = [aent(nc.sbuf_tensor(f"cvb{i}", [128, 512], F32)).ap() for i in range(2)]
        stash = aent(nc.sbuf_tensor("stash", [128, 102], F32)).ap()
        hs = [aent(nc.sbuf_tensor(f"hs{i}", [128, SLAB], BF16)).ap() for i in range(32)]
        Bsl = aent(nc.sbuf_tensor("Bsl", [128, SLAB], BF16)).ap()
        Csl = aent(nc.sbuf_tensor("Csl", [128, SLAB], BF16)).ap()
        dtsl = aent(nc.sbuf_tensor("dtsl", [128, SLAB], F32)).ap()
        dstt = aent(nc.sbuf_tensor("dstt", [128, SLAB], F32)).ap()
        abuf = aent(nc.sbuf_tensor("abuf", [64, 256], F32)).ap()
        cumc = aent(nc.sbuf_tensor("cumc", [128, 512], F32)).ap()
        dcay = aent(nc.sbuf_tensor("dcay", [128, 256], F32)).ap()
        ncT = aent(nc.sbuf_tensor("ncT", [128, 128], F32)).ap()
        ecT = aent(nc.sbuf_tensor("ecT", [128, 128], F32)).ap()
        dtT = aent(nc.sbuf_tensor("dtT", [128, 128], F32)).ap()
        dyT = aent(nc.sbuf_tensor("dyT", [128, 128], F32)).ap()
        Btm = aent(nc.sbuf_tensor("Btm", [128, 256], BF16)).ap()
        bdec = aent(nc.sbuf_tensor("bdec", [128, 512], BF16)).ap()
        Ee = aent(nc.sbuf_tensor("Ee", [128, 64], F32)).ap()
        Sst = aent(nc.sbuf_tensor("Sst", [128, 4096], F32)).ap()
        Sbf = aent(nc.sbuf_tensor("Sbf", [128, 4096], BF16)).ap()
        RB = aent(nc.sbuf_tensor("RB", [128, 1024], F32)).ap()
        tb1 = aent(nc.sbuf_tensor("tb1", [128, 1024], F32)).ap()
        tb2 = aent(nc.sbuf_tensor("tb2", [128, 1024], BF16)).ap()
        mtb = aent(nc.sbuf_tensor("mtb", [128, 1024], BF16)).ap()
        yev = aent(nc.sbuf_tensor("yev", [128, 64], F32)).ap()
        xg = [aent(nc.sbuf_tensor(f"xg{i}", [128, SLAB], BF16)).ap() for i in range(32)]
        gsl = xg
        sqb = [aent(nc.sbuf_tensor(f"sqb{i}", [128, SLAB], BF16)).ap() for i in range(2)]
        prb = [aent(nc.sbuf_tensor(f"prb{i}", [128, SLAB], F32)).ap() for i in range(2)]
        sqf = [aent(nc.sbuf_tensor(f"sqf{i}", [128, SLAB], F32)).ap() for i in range(2)]
        A_un = aent(nc.sbuf_tensor("A_un", [128, 8192], F32)).ap()
        wstg2 = [A_un[:, 0:2048], A_un[:, 2048:4096]]
        hsdt = [A_un[:, 0:2048].bitcast(BF16), A_un[:, 2048:4096].bitcast(BF16)]
        ytm = [A_un[:, 4096:6144].bitcast(BF16), A_un[:, 6144:8192].bitcast(BF16)]
        opst = A_un[:, 0:4096]
        opbf = [A_un[:, 4096:6144].bitcast(BF16), A_un[:, 6144:8192].bitcast(BF16)]

        S_.op("vector", lambda e: e.memset(stash, 0.0), inc=[("v", 1)])
        S_.op("vector", lambda e: e.memset(dtsl, 0.0), inc=[("v", 1)])
        S_.op("vector", lambda e: e.memset(cumc, 0.0), inc=[("v", 1)])
        S_.op("vector", lambda e: e.memset(dcay, 0.0), inc=[("v", 1)])
        S_.op("vector", lambda e: e.memset(Sst, 0.0), inc=[("v", 1)])
        S_.op("vector", lambda e: e.memset(Sbf, 0.0), inc=[("v", 1)])
        mt_s2 = {}
        mt_mul = {}
        mt_su3 = {}

        for slab in range(_BUILD_OPTS["nslab"]):
            t0 = slab * SLAB
            cast_tok = {}
            for kc in range(16):
                w = (S_.aft("a", "v") if kc < 2 else []) + cast_tok.get(kc % 2, [])
                tok = S_.op("sync", lambda e, kc=kc, t0=t0: e.dma_start(
                    out=x0st[kc % 2], in_=x0T[kc * 128:(kc + 1) * 128, t0:t0 + SLAB]),
                    inc=[(f"sx{kc % 2}", 16)], wait=w)
                cast_tok[kc % 2] = S_.op("scalar", lambda e, kc=kc: e.activation(
                    x0bf[kc], x0st[kc % 2], AF.Copy), inc=[("a", 1)],
                    wait=list(tok) + (S_.aft("t") if kc == 0 else []))
            x0_done = S_.aft("a")

            last_cast = {}
            cache_tok = {}
            mm_war = {}
            bank_ev = {}
            for mg in range((min(_BUILD_OPTS["nmg"], NMC)) if _BUILD_OPTS["inproj"] else 0):
                m0 = mg
                nmc_g = 1
                wcols = 128
                buf = wbf[mg % 2]
                war = mm_war.get(mg % 2, S_.aft("t") if slab > 0 else [])
                if slab == 0:
                    wsb = wstg2[mg % 2]
                    ldw = []
                    for kh in range(2):
                        ldw += S_.op("sync", lambda e, kh=kh, m0=m0, wcols=wcols,
                                     wsb=wsb: e.dma_start(
                            out=wsb[:, kh * 8 * wcols:(kh + 1) * 8 * wcols]
                                .rearrange("p (k c) -> p k c", k=8),
                            in_=ipw_d[kh * 1024:(kh + 1) * 1024, m0 * 128:m0 * 128 + wcols]
                                .rearrange("(k p) c -> p k c", p=128)),
                            inc=[(f"sw{mg % 2}", 16)],
                            wait=(list(war) + last_cast.get(mg % 2, []))
                            if kh == 0 else ())
                    wtok = S_.op("vector", lambda e, buf=buf, wcols=wcols, wsb=wsb:
                                 e.tensor_copy(buf, wsb),
                                 inc=[("v", 1)],
                                 wait=ldw[-1:] + cache_tok.get(mg % 2, [])
                                 + list(war))
                    last_cast[mg % 2] = wtok
                    if _BUILD_OPTS["cache"]:
                        cache_tok[mg % 2] = S_.op(
                            "gpsimd", lambda e, m0=m0, buf=buf, wcols=wcols: e.dma_start(
                                out=ipwbf_d[:, m0 * 128:m0 * 128 + wcols]
                                    .rearrange("(k p) c -> p k c", p=128),
                                in_=buf[:, :16 * wcols].rearrange("p (k c) -> p k c", k=16)),
                            inc=[(f"gc{mg % 2}", 16)], wait=wtok)
                else:
                    wtok = S_.op("sync", lambda e, m0=m0, buf=buf, wcols=wcols: e.dma_start(
                        out=buf[:, :16 * wcols].rearrange("p (k c) -> p k c", k=16),
                        in_=ipwbf_d[:, m0 * 128:m0 * 128 + wcols]
                            .rearrange("(k p) c -> p k c", p=128)),
                        inc=[(f"sw{mg % 2}", 16)],
                        wait=list(war) + S_.aft("gc0", "gc1"))
                for mi in range(nmc_g):
                    m = m0 + mi
                    bank = pb[4 + m % 4]
                    mm_tok = []
                    for k in range(16):
                        first = (k == 0)
                        mm_tok = S_.op("tensor", lambda e, k=k, mi=mi, bank=bank, buf=buf,
                                       wcols=wcols: e.matmul(
                            bank[:, :SLAB],
                            buf[:, k * wcols + mi * 128: k * wcols + (mi + 1) * 128],
                            x0bf[k], start=(k == 0), stop=(k == 15)),
                            inc=[("t", 1)] if k == 15 else (),
                            wait=(list(wtok) + x0_done
                                  + bank_ev.get(m % 4, [])) if first else ())
                    mm_war[mg % 2] = mm_tok
                    if m < 32:
                        bank_ev[m % 4] = S_.op(
                            "scalar", lambda e, bank=bank, m=m: e.activation(
                                gsl[m], bank[:, :SLAB], AF.Silu, bias=vcol("inb", m)),
                            inc=[("a", 1)], wait=mm_tok + S_.aft("v", "t"))
                    elif m < 66:
                        j = m - 32
                        hb = hbcb[j % 2]
                        cb = cvb[j % 2]
                        stok = S_.op("vector", lambda e, hb=hb, j=j: e.tensor_copy(
                            hb[:, 0:3], stash[:, j * 3:(j + 1) * 3]),
                            inc=[("v", 1)], wait=S_.aft("a"))
                        ev = S_.op("scalar", lambda e, bank=bank, hb=hb, m=m: e.activation(
                            hb[:, 3:515], bank[:, :SLAB], AF.Identity, bias=vcol("inb", m)),
                            inc=[("a", 1)], wait=mm_tok + stok)
                        bank_ev[m % 4] = ev
                        S_.op("vector", lambda e, hb=hb, j=j: e.tensor_copy(
                            stash[:, j * 3:(j + 1) * 3], hb[:, 512:515]),
                            inc=[("v", 1)], wait=ev)
                        cv = S_.op("vector", lambda e, hb=hb, cb=cb, j=j: e.tensor_scalar(
                            cb, hb[:, 0:512], vcol("convw", j * 4), None, ALU.mult),
                            inc=[("v", 1)], wait=())
                        for kk in range(1, 4):
                            cv = S_.op("vector", lambda e, hb=hb, cb=cb, j=j, kk=kk:
                                       e.scalar_tensor_tensor(
                                cb, hb[:, kk:512 + kk], vcol("convw", j * 4 + kk), cb,
                                ALU.mult, ALU.add), inc=[("v", 1)], wait=())
                        dst = hs[j] if j < 32 else (Bsl if j == 32 else Csl)
                        S_.op("scalar", lambda e, cb=cb, dst=dst: e.activation(
                            dst, cb, AF.Silu), inc=[("a", 1)], wait=cv)
                    else:
                        ev = S_.op("scalar", lambda e, bank=bank: e.activation(
                            dstt, bank[:, :SLAB], AF.Identity, bias=vcol("inb", 66)),
                            inc=[("a", 1)], wait=mm_tok + S_.aft("v"))
                        bank_ev[m % 4] = ev
                        z = dstt[0:64, :]
                        S_.op("vector", lambda e, z=z: e.tensor_scalar(
                            z, z, vcol("dtb")[0:64, :], None, ALU.add),
                            inc=[("v", 1)], wait=ev)
                        S_.op("scalar", lambda e, z=z: e.activation(
                            dtsl[0:64, :], z, AF.Abs), inc=[("a", 1)], wait=S_.aft("v"))
                        S_.op("scalar", lambda e: e.activation(
                            dtsl[0:64, :], dtsl[0:64, :], AF.Exp, scale=-1.0),
                            inc=[("a", 1)], wait=())
                        S_.op("scalar", lambda e: e.activation(
                            dtsl[0:64, :], dtsl[0:64, :], AF.Ln,
                            bias=vcol("ones")[0:64, :]), inc=[("a", 1)], wait=())
                        S_.op("vector", lambda e, z=z: e.tensor_scalar(
                            z, z, 0.0, None, ALU.max), inc=[("v", 1)],
                            wait=S_.aft("a"))
                        S_.op("vector", lambda e, z=z: e.tensor_add(
                            dtsl[0:64, :], dtsl[0:64, :], z),
                            inc=[("v", 1)], wait=S_.aft("a"))

            dbg_tap("convout", [((j * 128, (j + 1) * 128),
                                 hs[j] if j < 32 else (Bsl if j == 32 else Csl))
                                for j in range(34)] if slab == 0 else [])
            dbg_tap("dt", [((0, 64), dtsl[0:64, :])] if slab == 0 else [])
            dbg_tap("gate", [((j * 128, (j + 1) * 128), gsl[j])
                             for j in range(32)] if slab == 0 else [])

            if _BUILD_OPTS["ssd"]:
                for ci in range(2):
                    c = slab * 2 + ci
                    off = ci * 256
                    pre = S_.aft("a", "v")
                    S_.op("vector", lambda e, off=off: e.tensor_scalar(
                        abuf, dtsl[0:64, off:off + 256], expA[0:64, :], -1.0,
                        ALU.mult, ALU.mult), inc=[("v", 1)], wait=pre)
                    S_.op("vector", lambda e, off=off: e.tensor_tensor_scan(
                        cumc[0:64, off:off + 256], abuf, abuf, 0.0,
                        ALU.add, ALU.bypass), inc=[("v", 1)], wait=())
                    cum_tok = S_.aft("v")
                    S_.op("gpsimd", lambda e, c=c, off=off: e.dma_start(
                        out=cumsp_d[c * 16384:(c + 1) * 16384]
                            .rearrange("(h t) -> h t", h=64),
                        in_=cumc[0:64, off:off + 256]),
                        inc=[("gsp", 16)], wait=cum_tok)
                    S_.op("gpsimd", lambda e, c=c, off=off: e.dma_start(
                        out=cssp_d[c * 64:(c + 1) * 64].unsqueeze(1),
                        in_=cumc[0:64, off + 255:off + 256]),
                        inc=[("gsp", 16)], wait=())
            cumsp_tok = S_.aft("gsp")
            for ci in range(2 if _BUILD_OPTS["ssd"] else 0):
                c = slab * 2 + ci
                off = ci * 256
                pre = S_.aft("a", "v")
                eb = S_.op("gpsimd", lambda e, c=c: e.dma_start(
                    out=Ee[:, 0:64],
                    in_=cssp_d[c * 64:(c + 1) * 64][None, :].to_broadcast([128, 64])),
                    inc=[("gsp", 16)], wait=cumsp_tok + S_.aft("a", "v")),
                S_.op("scalar", lambda e: e.activation(Ee, Ee, AF.Exp),
                      inc=[("a", 1)], wait=eb[0])
                S_.op("scalar", lambda e, off=off: e.activation(
                    dcay[0:64, :], cumc[0:64, off:off + 256], AF.Exp,
                    bias=cumc[0:64, off + 255:off + 256], scale=-1.0),
                    inc=[("a", 1)], wait=())
                dcay_tok = S_.aft("a")
                ev_tok = {}
                for nm, srcf, dst, extra in (
                        ("cum", lambda i, off=off: cumc[:, off + i * 128:
                                                        off + (i + 1) * 128],
                         None, True),
                        ("dt", lambda i, off=off: dtsl[:, off + i * 128:
                                                       off + (i + 1) * 128],
                         dtT, False),
                        ("dy", lambda i: dcay[:, i * 128:(i + 1) * 128], dyT, False)):
                    for i in range(2):
                        bank = pb[2 + i]
                        tt = S_.op("tensor", lambda e, srcf=srcf, i=i, bank=bank:
                                   e.transpose(bank[:, 0:128], srcf(i), idf),
                                   inc=[("t", 1)],
                                   wait=(dcay_tok if nm == "dy" else cum_tok)
                                   + ev_tok.get(("e", i), S_.aft("v", "a", "p")))
                        if nm == "cum":
                            e1 = S_.op("vector", lambda e, i=i, bank=bank:
                                       e.tensor_scalar(
                                ncT[:, i * 64:(i + 1) * 64], bank[:, 0:64],
                                -1.0, None, ALU.mult), inc=[("v", 1)], wait=tt,
                                label=("ncTevac", slab, ci, i))
                            e2 = S_.op("scalar", lambda e, i=i, bank=bank:
                                       e.activation(
                                ecT[:, i * 64:(i + 1) * 64], bank[:, 0:64],
                                AF.Exp), inc=[("a", 1)], wait=tt + e1)
                            ev_tok[("e", i)] = e1 + e2
                        else:
                            e1 = S_.op("vector", lambda e, i=i, bank=bank, dst=dst:
                                       e.tensor_copy(
                                dst[:, i * 64:(i + 1) * 64], bank[:, 0:64]),
                                inc=[("v", 1)], wait=tt)
                            ev_tok[("e", i)] = e1
                for i in range(2):
                    bank = pbf[2 + i]
                    tt = S_.op("tensor", lambda e, i=i, off=off, bank=bank:
                               e.transpose(
                        bank[:, 0:128], Bsl[:, off + i * 128: off + (i + 1) * 128],
                        idb), inc=[("t", 1)], wait=ev_tok.get(("e", i), []) + pre)
                    ev_tok[("e", i)] = S_.op("vector", lambda e, i=i, bank=bank:
                                             e.tensor_copy(
                        Btm[:, i * 128:(i + 1) * 128], bank[:, 0:128]),
                        inc=[("v", 1)], wait=tt)
                hs_ev = {}
                for j in range(32):
                    for li in range(2):
                        par = (j * 2 + li) % 2
                        bank = pbf[4 + par]
                        tt = S_.op("tensor", lambda e, j=j, li=li, off=off,
                                   bank=bank: e.transpose(
                            bank[:, 0:128],
                            hs[j][:, off + li * 128: off + (li + 1) * 128], idb),
                            inc=[("t", 1)], wait=hs_ev.get(par, ev_tok.get(("e", li), [])))
                        ee = []
                        for hh in range(2):
                            ee += S_.op("vector", lambda e, j=j, li=li, hh=hh,
                                        bank=bank: e.tensor_scalar(
                                hsdt[li][:, j * 128 + hh * 64: j * 128 + (hh + 1) * 64],
                                bank[:, hh * 64:(hh + 1) * 64],
                                dtT[:, li * 64 + 2 * j + hh: li * 64 + 2 * j + hh + 1],
                                None, ALU.mult), inc=[("v", 1)],
                                wait=tt if hh == 0 else ())
                        hs_ev[par] = ee
                gm_tok = []
                for si in range(2):
                    gm_tok += S_.op("tensor", lambda e, si=si, off=off: e.matmul(
                        pb[si][:, 0:256], Bsl[:, off + si * 128: off + (si + 1) * 128],
                        Csl[:, off:off + 256], start=True, stop=True),
                        inc=[("t", 1)], wait=S_.aft("v", "a") + pre)
                hb_tok = {}
                mt_done = {}
                rb_tok = []

                def emit_mt(h):
                    nonlocal rb_tok
                    if h % 4 == 0:
                        rb_tok = S_.op("gpsimd", lambda e, c=c, h=h: e.dma_start(
                            out=RB,
                            in_=cumsp_d[c * 16384 + h * 256: c * 16384 + (h + 4) * 256]
                                [None, :].to_broadcast([128, 1024])),
                            inc=[(f"gb{(c * 16 + h // 4) % 2}", 16)],
                            label=("rb", slab, ci, h),
                            wait=(cumsp_tok if h == 0 else []) + S_.aft("v", "p"))
                    ro = (h % 4) * 256
                    hp = (h % 2) * 512
                    s1s = {}
                    for si in range(2):
                        s1s[si] = S_.op("vector", lambda e, si=si, h=h, ro=ro,
                                        hp=hp: e.scalar_tensor_tensor(
                            tb1[:, hp + si * 256: hp + (si + 1) * 256],
                            RB[:, ro:ro + 256],
                            ncT[:, si * 64 + h: si * 64 + h + 1],
                            ssdm[:, si * 256:(si + 1) * 256], ALU.add, ALU.add),
                            label=("s1", slab, ci, h, si),
                            inc=[("v", 1)],
                            wait=list(rb_tok)
                            + (gm_tok if si == 0 and h == 0 else [])
                            + mt_s2.pop((h % 2, si), []))
                    for si in range(2):
                        mt_s2[(h % 2, si)] = S_.op(
                            "scalar", lambda e, si=si, hp=hp: e.activation(
                                tb2[:, hp + si * 256: hp + (si + 1) * 256],
                                tb1[:, hp + si * 256: hp + (si + 1) * 256],
                                AF.Exp), inc=[("a", 1)],
                            wait=s1s[si] + mt_mul.pop((h % 2, si), []),
                            label=("s2", slab, ci, h, si))
                    mt_tok = []
                    for si in range(2):
                        mt_mul[(h % 2, si)] = S_.op(
                            "vector", lambda e, si=si, h=h, hp=hp: e.tensor_mul(
                                mtb[:, hp + si * 256: hp + (si + 1) * 256],
                                tb2[:, hp + si * 256: hp + (si + 1) * 256],
                                pb[si][:, 0:256]),
                            inc=[("v", 1)],
                            wait=mt_s2[(h % 2, si)]
                            + (hb_tok.get(("ymm", h % 2), [])
                               if si == 0 else []))
                        mt_tok += mt_mul[(h % 2, si)]
                    bd_tok = []
                    for si in range(2):
                        bd_tok += S_.op("vector", lambda e, si=si, h=h: e.tensor_scalar(
                            bdec[:, (h % 2) * 256 + si * 128:
                                 (h % 2) * 256 + (si + 1) * 128],
                            Btm[:, si * 128:(si + 1) * 128],
                            dyT[:, si * 64 + h: si * 64 + h + 1], None, ALU.mult),
                            inc=[("v", 1)],
                            wait=hb_tok.get(("lmm", h % 2), []) if si == 0 else ())
                    mt_done[h] = (mt_tok, bd_tok)

                emit_mt(0)
                for h in range(64):
                    mt_tok, bd_tok = mt_done.pop(h)
                    yb = pb[6 + h % 2]
                    lb = pb[4 + h % 2]
                    ymm = []
                    for li in range(2):
                        for si in range(2):
                            ymm = S_.op("tensor", lambda e, li=li, si=si, h=h, yb=yb:
                                        e.matmul(
                                yb[:, li * 128: li * 128 + 64],
                                mtb[:, (h % 2) * 512 + si * 256 + li * 128:
                                    (h % 2) * 512 + si * 256 + (li + 1) * 128],
                                hsdt[si][:, h * 64:(h + 1) * 64],
                                start=(si == 0), stop=(si == 1)),
                                inc=[("t", 1)] if si == 1 else (),
                                wait=(mt_tok + hb_tok.get(("yev", h % 2), []))
                                if (li == 0 and si == 0) else ())
                        if c > 0:
                            ymm = S_.op("tensor", lambda e, li=li, h=h, off=off,
                                        yb=yb: e.matmul(
                                yb[:, li * 128 + 64: li * 128 + 128],
                                Csl[:, off + li * 128: off + (li + 1) * 128],
                                Sbf[:, h * 64:(h + 1) * 64],
                                start=True, stop=True), inc=[("t", 1)], wait=())
                    hb_tok[("ymm", h % 2)] = ymm
                    lmm = []
                    for li in range(2):
                        lmm = S_.op("tensor", lambda e, li=li, h=h, lb=lb: e.matmul(
                            lb[:, 0:64],
                            bdec[:, (h % 2) * 256 + li * 128:
                                 (h % 2) * 256 + (li + 1) * 128],
                            hsdt[li][:, h * 64:(h + 1) * 64],
                            start=(li == 0), stop=(li == 1)),
                            inc=[("t", 1)] if li == 1 else (),
                            wait=(bd_tok + hb_tok.get(("sup", h % 2), []))
                            if li == 0 else ())
                    hb_tok[("lmm", h % 2)] = lmm
                    if h + 1 < 64:
                        emit_mt(h + 1)
                    yev_t = []
                    for li in range(2):
                        if c > 0:
                            ye = S_.op("vector", lambda e, li=li, h=h, yb=yb:
                                       e.tensor_scalar(
                                yev, yb[:, li * 128 + 64: li * 128 + 128],
                                ecT[:, li * 64 + h: li * 64 + h + 1], None, ALU.mult),
                                inc=[("v", 1)], wait=ymm)
                            yev_t += S_.op("vector", lambda e, li=li, h=h, yb=yb:
                                           e.tensor_add(
                                ytm[li][:, h * 64:(h + 1) * 64], yev,
                                yb[:, li * 128: li * 128 + 64]),
                                inc=[("v", 1)], wait=())
                        else:
                            yev_t += S_.op("vector", lambda e, li=li, h=h, yb=yb:
                                           e.tensor_copy(
                                ytm[li][:, h * 64:(h + 1) * 64],
                                yb[:, li * 128: li * 128 + 64]),
                                inc=[("v", 1)], wait=ymm)
                    hb_tok[("yev", h % 2)] = yev_t
                    su = S_.op("vector", lambda e, h=h: e.tensor_scalar(
                        Sst[:, h * 64:(h + 1) * 64], Sst[:, h * 64:(h + 1) * 64],
                        Ee[:, h:h + 1], None, ALU.mult),
                        inc=[("v", 1)], wait=lmm + mt_su3.pop(h, []))
                    su = S_.op("vector", lambda e, h=h, lb=lb: e.tensor_add(
                        Sst[:, h * 64:(h + 1) * 64], Sst[:, h * 64:(h + 1) * 64],
                        lb[:, 0:64]), inc=[("v", 1)], wait=())
                    hb_tok[("sup", h % 2)] = su
                    mt_su3[h] = S_.op("scalar", lambda e, h=h: e.activation(
                        Sbf[:, h * 64:(h + 1) * 64], Sst[:, h * 64:(h + 1) * 64],
                        AF.Copy), inc=[("a", 1)], wait=su)

                dbg_tap("cum", [((0, 64), cumc[0:64, 0:256])] if c == 0 else [])
                dbg_tap("hsdt", [((0, 128), hsdt[0]), ((128, 256), hsdt[1])]
                        if c == 0 else [])
                dbg_tap("S1", [((0, 128), Sst)] if c == 0 else [])

                xg_ev = {}
                for j in range(32 if _BUILD_OPTS["xg"] else 0):
                    par = j % 2
                    bank = pbf[6 + par]
                    tt = []
                    for li in range(2):
                        tt = S_.op("tensor", lambda e, j=j, li=li, bank=bank:
                                   e.transpose(
                            bank[:, li * 128:(li + 1) * 128],
                            ytm[li][:, j * 128:(j + 1) * 128], idb),
                            inc=[("t", 1)] if li == 1 else (),
                            wait=xg_ev.get(par, S_.aft("v")) if li == 0 else ())
                    s1 = S_.op("vector", lambda e, j=j, off=off,
                               bank=bank: e.scalar_tensor_tensor(
                        sqf[0][:, 0:256], hs[j][:, off: off + 256],
                        vcol("D", j), bank[:, 0:256], ALU.mult, ALU.add),
                        inc=[("v", 1)], wait=tt)
                    xg_ev[par] = S_.op("vector", lambda e, j=j, off=off:
                                       e.tensor_mul(
                        xg[j][:, off: off + 256], sqf[0][:, 0:256],
                        gsl[j][:, off: off + 256]),
                        inc=[("v", 1)], wait=())

            if not _BUILD_OPTS["op"]:
                continue
            sq_tok = []
            prev_mm = {}
            for j in range(32):
                sj = S_.op("scalar", lambda e, j=j: e.activation(
                    sqb[j % 2], xg[j], AF.Square), inc=[("a", 1)],
                    wait=(S_.aft("v") if j == 0 else []) + prev_mm.get(j % 2, []))
                sq_tok = S_.op("tensor", lambda e, j=j: e.matmul(
                    pb[5][0:1, 0:SLAB], onesb, sqb[j % 2],
                    start=(j == 0), stop=(j == 31)),
                    inc=[("t", 1)], wait=sj)
                prev_mm[j % 2] = sq_tok
            rv = S_.op("vector", lambda e: e.tensor_copy(rowb, pb[5][0:1, 0:SLAB]),
                       inc=[("v", 1)], wait=sq_tok)
            rv = S_.op("vector", lambda e: e.tensor_scalar(
                rowb, rowb, 1.0 / 4096.0, None, ALU.mult), inc=[("v", 1)], wait=())
            rv = S_.op("scalar", lambda e: e.activation(
                rowb, rowb, AF.Sqrt, bias=vcol("epsA")[0:1, :]), inc=[("a", 1)], wait=rv)
            rv = S_.op("vector", lambda e: e.reciprocal(rowb, rowb),
                       inc=[("v", 1)], wait=rv)
            rd = S_.op("gpsimd", lambda e, slab=slab: e.dma_start(
                out=rows_d[slab * 512:(slab + 1) * 512].unsqueeze(0),
                in_=rowb), inc=[("gsp", 16)], wait=rv)
            rb2 = S_.op("gpsimd", lambda e, slab=slab: e.dma_start(
                out=bcb[0], in_=rows_d[slab * 512:(slab + 1) * 512]
                    [None, :].to_broadcast([128, 512])),
                inc=[("gsp", 16)], wait=rd)
            for j in range(32):
                S_.op("vector", lambda e, j=j: e.scalar_tensor_tensor(
                    xg[j], xg[j], vcol("nw", j), bcb[0], ALU.mult, ALU.mult),
                    inc=[("v", 1)], wait=rb2)
            xgn_done = S_.aft("v")

            for mg in range(4):
                ld_all = []
                mm_tok = []
                kq_tok = {}
                for kq in range(4):
                    lw = S_.op("sync", lambda e, mg=mg, kq=kq: e.dma_start(
                        out=opst.rearrange("p (k c) -> p k c", k=8),
                        in_=opw_d[kq * 1024:(kq + 1) * 1024, mg * 512:(mg + 1) * 512]
                            .rearrange("(k p) c -> p k c", p=128)),
                        inc=[(f"sw{kq % 2}", 16)], wait=S_.aft("v") if (mg + kq) > 0 else S_.aft("t"))
                    cw = S_.op("vector", lambda e, kq=kq: e.tensor_copy(
                        opbf[kq % 2], opst), inc=[("v", 1)],
                        wait=list(lw) + kq_tok.get(kq % 2, S_.aft("t")))
                    for k8 in range(8):
                        for mi in range(4):
                            kk = kq * 8 + k8
                            tok = S_.op("tensor", lambda e, kq=kq, k8=k8, mi=mi, kk=kk:
                                        e.matmul(
                                pb[mi][:, 0:SLAB],
                                opbf[kq % 2][:, k8 * 512 + mi * 128: k8 * 512 + (mi + 1) * 128],
                                xg[kk], start=(kk == 0), stop=(kk == 31)),
                                inc=[("t", 1)] if (kk == 31 or (k8 == 7 and mi == 3))
                                else (),
                                wait=(list(cw) + xgn_done) if (k8 == 0 and mi == 0) else ())
                            if k8 == 7 and mi == 3:
                                kq_tok[kq % 2] = tok
                            if kk == 31:
                                mm_tok += tok
                for mi in range(4):
                    m = mg * 4 + mi
                    xl = S_.op("sync", lambda e, mi=mi, m=m, t0=t0: e.dma_start(
                        out=x0st[mi % 2], in_=x0T[m * 128:(m + 1) * 128, t0:t0 + SLAB]),
                        inc=[(f"sx{mi % 2}", 16)], wait=S_.aft("v"))
                    evt = S_.op("vector", lambda e, mi=mi, m=m: e.scalar_tensor_tensor(
                        prb[mi % 2], pb[mi][:, 0:SLAB], vcol("ob", m), x0st[mi % 2],
                        ALU.add, ALU.add), inc=[("v", 1)],
                        wait=list(xl) + mm_tok + prev_mm.get(("pr", m % 2), [])
                        + prev_mm.get(("sqa", m % 2), []) + S_.aft("gsp"))
                    S_.op("gpsimd", lambda e, mi=mi, m=m, t0=t0: e.dma_start(
                        out=prex1_d[m * 128:(m + 1) * 128, t0:t0 + SLAB],
                        in_=prb[mi % 2]), inc=[("gsp", 16)], wait=evt)
                    sq2 = S_.op("scalar", lambda e, mi=mi: e.activation(
                        sqf[mi % 2], prb[mi % 2], AF.Square), inc=[("a", 1)],
                        wait=evt + prev_mm.get(("sq", m % 2), []))
                    prev_mm[("sqa", m % 2)] = sq2
                    smm = S_.op("tensor", lambda e, m=m, mi=mi: e.matmul(
                        pb[6][0:1, 0:SLAB], vcol("ones"), prb[mi % 2],
                        start=(m == 0), stop=(m == 15)),
                        inc=[("t", 1)], wait=evt)
                    prev_mm[("pr", m % 2)] = smm
                    smm2 = S_.op("tensor", lambda e, m=m, mi=mi: e.matmul(
                        pb[7][0:1, 0:SLAB], vcol("ones"), sqf[mi % 2],
                        start=(m == 0), stop=(m == 15)),
                        inc=[("t", 1)], wait=sq2)
                    prev_mm[("sq", m % 2)] = smm2
            st_tok = S_.aft("t")
            rv = S_.op("vector", lambda e: e.tensor_scalar(
                rowb, pb[6][0:1, 0:SLAB], 1.0 / 2048.0, None, ALU.mult),
                inc=[("v", 1)], wait=st_tok)
            rv2 = S_.op("vector", lambda e: e.tensor_scalar(
                rowb2, pb[7][0:1, 0:SLAB], 1.0 / 2048.0, None, ALU.mult),
                inc=[("v", 1)], wait=())
            sq3 = S_.op("scalar", lambda e: e.activation(
                bcb[1][0:1, :], rowb, AF.Square), inc=[("a", 1)], wait=rv)
            vv = S_.op("vector", lambda e: e.tensor_sub(
                rowb2, rowb2, bcb[1][0:1, :]), inc=[("v", 1)], wait=sq3)
            s1 = S_.op("scalar", lambda e: e.activation(
                bcb[1][0:1, :], rowb2, AF.Sqrt, bias=vcol("epsA")[0:1, :]), inc=[("a", 1)], wait=vv)
            r1 = S_.op("vector", lambda e: e.reciprocal(
                bcb[1][0:1, :], bcb[1][0:1, :]), inc=[("v", 1)], wait=s1)
            S_.op("gpsimd", lambda e, slab=slab: e.dma_start(
                out=rows_d[2048 + slab * 512: 2048 + (slab + 1) * 512].unsqueeze(0),
                in_=rowb), inc=[("gsp", 16)], wait=rv)
            S_.op("gpsimd", lambda e, slab=slab: e.dma_start(
                out=rows_d[4096 + slab * 512: 4096 + (slab + 1) * 512].unsqueeze(0),
                in_=bcb[1][0:1, :]), inc=[("gsp", 16)], wait=r1)
            t1 = S_.op("vector", lambda e: e.tensor_mul(
                rowb2, rowb2, bcb[1][0:1, :]), inc=[("v", 1)], wait=r1)
            t1 = S_.op("vector", lambda e: e.tensor_mul(
                rowb2, rowb2, bcb[1][0:1, :]), inc=[("v", 1)], wait=())
            t2 = S_.op("scalar", lambda e: e.activation(
                rowb2, rowb2, AF.Sqrt, bias=vcol("epsF")[0:1, :]), inc=[("a", 1)], wait=t1)
            t3 = S_.op("vector", lambda e: e.reciprocal(rowb2, rowb2),
                       inc=[("v", 1)], wait=t2)
            S_.op("gpsimd", lambda e, slab=slab: e.dma_start(
                out=rows_d[6144 + slab * 512: 6144 + (slab + 1) * 512].unsqueeze(0),
                in_=rowb2), inc=[("gsp", 16)], wait=t3)

        dbg_tap("prex1", [((0, 2048), prex1_d[:, 0:512])])
        dbg_tap("rows", [((0, 1), rows_d[0:1024].unsqueeze(0)),
                         ((1, 2), rows_d[2048:3072].unsqueeze(0)),
                         ((2, 3), rows_d[4096:5120].unsqueeze(0)),
                         ((3, 4), rows_d[6144:7168].unsqueeze(0))])
        a_sc.close()

        S_.barrier()
        b_sc = ExitStack()
        bent = b_sc.enter_context
        XL = [bent(nc.sbuf_tensor(f"XL{i}", [128, S], BF16)).ap() for i in range(16)]
        prl = [bent(nc.sbuf_tensor(f"prl{i}", [128, SLAB], F32)).ap() for i in range(2)]
        x1b = [bent(nc.sbuf_tensor(f"x1b{i}", [128, SLAB], F32)).ap() for i in range(2)]
        bcB = [bent(nc.sbuf_tensor(f"bcB{i}", [128, SLAB], F32)).ap() for i in range(3)]
        if _BUILD_OPTS["xl"]:
            prev = {}
            for slab in range(NSLAB):
                t0 = slab * SLAB
                bc_tok = []
                for j, roff in enumerate((2048, 4096, 6144)):
                    bc_tok += S_.op("gpsimd", lambda e, j=j, roff=roff, slab=slab:
                                    e.dma_start(
                        out=bcB[j],
                        in_=rows_d[roff + slab * 512: roff + (slab + 1) * 512]
                            [None, :].to_broadcast([128, 512])),
                        inc=[("gbc", 16)], wait=S_.aft("v") if slab else ())
                for m in range(16):
                    pl = S_.op("sync", lambda e, m=m, t0=t0: e.dma_start(
                        out=prl[m % 2], in_=prex1_d[m * 128:(m + 1) * 128, t0:t0 + SLAB]),
                        inc=[(f"sx{m % 2}", 16)],
                        wait=prev.get(("sub", m % 2), S_.aft("gsp")))
                    s1 = S_.op("vector", lambda e, m=m: e.tensor_sub(
                        x1b[m % 2], prl[m % 2], bcB[0]),
                        inc=[("v", 1)], wait=list(pl) + (bc_tok if m == 0 else [])
                        + prev.get(("sp", m % 2), []))
                    prev[("sub", m % 2)] = s1
                    s1 = S_.op("vector", lambda e, m=m: e.tensor_mul(
                        x1b[m % 2], x1b[m % 2], bcB[1]), inc=[("v", 1)], wait=())
                    prev[("sp", m % 2)] = S_.op("gpsimd", lambda e, m=m, t0=t0:
                                                e.dma_start(
                        out=x1sp_d[m * 128:(m + 1) * 128, t0:t0 + SLAB],
                        in_=x1b[m % 2]), inc=[(f"gc{m % 2}", 16)], wait=s1)
                    prev[("xl", m % 2)] = S_.op("vector", lambda e, m=m, t0=t0:
                                                e.scalar_tensor_tensor(
                        XL[m][:, t0:t0 + SLAB], x1b[m % 2], vcol("l1", m), bcB[2],
                        ALU.mult, ALU.mult), inc=[("v", 1)], wait=())
            bl0 = bent(nc.sbuf_tensor("bl0", [128, 1024], F32)).ap()
            bl1 = bent(nc.sbuf_tensor("bl1", [128, 1024], F32)).ap()
            blo = bent(nc.sbuf_tensor("blo", [128, 1024], F32)).ap()
            for m in range(16):
                h0 = S_.op("sync", lambda e, m=m: e.dma_start(
                    out=bl0, in_=x1sp_d[m * 128:(m + 1) * 128, 0:1024]),
                    inc=[("sx0", 16)], wait=S_.aft("gc0", "gc1", "v") if m == 0
                    else S_.aft("gbc"))
                h1 = S_.op("sync", lambda e, m=m: e.dma_start(
                    out=bl1, in_=x1sp_d[m * 128:(m + 1) * 128, 1024:2048]),
                    inc=[("sx1", 16)], wait=())
                bl = S_.op("vector", lambda e, m=m: e.tensor_scalar(
                    blo, bl0, vcol("m0"), None, ALU.mult),
                    inc=[("v", 1)], wait=list(h0) + list(h1) + S_.aft("gbc"))
                bl = S_.op("vector", lambda e, m=m: e.scalar_tensor_tensor(
                    blo, bl1, vcol("m1"), blo, ALU.mult, ALU.add),
                    inc=[("v", 1)], wait=())
                S_.op("gpsimd", lambda e, m=m: e.dma_start(
                    out=x1own_d[m * 128:(m + 1) * 128, :],
                    in_=blo), inc=[("gbc", 16)], wait=bl)

        wst2p = [bent(nc.sbuf_tensor(f"wst2{i}", [128, 2048], F32)).ap()
                 for i in range(2)]
        wkbf = [bent(nc.sbuf_tensor(f"wkbf{i}", [128, 2048], BF16)).ap()
                for i in range(2)]
        kvb = [bent(nc.sbuf_tensor(f"kvb{i}", [128, SLAB], BF16)).ap()
               for i in range(2)]
        sqk = bent(nc.sbuf_tensor("sqk", [128, SLAB], BF16)).ap()
        if _BUILD_OPTS["kv"]:
            S_.op("vector", lambda e: e.memset(mxk, 0.0), inc=[("v", 1)])
            prev = {}
            for m in range(16):
                lw = []
                for kh in range(2):
                    lw += S_.op("sync", lambda e, kh=kh, m=m: e.dma_start(
                        out=wst2p[m % 2][:, kh * 1024:(kh + 1) * 1024]
                            .rearrange("p (k c) -> p k c", k=8),
                        in_=kwT_d[kh * 1024:(kh + 1) * 1024, m * 128:(m + 1) * 128]
                            .rearrange("(k p) c -> p k c", p=128)),
                        inc=[(f"sw{m % 2}", 16)],
                        wait=prev.get(("cst", m % 2), S_.aft("t", "v"))
                        if kh == 0 else ())
                cst = S_.op("vector", lambda e, m=m: e.tensor_copy(
                    wkbf[m % 2], wst2p[m % 2]), inc=[("v", 1)],
                    wait=lw[-1:] + prev.get(("mmk", m % 2), []))
                prev[("cst", m % 2)] = cst
                for tc in range(4):
                    mmt = []
                    for k in range(16):
                        mmt = S_.op("tensor", lambda e, m=m, tc=tc, k=k: e.matmul(
                            pb[tc % 2][:, 0:SLAB],
                            wkbf[m % 2][:, k * 128:(k + 1) * 128],
                            XL[k][:, tc * 512:(tc + 1) * 512],
                            start=(k == 0), stop=(k == 15)),
                            inc=[("t", 1)] if k == 15 else (),
                            wait=(list(cst) + prev.get(("kev", tc % 2), []))
                            if k == 0 else ())
                    if tc == 3:
                        prev[("mmk", m % 2)] = mmt
                    ev = S_.op("scalar", lambda e, m=m, tc=tc: e.activation(
                        kvb[tc % 2], pb[tc % 2][:, 0:SLAB], AF.Copy),
                        inc=[("a", 1)], wait=mmt + prev.get(("ksp", tc % 2), []))
                    prev[("kev", tc % 2)] = ev
                    prev[("ksp", tc % 2)] = S_.op(
                        "gpsimd", lambda e, m=m, tc=tc: e.dma_start(
                            out=kspill_d[m * 128:(m + 1) * 128,
                                         tc * 512:(tc + 1) * 512],
                            in_=kvb[tc % 2]), inc=[(f"gc{tc % 2}", 16)], wait=ev)
                    sq = S_.op("scalar", lambda e, tc=tc: e.activation(
                        sqk, kvb[tc % 2], AF.Square), inc=[("a", 1)],
                        wait=ev + prev.get(("sqk",), []))
                    smm = S_.op("tensor", lambda e: e.matmul(
                        pb[6][0:1, 0:SLAB], onesb, sqk, start=True, stop=True),
                        inc=[("t", 1)], wait=sq + prev.get(("nrm",), []))
                    prev[("sqk",)] = smm
                    rm = S_.op("vector", lambda e: e.reduce_max(
                        rowb[0:1, 0:1], pb[6][0:1, 0:SLAB], axis=AX),
                        inc=[("v", 1)], wait=smm)
                    prev[("nrm",)] = S_.op("vector", lambda e, m=m: e.tensor_max(
                        mxk[0:1, m:m + 1], mxk[0:1, m:m + 1], rowb[0:1, 0:1]),
                        inc=[("v", 1)], wait=())
        vvbf = [bent(nc.sbuf_tensor(f"vvbf{i}", [128, SLAB], BF16)).ap()
                for i in range(16)]
        vst = [bent(nc.sbuf_tensor(f"vst{i}", [128, SLAB], F32)).ap()
               for i in range(2)]
        if _BUILD_OPTS["kv"]:
            lastmm = []
            for vq in range(4):
                casts = []
                for k in range(16):
                    lw = S_.op("sync", lambda e, vq=vq, k=k: e.dma_start(
                        out=vst[k % 2],
                        in_=vwT_d[k * 128:(k + 1) * 128, vq * 512:(vq + 1) * 512]),
                        inc=[(f"sx{k % 2}", 16)],
                        wait=prev.get(("vst", k % 2), S_.aft("v", "a")))
                    casts = S_.op("vector", lambda e, k=k: e.tensor_copy(
                        vvbf[k], vst[k % 2]), inc=[("v", 1)],
                        wait=list(lw) + lastmm)
                    prev[("vst", k % 2)] = casts
                for tb in range(16):
                    mmt = []
                    for k in range(16):
                        mmt = S_.op("tensor", lambda e, vq=vq, tb=tb, k=k: e.matmul(
                            pb[tb % 2][:, 0:SLAB],
                            XL[k][:, tb * 128:(tb + 1) * 128], vvbf[k],
                            start=(k == 0), stop=(k == 15)),
                            inc=[("t", 1)] if k == 15 else (),
                            wait=(casts + prev.get(("vev", tb % 2), []))
                            if k == 0 else ())
                    if tb >= 14:
                        lastmm = lastmm + mmt if tb == 15 else list(mmt)
                    ev = S_.op("scalar", lambda e, tb=tb: e.activation(
                        kvb[tb % 2], pb[tb % 2][:, 0:SLAB], AF.Copy),
                        inc=[("a", 1)],
                        wait=mmt + prev.get(("vsp", tb % 2),
                                            prev.get(("ksp", tb % 2), [])))
                    prev[("vev", tb % 2)] = ev
                    prev[("vsp", tb % 2)] = S_.op(
                        "gpsimd", lambda e, vq=vq, tb=tb: e.dma_start(
                            out=vspill_d[tb * 128:(tb + 1) * 128,
                                         vq * 512:(vq + 1) * 512],
                            in_=kvb[tb % 2]), inc=[(f"gc{tb % 2}", 16)], wait=ev)
                lastmm = lastmm[-1:]
        dbg_tap("kfm", [((0, 2048), kspill_d[:, 0:512])])
        dbg_tap("vtm", [((0, 2048), vspill_d[:, 0:512])])
        S_.barrier()
        b_sc.close()

        x2_sc = ExitStack()
        x2t = [x2_sc.enter_context(
            nc.sbuf_tensor(f"x2t{i}", [128, 1024], BF16, side="right")).ap()
            for i in range(16)]
        de_sc = ExitStack()
        afm = [de_sc.enter_context(
            nc.sbuf_tensor(f"afm{i}", [128, 1024], BF16, side="right")).ap()
            for i in range(16)]
        d_sc = ExitStack()
        dent = d_sc.enter_context
        XLo = [dent(nc.sbuf_tensor(f"XLo{i}", [128, 1024], BF16)).ap()
               for i in range(16)]
        m01t = [dent(nc.sbuf_tensor(f"m01t{i}", [128, 1024], BF16)).ap()
                for i in range(16)]
        q_all = [dent(nc.sbuf_tensor(f"qall{i}", [128, 1024], BF16)).ap()
                 for i in range(16)]
        kh_b2 = [dent(nc.sbuf_tensor(f"kh_b{i}", [128, 2048], BF16)).ap()
                 for i in range(2)]
        vh_b0 = dent(nc.sbuf_tensor("vh_b0", [128, 16 * 129], BF16)).ap()
        vh_b2 = [vh_b0, vh_b0]
        qst0 = dent(nc.sbuf_tensor("qst0", [128, 2048], F32)).ap()
        qst = [qst0, qst0]
        prl2 = [qst0[:, 0:512], qst0[:, 512:1024]]
        qbf = [dent(nc.sbuf_tensor(f"qbf{i}", [128, 2048], BF16)).ap()
               for i in range(2)]
        praw = [bcb[0].bitcast(BF16)[:, 0:512], bcb[1].bitcast(BF16)[:, 0:512]]
        sqq = praw[0]
        pT = [dent(nc.sbuf_tensor(f"pT{i}", [128, SLAB], BF16)).ap()
              for i in range(2)]
        att = [dent(nc.sbuf_tensor(f"att{i}", [128, 128], BF16)).ap()
               for i in range(2)]
        rcp = dent(nc.sbuf_tensor("rcp", [128, 1], F32)).ap()
        ncol = dent(nc.sbuf_tensor("ncol", [128, 16], F32)).ap()

        if _BUILD_OPTS["att"]:
            for tc in range(2):
                b0 = S_.op("gpsimd", lambda e, tc=tc: e.dma_start(
                    out=bcb[0], in_=rows_d[6144 + tc * 512: 6144 + (tc + 1) * 512]
                        [None, :].to_broadcast([128, 512])),
                    inc=[("gbc", 16)], wait=S_.aft("v"))
                b1 = S_.op("gpsimd", lambda e, tc=tc: e.dma_start(
                    out=bcb[1], in_=rows_d[7168 + tc * 512: 7168 + (tc + 1) * 512]
                        [None, :].to_broadcast([128, 512])),
                    inc=[("gbc", 16)], wait=())
                bl = S_.op("vector", lambda e: e.tensor_scalar(
                    bcb[0], bcb[0], vcol("m0"), None, ALU.mult),
                    inc=[("v", 1)], wait=list(b0) + list(b1))
                bl = S_.op("vector", lambda e: e.scalar_tensor_tensor(
                    bcb[0], bcb[1], vcol("m1"), bcb[0], ALU.mult, ALU.add),
                    inc=[("v", 1)], wait=())
                prev2 = {}
                for m in range(16):
                    xo = S_.op("sync", lambda e, m=m, tc=tc: e.dma_start(
                        out=prl2[m % 2],
                        in_=x1own_d[m * 128:(m + 1) * 128, tc * 512:(tc + 1) * 512]),
                        inc=[(f"sx{m % 2}", 16)],
                        wait=prev2.get(m % 2, S_.aft("v", "gbc")))
                    prev2[m % 2] = S_.op("vector", lambda e, m=m, tc=tc:
                                         e.scalar_tensor_tensor(
                        XLo[m][:, tc * 512:(tc + 1) * 512], prl2[m % 2],
                        vcol("l1", m), bcb[0], ALU.mult, ALU.mult),
                        inc=[("v", 1)], wait=list(xo) + (bl if m == 0 else []))
            for m in range(16):
                S_.op("sync", lambda e, m=m: e.dma_start(
                    out=m01t[m], in_=m01_d[m * 128:(m + 1) * 128, :]),
                    inc=[("st", 16)], wait=())
            m01_tok = S_.aft("st")
            S_.op("vector", lambda e: e.memset(vh_b0, 1.0), inc=[("v", 1)])
            prevq = {}
            for h in range(16):
                lw = []
                for kh in range(2):
                    lw += S_.op("sync", lambda e, kh=kh, h=h: e.dma_start(
                        out=qst[h % 2][:, kh * 1024:(kh + 1) * 1024]
                            .rearrange("p (k c) -> p k c", k=8),
                        in_=qwT_d[kh * 1024:(kh + 1) * 1024, h * 128:(h + 1) * 128]
                            .rearrange("(k p) c -> p k c", p=128)),
                        inc=[(f"sw{h % 2}", 16)],
                        wait=prevq.get(("cst", (h - 1) % 2), S_.aft("v", "t"))
                        if kh == 0 else ())
                cst = S_.op("vector", lambda e, h=h: e.tensor_copy(
                    qbf[h % 2], qst[h % 2]), inc=[("v", 1)],
                    wait=lw[-1:] + prevq.get(("mm", h % 2), []))
                prevq[("cst", h % 2)] = cst
                for tc in range(2):
                    mmt = []
                    for k in range(16):
                        mmt = S_.op("tensor", lambda e, h=h, tc=tc, k=k: e.matmul(
                            pb[tc % 2][:, 0:SLAB],
                            qbf[h % 2][:, k * 128:(k + 1) * 128],
                            XLo[k][:, tc * 512:(tc + 1) * 512],
                            start=(k == 0), stop=(k == 15)),
                            inc=[("t", 1)] if k == 15 else (),
                            wait=(list(cst) + prevq.get(("ev", tc % 2), []))
                            if k == 0 else ())
                    if tc == 1:
                        prevq[("mm", h % 2)] = mmt
                    ev = S_.op("scalar", lambda e, h=h, tc=tc: e.activation(
                        q_all[h][:, tc * 512:(tc + 1) * 512],
                        pb[tc % 2][:, 0:SLAB], AF.Copy),
                        inc=[("a", 1)], wait=mmt)
                    prevq[("ev", tc % 2)] = ev
                    sq = S_.op("scalar", lambda e, h=h, tc=tc: e.activation(
                        sqq, q_all[h][:, tc * 512:(tc + 1) * 512], AF.Square),
                        inc=[("a", 1)], wait=())
                    smm = S_.op("tensor", lambda e: e.matmul(
                        pb[6][0:1, 0:SLAB], onesb, sqq, start=True, stop=True),
                        inc=[("t", 1)], wait=sq + prevq.get("nrm", []))
                    rm = S_.op("vector", lambda e: e.reduce_max(
                        rowb[0:1, 0:1], pb[6][0:1, 0:SLAB], axis=AX),
                        inc=[("v", 1)], wait=smm)
                    prevq["nrm"] = S_.op("vector", lambda e, h=h: e.tensor_max(
                        mxk[0:1, 16 + h:17 + h], mxk[0:1, 16 + h:17 + h],
                        rowb[0:1, 0:1]), inc=[("v", 1)], wait=())
            nr = S_.op("vector", lambda e: e.tensor_mul(
                mxk[0:1, 32:48], mxk[0:1, 0:16], mxk[0:1, 16:32]),
                inc=[("v", 1)], wait=prevq["nrm"])
            nr = S_.op("scalar", lambda e: e.activation(
                mxk[0:1, 32:48], mxk[0:1, 32:48], AF.Sqrt), inc=[("a", 1)], wait=nr)
            nr = S_.op("vector", lambda e: e.tensor_scalar(
                mxk[0:1, 32:48], mxk[0:1, 32:48], -SCALE, None, ALU.mult),
                inc=[("v", 1)], wait=nr)
            nr = S_.op("gpsimd", lambda e: e.dma_start(
                out=rows_d[8192:8208].unsqueeze(0), in_=mxk[0:1, 32:48]),
                inc=[("gsp", 16)], wait=nr)
            nr = S_.op("gpsimd", lambda e: e.dma_start(
                out=ncol, in_=rows_d[8192:8208][None, :].to_broadcast([128, 16])),
                inc=[("gsp", 16)], wait=nr)
            ncol_tok = S_.aft("gsp")
            prevh = {}
            for h in range(16):
                kh_b = kh_b2[h % 2]
                vh_b = vh_b2[h % 2]
                klw = S_.op("sync", lambda e, h=h, kh_b=kh_b: e.dma_start(
                    out=kh_b, in_=kspill_d[h * 128:(h + 1) * 128, :]),
                    inc=[(f"sx{h % 2}", 16)],
                    wait=prevh.get(("ks", h % 2),
                                   S_.aft("gc0", "gc1", "a", "v")))
                vlw = []
                for half in range(2):
                    vlw += S_.op("sync", lambda e, h=h, half=half, vh_b=vh_b:
                                 e.dma_start(
                        out=vh_b.rearrange("p (tb c) -> p tb c", c=129)
                            [:, half * 8:(half + 1) * 8, 0:128],
                        in_=vspill_d[:, h * 128:(h + 1) * 128]
                            .rearrange("(tb p) c -> p tb c", p=128)
                            [:, half * 8:(half + 1) * 8, :]),
                        inc=[(f"sx{h % 2}", 16)],
                        wait=prevh.get(("ks", (h - 1) % 2), []) if half == 0 else ())
                for tc in range(2):
                    pv_first = []
                    for kt in range(16):
                        smm = S_.op("tensor", lambda e, h=h, tc=tc, kt=kt,
                                    kh_b=kh_b: e.matmul(
                            pb[kt % 2][:, 0:SLAB],
                            kh_b[:, kt * 128:(kt + 1) * 128],
                            q_all[h][:, tc * 512:(tc + 1) * 512],
                            start=True, stop=True), inc=[("t", 1)],
                            wait=(list(klw) + vlw + ncol_tok + m01_tok
                                  + S_.aft("v", "a"))
                            if kt == 0 and tc == 0 else
                            prevh.get(("ex", kt % 2), []))
                        ex = S_.op("scalar", lambda e, h=h, tc=tc, kt=kt:
                                   e.activation(
                            praw[kt % 2], pb[kt % 2][:, 0:SLAB], AF.Exp,
                            bias=ncol[:, h:h + 1], scale=SCALE),
                            inc=[("a", 1)],
                            wait=smm + prevh.get(("mul", kt % 2), []))
                        prevh[("ex", kt % 2)] = ex
                        mul = S_.op("vector", lambda e, h=h, tc=tc, kt=kt:
                                    e.tensor_mul(
                            pT[kt % 2], praw[kt % 2],
                            m01t[kt][:, tc * 512:(tc + 1) * 512]),
                            inc=[("v", 1)],
                            wait=ex + prevh.get(("pv", kt % 2), []))
                        prevh[("mul", kt % 2)] = mul
                        pvt = []
                        for qs in range(4):
                            pvt = S_.op("tensor", lambda e, kt=kt, qs=qs,
                                        vh_b=vh_b: e.matmul(
                                pb[2 + qs][:, 0:129],
                                pT[kt % 2][:, qs * 128:(qs + 1) * 128],
                                vh_b.rearrange("p (tb c) -> p tb c", c=129)[:, kt, :],
                                start=(kt == 0), stop=(kt == 15)),
                                inc=[("t", 1)] if kt == 15 else (),
                                wait=mul if qs == 0 else ())
                        prevh[("pv", kt % 2)] = pvt if kt == 15 else mul
                    for qs in range(4):
                        rc = S_.op("vector", lambda e, qs=qs: e.reciprocal(
                            rcp, pb[2 + qs][:, 128:129]), inc=[("v", 1)],
                            wait=pvt)
                        at = S_.op("vector", lambda e, qs=qs: e.tensor_scalar(
                            att[qs % 2], pb[2 + qs][:, 0:128], rcp, None,
                            ALU.mult), inc=[("v", 1)],
                            wait=prevh.get(("atv", qs % 2), []))
                        tt = S_.op("tensor", lambda e, qs=qs: e.transpose(
                            pbf[6][:, 0:128], att[qs % 2], idb),
                            inc=[("t", 1)], wait=at + prevh.get(("afm",), []))
                        prevh[("afm",)] = S_.op(
                            "vector", lambda e, h=h, tc=tc, qs=qs: e.tensor_copy(
                                afm[h][:, tc * 512 + qs * 128:
                                       tc * 512 + (qs + 1) * 128],
                                pbf[6][:, 0:128]), inc=[("v", 1)], wait=tt)
                        prevh[("atv", qs % 2)] = prevh[("afm",)]
                prevh[("ks", h % 2)] = smm + pvt
        dbg_tap("attn", [((h * 128, (h + 1) * 128), afm[h]) for h in range(16)])
        d_sc.close()

        e_sc = ExitStack()
        eent = e_sc.enter_context
        hid = [eent(nc.sbuf_tensor(f"hid{i}", [128, 1024], F32)).ap()
               for i in range(16)]
        ost2 = [eent(nc.sbuf_tensor(f"ost{i}", [128, 2048], F32)).ap()
                for i in range(2)]
        obf = [eent(nc.sbuf_tensor(f"obf{i}", [128, 2048], BF16)).ap()
               for i in range(2)]
        x1r = [eent(nc.sbuf_tensor(f"x1r{i}", [128, SLAB], F32)).ap()
               for i in range(2)]
        sqe = [eent(nc.sbuf_tensor(f"sqe{i}", [128, SLAB], F32)).ap()
               for i in range(2)]
        if _BUILD_OPTS["moe"]:
            prevo = {}
            for m in range(16):
                lw = []
                for kh in range(2):
                    lw += S_.op("sync", lambda e, kh=kh, m=m: e.dma_start(
                        out=ost2[m % 2][:, kh * 1024:(kh + 1) * 1024]
                            .rearrange("p (k c) -> p k c", k=8),
                        in_=owT_d[kh * 1024:(kh + 1) * 1024, m * 128:(m + 1) * 128]
                            .rearrange("(k p) c -> p k c", p=128)),
                        inc=[(f"sw{m % 2}", 16)],
                        wait=prevo.get(("cst", m % 2), S_.aft("v", "t"))
                        if kh == 0 else ())
                cst = S_.op("vector", lambda e, m=m: e.tensor_copy(
                    obf[m % 2], ost2[m % 2]), inc=[("v", 1)],
                    wait=lw[-1:] + prevo.get(("mm", m % 2), []))
                prevo[("cst", m % 2)] = cst
                for tc in range(2):
                    mmt = []
                    for k in range(16):
                        mmt = S_.op("tensor", lambda e, m=m, tc=tc, k=k: e.matmul(
                            pb[tc % 2][:, 0:SLAB],
                            obf[m % 2][:, k * 128:(k + 1) * 128],
                            afm[k][:, tc * 512:(tc + 1) * 512],
                            start=(k == 0), stop=(k == 15)),
                            inc=[("t", 1)] if k == 15 else (),
                            wait=(list(cst) + prevo.get(("ev", tc % 2), []))
                            if k == 0 else ())
                    if tc == 1:
                        prevo[("mm", m % 2)] = mmt
                    xo = S_.op("sync", lambda e, m=m, tc=tc: e.dma_start(
                        out=x1r[tc % 2],
                        in_=x1own_d[m * 128:(m + 1) * 128, tc * 512:(tc + 1) * 512]),
                        inc=[(f"sx{tc % 2}", 16)],
                        wait=prevo.get(("x1", tc % 2), S_.aft("v")))
                    ev = S_.op("vector", lambda e, m=m, tc=tc: e.tensor_add(
                        hid[m][:, tc * 512:(tc + 1) * 512], pb[tc % 2][:, 0:SLAB],
                        x1r[tc % 2]), inc=[("v", 1)], wait=mmt + list(xo))
                    prevo[("ev", tc % 2)] = ev
                    prevo[("x1", tc % 2)] = ev
            for tc in range(2):
                smt = []
                prevs = {}
                for m in range(16):
                    sq = S_.op("scalar", lambda e, m=m, tc=tc: e.activation(
                        sqe[m % 2], hid[m][:, tc * 512:(tc + 1) * 512], AF.Square),
                        inc=[("a", 1)],
                        wait=(S_.aft("v") if m == 0 else [])
                        + prevs.get(m % 2, []))
                    smt = S_.op("tensor", lambda e, m=m: e.matmul(
                        pb[6][0:1, 0:SLAB], vcol("ones"), sqe[m % 2],
                        start=(m == 0), stop=(m == 15)),
                        inc=[("t", 1)], wait=sq)
                    prevs[m % 2] = smt
                rv = S_.op("vector", lambda e: e.tensor_scalar(
                    rowb, pb[6][0:1, 0:SLAB], 1.0 / 2048.0, None, ALU.mult),
                    inc=[("v", 1)], wait=smt)
                rv = S_.op("scalar", lambda e: e.activation(
                    rowb, rowb, AF.Sqrt, bias=vcol("epsF")[0:1, :]),
                    inc=[("a", 1)], wait=rv)
                rv = S_.op("vector", lambda e: e.reciprocal(rowb, rowb),
                           inc=[("v", 1)], wait=rv)
                rd = S_.op("gpsimd", lambda e, tc=tc: e.dma_start(
                    out=rows_d[8320 + tc * 512: 8320 + (tc + 1) * 512].unsqueeze(0),
                    in_=rowb), inc=[("gsp", 16)], wait=rv)
                rb = S_.op("gpsimd", lambda e, tc=tc: e.dma_start(
                    out=bcb[0], in_=rows_d[8320 + tc * 512: 8320 + (tc + 1) * 512]
                        [None, :].to_broadcast([128, 512])),
                    inc=[("gsp", 16)], wait=rd)
                for m in range(16):
                    x2w = S_.op("vector", lambda e, m=m, tc=tc:
                                e.scalar_tensor_tensor(
                        x2t[m][:, tc * 512:(tc + 1) * 512],
                        hid[m][:, tc * 512:(tc + 1) * 512], vcol("l2", m),
                        bcb[0], ALU.mult, ALU.mult), inc=[("v", 1)],
                        wait=rb if m == 0 else ())
            for m in range(16):
                S_.op("gpsimd", lambda e, m=m: e.dma_start(
                    out=hidsp_d[m * 128:(m + 1) * 128, :], in_=hid[m]),
                    inc=[(f"gc{m % 2}", 16)], wait=S_.aft("v"))
        dbg_tap("x2", [((m * 128, (m + 1) * 128), x2t[m]) for m in range(16)])
        de_sc.close()
        e_sc.close()
        S_.barrier()

        g_sc = ExitStack()
        gent = g_sc.enter_context
        acc = [gent(nc.sbuf_tensor(f"acc{i}", [128, SLAB], F32)).ap()
               for i in range(16)]
        w1bf = [gent(nc.sbuf_tensor(f"w1bf{i}", [128, 1408], BF16)).ap()
                for i in range(16)]
        w1st = [gent(nc.sbuf_tensor(f"w1st{i}", [128, 1408], F32)).ap()
                for i in range(2)]
        gsb2 = [gent(nc.sbuf_tensor(f"gsb2{i}", [128, SLAB], BF16)).ap()
                for i in range(22)]
        ab2 = gsb2
        w2bfa = gent(nc.sbuf_tensor("w2bfa", [128, 22 * SLAB], BF16)).ap()
        w2bf = [w2bfa[:, i * SLAB:(i + 1) * SLAB] for i in range(22)]
        Wb = [gent(nc.sbuf_tensor(f"Wb{i}", [128, SLAB], F32)).ap()
              for i in range(2)]
        gwb = gent(nc.sbuf_tensor("gwb", [128, 256], BF16)).ap()
        gwst = gent(nc.sbuf_tensor("gwst", [128, 256], F32)).ap()
        lbuf = gent(nc.sbuf_tensor("lbuf", [128, 16], F32)).ap()
        ebuf = gent(nc.sbuf_tensor("ebuf", [128, 16], F32)).ap()
        curb = gent(nc.sbuf_tensor("curb", [128, 16], F32)).ap()
        selb = gent(nc.sbuf_tensor("selb", [128, 16], F32)).ap()
        eqb = gent(nc.sbuf_tensor("eqb", [128, 16], F32)).ap()
        mxc = gent(nc.sbuf_tensor("mxc", [128, 1], F32)).ap()
        wfp = gent(nc.sbuf_tensor("wfp", [128, 128], BF16)).ap()
        wfT = gent(nc.sbuf_tensor("wfT", [16, 1024], F32)).ap()
        hrl = [gent(nc.sbuf_tensor(f"hrl{i}", [128, SLAB], F32)).ap()
               for i in range(2)]
        orow = gent(nc.sbuf_tensor("orow", [1, 512], F32)).ap()
        orow2 = gent(nc.sbuf_tensor("orow2", [1, 512], F32)).ap()
        obuf = [w1st[0][:, 512:1024], w1st[1][:, 512:1024]]
        hr2 = Wb
        sqg = [w1st[0][:, 0:512], w1st[1][:, 0:512]]
        if _BUILD_OPTS["moe"]:
            gw16 = gwst
            glw = []
            for k in range(16):
                glw += S_.op("sync", lambda e, k=k: e.dma_start(
                    out=gw16[:, k * 16:(k + 1) * 16],
                    in_=gwT_d[k * 128:(k + 1) * 128, :]),
                    inc=[("sw0", 16)], wait=S_.aft("v", "a") if k == 0 else ())
            gcst = S_.op("vector", lambda e: e.tensor_copy(gwb, gw16),
                         inc=[("v", 1)], wait=glw[-1:])
            S_.op("vector", lambda e: e.memset(wfp, 0.0), inc=[("v", 1)])

            prevr = {}
            for ti in range(8):
                rmm = []
                for k in range(16):
                    rmm = S_.op("tensor", lambda e, ti=ti, k=k: e.matmul(
                        pb[4][:, 0:16], x2t[k][:, ti * 128:(ti + 1) * 128],
                        gwb[:, k * 16:(k + 1) * 16],
                        start=(k == 0), stop=(k == 15)),
                        inc=[("t", 1)] if k == 15 else (),
                        wait=(gcst + S_.aft("v", "a")) if (ti == 0 and k == 0)
                        else (prevr.get("lb", []) if k == 0 else ()))
                lb = S_.op("vector", lambda e: e.tensor_copy(lbuf, pb[4][:, 0:16]),
                           inc=[("v", 1)], wait=rmm)
                prevr["lb"] = lb
                ng = S_.op("vector", lambda e: e.reduce_max(
                    mxc, lbuf, axis=AX, negate=True), inc=[("v", 1)], wait=())
                S_.op("scalar", lambda e: e.activation(
                    ebuf, lbuf, AF.Exp, bias=mxc), inc=[("a", 1)], wait=ng)
                S_.op("vector", lambda e: e.tensor_copy(curb, lbuf),
                      inc=[("v", 1)], wait=())
                S_.op("vector", lambda e: e.memset(selb, 0.0), inc=[("v", 1)])
                for it in range(4):
                    S_.op("vector", lambda e: e.reduce_max(mxc, curb, axis=AX),
                          inc=[("v", 1)], wait=S_.aft("a") if it == 0 else ())
                    S_.op("vector", lambda e: e.tensor_scalar(
                        eqb, curb, mxc, None, ALU.is_equal), inc=[("v", 1)], wait=())
                    S_.op("vector", lambda e: e.tensor_add(selb, selb, eqb),
                          inc=[("v", 1)], wait=())
                    if it < 3:
                        S_.op("vector", lambda e: e.scalar_tensor_tensor(
                            curb, eqb, -1.0e30, curb, ALU.mult, ALU.add),
                            inc=[("v", 1)], wait=())
                S_.op("vector", lambda e: e.tensor_scalar(
                    selb, selb, 1.0, None, ALU.min), inc=[("v", 1)], wait=())
                S_.op("vector", lambda e: e.tensor_mul(eqb, ebuf, selb),
                      inc=[("v", 1)], wait=S_.aft("a"))
                S_.op("vector", lambda e: e.tensor_reduce(
                    mxc, eqb, AX, ALU.add), inc=[("v", 1)], wait=())
                S_.op("vector", lambda e: e.reciprocal(mxc, mxc),
                      inc=[("v", 1)], wait=())
                wf = S_.op("vector", lambda e: e.tensor_scalar(
                    wfp[:, 0:16], eqb, mxc, None, ALU.mult),
                    inc=[("v", 1)], wait=prevr.get("tt", []))
                ttk = S_.op("tensor", lambda e: e.transpose(
                    pbf[5][:, 0:128], wfp, idb), inc=[("t", 1)],
                    wait=wf + prevr.get("wfev", []))
                prevr["tt"] = ttk
                prevr["wfev"] = S_.op("vector", lambda e, ti=ti: e.tensor_copy(
                    wfT[:, ti * 128:(ti + 1) * 128], pbf[5][0:16, 0:128]),
                    inc=[("v", 1)], wait=ttk)
            wsp = S_.op("gpsimd", lambda e: e.dma_start(
                out=wfsp_d.rearrange("(a b) -> a b", a=16), in_=wfT),
                inc=[("gsp", 16)], wait=prevr["wfev"])
            dbg_tap("wf", [((0, 16), wfsp_d.rearrange("(a b) -> a b", a=16))])

            ccache = {}
            for tc in range(2):
                if tc == 1:
                    S_.barrier()
                prevg = {}
                first_acc = [True] * 16
                for ep in range(9):
                    shared = (ep == 8)
                    for ei in range(2):
                        eg = ep * 2 + ei
                        if not shared:
                            prevg[("wb", ei)] = S_.op(
                                "gpsimd", lambda e, eg=eg, ei=ei, tc=tc: e.dma_start(
                                    out=Wb[ei],
                                    in_=wfsp_d[eg * 1024 + tc * 512:
                                               eg * 1024 + (tc + 1) * 512]
                                    [None, :].to_broadcast([128, 512])),
                                inc=[(f"wb{ei}", 16)],
                                wait=wsp + prevg.get(("wbuse", ei), []))
                        for ih in range(2):
                            q4 = ei * 2 + ih
                            is_g = (q4 < 2) if shared else (ih == 0)
                            castk = {}
                            for kb in range(16):
                                if shared:
                                    wsrc = shgT_d[kb * 128:(kb + 1) * 128,
                                                  q4 * 1408:(q4 + 1) * 1408]
                                    wdst = shgc_d[kb * 128:(kb + 1) * 128,
                                                  q4 * 1408:(q4 + 1) * 1408]
                                else:
                                    wsrc = w1T_d[eg, kb * 128:(kb + 1) * 128,
                                                 ih * 1408:(ih + 1) * 1408]
                                    wdst = w1c_d[eg, kb * 128:(kb + 1) * 128,
                                                 ih * 1408:(ih + 1) * 1408]
                                if tc == 0:
                                    lw = S_.op("sync", lambda e, wsrc=wsrc, kb=kb:
                                               e.dma_start(
                                        out=w1st[kb % 2], in_=wsrc),
                                        inc=[(f"sw{kb % 2}", 16)],
                                        wait=prevg.get(("w1c", kb % 2),
                                                       S_.aft("v", "a", "t",
                                                              "gc0", "gc1")))
                                    ceng = "vector" if kb % 2 == 0 else "scalar"
                                    cop = ((lambda e, kb=kb: e.tensor_copy(
                                        w1bf[kb], w1st[kb % 2]))
                                        if ceng == "vector" else
                                        (lambda e, kb=kb: e.activation(
                                            w1bf[kb], w1st[kb % 2], AF.Copy)))
                                    ck = S_.op(ceng, cop,
                                               wait=list(lw)
                                               + prevg.get(("w1u", kb), [])
                                               + (S_.aft("pc0", "pc1")
                                                  if kb < 2 else []))
                                    prevg[("w1c", kb % 2)] = ck
                                    S_.op("gpsimd", lambda e, kb=kb, wdst=wdst:
                                          e.dma_start(out=wdst, in_=w1bf[kb]),
                                          inc=[(f"pc{kb % 2}", 16)], wait=ck)
                                else:
                                    ck = S_.op("sync", lambda e, kb=kb, wdst=wdst:
                                               e.dma_start(
                                        out=w1bf[kb], in_=wdst),
                                        inc=[(f"sw{kb % 2}", 16)],
                                        wait=prevg.get(("w1u", kb),
                                                       S_.aft("v", "a", "t")))
                                    prevg[("w1c", kb % 2)] = ck
                                castk[kb] = ck
                            for wv, irng in ((0, range(0, 3)), (1, range(3, 11))):
                                mm_tok = {}
                                for kb in range(16):
                                    for i in irng:
                                        bnk = i if wv == 0 else i - 3
                                        w = []
                                        if i == irng.start:
                                            if wv == 0:
                                                if tc == 0:
                                                    w += castk[kb]
                                                elif kb == 0:
                                                    w += (castk[14]
                                                          + castk[15])
                                            if kb == 0:
                                                for ii in irng:
                                                    b2 = ii if wv == 0 else ii - 3
                                                    w += prevg.get(
                                                        ("bev", b2),
                                                        S_.aft("v", "a"))
                                        gm = S_.op("tensor", lambda e, i=i, kb=kb,
                                                   tc=tc, bnk=bnk: e.matmul(
                                            pb[bnk][:, 0:SLAB],
                                            w1bf[kb][:, i * 128:(i + 1) * 128],
                                            x2t[kb][:, tc * 512:(tc + 1) * 512],
                                            start=(kb == 0), stop=(kb == 15)),
                                            inc=[("t", 1)]
                                            if (kb == 15 or (wv == 1 and i == 10))
                                            else (), wait=w)
                                        if kb == 15:
                                            mm_tok[i] = gm
                                        if wv == 1 and i == 10:
                                            prevg[("w1u", kb)] = gm
                                for i in irng:
                                    bnk = i if wv == 0 else i - 3
                                    idx2 = ((q4 % 2) if shared else ei) * 11 + i
                                    if is_g:
                                        ev = S_.op("scalar", lambda e, idx2=idx2,
                                                   bnk=bnk: e.activation(
                                            gsb2[idx2], pb[bnk][:, 0:SLAB],
                                            AF.Silu), inc=[("a", 1)],
                                            wait=mm_tok[i]
                                            + prevg.get(("gsb", idx2), []))
                                        prevg[("gtok", idx2)] = ev
                                    else:
                                        if not shared:
                                            ev = S_.op("vector", lambda e,
                                                       idx2=idx2, ei=ei:
                                                       e.tensor_mul(
                                                gsb2[idx2], gsb2[idx2], Wb[ei]),
                                                inc=[("v", 1)],
                                                wait=prevg.get(("wb", ei), [])
                                                + prevg.get(("gtok", idx2), []))
                                            prevg[("wbuse", ei)] = ev
                                            gw_ = ()
                                        else:
                                            gw_ = prevg.get(("gtok", idx2), [])
                                        ev = S_.op("vector", lambda e, idx2=idx2,
                                                   bnk=bnk: e.tensor_mul(
                                            ab2[idx2], gsb2[idx2],
                                            pb[bnk][:, 0:SLAB]),
                                            inc=[("v", 1)],
                                            wait=mm_tok[i] + list(gw_))
                                        prevg[("gsb", idx2)] = ev
                                    prevg[("bev", bnk)] = ev
                    for hq in range(4):
                        castj = {}
                        ld_fin = []
                        if shared:
                            grps = [(2 * q, 2) for q in range(11)]
                        else:
                            grps = ([(2 * q, 2) for q in range(5)] + [(10, 1)]
                                    + [(11 + 2 * q, 2) for q in range(5)]
                                    + [(21, 1)])
                        for lp, (j0, cnt) in enumerate(grps):
                            par = lp % 2
                            if shared:
                                wsrc = shdT_d[j0 * 128:(j0 + cnt) * 128,
                                              hq * 512:(hq + 1) * 512]
                                wdst = shdc_d[j0 * 128:(j0 + cnt) * 128,
                                              hq * 512:(hq + 1) * 512]
                            else:
                                e2 = ep * 2 + j0 // 11
                                kb2 = j0 % 11
                                wsrc = w2T_d[e2, kb2 * 128:(kb2 + cnt) * 128,
                                             hq * 512:(hq + 1) * 512]
                                wdst = w2c_d[e2, kb2 * 128:(kb2 + cnt) * 128,
                                             hq * 512:(hq + 1) * 512]
                            if cnt == 2:
                                wsrc = wsrc.rearrange("(k p) c -> p k c", p=128)
                                wdst = wdst.rearrange("(k p) c -> p k c", p=128)
                            bslc = w2bfa[:, j0 * 512:(j0 + cnt) * 512]
                            bdst = (bslc.rearrange("p (k c) -> p k c", k=cnt)
                                    if cnt == 2 else bslc)
                            if tc == 0:
                                sdst = (w1st[par][:, 0:cnt * 512]
                                        .rearrange("p (k c) -> p k c", k=cnt)
                                        if cnt == 2 else w1st[par][:, 0:512])
                                lw = S_.op("sync", lambda e, wsrc=wsrc,
                                           sdst=sdst: e.dma_start(
                                    out=sdst, in_=wsrc),
                                    inc=[(f"sv{par}", 16)],
                                    wait=prevg.get(("w1c", par),
                                                   S_.aft("v", "a", "t",
                                                          "gc0", "gc1")))
                                ceng = "vector" if par == 0 else "scalar"
                                ssrc = w1st[par][:, 0:cnt * 512]
                                cop = ((lambda e, bslc=bslc, ssrc=ssrc:
                                        e.tensor_copy(bslc, ssrc))
                                       if ceng == "vector" else
                                       (lambda e, bslc=bslc, ssrc=ssrc:
                                        e.activation(bslc, ssrc, AF.Copy)))
                                ck = S_.op(ceng, cop,
                                           wait=list(lw)
                                           + prevg.get(("w2u", j0 + cnt - 1), [])
                                           + prevg.get(("w2u", j0), [])
                                           + (S_.aft("pc0", "pc1")
                                              if lp < 2 else []))
                                prevg[("w1c", par)] = ck
                                S_.op("gpsimd", lambda e, bdst=bdst, wdst=wdst:
                                      e.dma_start(out=wdst, in_=bdst),
                                      inc=[(f"pc{par}", 16)], wait=ck)
                            else:
                                ck = S_.op("sync", lambda e, bdst=bdst,
                                           wdst=wdst: e.dma_start(
                                    out=bdst, in_=wdst),
                                    inc=[(f"sv{par}", 16)],
                                    wait=prevg.get(("w2u", j0 + cnt - 1),
                                                   S_.aft("v", "a", "t"))
                                    + prevg.get(("w2u", j0), []))
                                prevg[("w1c", par)] = ck
                            ld_fin = (ld_fin + list(ck))[-2:]
                            for jj in range(j0, j0 + cnt):
                                castj[jj] = ck
                        dm_tok = {}
                        for j in range(22):
                            for hc in range(4):
                                w = []
                                if hc == 0:
                                    if tc == 0:
                                        w += castj[j]
                                    elif j == 0:
                                        w += ld_fin
                                    if hq == 0:
                                        w += prevg.get(("gsb", j), [])
                                    if j == 0:
                                        for b2 in range(4, 8):
                                            w += prevg.get(("bev", b2),
                                                           S_.aft("v", "a"))
                                dm = S_.op("tensor", lambda e, j=j, hc=hc:
                                           e.matmul(
                                    pb[4 + hc][:, 0:SLAB],
                                    w2bf[j][:, hc * 128:(hc + 1) * 128],
                                    ab2[j], start=(j == 0), stop=(j == 21)),
                                    inc=[("t", 1)] if (hc == 3 or j == 21)
                                    else (), wait=w)
                                if hc == 3:
                                    prevg[("w2u", j)] = dm
                                if j == 21:
                                    dm_tok[hc] = dm
                        for hc in range(4):
                            h = hq * 4 + hc
                            if first_acc[h]:
                                ev = S_.op("vector", lambda e, h=h, hc=hc:
                                           e.tensor_copy(
                                    acc[h], pb[4 + hc][:, 0:SLAB]),
                                    inc=[("v", 1)], wait=dm_tok[hc])
                                first_acc[h] = False
                            else:
                                ev = S_.op("vector", lambda e, h=h, hc=hc:
                                           e.tensor_add(
                                    acc[h], acc[h], pb[4 + hc][:, 0:SLAB]),
                                    inc=[("v", 1)], wait=dm_tok[hc])
                            prevg[("bev", 4 + hc)] = ev
                prevf = {}
                smt = smt2 = []
                for m in range(16):
                    hl = S_.op("sync", lambda e, m=m, tc=tc: e.dma_start(
                        out=hrl[m % 2],
                        in_=hidsp_d[m * 128:(m + 1) * 128, tc * 512:(tc + 1) * 512]),
                        inc=[(f"sx{m % 2}", 16)],
                        wait=prevf.get(("h", m % 2), S_.aft("v")))
                    xl2 = S_.op("sync", lambda e, m=m, tc=tc: e.dma_start(
                        out=hr2[m % 2],
                        in_=x1own_d[m * 128:(m + 1) * 128, tc * 512:(tc + 1) * 512]),
                        inc=[(f"sv{m % 2}", 16)],
                        wait=prevf.get(("x", m % 2),
                                       prevg.get(("w1c", m % 2), ())))
                    a1 = S_.op("vector", lambda e, m=m: e.tensor_add(
                        acc[m], acc[m], hrl[m % 2]), inc=[("v", 1)],
                        wait=list(hl) + list(xl2))
                    a1 = S_.op("vector", lambda e, m=m: e.tensor_add(
                        acc[m], acc[m], hr2[m % 2]), inc=[("v", 1)], wait=())
                    prevf[("h", m % 2)] = a1
                    prevf[("x", m % 2)] = a1
                    sq = S_.op("scalar", lambda e, m=m: e.activation(
                        sqg[m % 2], acc[m], AF.Square), inc=[("a", 1)],
                        wait=a1 + prevf.get(("sq", m % 2), []))
                    smt = S_.op("tensor", lambda e, m=m: e.matmul(
                        pb[6][0:1, 0:SLAB], vcol("ones"), acc[m],
                        start=(m == 0), stop=(m == 15)),
                        inc=[("t", 1)], wait=a1)
                    smt2 = S_.op("tensor", lambda e, m=m: e.matmul(
                        pb[7][0:1, 0:SLAB], vcol("ones"), sqg[m % 2],
                        start=(m == 0), stop=(m == 15)),
                        inc=[("t", 1)], wait=sq)
                    prevf[("sq", m % 2)] = smt2
                rv = S_.op("vector", lambda e: e.tensor_scalar(
                    rowb, pb[6][0:1, 0:SLAB], 1.0 / 2048.0, None, ALU.mult),
                    inc=[("v", 1)], wait=smt + smt2)
                rv2 = S_.op("vector", lambda e: e.tensor_scalar(
                    rowb2, pb[7][0:1, 0:SLAB], 1.0 / 2048.0, None, ALU.mult),
                    inc=[("v", 1)], wait=())
                sq3 = S_.op("scalar", lambda e: e.activation(
                    orow, rowb, AF.Square), inc=[("a", 1)], wait=rv)
                vv = S_.op("vector", lambda e: e.tensor_sub(
                    rowb2, rowb2, orow), inc=[("v", 1)], wait=sq3)
                s1 = S_.op("scalar", lambda e: e.activation(
                    orow, rowb2, AF.Sqrt, bias=vcol("epsA")[0:1, :]),
                    inc=[("a", 1)], wait=vv)
                r1 = S_.op("vector", lambda e: e.reciprocal(orow, orow),
                           inc=[("v", 1)], wait=s1)
                rd = S_.op("gpsimd", lambda e, tc=tc: e.dma_start(
                    out=rows_d[9344 + tc * 512: 9344 + (tc + 1) * 512].unsqueeze(0),
                    in_=rowb), inc=[("gsp", 16)], wait=rv)
                rd2 = S_.op("gpsimd", lambda e, tc=tc: e.dma_start(
                    out=rows_d[10368 + tc * 512: 10368 + (tc + 1) * 512].unsqueeze(0),
                    in_=orow), inc=[("gsp", 16)], wait=r1)
                b1 = S_.op("gpsimd", lambda e, tc=tc: e.dma_start(
                    out=bcb[0], in_=rows_d[9344 + tc * 512: 9344 + (tc + 1) * 512]
                        [None, :].to_broadcast([128, 512])),
                    inc=[("gsp", 16)], wait=S_.aft("gsp") + S_.aft("v"))
                b2 = S_.op("gpsimd", lambda e, tc=tc: e.dma_start(
                    out=bcb[1], in_=rows_d[10368 + tc * 512: 10368 + (tc + 1) * 512]
                        [None, :].to_broadcast([128, 512])),
                    inc=[("gsp", 16)], wait=())
                for m in range(16):
                    o1 = S_.op("vector", lambda e, m=m: e.tensor_sub(
                        obuf[m % 2], acc[m], bcb[0]), inc=[("v", 1)],
                        wait=(list(b1) + list(b2) if m == 0 else [])
                        + prevf.get(("ob", m % 2), []))
                    o1 = S_.op("vector", lambda e, m=m: e.tensor_mul(
                        obuf[m % 2], obuf[m % 2], bcb[1]), inc=[("v", 1)], wait=())
                    prevf[("ob", m % 2)] = S_.op(
                        "gpsimd", lambda e, m=m, tc=tc: e.dma_start(
                            out=out_d[m * 128:(m + 1) * 128,
                                      tc * 512:(tc + 1) * 512],
                            in_=obuf[m % 2]), inc=[(f"gc{m % 2}", 16)], wait=o1)
        g_sc.close()
        x2_sc.close()
        S_.wait("gpsimd", *S_.aft("gc0", "gc1", "st"))
        S_.barrier()

        with nc.Block() as block:
            @block.sync
            def _(e):
                for fn in S_.q["sync"]:
                    fn(e)

            @block.tensor
            def _(e):
                for fn in S_.q["tensor"]:
                    fn(e)

            @block.vector
            def _(e):
                for fn in S_.q["vector"]:
                    fn(e)

            @block.scalar
            def _(e):
                for fn in S_.q["scalar"]:
                    fn(e)

            @block.gpsimd
            def _(e):
                for fn in S_.q["gpsimd"]:
                    fn(e)
    return nc


_CACHE = {}


def kernel(**inputs):
    in_maps, idx = _host_prep(inputs)
    key = ("main", _BUILD_OPTS["dbg"], _BUILD_OPTS["nslab"], _BUILD_OPTS["ssd"], _BUILD_OPTS["xg"], _BUILD_OPTS["op"], _BUILD_OPTS["inproj"], _BUILD_OPTS["cache"], _BUILD_OPTS["nmg"], _BUILD_OPTS["heads"], _BUILD_OPTS["tpose"])
    if key not in _CACHE:
        _CACHE[key] = _build(idx)
    nc = _CACHE[key]
    res = run_bass_kernel_spmd(nc, in_maps, core_ids=list(range(8)))
    full = np.zeros((4, S, HID), np.float32)
    for c in range(8):
        b, r = c // 2, c % 2
        full[b, r * 1024:(r + 1) * 1024, :] = np.asarray(res.results[c]["out"]).T
    if _BUILD_OPTS["dbg"] is not None:
        kernel.dbg = [np.asarray(res.results[c].get("dbg")) for c in range(8)]
    return full
